# version 12
# baseline (speedup 1.0000x reference)
import numpy as np
import ml_dtypes
import concourse.bass as bass
import concourse.mybir as mybir
from concourse.bass_utils import run_bass_kernel_spmd

F32 = mybir.dt.float32
BF16 = mybir.dt.bfloat16
AF = mybir.ActivationFunctionType
ALU = mybir.AluOpType
AX = mybir.AxisListType

NCORES = 8
B, S, D = 4, 2048, 2048
TOK = 1024
NT = TOK // 128
EPS = 1e-5



class TBuf:
    __slots__ = ("name", "w_ev", "r_evs", "unsynced", "pending")

    def __init__(self, name=""):
        self.name = name
        self.w_ev = None
        self.r_evs = []
        self.unsynced = set()
        self.pending = None


class Prog:
    ENGINES = ("sync", "scalar", "tensor", "vector", "gpsimd")

    def __init__(self, nc, sems):
        self.nc = nc
        self.sems = sems
        self.count = {e: 0 for e in self.ENGINES}
        self.hwm = {e: {d: 0 for d in self.ENGINES} for e in self.ENGINES}
        self.code = {e: [] for e in self.ENGINES}

    def do(self, eng, fn, deps=(), reads=(), writes=(), signal=False, dma=False):
        dep_list = [(d, False) for d in deps if d is not None]
        for bu in reads:
            if bu.pending is not None and bu.pending != eng:
                raise RuntimeError(f"read of {bu.name} with pending write on {bu.pending}")
            if bu.w_ev is not None:
                dep_list.append((bu.w_ev, False))
        for bu in writes:
            bad = bu.unsynced - {eng}
            if bad:
                raise RuntimeError(f"write to {bu.name} with unsynced reads on {bad}")
            if bu.w_ev is not None:
                dep_list.append((bu.w_ev, False))
            dep_list.extend((r, True) for r in bu.r_evs)
        waits = []
        for ((de, dv, ddma), war) in dep_list:
            if de == eng and war and not ddma:
                continue
            if self.hwm[eng][de] >= dv:
                continue
            self.hwm[eng][de] = dv
            waits.append((de, dv))
        inc = 16 if dma else 1
        ev = None
        if signal or dma:
            self.count[eng] += inc
            ev = (eng, self.count[eng], dma)
        self.code[eng].append((waits, fn, self.sems[eng] if ev else None,
                               inc if ev else 0))
        for bu in reads:
            if ev is not None:
                bu.r_evs.append(ev)
                bu.unsynced.discard(eng)
            else:
                bu.unsynced.add(eng)
        for bu in writes:
            if ev is None:
                if bu.pending is not None and bu.pending != eng:
                    raise RuntimeError(f"cross-engine pending write to {bu.name}")
                bu.pending = eng
            else:
                bu.w_ev = ev
                bu.r_evs = []
                bu.unsynced = set()
                bu.pending = None
        return ev

    def full_barrier(self):
        evs = {e: self.barrier_event(e) for e in self.ENGINES}
        for e in self.ENGINES:
            deps = [evs[d] for d in self.ENGINES if d != e and evs[d] is not None]
            if deps:
                self.do(e, lambda en: None, deps=deps)

    def barrier_event(self, eng):
        return (eng, self.count[eng], True) if self.count[eng] else None

    def emit(self, blk):
        def make(engname):
            def body(eng):
                for waits, fn, sem, inc in self.code[engname]:
                    for (de, dv) in waits:
                        eng.wait_ge(self.sems[de], dv)
                    inst = fn(eng)
                    if sem is not None:
                        if inst is None:
                            inst = eng.nop()
                        inst.then_inc(sem, inc)
            return body
        blk.sync(make("sync"))
        blk.scalar(make("scalar"))
        blk.tensor(make("tensor"))
        blk.vector(make("vector"))
        blk.gpsimd(make("gpsimd"))


def force(ev):
    return (ev[0], ev[1], True) if ev is not None else None


def fnew(nc):
    ctxs = []
    sems = {}
    for e in Prog.ENGINES:
        cm = nc.semaphore(f"S_{e}")
        sems[e] = cm.__enter__()
        ctxs.append(cm)
    return sems, ctxs


def ln_stats(p, xt, sqscr, mv, rstd, mneg, bufs, xbuf=None, epsc=None,
             epsbuf=None, deps=(), width=D):
    b_stats, b_mv, b_rs = bufs
    n = width // 512
    rd = [xbuf] if xbuf is not None else []
    ev_s = p.do("vector", lambda e: e.reduce_sum(out=mv[:, 0:1], in_=xt, axis=AX.X),
                deps=deps, reads=rd, writes=[b_mv], signal=True)
    ev_q = None
    for i in range(n):
        ev_q = p.do("scalar", (lambda i: lambda e: e.activation(
            sqscr[:, 0:512], xt[:, i * 512:(i + 1) * 512], AF.Square,
            accum_out=mv[:, 1 + i:2 + i]))(i),
            deps=deps, reads=rd + [b_stats], writes=[b_mv, b_stats], signal=True)
    evqs = p.do("vector", (lambda n: lambda e: e.reduce_sum(
        out=mv[:, 5:6], in_=mv[:, 1:1 + n], axis=AX.X))(n),
        deps=[ev_q], writes=[b_mv], signal=True)
    ev_m = p.do("vector", (lambda width: lambda e: e.tensor_scalar_mul(
        out=mneg, in0=mv[:, 0:1], scalar1=-1.0 / width))(width),
        deps=[force(ev_s)], writes=[b_rs], signal=True)
    ev = p.do("vector", lambda e: e.tensor_tensor(out=mv[:, 6:7], in0=mneg,
                                                  in1=mneg, op=ALU.mult),
              deps=[force(ev_m)], reads=[b_rs], writes=[b_mv], signal=True)
    ev = p.do("vector", (lambda width: lambda e: e.scalar_tensor_tensor(
        out=mv[:, 6:7], in0=mv[:, 5:6], scalar=1.0 / width, in1=mv[:, 6:7],
        op0=ALU.mult, op1=ALU.subtract))(width),
        deps=[force(ev), force(evqs)], writes=[b_mv], signal=True)
    ev1 = p.do("scalar", lambda e: e.activation(rstd, mv[:, 6:7], AF.Ln, bias=epsc),
               deps=[ev], reads=[b_mv] + ([epsbuf] if epsbuf else []),
               writes=[b_rs], signal=True)
    ev1 = p.do("scalar", lambda e: e.activation(rstd, rstd, AF.Exp, scale=-0.5),
               deps=[ev1], writes=[b_rs], signal=True)
    return ev1, ev_m



H1 = 5461
NH1 = 43
H1P = NH1 * 128
FLATC = (H1 * D) // 128


def build_l1_ffn():
    nc = bass.Bass("TRN2", target_bir_lowering=False)
    sems, _ctx = fnew(nc)

    x_in = nc.dram_tensor("x", [TOK, D], F32, kind="ExternalInput").ap()
    w1 = nc.dram_tensor("w1", [H1, D], F32, kind="ExternalInput").ap()
    w3 = nc.dram_tensor("w3", [H1, D], F32, kind="ExternalInput").ap()
    w2 = nc.dram_tensor("w2", [D, H1], F32, kind="ExternalInput").ap()
    ones_in = nc.dram_tensor("ones", [128, 128], F32, kind="ExternalInput").ap()
    ident_in = nc.dram_tensor("ident", [128, 128], BF16, kind="ExternalInput").ap()
    out = nc.dram_tensor("out", [TOK, D], F32, kind="ExternalOutput").ap()

    u2t_dram = nc.dram_tensor("u2t", [16, NH1, 128, 128], BF16).ap()

    import contextlib
    with contextlib.ExitStack() as stk:
        sb = lambda *a: stk.enter_context(nc.sbuf_tensor(*a))
        xhat = sb("xhat", [128, 16 * TOK], BF16)
        gfm = sb("gfm", [128, NH1 * TOK], BF16)
        wstg = sb("wstg", [128, 2 * D], F32)
        ubf = sb("ubf", [128, 2 * D], BF16)
        ut = sb("ut", [128, 2 * D], BF16)
        u2row = sb("u2row", [128, H1P], BF16)
        xtb = sb("xt", [128, D], F32)
        ytm = sb("ytm", [128, 4 * D], BF16)
        yw = sb("yw", [128, D], F32)
        sil = sb("sil", [128, 512], F32)
        mv = sb("mv", [128, 8], F32)
        sc = sb("sc", [128, 16], F32)
        acc = sb("acc", [128, 4], F32)
        cst = sb("cst", [128, 2], F32)
        onesb = sb("onesb", [128, 128], F32)
        identb = sb("identb", [128, 128], BF16)
        PS = stk.enter_context(nc.psum_tensor("PS", [128, 4096], F32))
        blk = stk.enter_context(nc.Block())
        p = Prog(nc, sems)
        bank = lambda i: PS[:, i * 512:(i + 1) * 512]
        bankb = lambda i: PS[:, i * 512:(i + 1) * 512].bitcast(BF16)

        b_stage = [TBuf("stg0"), TBuf("stg1")]
        b_acc = TBuf("acc"); b_sc = TBuf("sc")
        b_ubf = TBuf("ubf"); b_ut = TBuf("ut"); b_u2r = TBuf("u2r")
        b_xhat = TBuf("xhat"); b_gfm = TBuf("gfm")
        b_xt = TBuf("xt"); b_ytm = TBuf("ytm"); b_yw = TBuf("yw")
        b_sil = TBuf("sil"); b_mv = TBuf("mv")
        b_rs = TBuf("rs"); b_cst = TBuf("cst")
        lnbufs = (b_ut, b_mv, b_rs)
        b_ps = [TBuf(f"ps{i}") for i in range(8)]

        p.do("vector", lambda e: e.memset(cst[:, 0:1], EPS), writes=[b_cst], signal=True)
        evo = p.do("sync", lambda e: e.dma_start(out=onesb[:], in_=ones_in), dma=True)
        evi = p.do("sync", lambda e: e.dma_start(out=identb[:], in_=ident_in), dma=True)
        p.do("vector", lambda e: e.memset(acc[:], 0.0), writes=[b_acc], signal=True)

        for j, w in enumerate([w1, w3, w2]):
            wf = w.flatten().rearrange("(p f) -> p f", p=128)
            c0 = 0
            while c0 < FLATC:
                c1 = min(c0 + 2032, FLATC)
                slot = 0
                sb = b_stage[slot]
                st = wstg[:, slot * 2 * D: slot * 2 * D + (c1 - c0)]
                evd = p.do("sync", (lambda wf, c0, c1, st: lambda e: e.dma_start(
                    out=st, in_=wf[:, c0:c1]))(wf, c0, c1, st),
                    writes=[sb], dma=True)
                evr = p.do("vector", (lambda st, j: lambda e: e.reduce_sum(
                    out=sc[:, 11 + j:12 + j], in_=st, axis=AX.X,
                    apply_absolute_value=True))(st, j),
                    deps=[evd], reads=[sb], writes=[b_sc], signal=True)
                p.do("vector", (lambda j: lambda e: e.tensor_tensor(
                    out=acc[:, j:j + 1], in0=acc[:, j:j + 1],
                    in1=sc[:, 11 + j:12 + j], op=ALU.add))(j),
                    deps=[force(evr)], reads=[b_sc], writes=[b_acc], signal=True)
                c0 = c1
        ev = p.do("tensor", lambda e: e.matmul(PS[0:1, 0:4], onesb[:, 0:1], acc[:, 0:4],
                                               start=True, stop=True),
                  deps=[evo], reads=[b_acc], writes=[b_ps[0]], signal=True)
        ev = p.do("vector", lambda e: e.tensor_scalar_mul(
            out=sc[0:1, 0:4], in0=PS[0:1, 0:4], scalar1=1.0 / (H1 * D)),
            deps=[ev], reads=[b_ps[0]], writes=[b_sc], signal=True)
        ev = p.do("tensor", lambda e: e.matmul(PS[:, 0:4], onesb[0:1, :], sc[0:1, 0:4],
                                               start=True, stop=True),
                  deps=[ev], reads=[b_sc], writes=[b_ps[0]], signal=True)
        ev = p.do("vector", lambda e: e.tensor_scalar_mul(out=sc[:, 0:3],
                                                          in0=PS[:, 0:3], scalar1=0.5),
                  deps=[ev], reads=[b_ps[0]], writes=[b_sc], signal=True)
        for j, (cp, cn) in enumerate([(3, 4), (5, 6), (7, 8)]):
            ev = p.do("vector", (lambda j, cp: lambda e: e.tensor_scalar_mul(
                out=sc[:, cp:cp + 1], in0=PS[:, j:j + 1], scalar1=0.5))(j, cp),
                deps=[ev], reads=[b_ps[0]], writes=[b_sc], signal=True)
            ev = p.do("vector", (lambda cp, cn: lambda e: e.tensor_scalar_mul(
                out=sc[:, cn:cn + 1], in0=sc[:, cp:cp + 1], scalar1=-1.0))(cp, cn),
                writes=[b_sc], signal=True)

        pieces = [(0, 2048, 16), (2048, 4096, 16), (4096, 5461, 11)]
        for ch in range(16):
            for (p0, p1, ntile) in pieces:
                slot = 0
                sb = b_stage[slot]
                st = wstg[:, slot * 2 * D: slot * 2 * D + (p1 - p0)]
                evd = p.do("sync", (lambda ch, p0, p1, st: lambda e: e.dma_start(
                    out=st, in_=w2[ch * 128:(ch + 1) * 128, p0:p1]))(ch, p0, p1, st),
                    writes=[sb], dma=True)
                npad = ntile * 128
                evz = None
                if p1 - p0 < npad:
                    evz = p.do("vector", (lambda p0, p1, npad: lambda e: e.memset(
                        ubf[:, (p1 - p0):npad], 0.0))(p0, p1, npad),
                        writes=[b_ubf], signal=True)
                ea = p.do("scalar", (lambda st, p0, p1: lambda e: e.activation(
                    ut[:, 0:(p1 - p0)], st, AF.Sign, bias=sc[:, 8:9]))(st, p0, p1),
                    deps=[evd], reads=[sb, b_sc], writes=[b_ut], signal=True)
                eb = p.do("scalar", (lambda st, p0, p1: lambda e: e.activation(
                    ubf[:, 0:(p1 - p0)], st, AF.Sign, bias=sc[:, 7:8]))(st, p0, p1),
                    deps=[ea], reads=[sb, b_sc], writes=[b_ubf], signal=True)
                ev = p.do("vector", (lambda p0, p1: lambda e: e.tensor_tensor(
                    out=ubf[:, 0:(p1 - p0)], in0=ubf[:, 0:(p1 - p0)],
                    in1=ut[:, 0:(p1 - p0)], op=ALU.add))(p0, p1),
                    deps=[eb, evz], reads=[b_ut], writes=[b_ubf], signal=True)
                for g in range(0, ntile, 8):
                    g1 = min(g + 8, ntile)
                    pb = 4 + ((g // 8) % 2)
                    evg = None
                    for t in range(g, g1):
                        evg = p.do("tensor", (lambda t, g, pb: lambda e: e.transpose(
                            bankb(pb)[:, (t - g) * 128:(t - g + 1) * 128],
                            ubf[:, t * 128:(t + 1) * 128], identb[:]))(t, g, pb),
                            deps=[evi, ev], reads=[b_ubf], writes=[b_ps[pb]],
                            signal=(t == g1 - 1))
                    evc = p.do("vector", (lambda g, g1, pb: lambda e: e.tensor_copy(
                        out=ut[:, 0:(g1 - g) * 128],
                        in_=bankb(pb)[:, 0:(g1 - g) * 128]))(g, g1, pb),
                        deps=[evg], reads=[b_ps[pb]], writes=[b_ut], signal=True)
                    t0g = p0 // 128 + g
                    p.do("sync", (lambda ch, t0g, g, g1: lambda e: e.dma_start(
                        out=u2t_dram[ch, t0g:t0g + (g1 - g)].rearrange("t p d -> p t d"),
                        in_=ut[:, 0:(g1 - g) * 128].rearrange("p (t d) -> p t d", d=128)))(ch, t0g, g, g1),
                        deps=[evc], reads=[b_ut], dma=True)
        ev_u2done = p.barrier_event("sync")

        for t in range(NT):
            evd = p.do("sync", (lambda t: lambda e: e.dma_start(
                out=xtb[:], in_=x_in[t * 128:(t + 1) * 128, :]))(t),
                writes=[b_xt], dma=True)
            er, em = ln_stats(p, xtb[:], ut[:, 0:D], mv, sc[:, 9:10],
                              sc[:, 10:11], lnbufs, xbuf=b_xt,
                              epsc=cst[:, 0:1], epsbuf=b_cst, deps=[evd])
            ev = p.do("vector", lambda e: e.tensor_scalar(
                out=ubf[:, 0:D], in0=xtb[:], scalar1=sc[:, 10:11],
                scalar2=sc[:, 9:10], op0=ALU.add, op1=ALU.mult),
                deps=[er, em], reads=[b_xt, b_rs], writes=[b_ubf], signal=True)
            for g in range(2):
                pb = 6 + g
                evg = None
                for dc in range(g * 8, g * 8 + 8):
                    evg = p.do("tensor", (lambda dc, g, pb: lambda e: e.transpose(
                        bankb(pb)[:, (dc - g * 8) * 128:(dc - g * 8 + 1) * 128],
                        ubf[:, dc * 128:(dc + 1) * 128], identb[:]))(dc, g, pb),
                        deps=[evi, ev], reads=[b_ubf], writes=[b_ps[pb]],
                        signal=(dc == g * 8 + 7))
                for dc in range(g * 8, g * 8 + 8):
                    p.do("vector", (lambda t, g, dc, pb: lambda e: e.tensor_copy(
                        out=xhat[:, dc * TOK + t * 128:dc * TOK + (t + 1) * 128],
                        in_=bankb(pb)[:, (dc - g * 8) * 128:(dc - g * 8 + 1) * 128]))(t, g, dc, pb),
                        deps=[evg], reads=[b_ps[pb]], writes=[b_xhat],
                        signal=(dc == g * 8 + 7))

        for hc in range(NH1):
            r0, r1 = hc * 128, min((hc + 1) * 128, H1)
            nr = r1 - r0
            slot = 0
            sb = b_stage[slot]
            st1 = wstg[:, slot * 2 * D: slot * 2 * D + D]
            st3 = wstg[:, slot * 2 * D + D: slot * 2 * D + 2 * D]
            ed1 = p.do("sync", (lambda r0, r1, st1: lambda e: e.dma_start(
                out=st1[0:r1 - r0, :], in_=w1[r0:r1, :]))(r0, r1, st1),
                writes=[sb], dma=True)
            ed3 = p.do("sync", (lambda r0, r1, st3: lambda e: e.dma_start(
                out=st3[0:r1 - r0, :], in_=w3[r0:r1, :]))(r0, r1, st3),
                writes=[sb], dma=True)
            ev_u = []
            if nr < 128:
                ev_u.append(p.do("vector", lambda e: e.memset(ubf[64:128, 0:2 * D], 0.0),
                                 writes=[b_ubf], signal=True))
            for (st, evd, off, cp, cn) in [(st1, ed1, 0, 3, 4), (st3, ed3, D, 5, 6)]:
                ea = p.do("scalar", (lambda st, off, cn, nr: lambda e: e.activation(
                    ut[0:nr, off:off + D], st[0:nr, :], AF.Sign,
                    bias=sc[0:nr, cn:cn + 1]))(st, off, cn, nr),
                    deps=[evd], reads=[sb, b_sc], writes=[b_ut], signal=True)
                eb = p.do("scalar", (lambda st, off, cp, nr: lambda e: e.activation(
                    ubf[0:nr, off:off + D], st[0:nr, :], AF.Sign,
                    bias=sc[0:nr, cp:cp + 1]))(st, off, cp, nr),
                    deps=[ea], reads=[sb, b_sc], writes=[b_ubf], signal=True)
                evq = p.do("vector", (lambda off, nr: lambda e: e.tensor_tensor(
                    out=ubf[0:nr, off:off + D], in0=ubf[0:nr, off:off + D],
                    in1=ut[0:nr, off:off + D], op=ALU.add))(off, nr),
                    deps=[eb], reads=[b_ut], writes=[b_ubf], signal=True)
                ev_u.append(evq)
            for (off, pb0) in [(0, 4), (D, 5)]:
                for g in range(2):
                    pb = pb0 if g == 0 else (pb0 ^ 1) if False else pb0
                    evg = None
                    for dc in range(g * 8, g * 8 + 8):
                        evg = p.do("tensor", (lambda off, pb, dc, g: lambda e: e.transpose(
                            bankb(pb)[:, (dc - g * 8) * 128:(dc - g * 8 + 1) * 128],
                            ubf[:, off + dc * 128:off + (dc + 1) * 128],
                            identb[:]))(off, pb, dc, g),
                            deps=[evi] + ev_u, reads=[b_ubf], writes=[b_ps[pb]],
                            signal=(dc == g * 8 + 7))
                    evuc = p.do("vector", (lambda off, pb, g: lambda e: e.tensor_copy(
                        out=ut[:, off + g * 1024:off + (g + 1) * 1024],
                        in_=bankb(pb)[:, 0:1024]))(off, pb, g),
                        deps=[evg], reads=[b_ps[pb]], writes=[b_ut], signal=True)
            for tc in range(2):
                e1 = e3 = None
                for dc in range(16):
                    e1 = p.do("tensor", (lambda tc, dc: lambda e: e.matmul(
                        bank(0 + tc), ut[:, dc * 128:(dc + 1) * 128],
                        xhat[:, dc * TOK + tc * 512:dc * TOK + (tc + 1) * 512],
                        start=(dc == 0), stop=(dc == 15)))(tc, dc),
                        reads=[b_ut, b_xhat], writes=[b_ps[0 + tc]],
                        signal=(dc == 15))
                for dc in range(16):
                    e3 = p.do("tensor", (lambda tc, dc: lambda e: e.matmul(
                        bank(2 + tc), ut[:, D + dc * 128:D + (dc + 1) * 128],
                        xhat[:, dc * TOK + tc * 512:dc * TOK + (tc + 1) * 512],
                        start=(dc == 0), stop=(dc == 15)))(tc, dc),
                        reads=[b_ut, b_xhat], writes=[b_ps[2 + tc]],
                        signal=(dc == 15))
                es = p.do("scalar", (lambda tc: lambda e: e.activation(
                    sil[:], bank(0 + tc), AF.Silu, scale=sc[:, 0:1]))(tc),
                    deps=[e1], reads=[b_ps[0 + tc], b_sc], writes=[b_sil], signal=True)
                p.do("vector", (lambda hc, tc: lambda e: e.scalar_tensor_tensor(
                    out=gfm[:, hc * TOK + tc * 512:hc * TOK + (tc + 1) * 512],
                    in0=bank(2 + tc), scalar=sc[:, 1:2], in1=sil[:],
                    op0=ALU.mult, op1=ALU.mult))(hc, tc),
                    deps=[es, e3], reads=[b_ps[2 + tc], b_sil, b_sc],
                    writes=[b_gfm], signal=True)

        for tc in range(2):
            for dc in range(16):
                evd = p.do("sync", (lambda dc: lambda e: e.dma_start(
                    out=u2row[:].rearrange("p (t d) -> p t d", d=128),
                    in_=u2t_dram[dc, :].rearrange("t p d -> p t d")))(dc),
                    deps=[ev_u2done], writes=[b_u2r], dma=True)
                evm = None
                pb = dc % 2
                for hc in range(NH1):
                    evm = p.do("tensor", (lambda tc, hc, pb: lambda e: e.matmul(
                        bank(pb), u2row[:, hc * 128:(hc + 1) * 128],
                        gfm[:, hc * TOK + tc * 512:hc * TOK + (tc + 1) * 512],
                        start=(hc == 0), stop=(hc == NH1 - 1)))(tc, hc, pb),
                        deps=[evd], reads=[b_u2r, b_gfm],
                        writes=[b_ps[pb]], signal=(hc == NH1 - 1))
                ev = p.do("vector", (lambda pb: lambda e: e.tensor_scalar_mul(
                    out=sil[:].bitcast(BF16)[:, 0:512], in0=bank(pb),
                    scalar1=sc[:, 2:3]))(pb),
                    deps=[evm], reads=[b_ps[pb], b_sc], writes=[b_sil], signal=True)
                evt = None
                for q in range(4):
                    evt = p.do("tensor", (lambda q: lambda e: e.transpose(
                        bankb(6)[:, q * 128:(q + 1) * 128],
                        sil[:].bitcast(BF16)[:, q * 128:(q + 1) * 128],
                        identb[:]))(q),
                        deps=[evi, ev], reads=[b_sil], writes=[b_ps[6]],
                        signal=(q == 3))
                for q in range(4):
                    p.do("vector", (lambda dc, q: lambda e: e.tensor_copy(
                        out=ytm[:, q * D + dc * 128:q * D + (dc + 1) * 128],
                        in_=bankb(6)[:, q * 128:(q + 1) * 128]))(dc, q),
                        deps=[evt], reads=[b_ps[6]], writes=[b_ytm],
                        signal=(q == 3))
            for q in range(4):
                t = tc * 4 + q
                yt = ytm[:, q * D:(q + 1) * D]
                evd = p.do("sync", (lambda t: lambda e: e.dma_start(
                    out=xtb[:], in_=x_in[t * 128:(t + 1) * 128, :]))(t),
                    writes=[b_xt], dma=True)
                er, em = ln_stats(p, yt, ut[:, 0:D], mv, sc[:, 9:10],
                                  sc[:, 10:11], lnbufs, xbuf=b_ytm,
                                  epsc=cst[:, 0:1], epsbuf=b_cst)
                ev = p.do("vector", (lambda yt: lambda e: e.tensor_scalar(
                    out=yw[:], in0=yt, scalar1=sc[:, 10:11], scalar2=sc[:, 9:10],
                    op0=ALU.add, op1=ALU.mult))(yt),
                    deps=[er, em], reads=[b_ytm, b_rs], writes=[b_yw], signal=True)
                ev = p.do("vector", lambda e: e.tensor_add(out=yw[:], in0=yw[:], in1=xtb[:]),
                          deps=[evd, ev], reads=[b_xt], writes=[b_yw], signal=True)
                p.do("sync", (lambda t: lambda e: e.dma_start(
                    out=out[t * 128:(t + 1) * 128, :], in_=yw[:]))(t),
                    deps=[ev], reads=[b_yw], dma=True)

        fin = p.barrier_event("sync")
        p.do("sync", lambda e: None, deps=[fin])
        p.emit(blk)
    return nc



_ONES = np.ones((128, 128), dtype=np.float32)
_IDENT = np.eye(128, dtype=ml_dtypes.bfloat16)

_L1FFN_NC = None


def run_l1_ffn(x_full, w1, w3, w2):
    global _L1FFN_NC
    if _L1FFN_NC is None:
        _L1FFN_NC = build_l1_ffn()
    in_maps = []
    for c in range(NCORES):
        sq, hf = c // 2, c % 2
        xs = np.ascontiguousarray(x_full[sq, hf * TOK:(hf + 1) * TOK, :])
        in_maps.append({"x": xs, "w1": w1, "w3": w3, "w2": w2,
                        "ones": _ONES, "ident": _IDENT})
    res = run_bass_kernel_spmd(_L1FFN_NC, in_maps, core_ids=list(range(NCORES)))
    global _LAST_RES
    _LAST_RES = res
    out = np.empty((B, S, D), dtype=np.float32)
    for c in range(NCORES):
        sq, hf = c // 2, c % 2
        out[sq, hf * TOK:(hf + 1) * TOK, :] = res.results[c]["out"]
    return out




I0 = 8192
NI0 = 64


def build_l0_ffn():
    nc = bass.Bass("TRN2", target_bir_lowering=False)
    sems, _ctx = fnew(nc)

    x_in = nc.dram_tensor("x", [TOK, D], F32, kind="ExternalInput").ap()
    wg_in = nc.dram_tensor("wg", [D, I0], F32, kind="ExternalInput").ap()
    wu_in = nc.dram_tensor("wu", [D, I0], F32, kind="ExternalInput").ap()
    wd_in = nc.dram_tensor("wd", [I0, D], F32, kind="ExternalInput").ap()
    ident_in = nc.dram_tensor("ident", [128, 128], BF16, kind="ExternalInput").ap()
    out = nc.dram_tensor("out", [TOK, D], F32, kind="ExternalOutput").ap()

    import contextlib
    with contextlib.ExitStack() as stk:
        sb = lambda *a: stk.enter_context(nc.sbuf_tensor(*a))
        xhat = sb("xhat", [128, 16 * TOK], BF16)
        hfm = sb("hfm", [128, NI0 * TOK], BF16)
        arena = sb("arena", [128, 16384], BF16)
        sil = sb("sil", [128, 512], F32)
        hsb = sb("hsb", [128, 512], F32)
        hspb = sb("hsp", [128, 512], BF16)
        sp = sb("sp", [128, 1408], F32)
        mv = sb("mv", [128, 8], F32)
        sc = sb("sc", [128, 4], F32)
        cst = sb("cst", [128, 2], F32)
        identb = sb("identb", [128, 128], BF16)
        PS = stk.enter_context(nc.psum_tensor("PS", [128, 4096], F32))
        blk = stk.enter_context(nc.Block())

        p = Prog(nc, sems)
        bank = lambda i: PS[:, i * 512:(i + 1) * 512]
        bankb = lambda i: PS[:, i * 512:(i + 1) * 512].bitcast(BF16)
        arenaF = arena.bitcast(F32)
        hfmF = hfm.bitcast(F32)
        yw = hfmF[:, 0:D]
        xtb = hfmF[:, D:2 * D]
        xhatF = xhat.bitcast(F32)
        wg_v = arena[:, 0:8192].rearrange("p (ic c) -> p ic c", c=512)
        wu_v = arena[:, 8192:16384].rearrange("p (ic c) -> p ic c", c=512)

        b_xhat = TBuf("xhat"); b_hfm = TBuf("hfm")
        b_wg = TBuf("wg"); b_wu = TBuf("wu"); b_wd = [TBuf("wd0"), TBuf("wd1")]
        b_xt = TBuf("xt"); b_y = TBuf("y")
        b_sil = TBuf("sil"); b_hsb = TBuf("hsb"); b_hsp = TBuf("hsp")
        b_sp = TBuf("sp"); b_yw = TBuf("yw"); b_mv = TBuf("mv")
        b_rs = TBuf("rs"); b_cst = TBuf("cst")
        b_ps = [TBuf(f"ps{i}") for i in range(8)]
        lnbufs = (b_sil, b_mv, b_rs)

        evi = p.do("sync", lambda e: e.dma_start(out=identb[:], in_=ident_in), dma=True)
        p.do("vector", lambda e: e.memset(cst[:, 0:1], EPS), writes=[b_cst], signal=True)

        for t in range(NT):
            evd = p.do("sync", (lambda t: lambda e: e.dma_start(
                out=xtb[:], in_=x_in[t * 128:(t + 1) * 128, :]))(t),
                writes=[b_xt], dma=True)
            er, em = ln_stats(p, xtb[:], sil.bitcast(BF16), mv, sc[:, 0:1],
                              sc[:, 1:2], lnbufs, xbuf=b_xt, epsc=cst[:, 0:1],
                              epsbuf=b_cst, deps=[evd])
            ev = p.do("vector", lambda e: e.tensor_scalar(
                out=arena[:, 8192:8192 + D],
                in0=xtb[:], scalar1=sc[:, 1:2], scalar2=sc[:, 0:1],
                op0=ALU.add, op1=ALU.mult),
                deps=[er, em], reads=[b_xt, b_rs], writes=[b_wu], signal=True)
            for g in range(2):
                pb = 6 + g
                evg = None
                for dc in range(g * 8, g * 8 + 8):
                    evg = p.do("tensor", (lambda dc, g, pb: lambda e: e.transpose(
                        bankb(pb)[:, (dc - g * 8) * 128:(dc - g * 8 + 1) * 128],
                        arena[:, 8192 + dc * 128:8192 + (dc + 1) * 128],
                        identb[:]))(dc, g, pb),
                        deps=[evi, ev], reads=[b_wu], writes=[b_ps[pb]],
                        signal=(dc == g * 8 + 7))
                for dc in range(g * 8, g * 8 + 8):
                    p.do("vector", (lambda t, dc, pb, g: lambda e: e.tensor_copy(
                        out=xhat[:, dc * TOK + t * 128:dc * TOK + (t + 1) * 128],
                        in_=bankb(pb)[:, (dc - g * 8) * 128:(dc - g * 8 + 1) * 128]))(t, dc, pb, g),
                        deps=[evg], reads=[b_ps[pb]], writes=[b_xhat],
                        signal=(dc == g * 8 + 7))

        p.full_barrier()
        preln_done = [p.barrier_event(e) for e in ("vector", "tensor", "scalar")]
        for oc in range(16):
            eg = p.do("gpsimd", (lambda oc: lambda e: e.dma_start(
                out=wg_v, in_=wg_in[:, oc * 512:(oc + 1) * 512]
                .rearrange("(ic p) c -> p ic c", p=128)))(oc),
                deps=preln_done if oc == 0 else (), writes=[b_wg], dma=True)
            eu = p.do("gpsimd", (lambda oc: lambda e: e.dma_start(
                out=wu_v, in_=wu_in[:, oc * 512:(oc + 1) * 512]
                .rearrange("(ic p) c -> p ic c", p=128)))(oc),
                deps=preln_done if oc == 0 else (), writes=[b_wu], dma=True)
            for t in range(NT):
                pg, pu = (0, 1) if t % 2 == 0 else (2, 3)
                e1 = e2 = None
                for ic in range(16):
                    e1 = p.do("tensor", (lambda oc, t, ic, pg: lambda e: e.matmul(
                        bank(pg), xhat[:, ic * TOK + t * 128:ic * TOK + (t + 1) * 128],
                        wg_v[:, ic, :], start=(ic == 0), stop=(ic == 15)))(oc, t, ic, pg),
                        deps=[eg], reads=[b_wg, b_xhat], writes=[b_ps[pg]],
                        signal=(ic == 15))
                for ic in range(16):
                    e2 = p.do("tensor", (lambda oc, t, ic, pu: lambda e: e.matmul(
                        bank(pu), xhat[:, ic * TOK + t * 128:ic * TOK + (t + 1) * 128],
                        wu_v[:, ic, :], start=(ic == 0), stop=(ic == 15)))(oc, t, ic, pu),
                        deps=[eu], reads=[b_wu, b_xhat], writes=[b_ps[pu]],
                        signal=(ic == 15))
                es = p.do("scalar", (lambda pg: lambda e: e.activation(
                    sil[:], bank(pg), AF.Silu))(pg),
                    deps=[e1], reads=[b_ps[pg]], writes=[b_sil], signal=True)
                eh = p.do("vector", (lambda pu: lambda e: e.scalar_tensor_tensor(
                    out=hsb[:], in0=bank(pu), scalar=1.0, in1=sil[:],
                    op0=ALU.mult, op1=ALU.mult))(pu),
                    deps=[es, e2], reads=[b_ps[pu], b_sil], writes=[b_hsb],
                    signal=True)
                h3v = hsb[:].rearrange("p (a b) -> p a b", b=2)
                epm = p.do("vector", lambda e: e.reduce_max(
                    out=sp[:, 0:256], in_=h3v, axis=AX.X, apply_absolute_value=True),
                    deps=[force(eh)], reads=[b_hsb], writes=[b_sp], signal=True)
                epn = p.do("vector", lambda e: e.tensor_reduce(
                    out=sp[:, 256:512], in_=h3v, axis=AX.X, op=ALU.min,
                    apply_absolute_value=True),
                    deps=[force(eh)], reads=[b_hsb], writes=[b_sp], signal=True)
                em1 = p.do("vector", lambda e: e.tensor_reduce(
                    out=sp[:, 512:640], in_=sp[:, 0:256].rearrange("p (a b) -> p a b", b=2),
                    axis=AX.X, op=ALU.min),
                    deps=[force(epm)], writes=[b_sp], signal=True)
                em2 = p.do("vector", lambda e: e.reduce_max(
                    out=sp[:, 640:768], in_=sp[:, 256:512].rearrange("p (a b) -> p a b", b=2),
                    axis=AX.X),
                    deps=[force(epn)], writes=[b_sp], signal=True)
                em3 = p.do("vector", lambda e: e.tensor_max(
                    out=sp[:, 768:896], in0=sp[:, 512:640], in1=sp[:, 640:768]),
                    deps=[force(em1), force(em2)], writes=[b_sp], signal=True)
                ea = p.do("scalar", lambda e: e.activation(sp[:, 896:1408], hsb[:], AF.Abs),
                          deps=[eh], reads=[b_hsb], writes=[b_sp], signal=True)
                evj = None
                for j in range(4):
                    evj = p.do("vector", (lambda j: lambda e: e.tensor_tensor(
                        out=sp[:, 896 + j:1408:4], in0=sp[:, 896 + j:1408:4],
                        in1=sp[:, 768:896], op=ALU.is_ge))(j),
                        deps=[force(em3), force(ea)], writes=[b_sp],
                        signal=(j == 3))
                ehs = p.do("vector", lambda e: e.tensor_mul(
                    out=hspb[:], in0=hsb[:], in1=sp[:, 896:1408]),
                    deps=[force(evj)], reads=[b_hsb, b_sp], writes=[b_hsp],
                    signal=True)
                pb = 6 + t % 2
                evt = None
                for q in range(4):
                    evt = p.do("tensor", (lambda q, pb: lambda e: e.transpose(
                        bankb(pb)[:, q * 128:(q + 1) * 128],
                        hspb[:, q * 128:(q + 1) * 128], identb[:]))(q, pb),
                        deps=[evi, ehs], reads=[b_hsp], writes=[b_ps[pb]],
                        signal=(q == 3))
                for q in range(4):
                    p.do("vector", (lambda oc, t, q, pb: lambda e: e.tensor_copy(
                        out=hfm[:, (4 * oc + q) * TOK + t * 128:(4 * oc + q) * TOK + (t + 1) * 128],
                        in_=bankb(pb)[:, q * 128:(q + 1) * 128]))(oc, t, q, pb),
                        deps=[evt], reads=[b_ps[pb]], writes=[b_hfm],
                        signal=(q == 3))

        p.full_barrier()
        ph1_done = [p.barrier_event(e) for e in ("vector", "tensor", "scalar", "gpsimd")]
        for dc in range(4):
            evm_all = []
            for hc in range(NI0):
                slot = hc % 2
                wd_s = sil.bitcast(BF16)[:, slot * 512:(slot + 1) * 512]
                evd = p.do("gpsimd", (lambda hc, dc, wd_s: lambda e: e.dma_start(
                    out=wd_s, in_=wd_in[hc * 128:(hc + 1) * 128,
                                        dc * 512:(dc + 1) * 512]))(hc, dc, wd_s),
                    deps=ph1_done if (dc == 0 and hc < 2) else (),
                    writes=[b_wd[slot]], dma=True)
                for t in range(NT):
                    evm = p.do("tensor", (lambda hc, t, wd_s: lambda e: e.matmul(
                        bank(t), hfm[:, hc * TOK + t * 128:hc * TOK + (t + 1) * 128],
                        wd_s, start=(hc == 0), stop=(hc == NI0 - 1)))(hc, t, wd_s),
                        deps=[evd], reads=[b_wd[slot], b_hfm],
                        writes=[b_ps[t]], signal=(hc == NI0 - 1 or t == NT - 1))
                    if hc == NI0 - 1:
                        evm_all.append(evm)
            for t in range(NT):
                ytgt = xhatF if t < 4 else arenaF
                p.do("vector", (lambda t, dc, ytgt: lambda e: e.tensor_copy(
                    out=ytgt[:, (t % 4) * D + dc * 512:(t % 4) * D + dc * 512 + 512],
                    in_=bank(t)))(t, dc, ytgt),
                    deps=[evm_all[t]], reads=[b_ps[t]], writes=[b_y], signal=True)

        p.full_barrier()
        ph2_done = [p.barrier_event("tensor"), p.barrier_event("vector"),
                    p.barrier_event("gpsimd")]
        for t in range(NT):
            ytgt = xhatF if t < 4 else arenaF
            yt = ytgt[:, (t % 4) * D:(t % 4 + 1) * D]
            evd = p.do("sync", (lambda t: lambda e: e.dma_start(
                out=xtb[:], in_=x_in[t * 128:(t + 1) * 128, :]))(t),
                deps=ph2_done if t == 0 else (), writes=[b_xt], dma=True)
            er, em = ln_stats(p, yt, sil.bitcast(BF16), mv, sc[:, 0:1],
                              sc[:, 1:2], lnbufs, xbuf=b_y, epsc=cst[:, 0:1],
                              epsbuf=b_cst, deps=ph2_done if t == 0 else ())
            ev = p.do("vector", (lambda yt: lambda e: e.tensor_scalar(
                out=yw[:], in0=yt, scalar1=sc[:, 1:2], scalar2=sc[:, 0:1],
                op0=ALU.add, op1=ALU.mult))(yt),
                deps=[er, em], reads=[b_y, b_rs], writes=[b_yw], signal=True)
            ev = p.do("vector", lambda e: e.tensor_add(out=yw[:], in0=yw[:], in1=xtb[:]),
                      deps=[evd, force(ev)], reads=[b_xt], writes=[b_yw], signal=True)
            p.do("sync", (lambda t: lambda e: e.dma_start(
                out=out[t * 128:(t + 1) * 128, :], in_=yw[:]))(t),
                deps=[ev], reads=[b_yw], dma=True)

        fin = p.barrier_event("sync")
        p.do("sync", lambda e: None, deps=[fin])
        p.emit(blk)
    return nc


_L0FFN_NC = None


def run_l0_ffn(x_full, wg, wu, wd):
    global _L0FFN_NC
    if _L0FFN_NC is None:
        _L0FFN_NC = build_l0_ffn()
    in_maps = []
    for c in range(NCORES):
        sq_, hf = c // 2, c % 2
        xs = np.ascontiguousarray(x_full[sq_, hf * TOK:(hf + 1) * TOK, :])
        in_maps.append({"x": xs, "wg": wg, "wu": wu, "wd": wd, "ident": _IDENT})
    res = run_bass_kernel_spmd(_L0FFN_NC, in_maps, core_ids=list(range(NCORES)))
    global _L0RES
    _L0RES = res
    out = np.empty((B, S, D), dtype=np.float32)
    for c in range(NCORES):
        sq_, hf = c // 2, c % 2
        out[sq_, hf * TOK:(hf + 1) * TOK, :] = res.results[c]["out"]
    return out



NH_L1, NKV_L1, HD_L1 = 32, 8, 64

def build_l1_attn():
    nc = bass.Bass("TRN2", target_bir_lowering=False)
    sems, _ctx = fnew(nc)

    x_in = nc.dram_tensor("x", [S, D], F32, kind="ExternalInput").ap()
    qw_in = nc.dram_tensor("qw", [2048, 2048], F32, kind="ExternalInput").ap()
    krw_in = nc.dram_tensor("krw", [256, 2048], F32, kind="ExternalInput").ap()
    dww_in = nc.dram_tensor("dww", [256, 2048], F32, kind="ExternalInput").ap()
    upk_in = nc.dram_tensor("upk", [256, 256], F32, kind="ExternalInput").ap()
    upv_in = nc.dram_tensor("upv", [512, 256], F32, kind="ExternalInput").ap()
    ow_in = nc.dram_tensor("ow", [2048, 2048], F32, kind="ExternalInput").ap()
    mask_in = nc.dram_tensor("mask", [S, TOK], BF16, kind="ExternalInput").ap()
    ckc_in = nc.dram_tensor("ckc", [S, 16], F32, kind="ExternalInput").ap()
    cks_in = nc.dram_tensor("cks", [S, 16], F32, kind="ExternalInput").ap()
    cqc_in = nc.dram_tensor("cqc", [TOK, 32], F32, kind="ExternalInput").ap()
    cqs_in = nc.dram_tensor("cqs", [TOK, 32], F32, kind="ExternalInput").ap()
    ident_in = nc.dram_tensor("ident", [128, 128], BF16, kind="ExternalInput").ap()
    ones_in = nc.dram_tensor("ones", [128, 128], F32, kind="ExternalInput").ap()
    out = nc.dram_tensor("out", [TOK, D], F32, kind="ExternalOutput").ap()

    import contextlib
    with contextlib.ExitStack() as stk:
        sb = lambda *a: stk.enter_context(nc.sbuf_tensor(*a))
        xhat = sb("xhat", [128, 16 * S], BF16)
        vbuf = sb("vbuf", [128, 16 * 520], BF16)
        ckv = sb("ckv", [128, 2 * S], BF16)
        krf = sb("krf", [128, 2 * S], BF16)
        kcf = sb("kcf", [128, 2 * S], BF16)
        qro = sb("qro", [128, 8 * TOK], BF16)
        qno = sb("qno", [128, 8 * TOK], BF16)
        maskb = sb("maskb", [128, 16 * 512], BF16)
        wstage = sb("wstage", [128, 8192], BF16)
        piece = sb("piece", [128, 1024], BF16)
        krT = sb("krT", [128, 16 * 256], BF16)
        dwT = sb("dwT", [128, 16 * 256], BF16)
        upkT = sb("upkT", [128, 2 * 256], BF16)
        upvT = sb("upvT", [128, 2 * 512], BF16)
        ckc = sb("ckc_s", [128, 16 * 16], F32)
        cks = sb("cks_s", [128, 16 * 16], F32)
        cqc = sb("cqc_s", [128, 8 * 32], F32)
        cqs = sb("cqs_s", [128, 8 * 32], F32)
        ptl = sb("ptl", [128, 2 * 512], BF16)
        qtm = sb("qtm", [128, 512], BF16)
        rsc = sb("rsc", [128, 512], F32)
        rb = sb("rb", [128, 512], F32)
        mv = sb("mv", [128, 8], F32)
        sc = sb("sc", [128, 4], F32)
        cst = sb("cst", [128, 2], F32)
        identb = sb("identb", [128, 128], BF16)
        onesb = sb("onesb", [128, 128], F32)
        PS = stk.enter_context(nc.psum_tensor("PS", [128, 4096], F32))
        blk = stk.enter_context(nc.Block())

        p = Prog(nc, sems)
        bank = lambda i: PS[:, i * 512:(i + 1) * 512]
        bankb = lambda i: PS[:, i * 512:(i + 1) * 512].bitcast(BF16)
        attn_fm = xhat[:, 0:16 * TOK]
        ybf = xhat[:, 16 * TOK:32 * TOK]
        qroF = qro.bitcast(F32)
        xtb = qroF[:, 0:D]
        yw = qroF[:, D:2 * D]

        names = ["xhat", "vbuf", "ckv", "krf", "kcf", "qro", "qno", "maskb",
                 "wstage", "piece", "krT", "dwT", "upkT", "upvT", "crep",
                 "ptl", "qtm", "rsc", "xtb", "yw", "lrow", "rb", "mv", "rs",
                 "cst", "attn", "ybf"]
        bufs = {n: TBuf(n) for n in names}
        b_ps = [TBuf(f"ps{i}") for i in range(8)]
        bu = lambda n: bufs[n]
        lnbufs = (bu("rsc"), bu("mv"), bu("rs"))

        evi = p.do("sync", lambda e: e.dma_start(out=identb[:], in_=ident_in), dma=True)
        evo = p.do("sync", lambda e: e.dma_start(out=onesb[:], in_=ones_in), dma=True)
        p.do("vector", lambda e: e.memset(cst[:, 0:1], EPS), writes=[bu("cst")], signal=True)
        p.do("vector", lambda e: e.memset(vbuf[:], 1.0), writes=[bu("vbuf")], signal=True)
        for (t_in, t_sb, nj, ntt) in [(ckc_in, ckc, 16, 16), (cks_in, cks, 16, 16),
                                      (cqc_in, cqc, 32, 8), (cqs_in, cqs, 32, 8)]:
            p.do("sync", (lambda t_in, t_sb, nj, ntt: lambda e: e.dma_start(
                out=t_sb[:].rearrange("p (tt j) -> p tt j", tt=ntt),
                in_=t_in.rearrange("(tt p) j -> p tt j", p=128)))(t_in, t_sb, nj, ntt),
                writes=[bu("crep")], dma=True)

        def prep_T(w_in, dst, nout, nin, dstbuf):
            for r in range(nout // 128):
                evd = p.do("gpsimd", (lambda w_in, r, nin: lambda e: e.dma_start(
                    out=wstage[:, 0:nin], in_=w_in[r * 128:(r + 1) * 128, :]))(w_in, r, nin),
                    writes=[bu("wstage")], dma=True)
                for g in range(0, nin // 128, 8):
                    g1 = min(g + 8, nin // 128)
                    pb = (g // 8) % 2
                    evg = None
                    for ic in range(g, g1):
                        evg = p.do("tensor", (lambda ic, g, pb: lambda e: e.transpose(
                            bankb(pb)[:, (ic - g) * 128:(ic - g + 1) * 128],
                            wstage[:, ic * 128:(ic + 1) * 128], identb[:]))(ic, g, pb),
                            deps=[evi, evd], reads=[bu("wstage")], writes=[b_ps[pb]],
                            signal=(ic == g1 - 1))
                    for ic in range(g, g1):
                        p.do("vector", (lambda ic, g, pb, r, nout: lambda e: e.tensor_copy(
                            out=dst[:, ic * nout + r * 128:ic * nout + (r + 1) * 128],
                            in_=bankb(pb)[:, (ic - g) * 128:(ic - g + 1) * 128]))(ic, g, pb, r, nout),
                            deps=[evg], reads=[b_ps[pb]], writes=[dstbuf],
                            signal=(ic == g1 - 1))
        prep_T(krw_in, krT, 256, 2048, bu("krT"))
        prep_T(dww_in, dwT, 256, 2048, bu("dwT"))
        prep_T(upk_in, upkT, 256, 256, bu("upkT"))
        prep_T(upv_in, upvT, 512, 256, bu("upvT"))

        for t in range(16):
            evd = p.do("sync", (lambda t: lambda e: e.dma_start(
                out=xtb[:], in_=x_in[t * 128:(t + 1) * 128, :]))(t),
                writes=[bu("xtb")], dma=True)
            er, em = ln_stats(p, xtb[:], rsc[:].bitcast(BF16)[:, 0:512], mv,
                              sc[:, 0:1], sc[:, 1:2], lnbufs, xbuf=bu("xtb"),
                              epsc=cst[:, 0:1], epsbuf=bu("cst"), deps=[evd])
            ev = p.do("vector", lambda e: e.tensor_scalar(
                out=wstage[:, 4096:4096 + D], in0=xtb[:], scalar1=sc[:, 1:2],
                scalar2=sc[:, 0:1], op0=ALU.add, op1=ALU.mult),
                deps=[er, em], reads=[bu("xtb"), bu("rs")], writes=[bu("wstage")],
                signal=True)
            for g in range(2):
                pb = 6 + g
                evg = None
                for dc in range(g * 8, g * 8 + 8):
                    evg = p.do("tensor", (lambda dc, g, pb: lambda e: e.transpose(
                        bankb(pb)[:, (dc - g * 8) * 128:(dc - g * 8 + 1) * 128],
                        wstage[:, 4096 + dc * 128:4096 + (dc + 1) * 128],
                        identb[:]))(dc, g, pb),
                        deps=[evi, ev], reads=[bu("wstage")], writes=[b_ps[pb]],
                        signal=(dc == g * 8 + 7))
                for dc in range(g * 8, g * 8 + 8):
                    p.do("vector", (lambda t, dc, pb, g: lambda e: e.tensor_copy(
                        out=xhat[:, dc * S + t * 128:dc * S + (t + 1) * 128],
                        in_=bankb(pb)[:, (dc - g * 8) * 128:(dc - g * 8 + 1) * 128]))(t, dc, pb, g),
                        deps=[evg], reads=[b_ps[pb]], writes=[bu("xhat")],
                        signal=(dc == g * 8 + 7))

        p.full_barrier()
        for m in range(2):
            for ts in range(4):
                evm = None
                for ic in range(16):
                    evm = p.do("tensor", (lambda m, ts, ic: lambda e: e.matmul(
                        bank(m), dwT[:, ic * 256 + m * 128:ic * 256 + (m + 1) * 128],
                        xhat[:, ic * S + ts * 512:ic * S + (ts + 1) * 512],
                        start=(ic == 0), stop=(ic == 15)))(m, ts, ic),
                        reads=[bu("dwT"), bu("xhat")], writes=[b_ps[m]],
                        signal=(ic == 15))
                p.do("vector", (lambda m, ts: lambda e: e.tensor_copy(
                    out=ckv[:, m * S + ts * 512:m * S + (ts + 1) * 512],
                    in_=bank(m)))(m, ts),
                    deps=[evm], reads=[b_ps[m]], writes=[bu("ckv")], signal=True)

        from concourse.bass import broadcast_tensor_aps
        for t in range(16):
            evm = None
            for ic in range(16):
                evm = p.do("tensor", (lambda t, ic: lambda e: e.matmul(
                    PS[:, 2048:2304], xhat[:, ic * S + t * 128:ic * S + (t + 1) * 128],
                    krT[:, ic * 256:(ic + 1) * 256],
                    start=(ic == 0), stop=(ic == 15)))(t, ic),
                    reads=[bu("krT"), bu("xhat")], writes=[b_ps[4]],
                    signal=(ic == 15))
            kr_ps = PS[:, 2048:2304].rearrange("p (h two j) -> p h two j", two=2, j=16)
            av, bv = kr_ps[:, :, 0, :], kr_ps[:, :, 1, :]
            ct = ckc[:, t * 16:(t + 1) * 16].rearrange("p (h j) -> p h j", h=1)
            st = cks[:, t * 16:(t + 1) * 16].rearrange("p (h j) -> p h j", h=1)
            _, ctb = broadcast_tensor_aps(av, ct)
            _, stb = broadcast_tensor_aps(av, st)
            qv = qtm[:, 0:256].rearrange("p (h two j) -> p h two j", two=2, j=16)
            x1o, x2o = qv[:, :, 0, :], qv[:, :, 1, :]
            wsv = rsc[:, 0:256].rearrange("p (h two j) -> p h two j", two=2, j=16)
            t1v, t2v = wsv[:, :, 0, :], wsv[:, :, 1, :]
            e1 = p.do("vector", (lambda av, ctb, x1o: lambda e: e.tensor_tensor(
                out=x1o, in0=av, in1=ctb, op=ALU.mult))(av, ctb, x1o),
                deps=[evm], reads=[bu("crep"), b_ps[4]], writes=[bu("qtm")], signal=True)
            e1 = p.do("vector", (lambda av, stb, x2o: lambda e: e.tensor_tensor(
                out=x2o, in0=av, in1=stb, op=ALU.mult))(av, stb, x2o),
                reads=[bu("crep"), b_ps[4]], writes=[bu("qtm")], signal=True)
            e2 = p.do("vector", (lambda bv, stb, t1v: lambda e: e.tensor_tensor(
                out=t1v, in0=bv, in1=stb, op=ALU.mult))(bv, stb, t1v),
                reads=[bu("crep"), b_ps[4]], writes=[bu("rsc")], signal=True)
            e2 = p.do("vector", (lambda bv, ctb, t2v: lambda e: e.tensor_tensor(
                out=t2v, in0=bv, in1=ctb, op=ALU.mult))(bv, ctb, t2v),
                reads=[bu("crep"), b_ps[4]], writes=[bu("rsc")], signal=True)
            e3 = p.do("vector", (lambda x1o, t1v: lambda e: e.tensor_tensor(
                out=x1o, in0=x1o, in1=t1v, op=ALU.subtract))(x1o, t1v),
                deps=[force(e1), force(e2)], reads=[bu("rsc")], writes=[bu("qtm")], signal=True)
            e3 = p.do("vector", (lambda x2o, t2v: lambda e: e.tensor_tensor(
                out=x2o, in0=x2o, in1=t2v, op=ALU.add))(x2o, t2v),
                deps=[force(e3)], reads=[bu("rsc")], writes=[bu("qtm")], signal=True)
            evt = None
            for cc2 in range(2):
                evt = p.do("tensor", (lambda cc2: lambda e: e.transpose(
                    bankb(5)[:, cc2 * 128:(cc2 + 1) * 128],
                    qtm[:, cc2 * 128:(cc2 + 1) * 128], identb[:]))(cc2),
                    deps=[evi, e3], reads=[bu("qtm")], writes=[b_ps[5]],
                    signal=(cc2 == 1))
            for cc2 in range(2):
                p.do("vector", (lambda t, cc2: lambda e: e.tensor_copy(
                    out=krf[:, cc2 * S + t * 128:cc2 * S + (t + 1) * 128],
                    in_=bankb(5)[:, cc2 * 128:(cc2 + 1) * 128]))(t, cc2),
                    deps=[evt], reads=[b_ps[5]], writes=[bu("krf")],
                    signal=(cc2 == 1))

        for m in range(2):
            for ts in range(4):
                evm = None
                for ic in range(2):
                    evm = p.do("tensor", (lambda m, ts, ic: lambda e: e.matmul(
                        bank(2 + m), upkT[:, ic * 256 + m * 128:ic * 256 + (m + 1) * 128],
                        ckv[:, ic * S + ts * 512:ic * S + (ts + 1) * 512],
                        start=(ic == 0), stop=(ic == 1)))(m, ts, ic),
                        reads=[bu("upkT"), bu("ckv")], writes=[b_ps[2 + m]],
                        signal=(ic == 1))
                p.do("vector", (lambda m, ts: lambda e: e.tensor_copy(
                    out=kcf[:, m * S + ts * 512:m * S + (ts + 1) * 512],
                    in_=bank(2 + m)))(m, ts),
                    deps=[evm], reads=[b_ps[2 + m]], writes=[bu("kcf")], signal=True)

        for kt in range(16):
            evm = None
            for ic in range(2):
                evm = p.do("tensor", (lambda kt, ic: lambda e: e.matmul(
                    bank(6), ckv[:, ic * S + kt * 128:ic * S + (kt + 1) * 128],
                    upvT[:, ic * 512:(ic + 1) * 512],
                    start=(ic == 0), stop=(ic == 1)))(kt, ic),
                    reads=[bu("upvT"), bu("ckv")], writes=[b_ps[6]],
                    signal=(ic == 1))
            p.do("vector", (lambda kt: lambda e: e.tensor_copy(
                out=vbuf[:, kt * 520:(kt + 1) * 520]
                .rearrange("p (h e) -> p h e", e=65)[:, :, 0:64],
                in_=bank(6).rearrange("p (h d) -> p h d", d=64)))(kt),
                deps=[evm], reads=[b_ps[6]], writes=[bu("vbuf")], signal=True)

        for oc in range(4):
            evd = p.do("gpsimd", (lambda oc: lambda e: e.dma_start(
                out=wstage[:, 0:8192].rearrange("p (r c) -> p r c", r=4),
                in_=qw_in[oc * 512:(oc + 1) * 512, :]
                .rearrange("(r p) c -> p r c", p=128)))(oc),
                writes=[bu("wstage")], dma=True)
            for tg in range(2):
                for ic in range(16):
                    pcb = ic % 2
                    evt = None
                    for r in range(4):
                        evt = p.do("tensor", (lambda ic, r, pcb: lambda e: e.transpose(
                            bankb(4 + pcb)[:, r * 128:(r + 1) * 128],
                            wstage[:, r * 2048 + ic * 128:r * 2048 + (ic + 1) * 128],
                            identb[:]))(ic, r, pcb),
                            deps=[evi, evd], reads=[bu("wstage")], writes=[b_ps[4 + pcb]],
                            signal=(r == 3))
                    evp = p.do("vector", (lambda pcb: lambda e: e.tensor_copy(
                        out=piece[:, pcb * 512:(pcb + 1) * 512],
                        in_=bankb(4 + pcb)[:, 0:512]))(pcb),
                        deps=[evt], reads=[b_ps[4 + pcb]], writes=[bu("piece")], signal=True)
                    for tl in range(4):
                        t = tg * 4 + tl
                        p.do("tensor", (lambda ic, t, tl, pcb: lambda e: e.matmul(
                            bank(tl), xhat[:, ic * S + t * 128:ic * S + (t + 1) * 128],
                            piece[:, pcb * 512:(pcb + 1) * 512],
                            start=(ic == 0), stop=(ic == 15)))(ic, t, tl, pcb),
                            deps=[evp], reads=[bu("piece"), bu("xhat")],
                            writes=[b_ps[tl]], signal=(ic == 15 or tl == 3))
                for tl in range(4):
                    t = tg * 4 + tl
                    qp = bank(tl).rearrange("p (h two j) -> p h two j", two=2, j=32)
                    av, bv = qp[:, :, 0, :], qp[:, :, 1, :]
                    ct = cqc[:, t * 32:(t + 1) * 32].rearrange("p (h j) -> p h j", h=1)
                    st = cqs[:, t * 32:(t + 1) * 32].rearrange("p (h j) -> p h j", h=1)
                    _, ctb = broadcast_tensor_aps(av, ct)
                    _, stb = broadcast_tensor_aps(av, st)
                    wsv = rsc[:].rearrange("p (h two j) -> p h two j", two=2, j=32)
                    t1v, t2v = wsv[:, :, 0, :], wsv[:, :, 1, :]
                    e1 = p.do("vector", (lambda av, ctb, t1v: lambda e: e.tensor_tensor(
                        out=t1v, in0=av, in1=ctb, op=ALU.mult))(av, ctb, t1v),
                        reads=[bu("crep"), b_ps[tl]], writes=[bu("rsc")], signal=True)
                    e1 = p.do("vector", (lambda bv, stb, t2v: lambda e: e.tensor_tensor(
                        out=t2v, in0=bv, in1=stb, op=ALU.mult))(bv, stb, t2v),
                        reads=[bu("crep"), b_ps[tl]], writes=[bu("rsc")], signal=True)
                    qv = qtm[:].rearrange("p (h q4 j) -> p h q4 j", q4=4, j=16)
                    ev1 = p.do("vector", (lambda t1v, t2v, qv: lambda e: e.tensor_tensor(
                        out=qv[:, :, 0, :],
                        in0=t1v.rearrange("p h (s j) -> p h s j", s=2)[:, :, 0, :],
                        in1=t2v.rearrange("p h (s j) -> p h s j", s=2)[:, :, 0, :],
                        op=ALU.subtract))(t1v, t2v, qv),
                        deps=[force(e1)], reads=[bu("rsc")], writes=[bu("qtm")], signal=True)
                    ev1 = p.do("vector", (lambda t1v, t2v, qv: lambda e: e.tensor_tensor(
                        out=qv[:, :, 2, :],
                        in0=t1v.rearrange("p h (s j) -> p h s j", s=2)[:, :, 1, :],
                        in1=t2v.rearrange("p h (s j) -> p h s j", s=2)[:, :, 1, :],
                        op=ALU.subtract))(t1v, t2v, qv),
                        reads=[bu("rsc")], writes=[bu("qtm")], signal=True)
                    e2 = p.do("vector", (lambda av, stb, t1v: lambda e: e.tensor_tensor(
                        out=t1v, in0=av, in1=stb, op=ALU.mult))(av, stb, t1v),
                        deps=[force(ev1)], reads=[bu("crep"), b_ps[tl]],
                        writes=[bu("rsc")], signal=True)
                    e2 = p.do("vector", (lambda bv, ctb, t2v: lambda e: e.tensor_tensor(
                        out=t2v, in0=bv, in1=ctb, op=ALU.mult))(bv, ctb, t2v),
                        reads=[bu("crep"), b_ps[tl]], writes=[bu("rsc")], signal=True)
                    ev2 = p.do("vector", (lambda t1v, t2v, qv: lambda e: e.tensor_tensor(
                        out=qv[:, :, 1, :],
                        in0=t1v.rearrange("p h (s j) -> p h s j", s=2)[:, :, 0, :],
                        in1=t2v.rearrange("p h (s j) -> p h s j", s=2)[:, :, 0, :],
                        op=ALU.add))(t1v, t2v, qv),
                        deps=[force(e2)], reads=[bu("rsc")], writes=[bu("qtm")], signal=True)
                    ev2 = p.do("vector", (lambda t1v, t2v, qv: lambda e: e.tensor_tensor(
                        out=qv[:, :, 3, :],
                        in0=t1v.rearrange("p h (s j) -> p h s j", s=2)[:, :, 1, :],
                        in1=t2v.rearrange("p h (s j) -> p h s j", s=2)[:, :, 1, :],
                        op=ALU.add))(t1v, t2v, qv),
                        reads=[bu("rsc")], writes=[bu("qtm")], signal=True)
                    evt = None
                    for r4 in range(4):
                        evt = p.do("tensor", (lambda r4: lambda e: e.transpose(
                            bankb(7)[:, r4 * 128:(r4 + 1) * 128],
                            qtm[:, r4 * 128:(r4 + 1) * 128], identb[:]))(r4),
                            deps=[evi, force(ev1), force(ev2)], reads=[bu("qtm")],
                            writes=[b_ps[7]], signal=(r4 == 3))
                    for hh in range(8):
                        qh = oc * 8 + hh
                        h = qh // 4
                        rbase = (h % 4) * 32
                        cblk = (h // 4) * 4 + (qh % 4)
                        prow = (hh % 2) * 64
                        pcol = (hh // 2) * 128
                        p.do("vector", (lambda prow, pcol, rbase, cblk, t: lambda e: e.tensor_copy(
                            out=qro[rbase:rbase + 32, cblk * TOK + t * 128:cblk * TOK + (t + 1) * 128],
                            in_=bankb(7)[prow:prow + 32, pcol:pcol + 128]))(prow, pcol, rbase, cblk, t),
                            deps=[evt], reads=[b_ps[7]], writes=[bu("qro")],
                            signal=(hh == 7))
                        p.do("vector", (lambda prow, pcol, rbase, cblk, t: lambda e: e.tensor_copy(
                            out=qno[rbase:rbase + 32, cblk * TOK + t * 128:cblk * TOK + (t + 1) * 128],
                            in_=bankb(7)[prow + 32:prow + 64, pcol:pcol + 128]))(prow, pcol, rbase, cblk, t),
                            deps=[evt], reads=[b_ps[7]], writes=[bu("qno")],
                            signal=(hh == 7))

        p.full_barrier()
        SCL = float(HD_L1) ** -0.5
        for qc in range(2):
            evmk = p.do("sync", (lambda qc: lambda e: e.dma_start(
                out=maskb[:].rearrange("p (tt c) -> p tt c", tt=16),
                in_=mask_in[:, qc * 512:(qc + 1) * 512]
                .rearrange("(tt p) c -> p tt c", p=128)))(qc),
                writes=[bu("maskb")], dma=True)
            for qh in range(NH_L1):
                h = qh // 4
                rbase = (h % 4) * 32
                cc = h // 4
                cblk = (h // 4) * 4 + (qh % 4)
                pav = 2 + (qh % 2)
                for kt in range(16):
                    pbs = kt % 2
                    tp = (rbase, 0) if rbase == 96 else None
                    ev1 = p.do("tensor", (lambda rbase, cc, cblk, kt, qc, pbs, tp: lambda e: e.matmul(
                        bank(pbs), krf[rbase:rbase + 32, cc * S + kt * 128:cc * S + (kt + 1) * 128],
                        qro[rbase:rbase + 32, cblk * TOK + qc * 512:cblk * TOK + (qc + 1) * 512],
                        start=True, stop=False, tile_position=tp))(rbase, cc, cblk, kt, qc, pbs, tp),
                        reads=[bu("krf"), bu("qro")], writes=[b_ps[pbs]])
                    ev1 = p.do("tensor", (lambda rbase, cc, cblk, kt, qc, pbs, tp: lambda e: e.matmul(
                        bank(pbs), kcf[rbase:rbase + 32, cc * S + kt * 128:cc * S + (kt + 1) * 128],
                        qno[rbase:rbase + 32, cblk * TOK + qc * 512:cblk * TOK + (qc + 1) * 512],
                        start=False, stop=True, tile_position=tp))(rbase, cc, cblk, kt, qc, pbs, tp),
                        reads=[bu("kcf"), bu("qno")], writes=[b_ps[pbs]], signal=True)
                    ev2 = p.do("scalar", (lambda pbs, kt: lambda e: e.activation(
                        ptl[:, (kt % 2) * 512:(kt % 2 + 1) * 512], bank(pbs),
                        AF.Exp, scale=SCL))(pbs, kt),
                        deps=[ev1], reads=[b_ps[pbs]], writes=[bu("ptl")], signal=True)
                    ev3 = p.do("vector", (lambda kt: lambda e: e.tensor_mul(
                        out=ptl[:, (kt % 2) * 512:(kt % 2 + 1) * 512],
                        in0=ptl[:, (kt % 2) * 512:(kt % 2 + 1) * 512],
                        in1=maskb[:, kt * 512:(kt + 1) * 512]))(kt),
                        deps=[ev2, evmk], reads=[bu("maskb")], writes=[bu("ptl")],
                        signal=True)
                    p.do("tensor", (lambda kt, h, pav: lambda e: e.matmul(
                        PS[0:65, pav * 512:(pav + 1) * 512],
                        vbuf[:, kt * 520 + h * 65:kt * 520 + (h + 1) * 65],
                        ptl[:, (kt % 2) * 512:(kt % 2 + 1) * 512],
                        start=(kt == 0), stop=(kt == 15)))(kt, h, pav),
                        deps=[ev3], reads=[bu("vbuf"), bu("ptl")],
                        writes=[b_ps[pav]], signal=True)
                evc = p.do("vector", (lambda pav: lambda e: e.tensor_copy(
                    out=rb[0:1, 0:512], in_=PS[64:65, pav * 512:(pav + 1) * 512]))(pav),
                    reads=[b_ps[pav]], writes=[bu("rb")], signal=True)
                evc = p.do("vector", lambda e: e.reciprocal(out=rb[0:1, 0:512], in_=rb[0:1, 0:512]),
                           deps=[force(evc)], writes=[bu("rb")], signal=True)
                evb = p.do("tensor", lambda e: e.matmul(
                    bank(6)[0:64, :], onesb[0:1, 0:64], rb[0:1, 0:512],
                    start=True, stop=True),
                    deps=[evc, evo], reads=[bu("rb")], writes=[b_ps[6]], signal=True)
                evb = p.do("vector", lambda e: e.tensor_copy(out=rb[0:64, 0:512], in_=bank(6)[0:64, :]),
                           deps=[evb], reads=[b_ps[6]], writes=[bu("rb")], signal=True)
                arow = (qh % 2) * 64
                acol = (qh // 2) * TOK
                p.do("vector", (lambda arow, acol, qc, pav: lambda e: e.tensor_tensor(
                    out=attn_fm[arow:arow + 64, acol + qc * 512:acol + (qc + 1) * 512],
                    in0=PS[0:64, pav * 512:(pav + 1) * 512], in1=rb[0:64, 0:512],
                    op=ALU.mult))(arow, acol, qc, pav),
                    deps=[evb], reads=[b_ps[pav], bu("rb")], writes=[bu("attn")],
                    signal=True)

        p.full_barrier()
        for oc in range(4):
            evd = p.do("gpsimd", (lambda oc: lambda e: e.dma_start(
                out=wstage[:, 0:8192].rearrange("p (r c) -> p r c", r=4),
                in_=ow_in[oc * 512:(oc + 1) * 512, :]
                .rearrange("(r p) c -> p r c", p=128)))(oc),
                writes=[bu("wstage")], dma=True)
            for tg in range(2):
                for ic in range(16):
                    pcb = ic % 2
                    evt = None
                    for r in range(4):
                        evt = p.do("tensor", (lambda ic, r, pcb: lambda e: e.transpose(
                            bankb(4 + pcb)[:, r * 128:(r + 1) * 128],
                            wstage[:, r * 2048 + ic * 128:r * 2048 + (ic + 1) * 128],
                            identb[:]))(ic, r, pcb),
                            deps=[evi, evd], reads=[bu("wstage")], writes=[b_ps[4 + pcb]],
                            signal=(r == 3))
                    evp = p.do("vector", (lambda pcb: lambda e: e.tensor_copy(
                        out=piece[:, pcb * 512:(pcb + 1) * 512],
                        in_=bankb(4 + pcb)[:, 0:512]))(pcb),
                        deps=[evt], reads=[b_ps[4 + pcb]], writes=[bu("piece")], signal=True)
                    for tl in range(4):
                        t = tg * 4 + tl
                        p.do("tensor", (lambda ic, t, tl, pcb: lambda e: e.matmul(
                            bank(tl), attn_fm[:, ic * TOK + t * 128:ic * TOK + (t + 1) * 128],
                            piece[:, pcb * 512:(pcb + 1) * 512],
                            start=(ic == 0), stop=(ic == 15)))(ic, t, tl, pcb),
                            deps=[evp], reads=[bu("piece"), bu("attn")],
                            writes=[b_ps[tl]], signal=(ic == 15 or tl == 3))
                for tl in range(4):
                    t = tg * 4 + tl
                    p.do("vector", (lambda t, tl, oc: lambda e: e.tensor_copy(
                        out=ybf[:, t * D + oc * 512:t * D + (oc + 1) * 512],
                        in_=bank(tl)))(t, tl, oc),
                        reads=[b_ps[tl]], writes=[bu("ybf")], signal=True)

        p.full_barrier()
        for t in range(NT):
            yt = ybf[:, t * D:(t + 1) * D]
            evd = p.do("sync", (lambda t: lambda e: e.dma_start(
                out=xtb[:], in_=x_in[t * 128:(t + 1) * 128, :]))(t),
                writes=[bu("xtb")], dma=True)
            er, em = ln_stats(p, yt, rsc[:].bitcast(BF16)[:, 0:512], mv,
                              sc[:, 0:1], sc[:, 1:2], lnbufs, xbuf=bu("ybf"),
                              epsc=cst[:, 0:1], epsbuf=bu("cst"))
            ev = p.do("vector", (lambda yt: lambda e: e.tensor_scalar(
                out=yw[:], in0=yt, scalar1=sc[:, 1:2], scalar2=sc[:, 0:1],
                op0=ALU.add, op1=ALU.mult))(yt),
                deps=[er, em], reads=[bu("ybf"), bu("rs")], writes=[bu("yw")], signal=True)
            ev = p.do("vector", lambda e: e.tensor_add(out=yw[:], in0=yw[:], in1=xtb[:]),
                      deps=[evd, force(ev)], reads=[bu("xtb")], writes=[bu("yw")], signal=True)
            p.do("sync", (lambda t: lambda e: e.dma_start(
                out=out[t * 128:(t + 1) * 128, :], in_=yw[:]))(t),
                deps=[ev], reads=[bu("yw")], dma=True)

        fin = p.barrier_event("sync")
        p.do("sync", lambda e: None, deps=[fin])
        p.emit(blk)
    return nc


_L1ATTN_NC = None
_L1A_CONST = {}


def _l1a_consts():
    if _L1A_CONST:
        return _L1A_CONST
    pos = np.arange(S, dtype=np.float64)
    invq = 1.0 / 10000.0 ** (np.arange(0, 64, 2, dtype=np.float64) / 64)
    fq = pos[:, None] * invq[None, :]
    _L1A_CONST["cqc"] = np.cos(fq).astype(np.float32)
    _L1A_CONST["cqs"] = np.sin(fq).astype(np.float32)
    _L1A_CONST["ckc"] = np.cos(fq[:, 0:16]).astype(np.float32)
    _L1A_CONST["cks"] = np.sin(fq[:, 0:16]).astype(np.float32)
    return _L1A_CONST


def run_l1_attn(x_full, qw, krw, dww, upk, upv, ow):
    global _L1ATTN_NC
    if _L1ATTN_NC is None:
        _L1ATTN_NC = build_l1_attn()
    cst = _l1a_consts()
    in_maps = []
    for c in range(NCORES):
        sq_, hf = c // 2, c % 2
        own = slice(hf * TOK, (hf + 1) * TOK)
        oth = slice((1 - hf) * TOK, (2 - hf) * TOK)
        perm = np.concatenate([np.arange(hf * TOK, (hf + 1) * TOK),
                               np.arange((1 - hf) * TOK, (2 - hf) * TOK)])
        xs = np.ascontiguousarray(x_full[sq_][perm])
        qidx = np.arange(hf * TOK, (hf + 1) * TOK)
        mask = (perm[:, None] <= qidx[None, :]).astype(ml_dtypes.bfloat16)
        in_maps.append({
            "x": xs, "qw": qw, "krw": krw, "dww": dww, "upk": upk,
            "upv": upv, "ow": ow, "mask": np.ascontiguousarray(mask),
            "ckc": np.ascontiguousarray(cst["ckc"][perm]),
            "cks": np.ascontiguousarray(cst["cks"][perm]),
            "cqc": np.ascontiguousarray(cst["cqc"][qidx]),
            "cqs": np.ascontiguousarray(cst["cqs"][qidx]),
            "ident": _IDENT, "ones": _ONES,
        })
    res = run_bass_kernel_spmd(_L1ATTN_NC, in_maps, core_ids=list(range(NCORES)))
    out = np.empty((B, S, D), dtype=np.float32)
    for c in range(NCORES):
        sq_, hf = c // 2, c % 2
        out[sq_, hf * TOK:(hf + 1) * TOK, :] = res.results[c]["out"]
    return out


def build_l0_attn():
    nc = bass.Bass("TRN2", target_bir_lowering=False)
    sems, _ctx = fnew(nc)
    I32 = mybir.dt.int32

    x_in = nc.dram_tensor("x", [S, D], F32, kind="ExternalInput").ap()
    wdkv_in = nc.dram_tensor("wdkv", [D, 512], F32, kind="ExternalInput").ap()
    wuk_in = nc.dram_tensor("wuk", [512, 2048], F32, kind="ExternalInput").ap()
    wuv_in = nc.dram_tensor("wuv", [512, 2048], F32, kind="ExternalInput").ap()
    wdq_in = nc.dram_tensor("wdq", [D, 1024], F32, kind="ExternalInput").ap()
    wuq_in = nc.dram_tensor("wuq", [1024, 2048], F32, kind="ExternalInput").ap()
    wqr_in = nc.dram_tensor("wqr", [1024, 1024], F32, kind="ExternalInput").ap()
    wkr_in = nc.dram_tensor("wkr", [D, 64], F32, kind="ExternalInput").ap()
    wout_in = nc.dram_tensor("wout", [D, D], F32, kind="ExternalInput").ap()
    delta_in = nc.dram_tensor("delta", [32], F32, kind="ExternalInput").ap()
    mask_in = nc.dram_tensor("mask", [S, TOK], BF16, kind="ExternalInput").ap()
    c0_in = nc.dram_tensor("c0", [S, 32], F32, kind="ExternalInput").ap()
    s0_in = nc.dram_tensor("s0", [S, 32], F32, kind="ExternalInput").ap()
    ident_in = nc.dram_tensor("ident", [128, 128], BF16, kind="ExternalInput").ap()
    ones_in = nc.dram_tensor("ones", [128, 128], F32, kind="ExternalInput").ap()
    out = nc.dram_tensor("out", [TOK, D], F32, kind="ExternalOutput").ap()

    import contextlib
    with contextlib.ExitStack() as stk:
        sb = lambda *a: stk.enter_context(nc.sbuf_tensor(*a))
        xhat = sb("xhat", [128, 16 * S], BF16)
        ckv = sb("ckv", [128, 4 * S], BF16)
        cq = sb("cq", [128, 8 * TOK], BF16)
        qrf = sb("qrf", [128, 8 * TOK], BF16)
        krfb = sb("krfb", [128, S], BF16)
        kcb = sb("kcb", [128, S], BF16)
        qcb = sb("qcb", [128, TOK], BF16)
        vg = sb("vg", [128, 16 * 512], BF16)
        wstage = sb("wstage", [128, 8192], BF16)
        maskb = sb("maskb", [128, 2 * 16 * 512], BF16)
        ctf = sb("ctf", [128, 512], F32)
        stf = sb("stf", [128, 512], F32)
        rsc = sb("rsc", [128, 2048], F32)
        qtm = sb("qtm", [128, 1024], BF16)
        ptl = sb("ptl", [128, 2 * 512], BF16)
        rb = sb("rb", [128, 512], F32)
        dsc = sb("dsc", [1, 256], F32)
        dsi = sb("dsi", [1, 32], I32)
        cdb = sb("cdb", [128, 64], F32)
        mv = sb("mv", [128, 8], F32)
        sc = sb("sc", [128, 4], F32)
        cst = sb("cst", [128, 2], F32)
        identb = sb("identb", [128, 128], BF16)
        onesb = sb("onesb", [128, 128], F32)
        onebf = sb("onebf", [128, 1], BF16)
        PS = stk.enter_context(nc.psum_tensor("PS", [128, 4096], F32))
        blk = stk.enter_context(nc.Block())

        p = Prog(nc, sems)
        bank = lambda i: PS[:, i * 512:(i + 1) * 512]
        bankb = lambda i: PS[:, i * 512:(i + 1) * 512].bitcast(BF16)
        attn_fm = xhat[:, 0:16 * TOK]
        ybf = xhat[:, 16 * TOK:32 * TOK]
        cqF = cq.bitcast(F32)
        xtb = cqF[:, 0:D]
        yw = cqF[:, D:2 * D]
        wstF = wstage.bitcast(F32)
        c0t = wstF[:, 0:512]
        s0t = wstF[:, 512:1024]

        names = ["xhat", "ckv", "cq", "qrf", "krfb", "kcb", "qcb", "vg",
                 "wstage", "maskb", "tab", "rsc", "qtm", "ptl", "rb",
                 "dsc", "cdb", "mv", "rs", "cst", "attn", "ybf", "xtb", "yw",
                 "ones"]
        bufs = {n: TBuf(n) for n in names}
        b_ps = [TBuf(f"ps{i}") for i in range(8)]
        bu = lambda n: bufs[n]
        lnbufs = (bu("rsc"), bu("mv"), bu("rs"))
        TWO_PI = float(2.0 * np.pi)

        evi = p.do("sync", lambda e: e.dma_start(out=identb[:], in_=ident_in), dma=True)
        evo = p.do("sync", lambda e: e.dma_start(out=onesb[:], in_=ones_in), dma=True)
        p.do("vector", lambda e: e.memset(cst[:, 0:1], EPS), writes=[bu("cst")], signal=True)
        p.do("vector", lambda e: e.memset(onebf[:], 1.0), writes=[bu("ones")], signal=True)
        evc0 = p.do("sync", lambda e: e.dma_start(
            out=c0t.rearrange("p (tt j) -> p tt j", j=32),
            in_=c0_in.rearrange("(tt p) j -> p tt j", p=128)), writes=[bu("wstage")], dma=True)
        evs0 = p.do("sync", lambda e: e.dma_start(
            out=s0t.rearrange("p (tt j) -> p tt j", j=32),
            in_=s0_in.rearrange("(tt p) j -> p tt j", p=128)), writes=[bu("wstage")], dma=True)
        evd = p.do("sync", lambda e: e.dma_start(
            out=dsc[0:1, 0:32], in_=delta_in.rearrange("(a j) -> a j", a=1)),
            writes=[bu("dsc")], dma=True)
        ev = p.do("vector", lambda e: e.tensor_scalar_mul(
            out=dsc[0:1, 32:64], in0=dsc[0:1, 0:32], scalar1=1.0 / TWO_PI),
            deps=[evd], writes=[bu("dsc")], signal=True)
        ev = p.do("vector", lambda e: e.tensor_copy(out=dsi[0:1, :], in_=dsc[0:1, 32:64]),
                  deps=[force(ev)], writes=[bu("dsc")], signal=True)
        ev = p.do("vector", lambda e: e.tensor_copy(out=dsc[0:1, 32:64], in_=dsi[0:1, :]),
                  deps=[force(ev)], writes=[bu("dsc")], signal=True)
        ev = p.do("vector", lambda e: e.scalar_tensor_tensor(
            out=dsc[0:1, 64:96], in0=dsc[0:1, 32:64], scalar=-TWO_PI,
            in1=dsc[0:1, 0:32], op0=ALU.mult, op1=ALU.add),
            deps=[force(ev)], writes=[bu("dsc")], signal=True)
        ev1 = p.do("scalar", lambda e: e.activation(dsc[0:1, 96:128], dsc[0:1, 64:96], AF.Sin),
                   deps=[ev], writes=[bu("dsc")], signal=True)
        ev2 = p.do("scalar", lambda e: e.activation(dsc[0:1, 128:160], dsc[0:1, 64:96],
                                                    AF.Sin, scale=0.5),
                   deps=[force(ev1)], writes=[bu("dsc")], signal=True)
        ev2 = p.do("scalar", lambda e: e.activation(dsc[0:1, 160:192], dsc[0:1, 128:160], AF.Square),
                   deps=[force(ev2)], writes=[bu("dsc")], signal=True)
        ev2 = p.do("vector", lambda e: e.tensor_scalar(
            out=dsc[0:1, 192:224], in0=dsc[0:1, 160:192], scalar1=-2.0, scalar2=1.0,
            op0=ALU.mult, op1=ALU.add),
            deps=[ev2], writes=[bu("dsc")], signal=True)
        evb = p.do("tensor", lambda e: e.matmul(PS[:, 0:32], onesb[0:1, :],
                                                dsc[0:1, 192:224],
                                                start=True, stop=True),
                   deps=[evo, ev2], reads=[bu("dsc")], writes=[b_ps[0]], signal=True)
        evb2 = p.do("tensor", lambda e: e.matmul(PS[:, 64:96], onesb[0:1, :], dsc[0:1, 96:128],
                                                 start=True, stop=True),
                    deps=[evo, ev1], reads=[bu("dsc")], writes=[b_ps[0]], signal=True)
        evb = p.do("vector", lambda e: e.tensor_copy(out=cdb[:, 0:32], in_=PS[:, 0:32]),
                   deps=[evb], reads=[b_ps[0]], writes=[bu("cdb")], signal=True)
        evb = p.do("vector", lambda e: e.tensor_copy(out=cdb[:, 32:64], in_=PS[:, 64:96]),
                   deps=[evb2], reads=[b_ps[0]], writes=[bu("cdb")], signal=True)
        from concourse.bass import broadcast_tensor_aps
        c0v = c0t.rearrange("p (tt j) -> p tt j", j=32)
        s0v = s0t.rearrange("p (tt j) -> p tt j", j=32)
        cdv = cdb[:, 0:32].rearrange("p (tt j) -> p tt j", tt=1)
        sdv = cdb[:, 32:64].rearrange("p (tt j) -> p tt j", tt=1)
        _, cdvb = broadcast_tensor_aps(c0v, cdv)
        _, sdvb = broadcast_tensor_aps(c0v, sdv)
        ctv = ctf[:].rearrange("p (tt j) -> p tt j", j=32)
        stv = stf[:].rearrange("p (tt j) -> p tt j", j=32)
        rv = rsc[:, 0:512].rearrange("p (tt j) -> p tt j", j=32)
        e1 = p.do("vector", lambda e: e.tensor_tensor(out=ctv, in0=c0v, in1=cdvb, op=ALU.mult),
                  deps=[evb, evc0], reads=[bu("cdb"), bu("wstage")], writes=[bu("tab")], signal=True)
        e1 = p.do("vector", lambda e: e.tensor_tensor(out=rv, in0=s0v, in1=sdvb, op=ALU.mult),
                  deps=[evs0], reads=[bu("cdb"), bu("wstage")], writes=[bu("rsc")], signal=True)
        e1 = p.do("vector", lambda e: e.tensor_tensor(out=ctv, in0=ctv, in1=rv, op=ALU.subtract),
                  deps=[force(e1)], reads=[bu("rsc")], writes=[bu("tab")], signal=True)
        e2 = p.do("vector", lambda e: e.tensor_tensor(out=stv, in0=s0v, in1=cdvb, op=ALU.mult),
                  reads=[bu("cdb"), bu("wstage")], writes=[bu("tab")], signal=True)
        e2 = p.do("vector", lambda e: e.tensor_tensor(out=rv, in0=c0v, in1=sdvb, op=ALU.mult),
                  reads=[bu("cdb"), bu("wstage")], writes=[bu("rsc")], signal=True)
        e2 = p.do("vector", lambda e: e.tensor_tensor(out=stv, in0=stv, in1=rv, op=ALU.add),
                  deps=[force(e2)], reads=[bu("rsc")], writes=[bu("tab")], signal=True)

        for t in range(16):
            evdx = p.do("sync", (lambda t: lambda e: e.dma_start(
                out=xtb[:], in_=x_in[t * 128:(t + 1) * 128, :]))(t),
                writes=[bu("xtb")], dma=True)
            er, em = ln_stats(p, xtb[:], qtm[:, 0:512], mv, sc[:, 0:1], sc[:, 1:2],
                              lnbufs, xbuf=bu("xtb"), epsc=cst[:, 0:1],
                              epsbuf=bu("cst"), deps=[evdx])
            ev = p.do("vector", lambda e: e.tensor_scalar(
                out=wstage[:, 4096:4096 + D], in0=xtb[:], scalar1=sc[:, 1:2],
                scalar2=sc[:, 0:1], op0=ALU.add, op1=ALU.mult),
                deps=[er, em], reads=[bu("xtb"), bu("rs")], writes=[bu("wstage")],
                signal=True)
            for g in range(2):
                pb = 6 + g
                evg = None
                for dc in range(g * 8, g * 8 + 8):
                    evg = p.do("tensor", (lambda dc, g, pb: lambda e: e.transpose(
                        bankb(pb)[:, (dc - g * 8) * 128:(dc - g * 8 + 1) * 128],
                        wstage[:, 4096 + dc * 128:4096 + (dc + 1) * 128],
                        identb[:]))(dc, g, pb),
                        deps=[evi, ev], reads=[bu("wstage")], writes=[b_ps[pb]],
                        signal=(dc == g * 8 + 7))
                for dc in range(g * 8, g * 8 + 8):
                    p.do("vector", (lambda t, dc, pb, g: lambda e: e.tensor_copy(
                        out=xhat[:, dc * S + t * 128:dc * S + (t + 1) * 128],
                        in_=bankb(pb)[:, (dc - g * 8) * 128:(dc - g * 8 + 1) * 128]))(t, dc, pb, g),
                        deps=[evg], reads=[b_ps[pb]], writes=[bu("xhat")],
                        signal=(dc == g * 8 + 7))

        p.full_barrier()
        evw = p.do("gpsimd", lambda e: e.dma_start(
            out=wstage[:, 0:8192].rearrange("p (ic c) -> p ic c", c=512),
            in_=wdkv_in.rearrange("(ic p) c -> p ic c", p=128)),
            writes=[bu("wstage")], dma=True)
        for m in range(4):
            for ts in range(4):
                evm = None
                for ic in range(16):
                    evm = p.do("tensor", (lambda m, ts, ic: lambda e: e.matmul(
                        bank(m % 2), wstage[:, ic * 512 + m * 128:ic * 512 + (m + 1) * 128],
                        xhat[:, ic * S + ts * 512:ic * S + (ts + 1) * 512],
                        start=(ic == 0), stop=(ic == 15)))(m, ts, ic),
                        deps=[evw], reads=[bu("wstage"), bu("xhat")], writes=[b_ps[m % 2]],
                        signal=(ic == 15))
                p.do("vector", (lambda m, ts: lambda e: e.tensor_copy(
                    out=ckv[:, m * S + ts * 512:m * S + (ts + 1) * 512],
                    in_=bank(m % 2)))(m, ts),
                    deps=[evm], reads=[b_ps[m % 2]], writes=[bu("ckv")], signal=True)

        for half in range(2):
            evw = p.do("gpsimd", (lambda half: lambda e: e.dma_start(
                out=wstage[:, 0:8192].rearrange("p (ic c) -> p ic c", c=512),
                in_=wdq_in[:, half * 512:(half + 1) * 512]
                .rearrange("(ic p) c -> p ic c", p=128)))(half),
                writes=[bu("wstage")], dma=True)
            for m in range(4):
                mg = half * 4 + m
                for ts in range(2):
                    evm = None
                    for ic in range(16):
                        evm = p.do("tensor", (lambda m, ts, ic: lambda e: e.matmul(
                            bank(2 + m % 2), wstage[:, ic * 512 + m * 128:ic * 512 + (m + 1) * 128],
                            xhat[:, ic * S + ts * 512:ic * S + (ts + 1) * 512],
                            start=(ic == 0), stop=(ic == 15)))(m, ts, ic),
                            deps=[evw], reads=[bu("wstage"), bu("xhat")],
                            writes=[b_ps[2 + m % 2]], signal=(ic == 15))
                    p.do("vector", (lambda mg, ts: lambda e: e.tensor_copy(
                        out=cq[:, mg * TOK + ts * 512:mg * TOK + (ts + 1) * 512],
                        in_=bank(2 + mg % 2)))(mg, ts),
                        deps=[evm], reads=[b_ps[2 + mg % 2]], writes=[bu("cq")], signal=True)

        evw = p.do("gpsimd", lambda e: e.dma_start(
            out=wstage[:, 0:1024].rearrange("p (ic c) -> p ic c", c=64),
            in_=wkr_in.rearrange("(ic p) c -> p ic c", p=128)),
            writes=[bu("wstage")], dma=True)

        def pope(psum_ap, width, t, outtm, deps0, tabc=None, tabs=None):
            nh = width // 64
            evA = p.do("scalar", (lambda: lambda e: e.activation(
                rsc[:, 0:width], psum_ap, AF.Abs))(),
                deps=deps0, reads=b_ps_cur, writes=[bu("rsc")], signal=True)
            evA = p.do("scalar", (lambda: lambda e: e.activation(
                rsc[:, 0:width], rsc[:, 0:width], AF.Exp, scale=-1.0))(),
                deps=[force(evA)], writes=[bu("rsc")], signal=True)
            evA = p.do("scalar", (lambda: lambda e: e.activation(
                rsc[:, 0:width], rsc[:, 0:width], AF.Ln, bias=1.0))(),
                deps=[force(evA)], writes=[bu("rsc")], signal=True)
            sc2 = rsc[:, width:2 * width]
            evR = p.do("vector", (lambda: lambda e: e.tensor_relu(
                out=sc2, in_=psum_ap))(),
                deps=deps0, reads=b_ps_cur, writes=[bu("rsc")], signal=True)
            evA = p.do("vector", (lambda: lambda e: e.tensor_tensor(
                out=rsc[:, 0:width], in0=rsc[:, 0:width], in1=sc2,
                op=ALU.add))(),
                deps=[evA, force(evR)], writes=[bu("rsc")], signal=True)
            muv = rsc[:, 0:width].rearrange("p (h two j) -> p h two j", two=2, j=32)
            m1, m2 = muv[:, :, 0, :], muv[:, :, 1, :]
            ct1 = ctf[:, t * 32:(t + 1) * 32].rearrange("p (h j) -> p h j", h=1)
            st1 = stf[:, t * 32:(t + 1) * 32].rearrange("p (h j) -> p h j", h=1)
            _, ctb = broadcast_tensor_aps(m1, ct1)
            _, stb = broadcast_tensor_aps(m1, st1)
            ov = outtm.rearrange("p (h two j) -> p h two j", two=2, j=32)
            x1, x2 = ov[:, :, 0, :], ov[:, :, 1, :]
            s3 = sc2.rearrange("p (h two j) -> p h two j", two=2, j=32)
            t1, t2 = s3[:, :, 0, :], s3[:, :, 1, :]
            eX = p.do("vector", (lambda: lambda e: e.tensor_tensor(
                out=x1, in0=m1, in1=ctb, op=ALU.mult))(),
                deps=[force(evA)], reads=[bu("tab")], writes=[bu("outtm")], signal=True)
            eX = p.do("vector", (lambda: lambda e: e.tensor_tensor(
                out=x2, in0=m1, in1=stb, op=ALU.mult))(),
                reads=[bu("tab")], writes=[bu("outtm")], signal=True)
            eY = p.do("vector", (lambda: lambda e: e.tensor_tensor(
                out=t1, in0=m2, in1=stb, op=ALU.mult))(),
                reads=[bu("tab")], writes=[bu("rsc")], signal=True)
            eY = p.do("vector", (lambda: lambda e: e.tensor_tensor(
                out=t2, in0=m2, in1=ctb, op=ALU.mult))(),
                reads=[bu("tab")], writes=[bu("rsc")], signal=True)
            eZ = p.do("vector", (lambda: lambda e: e.tensor_tensor(
                out=x1, in0=x1, in1=t1, op=ALU.subtract))(),
                deps=[force(eX), force(eY)], reads=[bu("rsc")], writes=[bu("outtm")], signal=True)
            eZ = p.do("vector", (lambda: lambda e: e.tensor_tensor(
                out=x2, in0=x2, in1=t2, op=ALU.add))(),
                deps=[force(eZ)], reads=[bu("rsc")], writes=[bu("outtm")], signal=True)
            return eZ

        bufs["outtm"] = TBuf("outtm")
        for t in range(16):
            evm = None
            for ic in range(16):
                evm = p.do("tensor", (lambda t, ic: lambda e: e.matmul(
                    PS[:, 2048:2112], xhat[:, ic * S + t * 128:ic * S + (t + 1) * 128],
                    wstage[:, ic * 64:(ic + 1) * 64],
                    start=(ic == 0), stop=(ic == 15)))(t, ic),
                    deps=[evw], reads=[bu("wstage"), bu("xhat")], writes=[b_ps[4]],
                    signal=(ic == 15))
            b_ps_cur = [b_ps[4]]
            ktm = qtm[:, 512:576]
            eZ = pope(PS[:, 2048:2112], 64, t, ktm, [evm])
            evt = p.do("tensor", (lambda: lambda e: e.transpose(
                bankb(5)[0:64, 0:128], ktm, identb[:]))(),
                deps=[evi, eZ], reads=[bu("qtm")], writes=[b_ps[5]], signal=True)
            p.do("vector", (lambda t: lambda e: e.tensor_copy(
                out=krfb[0:64, t * 128:(t + 1) * 128], in_=bankb(5)[0:64, 0:128]))(t),
                deps=[evt], reads=[b_ps[5]], writes=[bu("krfb")], signal=True)
            p.do("vector", (lambda t: lambda e: e.tensor_copy(
                out=krfb[64:128, t * 128:(t + 1) * 128], in_=bankb(5)[0:64, 0:128]))(t),
                deps=[evt], reads=[b_ps[5]], writes=[bu("krfb")], signal=True)

        p.full_barrier()
        evw = p.do("gpsimd", lambda e: e.dma_start(
            out=wstage[:, 0:8192].rearrange("p (ic c) -> p ic c", c=1024),
            in_=wqr_in.rearrange("(ic p) c -> p ic c", p=128)),
            writes=[bu("wstage")], dma=True)
        for t in range(NT):
            for oc2 in range(2):
                evm = None
                for ic in range(8):
                    evm = p.do("tensor", (lambda t, oc2, ic: lambda e: e.matmul(
                        bank(4 + oc2), cq[:, ic * TOK + t * 128:ic * TOK + (t + 1) * 128],
                        wstage[:, ic * 1024 + oc2 * 512:ic * 1024 + (oc2 + 1) * 512],
                        start=(ic == 0), stop=(ic == 7)))(t, oc2, ic),
                        deps=[evw], reads=[bu("wstage"), bu("cq")], writes=[b_ps[4 + oc2]],
                        signal=(ic == 7))
            b_ps_cur = [b_ps[4], b_ps[5]]
            qr_ps = PS[:, 2048:3072]
            eZ = pope(qr_ps, 1024, t, qtm[:, 0:1024], [p.barrier_event("tensor")])
            evt = None
            for rc in range(8):
                evt = p.do("tensor", (lambda rc: lambda e: e.transpose(
                    bankb(6)[:, rc * 128:(rc + 1) * 128],
                    qtm[:, rc * 128:(rc + 1) * 128], identb[:]))(rc),
                    deps=[evi, eZ], reads=[bu("qtm")], writes=[b_ps[6]],
                    signal=(rc == 7))
            for rc in range(8):
                p.do("vector", (lambda t, rc: lambda e: e.tensor_copy(
                    out=qrf[:, rc * TOK + t * 128:rc * TOK + (t + 1) * 128],
                    in_=bankb(6)[:, rc * 128:(rc + 1) * 128]))(t, rc),
                    deps=[evt], reads=[b_ps[6]], writes=[bu("qrf")],
                    signal=(rc == 7))

        p.full_barrier()
        SCL = float(192) ** -0.5
        evmk = [None, None]
        for qc in range(2):
            evmk[qc] = p.do("sync", (lambda qc: lambda e: e.dma_start(
                out=maskb[:, qc * 8192:(qc + 1) * 8192]
                .rearrange("p (tt c) -> p tt c", tt=16),
                in_=mask_in[:, qc * 512:(qc + 1) * 512]
                .rearrange("(tt p) c -> p tt c", p=128)))(qc),
                writes=[bu("maskb")], dma=True)
        for g in range(4):
            evwv = p.do("gpsimd", (lambda g: lambda e: e.dma_start(
                out=wstage[:, 0:2048].rearrange("p (ic c) -> p ic c", c=512),
                in_=wuv_in[:, g * 512:(g + 1) * 512]
                .rearrange("(ic p) c -> p ic c", p=128)))(g),
                writes=[bu("wstage")], dma=True)
            for kt in range(16):
                evm = None
                for ic in range(4):
                    evm = p.do("tensor", (lambda g, kt, ic: lambda e: e.matmul(
                        bank(4), ckv[:, ic * S + kt * 128:ic * S + (kt + 1) * 128],
                        wstage[:, ic * 512:(ic + 1) * 512],
                        start=(ic == 0), stop=(ic == 3)))(g, kt, ic),
                        deps=[evwv], reads=[bu("wstage"), bu("ckv")], writes=[b_ps[4]],
                        signal=(ic == 3))
                p.do("vector", (lambda kt: lambda e: e.tensor_copy(
                    out=vg[:, kt * 512:(kt + 1) * 512], in_=bank(4)))(kt),
                    deps=[evm], reads=[b_ps[4]], writes=[bu("vg")], signal=True)
            for hl in range(4):
                h = g * 4 + hl
                ksl = 0
                evwk = p.do("gpsimd", (lambda h: lambda e: e.dma_start(
                    out=wstage[:, 2048:2560].rearrange("p (ic c) -> p ic c", c=128),
                    in_=wuk_in[:, h * 128:(h + 1) * 128]
                    .rearrange("(ic p) c -> p ic c", p=128)))(h),
                    writes=[bu("wstage")], dma=True)
                for ts in range(4):
                    evm = None
                    for ic in range(4):
                        evm = p.do("tensor", (lambda h, ts, ic: lambda e: e.matmul(
                            bank(5), wstage[:, 2048 + ic * 128:2048 + (ic + 1) * 128],
                            ckv[:, ic * S + ts * 512:ic * S + (ts + 1) * 512],
                            start=(ic == 0), stop=(ic == 3)))(h, ts, ic),
                            deps=[evwk], reads=[bu("wstage"), bu("ckv")], writes=[b_ps[5]],
                            signal=(ic == 3))
                    p.do("vector", (lambda ksl, ts: lambda e: e.tensor_copy(
                        out=kcb[:, ksl * S + ts * 512:ksl * S + (ts + 1) * 512],
                        in_=bank(5)))(ksl, ts),
                        deps=[evm], reads=[b_ps[5]], writes=[bu("kcb")], signal=True)
                evwq = p.do("gpsimd", (lambda h: lambda e: e.dma_start(
                    out=wstage[:, 4096:5120].rearrange("p (ic c) -> p ic c", c=128),
                    in_=wuq_in[:, h * 128:(h + 1) * 128]
                    .rearrange("(ic p) c -> p ic c", p=128)))(h),
                    writes=[bu("wstage")], dma=True)
                for ts in range(2):
                    evm = None
                    for ic in range(8):
                        evm = p.do("tensor", (lambda h, ts, ic: lambda e: e.matmul(
                            bank(6), wstage[:, 4096 + ic * 128:4096 + (ic + 1) * 128],
                            cq[:, ic * TOK + ts * 512:ic * TOK + (ts + 1) * 512],
                            start=(ic == 0), stop=(ic == 7)))(h, ts, ic),
                            deps=[evwq], reads=[bu("wstage"), bu("cq")], writes=[b_ps[6]],
                            signal=(ic == 7))
                    p.do("vector", (lambda ksl, ts: lambda e: e.tensor_copy(
                        out=qcb[:, ksl * TOK + ts * 512:ksl * TOK + (ts + 1) * 512],
                        in_=bank(6)))(ksl, ts),
                        deps=[evm], reads=[b_ps[6]], writes=[bu("qcb")], signal=True)
                for qc in range(2):
                    for kt in range(16):
                        pbs = kt % 2
                        ev1 = p.do("tensor", (lambda ksl, kt, qc, pbs: lambda e: e.matmul(
                            bank(pbs), kcb[:, ksl * S + kt * 128:ksl * S + (kt + 1) * 128],
                            qcb[:, ksl * TOK + qc * 512:ksl * TOK + (qc + 1) * 512],
                            start=True, stop=False))(ksl, kt, qc, pbs),
                            reads=[bu("kcb"), bu("qcb")], writes=[b_ps[pbs]],
                            signal=True)
                        ev1 = p.do("tensor", (lambda h, kt, qc, pbs: lambda e: e.matmul(
                            bank(pbs), krfb[(h % 2) * 64:(h % 2) * 64 + 64,
                                            kt * 128:(kt + 1) * 128],
                            qrf[(h % 2) * 64:(h % 2) * 64 + 64,
                                (h // 2) * TOK + qc * 512:(h // 2) * TOK + (qc + 1) * 512],
                            start=False, stop=True,
                            tile_position=((h % 2) * 64, 0)))(h, kt, qc, pbs),
                            reads=[bu("krfb"), bu("qrf")], writes=[b_ps[pbs]], signal=True)
                        ev2 = p.do("scalar", (lambda pbs, kt: lambda e: e.activation(
                            ptl[:, (kt % 2) * 512:(kt % 2 + 1) * 512], bank(pbs),
                            AF.Exp, scale=SCL))(pbs, kt),
                            deps=[ev1], reads=[b_ps[pbs]], writes=[bu("ptl")], signal=True)
                        ev3 = p.do("vector", (lambda kt, qc: lambda e: e.tensor_mul(
                            out=ptl[:, (kt % 2) * 512:(kt % 2 + 1) * 512],
                            in0=ptl[:, (kt % 2) * 512:(kt % 2 + 1) * 512],
                            in1=maskb[:, qc * 8192 + kt * 512:qc * 8192 + (kt + 1) * 512]))(kt, qc),
                            deps=[ev2, evmk[qc]], reads=[bu("maskb")], writes=[bu("ptl")],
                            signal=True)
                        pav = 2 + (h % 2)
                        p.do("tensor", (lambda kt, hl, pav: lambda e: e.matmul(
                            bank(pav), vg[:, kt * 512 + hl * 128:kt * 512 + (hl + 1) * 128],
                            ptl[:, (kt % 2) * 512:(kt % 2 + 1) * 512],
                            start=(kt == 0), stop=(kt == 15)))(kt, hl, pav),
                            deps=[ev3], reads=[bu("vg"), bu("ptl")],
                            writes=[b_ps[pav]], signal=True)
                        p.do("tensor", (lambda kt: lambda e: e.matmul(
                            PS[0:1, 3584:4096], onebf[:],
                            ptl[:, (kt % 2) * 512:(kt % 2 + 1) * 512],
                            start=(kt == 0), stop=(kt == 15)))(kt),
                            deps=[ev3], reads=[bu("ones"), bu("ptl")],
                            writes=[b_ps[7]], signal=True)
                    pav = 2 + (h % 2)
                    evc = p.do("vector", lambda e: e.tensor_copy(
                        out=rb[0:1, 0:512], in_=PS[0:1, 3584:4096]),
                        reads=[b_ps[7]], writes=[bu("rb")], signal=True)
                    evc = p.do("vector", lambda e: e.reciprocal(
                        out=rb[0:1, 0:512], in_=rb[0:1, 0:512]),
                        deps=[force(evc)], writes=[bu("rb")], signal=True)
                    evb = p.do("tensor", lambda e: e.matmul(
                        bank(6), onesb[0:1, :], rb[0:1, 0:512], start=True, stop=True),
                        deps=[evc, evo], reads=[bu("rb")], writes=[b_ps[6]], signal=True)
                    evb = p.do("vector", lambda e: e.tensor_copy(
                        out=rb[:, 0:512], in_=bank(6)),
                        deps=[evb], reads=[b_ps[6]], writes=[bu("rb")], signal=True)
                    p.do("vector", (lambda h, qc, pav: lambda e: e.tensor_tensor(
                        out=attn_fm[:, h * TOK + qc * 512:h * TOK + (qc + 1) * 512],
                        in0=bank(pav), in1=rb[:, 0:512], op=ALU.mult))(h, qc, pav),
                        deps=[evb], reads=[b_ps[pav], bu("rb")], writes=[bu("attn")],
                        signal=True)

        p.full_barrier()
        for oc in range(4):
            evm_all = {}
            for ic in range(16):
                wsl = ic % 2
                evwo = p.do("gpsimd", (lambda oc, ic, wsl: lambda e: e.dma_start(
                    out=wstage[:, wsl * 512:(wsl + 1) * 512],
                    in_=wout_in[ic * 128:(ic + 1) * 128,
                                oc * 512:(oc + 1) * 512]))(oc, ic, wsl),
                    writes=[bu("wstage")], dma=True)
                for t in range(NT):
                    evm = p.do("tensor", (lambda ic, t, wsl: lambda e: e.matmul(
                        bank(t), attn_fm[:, ic * TOK + t * 128:ic * TOK + (t + 1) * 128],
                        wstage[:, wsl * 512:(wsl + 1) * 512],
                        start=(ic == 0), stop=(ic == 15)))(ic, t, wsl),
                        deps=[evwo], reads=[bu("wstage"), bu("attn")],
                        writes=[b_ps[t]], signal=(ic == 15 or t == NT - 1))
                    if ic == 15:
                        evm_all[t] = evm
            for t in range(NT):
                p.do("vector", (lambda t, oc: lambda e: e.tensor_copy(
                    out=ybf[:, t * D + oc * 512:t * D + (oc + 1) * 512],
                    in_=bank(t)))(t, oc),
                    deps=[evm_all[t]], reads=[b_ps[t]], writes=[bu("ybf")],
                    signal=True)

        p.full_barrier()
        for t in range(NT):
            yt = ybf[:, t * D:(t + 1) * D]
            evdx = p.do("sync", (lambda t: lambda e: e.dma_start(
                out=xtb[:], in_=x_in[t * 128:(t + 1) * 128, :]))(t),
                writes=[bu("xtb")], dma=True)
            er, em = ln_stats(p, yt, qtm[:, 0:512], mv, sc[:, 0:1], sc[:, 1:2],
                              lnbufs, xbuf=bu("ybf"), epsc=cst[:, 0:1],
                              epsbuf=bu("cst"))
            ev = p.do("vector", (lambda yt: lambda e: e.tensor_scalar(
                out=yw[:], in0=yt, scalar1=sc[:, 1:2], scalar2=sc[:, 0:1],
                op0=ALU.add, op1=ALU.mult))(yt),
                deps=[er, em], reads=[bu("ybf"), bu("rs")], writes=[bu("yw")],
                signal=True)
            ev = p.do("vector", lambda e: e.tensor_add(out=yw[:], in0=yw[:], in1=xtb[:]),
                      deps=[evdx, force(ev)], reads=[bu("xtb")], writes=[bu("yw")],
                      signal=True)
            p.do("sync", (lambda t: lambda e: e.dma_start(
                out=out[t * 128:(t + 1) * 128, :], in_=yw[:]))(t),
                deps=[ev], reads=[bu("yw")], dma=True)

        fin = p.barrier_event("sync")
        p.do("sync", lambda e: None, deps=[fin])
        p.emit(blk)
    return nc


_L0ATTN_NC = None
_L0A_CONST = {}


def _l0a_consts():
    if _L0A_CONST:
        return _L0A_CONST
    pos = np.arange(S, dtype=np.float64)
    theta = 1.0 / 10000.0 ** (2.0 * np.arange(32, dtype=np.float64) / 64)
    ang = pos[:, None] * theta[None, :]
    ang = np.angle(np.exp(1j * ang))
    _L0A_CONST["c0"] = np.cos(ang).astype(np.float32)
    _L0A_CONST["s0"] = np.sin(ang).astype(np.float32)
    return _L0A_CONST


def run_l0_attn(x_full, wdkv, wuk, wuv, wdq, wuq, wqr, wkr, wout, delta):
    global _L0ATTN_NC
    if _L0ATTN_NC is None:
        _L0ATTN_NC = build_l0_attn()
    cst = _l0a_consts()
    in_maps = []
    for c in range(NCORES):
        sq_, hf = c // 2, c % 2
        perm = np.concatenate([np.arange(hf * TOK, (hf + 1) * TOK),
                               np.arange((1 - hf) * TOK, (2 - hf) * TOK)])
        xs = np.ascontiguousarray(x_full[sq_][perm])
        qidx = np.arange(hf * TOK, (hf + 1) * TOK)
        mask = (perm[:, None] <= qidx[None, :]).astype(ml_dtypes.bfloat16)
        in_maps.append({
            "x": xs, "wdkv": wdkv, "wuk": wuk, "wuv": wuv, "wdq": wdq,
            "wuq": wuq, "wqr": wqr, "wkr": wkr, "wout": wout, "delta": delta,
            "mask": np.ascontiguousarray(mask),
            "c0": np.ascontiguousarray(cst["c0"][perm]),
            "s0": np.ascontiguousarray(cst["s0"][perm]),
            "ident": _IDENT, "ones": _ONES,
        })
    res = run_bass_kernel_spmd(_L0ATTN_NC, in_maps, core_ids=list(range(NCORES)))
    out = np.empty((B, S, D), dtype=np.float32)
    for c in range(NCORES):
        sq_, hf = c // 2, c % 2
        out[sq_, hf * TOK:(hf + 1) * TOK, :] = res.results[c]["out"]
    return out



def emit_l1ffn(C, A, x_in, w1, w3, w2, ones_in, ident_in, out, u2t_dram):
    nc, p = C.nc, C.p
    PS = C.PS
    xhat = A.take(16 * TOK, BF16)
    gfm = A.take(NH1 * TOK, BF16)
    wstg = A.take(2 * D, F32)
    ubf = A.take(2 * D, BF16)
    ut = A.take(2 * D, BF16)
    u2row = A.take(H1P, BF16)
    xtb = A.take(D, F32)
    ytm = A.take(4 * D, BF16)
    yw = A.take(D, F32)
    sil = A.take(512, F32)
    mv = A.take(8, F32)
    sc = A.take(16, F32)
    acc = A.take(4, F32)
    bank = lambda i: PS[:, i * 512:(i + 1) * 512]
    bankb = lambda i: PS[:, i * 512:(i + 1) * 512].bitcast(BF16)

    b_stage = [TBuf("stg0"), TBuf("stg1")]
    b_acc = TBuf("acc"); b_sc = TBuf("sc")
    b_ubf = TBuf("ubf"); b_ut = TBuf("ut"); b_u2r = TBuf("u2r")
    b_xhat = TBuf("xhat"); b_gfm = TBuf("gfm")
    b_xt = TBuf("xt"); b_ytm = TBuf("ytm"); b_yw = TBuf("yw")
    b_sil = TBuf("sil"); b_mv = TBuf("mv")
    b_rs = TBuf("rs"); b_cst = C.b_cst
    lnbufs = (b_ut, b_mv, b_rs)
    b_ps = [TBuf(f"ps{i}") for i in range(8)]
    evo = C.evo
    evi = C.evi
    p.do("vector", lambda e: e.memset(acc[:], 0.0), writes=[b_acc], signal=True)

    for j, w in enumerate([w1, w3, w2]):
        wf = w.flatten().rearrange("(p f) -> p f", p=128)
        c0 = 0
        while c0 < FLATC:
            c1 = min(c0 + 2032, FLATC)
            slot = (c0 // 2032) % 2
            sb = b_stage[slot]
            st = wstg[:, slot * 2048: slot * 2048 + (c1 - c0)]
            evd = p.do("sync", (lambda wf, c0, c1, st: lambda e: e.dma_start(
                out=st, in_=wf[:, c0:c1]))(wf, c0, c1, st),
                writes=[sb], dma=True)
            evr = p.do("vector", (lambda st, j: lambda e: e.reduce_sum(
                out=sc[:, 11 + j:12 + j], in_=st, axis=AX.X,
                apply_absolute_value=True))(st, j),
                deps=[evd], reads=[sb], writes=[b_sc], signal=True)
            p.do("vector", (lambda j: lambda e: e.tensor_tensor(
                out=acc[:, j:j + 1], in0=acc[:, j:j + 1],
                in1=sc[:, 11 + j:12 + j], op=ALU.add))(j),
                deps=[force(evr)], reads=[b_sc], writes=[b_acc], signal=True)
            c0 = c1
    ev = p.do("tensor", lambda e: e.matmul(PS[0:1, 0:4], C.onesb[:, 0:1], acc[:, 0:4],
                                           start=True, stop=True),
              deps=[evo], reads=[b_acc], writes=[b_ps[0]], signal=True)
    ev = p.do("vector", lambda e: e.tensor_scalar_mul(
        out=sc[0:1, 0:4], in0=PS[0:1, 0:4], scalar1=1.0 / (H1 * D)),
        deps=[ev], reads=[b_ps[0]], writes=[b_sc], signal=True)
    ev = p.do("tensor", lambda e: e.matmul(PS[:, 0:4], C.onesb[0:1, :], sc[0:1, 0:4],
                                           start=True, stop=True),
              deps=[ev], reads=[b_sc], writes=[b_ps[0]], signal=True)
    ev = p.do("vector", lambda e: e.tensor_scalar_mul(out=sc[:, 0:3],
                                                      in0=PS[:, 0:3], scalar1=0.5),
              deps=[ev], reads=[b_ps[0]], writes=[b_sc], signal=True)
    for j, (cp, cn) in enumerate([(3, 4), (5, 6), (7, 8)]):
        ev = p.do("vector", (lambda j, cp: lambda e: e.tensor_scalar_mul(
            out=sc[:, cp:cp + 1], in0=PS[:, j:j + 1], scalar1=0.5))(j, cp),
            deps=[ev], reads=[b_ps[0]], writes=[b_sc], signal=True)
        ev = p.do("vector", (lambda cp, cn: lambda e: e.tensor_scalar_mul(
            out=sc[:, cn:cn + 1], in0=sc[:, cp:cp + 1], scalar1=-1.0))(cp, cn),
            writes=[b_sc], signal=True)

    pieces = [(0, 2048, 16), (2048, 4096, 16), (4096, 5461, 11)]
    for ch in range(16):
        for pidx, (p0, p1, ntile) in enumerate(pieces):
            slot = (ch * 3 + pidx) % 2
            sb = b_stage[slot]
            st = wstg[:, slot * 2048: slot * 2048 + (p1 - p0)]
            evd = p.do("sync", (lambda ch, p0, p1, st: lambda e: e.dma_start(
                out=st, in_=w2[ch * 128:(ch + 1) * 128, p0:p1]))(ch, p0, p1, st),
                writes=[sb], dma=True)
            npad = ntile * 128
            evz = None
            if p1 - p0 < npad:
                evz = p.do("vector", (lambda p0, p1, npad: lambda e: e.memset(
                    ubf[:, (p1 - p0):npad], 0.0))(p0, p1, npad),
                    writes=[b_ubf], signal=True)
            ea = p.do("scalar", (lambda st, p0, p1: lambda e: e.activation(
                ut[:, 0:(p1 - p0)], st, AF.Sign, bias=sc[:, 8:9]))(st, p0, p1),
                deps=[evd], reads=[sb, b_sc], writes=[b_ut], signal=True)
            eb = p.do("scalar", (lambda st, p0, p1: lambda e: e.activation(
                ubf[:, 0:(p1 - p0)], st, AF.Sign, bias=sc[:, 7:8]))(st, p0, p1),
                deps=[ea], reads=[sb, b_sc], writes=[b_ubf], signal=True)
            ev = p.do("vector", (lambda p0, p1: lambda e: e.tensor_tensor(
                out=ubf[:, 0:(p1 - p0)], in0=ubf[:, 0:(p1 - p0)],
                in1=ut[:, 0:(p1 - p0)], op=ALU.add))(p0, p1),
                deps=[eb, evz], reads=[b_ut], writes=[b_ubf], signal=True)
            for g in range(0, ntile, 8):
                g1 = min(g + 8, ntile)
                pb = 4 + ((g // 8) % 2)
                evg = None
                for t in range(g, g1):
                    evg = p.do("tensor", (lambda t, g, pb: lambda e: e.transpose(
                        bankb(pb)[:, (t - g) * 128:(t - g + 1) * 128],
                        ubf[:, t * 128:(t + 1) * 128], C.identb[:]))(t, g, pb),
                        deps=[evi, ev], reads=[b_ubf], writes=[b_ps[pb]],
                        signal=(t == g1 - 1))
                evc = p.do("vector", (lambda g, g1, pb: lambda e: e.tensor_copy(
                    out=ut[:, 0:(g1 - g) * 128],
                    in_=bankb(pb)[:, 0:(g1 - g) * 128]))(g, g1, pb),
                    deps=[evg], reads=[b_ps[pb]], writes=[b_ut], signal=True)
                t0g = p0 // 128 + g
                p.do("sync", (lambda ch, t0g, g, g1: lambda e: e.dma_start(
                    out=u2t_dram[ch, t0g:t0g + (g1 - g)].rearrange("t p d -> p t d"),
                    in_=ut[:, 0:(g1 - g) * 128].rearrange("p (t d) -> p t d", d=128)))(ch, t0g, g, g1),
                    deps=[evc], reads=[b_ut], dma=True)
    ev_u2done = p.barrier_event("sync")

    for t in range(NT):
        evd = p.do("sync", (lambda t: lambda e: e.dma_start(
            out=xtb[:], in_=x_in[t * 128:(t + 1) * 128, :]))(t),
            writes=[b_xt], dma=True)
        er, em = ln_stats(p, xtb[:], ut[:, 0:D], mv, sc[:, 9:10],
                          sc[:, 10:11], lnbufs, xbuf=b_xt,
                          epsc=C.cst[:, 0:1], epsbuf=b_cst, deps=[evd])
        ev = p.do("vector", lambda e: e.tensor_scalar(
            out=ubf[:, 0:D], in0=xtb[:], scalar1=sc[:, 10:11],
            scalar2=sc[:, 9:10], op0=ALU.add, op1=ALU.mult),
            deps=[er, em], reads=[b_xt, b_rs], writes=[b_ubf], signal=True)
        for g in range(2):
            pb = 6 + g
            evg = None
            for dc in range(g * 8, g * 8 + 8):
                evg = p.do("tensor", (lambda dc, g, pb: lambda e: e.transpose(
                    bankb(pb)[:, (dc - g * 8) * 128:(dc - g * 8 + 1) * 128],
                    ubf[:, dc * 128:(dc + 1) * 128], C.identb[:]))(dc, g, pb),
                    deps=[evi, ev], reads=[b_ubf], writes=[b_ps[pb]],
                    signal=(dc == g * 8 + 7))
            for dc in range(g * 8, g * 8 + 8):
                p.do("vector", (lambda t, g, dc, pb: lambda e: e.tensor_copy(
                    out=xhat[:, dc * TOK + t * 128:dc * TOK + (t + 1) * 128],
                    in_=bankb(pb)[:, (dc - g * 8) * 128:(dc - g * 8 + 1) * 128]))(t, g, dc, pb),
                    deps=[evg], reads=[b_ps[pb]], writes=[b_xhat],
                    signal=(dc == g * 8 + 7))

    for hc in range(NH1):
        r0, r1 = hc * 128, min((hc + 1) * 128, H1)
        nr = r1 - r0
        slot = 0
        sb = b_stage[slot]
        st1 = wstg[:, slot * 2 * D: slot * 2 * D + D]
        st3 = wstg[:, slot * 2 * D + D: slot * 2 * D + 2 * D]
        ed1 = p.do("sync", (lambda r0, r1, st1: lambda e: e.dma_start(
            out=st1[0:r1 - r0, :], in_=w1[r0:r1, :]))(r0, r1, st1),
            writes=[sb], dma=True)
        ed3 = p.do("sync", (lambda r0, r1, st3: lambda e: e.dma_start(
            out=st3[0:r1 - r0, :], in_=w3[r0:r1, :]))(r0, r1, st3),
            writes=[sb], dma=True)
        ev_u = []
        if nr < 128:
            ev_u.append(p.do("vector", lambda e: e.memset(ubf[64:128, 0:2 * D], 0.0),
                             writes=[b_ubf], signal=True))
        for (st, evd, off, cp, cn) in [(st1, ed1, 0, 3, 4), (st3, ed3, D, 5, 6)]:
            ea = p.do("scalar", (lambda st, off, cn, nr: lambda e: e.activation(
                ut[0:nr, off:off + D], st[0:nr, :], AF.Sign,
                bias=sc[0:nr, cn:cn + 1]))(st, off, cn, nr),
                deps=[evd], reads=[sb, b_sc], writes=[b_ut], signal=True)
            eb = p.do("scalar", (lambda st, off, cp, nr: lambda e: e.activation(
                ubf[0:nr, off:off + D], st[0:nr, :], AF.Sign,
                bias=sc[0:nr, cp:cp + 1]))(st, off, cp, nr),
                deps=[ea], reads=[sb, b_sc], writes=[b_ubf], signal=True)
            evq = p.do("vector", (lambda off, nr: lambda e: e.tensor_tensor(
                out=ubf[0:nr, off:off + D], in0=ubf[0:nr, off:off + D],
                in1=ut[0:nr, off:off + D], op=ALU.add))(off, nr),
                deps=[eb], reads=[b_ut], writes=[b_ubf], signal=True)
            ev_u.append(evq)
        for (off, pb0) in [(0, 4), (D, 5)]:
            for g in range(2):
                pb = pb0 if g == 0 else (pb0 ^ 1) if False else pb0
                evg = None
                for dc in range(g * 8, g * 8 + 8):
                    evg = p.do("tensor", (lambda off, pb, dc, g: lambda e: e.transpose(
                        bankb(pb)[:, (dc - g * 8) * 128:(dc - g * 8 + 1) * 128],
                        ubf[:, off + dc * 128:off + (dc + 1) * 128],
                        C.identb[:]))(off, pb, dc, g),
                        deps=[evi] + ev_u, reads=[b_ubf], writes=[b_ps[pb]],
                        signal=(dc == g * 8 + 7))
                evuc = p.do("vector", (lambda off, pb, g: lambda e: e.tensor_copy(
                    out=ut[:, off + g * 1024:off + (g + 1) * 1024],
                    in_=bankb(pb)[:, 0:1024]))(off, pb, g),
                    deps=[evg], reads=[b_ps[pb]], writes=[b_ut], signal=True)
        for tc in range(2):
            e1 = e3 = None
            for dc in range(16):
                e1 = p.do("tensor", (lambda tc, dc: lambda e: e.matmul(
                    bank(0 + tc), ut[:, dc * 128:(dc + 1) * 128],
                    xhat[:, dc * TOK + tc * 512:dc * TOK + (tc + 1) * 512],
                    start=(dc == 0), stop=(dc == 15)))(tc, dc),
                    reads=[b_ut, b_xhat], writes=[b_ps[0 + tc]],
                    signal=(dc == 15))
            for dc in range(16):
                e3 = p.do("tensor", (lambda tc, dc: lambda e: e.matmul(
                    bank(2 + tc), ut[:, D + dc * 128:D + (dc + 1) * 128],
                    xhat[:, dc * TOK + tc * 512:dc * TOK + (tc + 1) * 512],
                    start=(dc == 0), stop=(dc == 15)))(tc, dc),
                    reads=[b_ut, b_xhat], writes=[b_ps[2 + tc]],
                    signal=(dc == 15))
            es = p.do("scalar", (lambda tc: lambda e: e.activation(
                sil[:], bank(0 + tc), AF.Silu, scale=sc[:, 0:1]))(tc),
                deps=[e1], reads=[b_ps[0 + tc], b_sc], writes=[b_sil], signal=True)
            p.do("vector", (lambda hc, tc: lambda e: e.scalar_tensor_tensor(
                out=gfm[:, hc * TOK + tc * 512:hc * TOK + (tc + 1) * 512],
                in0=bank(2 + tc), scalar=sc[:, 1:2], in1=sil[:],
                op0=ALU.mult, op1=ALU.mult))(hc, tc),
                deps=[es, e3], reads=[b_ps[2 + tc], b_sil, b_sc],
                writes=[b_gfm], signal=True)

    for tc in range(2):
        for dc in range(16):
            evd = p.do("sync", (lambda dc: lambda e: e.dma_start(
                out=u2row[:].rearrange("p (t d) -> p t d", d=128),
                in_=u2t_dram[dc, :].rearrange("t p d -> p t d")))(dc),
                deps=[ev_u2done], writes=[b_u2r], dma=True)
            evm = None
            pb = dc % 2
            for hc in range(NH1):
                evm = p.do("tensor", (lambda tc, hc, pb: lambda e: e.matmul(
                    bank(pb), u2row[:, hc * 128:(hc + 1) * 128],
                    gfm[:, hc * TOK + tc * 512:hc * TOK + (tc + 1) * 512],
                    start=(hc == 0), stop=(hc == NH1 - 1)))(tc, hc, pb),
                    deps=[evd], reads=[b_u2r, b_gfm],
                    writes=[b_ps[pb]], signal=(hc == NH1 - 1))
            ev = p.do("vector", (lambda pb: lambda e: e.tensor_scalar_mul(
                out=sil[:].bitcast(BF16)[:, 0:512], in0=bank(pb),
                scalar1=sc[:, 2:3]))(pb),
                deps=[evm], reads=[b_ps[pb], b_sc], writes=[b_sil], signal=True)
            evt = None
            for q in range(4):
                evt = p.do("tensor", (lambda q: lambda e: e.transpose(
                    bankb(6)[:, q * 128:(q + 1) * 128],
                    sil[:].bitcast(BF16)[:, q * 128:(q + 1) * 128],
                    C.identb[:]))(q),
                    deps=[evi, ev], reads=[b_sil], writes=[b_ps[6]],
                    signal=(q == 3))
            for q in range(4):
                p.do("vector", (lambda dc, q: lambda e: e.tensor_copy(
                    out=ytm[:, q * D + dc * 128:q * D + (dc + 1) * 128],
                    in_=bankb(6)[:, q * 128:(q + 1) * 128]))(dc, q),
                    deps=[evt], reads=[b_ps[6]], writes=[b_ytm],
                    signal=(q == 3))
        for q in range(4):
            t = tc * 4 + q
            yt = ytm[:, q * D:(q + 1) * D]
            evd = p.do("sync", (lambda t: lambda e: e.dma_start(
                out=xtb[:], in_=x_in[t * 128:(t + 1) * 128, :]))(t),
                writes=[b_xt], dma=True)
            er, em = ln_stats(p, yt, ut[:, 0:D], mv, sc[:, 9:10],
                              sc[:, 10:11], lnbufs, xbuf=b_ytm,
                              epsc=C.cst[:, 0:1], epsbuf=b_cst)
            ev = p.do("vector", (lambda yt: lambda e: e.tensor_scalar(
                out=yw[:], in0=yt, scalar1=sc[:, 10:11], scalar2=sc[:, 9:10],
                op0=ALU.add, op1=ALU.mult))(yt),
                deps=[er, em], reads=[b_ytm, b_rs], writes=[b_yw], signal=True)
            ev = p.do("vector", lambda e: e.tensor_add(out=yw[:], in0=yw[:], in1=xtb[:]),
                      deps=[evd, ev], reads=[b_xt], writes=[b_yw], signal=True)
            p.do("sync", (lambda t: lambda e: e.dma_start(
                out=out[t * 128:(t + 1) * 128, :], in_=yw[:]))(t),
                deps=[ev], reads=[b_yw], dma=True)

    fin = p.barrier_event("sync")
    p.do("sync", lambda e: None, deps=[fin])


def emit_l0ffn(C, A, x_in, wg_in, wu_in, wd_in, ident_in, out):
    nc, p = C.nc, C.p
    PS = C.PS
    xhat = A.take(16 * TOK, BF16)
    hfm = A.take(NI0 * TOK, BF16)
    arena = A.take(16384, BF16)
    sil = A.take(512, F32)
    hsb = A.take(512, F32)
    hspb = A.take(512, BF16)
    sp = A.take(1408, F32)
    mv = A.take(8, F32)
    sc = A.take(4, F32)
                
    p = C.p
    bank = lambda i: PS[:, i * 512:(i + 1) * 512]
    bankb = lambda i: PS[:, i * 512:(i + 1) * 512].bitcast(BF16)
    arenaF = arena.bitcast(F32)
    hfmF = hfm.bitcast(F32)
    yw = hfmF[:, 0:D]
    xtb = hfmF[:, D:2 * D]
    xhatF = xhat.bitcast(F32)
    wg_v = arena[:, 0:8192].rearrange("p (ic c) -> p ic c", c=512)
    wu_v = arena[:, 8192:16384].rearrange("p (ic c) -> p ic c", c=512)

    b_xhat = TBuf("xhat"); b_hfm = TBuf("hfm")
    b_wg = TBuf("wg"); b_wu = TBuf("wu"); b_wd = [TBuf("wd0"), TBuf("wd1")]
    b_xt = TBuf("xt"); b_y = TBuf("y")
    b_sil = TBuf("sil"); b_hsb = TBuf("hsb"); b_hsp = TBuf("hsp")
    b_sp = TBuf("sp"); b_yw = TBuf("yw"); b_mv = TBuf("mv")
    b_rs = TBuf("rs"); b_cst = C.b_cst
    b_ps = [TBuf(f"ps{i}") for i in range(8)]
    lnbufs = (b_sil, b_mv, b_rs)
    evi = C.evi

    for t in range(NT):
        evd = p.do("sync", (lambda t: lambda e: e.dma_start(
            out=xtb[:], in_=x_in[t * 128:(t + 1) * 128, :]))(t),
            writes=[b_xt], dma=True)
        er, em = ln_stats(p, xtb[:], sil.bitcast(BF16), mv, sc[:, 0:1],
                          sc[:, 1:2], lnbufs, xbuf=b_xt, epsc=C.cst[:, 0:1],
                          epsbuf=b_cst, deps=[evd])
        ev = p.do("vector", lambda e: e.tensor_scalar(
            out=arena[:, 8192:8192 + D],
            in0=xtb[:], scalar1=sc[:, 1:2], scalar2=sc[:, 0:1],
            op0=ALU.add, op1=ALU.mult),
            deps=[er, em], reads=[b_xt, b_rs], writes=[b_wu], signal=True)
        for g in range(2):
            pb = 6 + g
            evg = None
            for dc in range(g * 8, g * 8 + 8):
                evg = p.do("tensor", (lambda dc, g, pb: lambda e: e.transpose(
                    bankb(pb)[:, (dc - g * 8) * 128:(dc - g * 8 + 1) * 128],
                    arena[:, 8192 + dc * 128:8192 + (dc + 1) * 128],
                    C.identb[:]))(dc, g, pb),
                    deps=[evi, ev], reads=[b_wu], writes=[b_ps[pb]],
                    signal=(dc == g * 8 + 7))
            for dc in range(g * 8, g * 8 + 8):
                p.do("vector", (lambda t, dc, pb, g: lambda e: e.tensor_copy(
                    out=xhat[:, dc * TOK + t * 128:dc * TOK + (t + 1) * 128],
                    in_=bankb(pb)[:, (dc - g * 8) * 128:(dc - g * 8 + 1) * 128]))(t, dc, pb, g),
                    deps=[evg], reads=[b_ps[pb]], writes=[b_xhat],
                    signal=(dc == g * 8 + 7))

    p.full_barrier()
    preln_done = [p.barrier_event(e) for e in ("vector", "tensor", "scalar")]
    for oc in range(16):
        eg = p.do("gpsimd", (lambda oc: lambda e: e.dma_start(
            out=wg_v, in_=wg_in[:, oc * 512:(oc + 1) * 512]
            .rearrange("(ic p) c -> p ic c", p=128)))(oc),
            deps=preln_done if oc == 0 else (), writes=[b_wg], dma=True)
        eu = p.do("gpsimd", (lambda oc: lambda e: e.dma_start(
            out=wu_v, in_=wu_in[:, oc * 512:(oc + 1) * 512]
            .rearrange("(ic p) c -> p ic c", p=128)))(oc),
            deps=preln_done if oc == 0 else (), writes=[b_wu], dma=True)
        for t in range(NT):
            pg, pu = (0, 1) if t % 2 == 0 else (2, 3)
            e1 = e2 = None
            for ic in range(16):
                e1 = p.do("tensor", (lambda oc, t, ic, pg: lambda e: e.matmul(
                    bank(pg), xhat[:, ic * TOK + t * 128:ic * TOK + (t + 1) * 128],
                    wg_v[:, ic, :], start=(ic == 0), stop=(ic == 15)))(oc, t, ic, pg),
                    deps=[eg], reads=[b_wg, b_xhat], writes=[b_ps[pg]],
                    signal=(ic == 15))
            for ic in range(16):
                e2 = p.do("tensor", (lambda oc, t, ic, pu: lambda e: e.matmul(
                    bank(pu), xhat[:, ic * TOK + t * 128:ic * TOK + (t + 1) * 128],
                    wu_v[:, ic, :], start=(ic == 0), stop=(ic == 15)))(oc, t, ic, pu),
                    deps=[eu], reads=[b_wu, b_xhat], writes=[b_ps[pu]],
                    signal=(ic == 15))
            es = p.do("scalar", (lambda pg: lambda e: e.activation(
                sil[:], bank(pg), AF.Silu))(pg),
                deps=[e1], reads=[b_ps[pg]], writes=[b_sil], signal=True)
            eh = p.do("vector", (lambda pu: lambda e: e.scalar_tensor_tensor(
                out=hsb[:], in0=bank(pu), scalar=1.0, in1=sil[:],
                op0=ALU.mult, op1=ALU.mult))(pu),
                deps=[es, e2], reads=[b_ps[pu], b_sil], writes=[b_hsb],
                signal=True)
            h3v = hsb[:].rearrange("p (a b) -> p a b", b=2)
            epm = p.do("vector", lambda e: e.reduce_max(
                out=sp[:, 0:256], in_=h3v, axis=AX.X, apply_absolute_value=True),
                deps=[force(eh)], reads=[b_hsb], writes=[b_sp], signal=True)
            epn = p.do("vector", lambda e: e.tensor_reduce(
                out=sp[:, 256:512], in_=h3v, axis=AX.X, op=ALU.min,
                apply_absolute_value=True),
                deps=[force(eh)], reads=[b_hsb], writes=[b_sp], signal=True)
            em1 = p.do("vector", lambda e: e.tensor_reduce(
                out=sp[:, 512:640], in_=sp[:, 0:256].rearrange("p (a b) -> p a b", b=2),
                axis=AX.X, op=ALU.min),
                deps=[force(epm)], writes=[b_sp], signal=True)
            em2 = p.do("vector", lambda e: e.reduce_max(
                out=sp[:, 640:768], in_=sp[:, 256:512].rearrange("p (a b) -> p a b", b=2),
                axis=AX.X),
                deps=[force(epn)], writes=[b_sp], signal=True)
            em3 = p.do("vector", lambda e: e.tensor_max(
                out=sp[:, 768:896], in0=sp[:, 512:640], in1=sp[:, 640:768]),
                deps=[force(em1), force(em2)], writes=[b_sp], signal=True)
            ea = p.do("scalar", lambda e: e.activation(sp[:, 896:1408], hsb[:], AF.Abs),
                      deps=[eh], reads=[b_hsb], writes=[b_sp], signal=True)
            evj = None
            for j in range(4):
                evj = p.do("vector", (lambda j: lambda e: e.tensor_tensor(
                    out=sp[:, 896 + j:1408:4], in0=sp[:, 896 + j:1408:4],
                    in1=sp[:, 768:896], op=ALU.is_ge))(j),
                    deps=[force(em3), force(ea)], writes=[b_sp],
                    signal=(j == 3))
            ehs = p.do("vector", lambda e: e.tensor_mul(
                out=hspb[:], in0=hsb[:], in1=sp[:, 896:1408]),
                deps=[force(evj)], reads=[b_hsb, b_sp], writes=[b_hsp],
                signal=True)
            pb = 6 + t % 2
            evt = None
            for q in range(4):
                evt = p.do("tensor", (lambda q, pb: lambda e: e.transpose(
                    bankb(pb)[:, q * 128:(q + 1) * 128],
                    hspb[:, q * 128:(q + 1) * 128], C.identb[:]))(q, pb),
                    deps=[evi, ehs], reads=[b_hsp], writes=[b_ps[pb]],
                    signal=(q == 3))
            for q in range(4):
                p.do("vector", (lambda oc, t, q, pb: lambda e: e.tensor_copy(
                    out=hfm[:, (4 * oc + q) * TOK + t * 128:(4 * oc + q) * TOK + (t + 1) * 128],
                    in_=bankb(pb)[:, q * 128:(q + 1) * 128]))(oc, t, q, pb),
                    deps=[evt], reads=[b_ps[pb]], writes=[b_hfm],
                    signal=(q == 3))

    p.full_barrier()
    ph1_done = [p.barrier_event(e) for e in ("vector", "tensor", "scalar", "gpsimd")]
    for dc in range(4):
        evm_all = []
        for hc in range(NI0):
            slot = hc % 2
            wd_s = sil.bitcast(BF16)[:, slot * 512:(slot + 1) * 512]
            evd = p.do("gpsimd", (lambda hc, dc, wd_s: lambda e: e.dma_start(
                out=wd_s, in_=wd_in[hc * 128:(hc + 1) * 128,
                                    dc * 512:(dc + 1) * 512]))(hc, dc, wd_s),
                deps=ph1_done if (dc == 0 and hc < 2) else (),
                writes=[b_wd[slot]], dma=True)
            for t in range(NT):
                evm = p.do("tensor", (lambda hc, t, wd_s: lambda e: e.matmul(
                    bank(t), hfm[:, hc * TOK + t * 128:hc * TOK + (t + 1) * 128],
                    wd_s, start=(hc == 0), stop=(hc == NI0 - 1)))(hc, t, wd_s),
                    deps=[evd], reads=[b_wd[slot], b_hfm],
                    writes=[b_ps[t]], signal=(hc == NI0 - 1 or t == NT - 1))
                if hc == NI0 - 1:
                    evm_all.append(evm)
        for t in range(NT):
            ytgt = xhatF if t < 4 else arenaF
            p.do("vector", (lambda t, dc, ytgt: lambda e: e.tensor_copy(
                out=ytgt[:, (t % 4) * D + dc * 512:(t % 4) * D + dc * 512 + 512],
                in_=bank(t)))(t, dc, ytgt),
                deps=[evm_all[t]], reads=[b_ps[t]], writes=[b_y], signal=True)

    p.full_barrier()
    ph2_done = [p.barrier_event("tensor"), p.barrier_event("vector"),
                p.barrier_event("gpsimd")]
    for t in range(NT):
        ytgt = xhatF if t < 4 else arenaF
        yt = ytgt[:, (t % 4) * D:(t % 4 + 1) * D]
        evd = p.do("sync", (lambda t: lambda e: e.dma_start(
            out=xtb[:], in_=x_in[t * 128:(t + 1) * 128, :]))(t),
            deps=ph2_done if t == 0 else (), writes=[b_xt], dma=True)
        er, em = ln_stats(p, yt, sil.bitcast(BF16), mv, sc[:, 0:1],
                          sc[:, 1:2], lnbufs, xbuf=b_y, epsc=C.cst[:, 0:1],
                          epsbuf=b_cst, deps=ph2_done if t == 0 else ())
        ev = p.do("vector", (lambda yt: lambda e: e.tensor_scalar(
            out=yw[:], in0=yt, scalar1=sc[:, 1:2], scalar2=sc[:, 0:1],
            op0=ALU.add, op1=ALU.mult))(yt),
            deps=[er, em], reads=[b_y, b_rs], writes=[b_yw], signal=True)
        ev = p.do("vector", lambda e: e.tensor_add(out=yw[:], in0=yw[:], in1=xtb[:]),
                  deps=[evd, force(ev)], reads=[b_xt], writes=[b_yw], signal=True)
        p.do("sync", (lambda t: lambda e: e.dma_start(
            out=out[t * 128:(t + 1) * 128, :], in_=yw[:]))(t),
            deps=[ev], reads=[b_yw], dma=True)

    fin = p.barrier_event("sync")
    p.do("sync", lambda e: None, deps=[fin])


def emit_l1attn(C, A, xg, xo, qw_in, krw_in, dww_in, upk_in, upv_in, ow_in, mask_in, ckc_in, cks_in, cqc_in, cqs_in, ident_in, ones_in, out):
    nc, p = C.nc, C.p
    PS = C.PS
    xhat = A.take(16 * S, BF16)
    vbuf = A.take(16 * 520, BF16)
    ckv = A.take(2 * S, BF16)
    krf = A.take(2 * S, BF16)
    kcf = A.take(2 * S, BF16)
    qro = A.take(8 * TOK, BF16)
    qno = A.take(8 * TOK, BF16)
    maskb = A.take(16 * 512, BF16)
    wstage = A.take(8192, BF16)
    piece = A.take(1024, BF16)
    krT = A.take(16 * 256, BF16)
    dwT = A.take(16 * 256, BF16)
    upkT = A.take(2 * 256, BF16)
    upvT = A.take(2 * 512, BF16)
    ckc = A.take(16 * 16, F32)
    cks = A.take(16 * 16, F32)
    cqc = A.take(8 * 32, F32)
    cqs = A.take(8 * 32, F32)
    ptl = A.take(4 * 512, BF16)
    qtm = A.take(512, BF16)
    rsc = A.take(512, F32)
    rb = A.take(512, F32)
    mv = A.take(8, F32)
    sc = A.take(4, F32)
                    
    p = C.p
    bank = lambda i: PS[:, i * 512:(i + 1) * 512]
    bankb = lambda i: PS[:, i * 512:(i + 1) * 512].bitcast(BF16)
    attn_fm = xhat[:, 0:16 * TOK]
    ybf = xhat[:, 16 * TOK:32 * TOK]
    qroF = qro.bitcast(F32)
    xtb = qroF[:, 0:D]
    yw = qroF[:, D:2 * D]

    names = ["xhat", "vbuf", "ckv", "krf", "kcf", "qro", "qno", "maskb",
             "wstage", "piece", "krT", "dwT", "upkT", "upvT", "crep",
             "ptl", "qtm", "rsc", "xtb", "yw", "lrow", "rb", "mv", "rs",
             "C.cst", "attn", "ybf"]
    bufs = {n: TBuf(n) for n in names}
    b_ps = [TBuf(f"ps{i}") for i in range(8)]
    bufs["cst"] = C.b_cst
    bu = lambda n: bufs[n]
    lnbufs = (bu("rsc"), bu("mv"), bu("rs"))
    evi = C.evi
    evo = C.evo
    p.do("vector", lambda e: e.memset(vbuf[:], 1.0), writes=[bu("vbuf")], signal=True)
    for (t_in, t_sb, nj, ntt) in [(ckc_in, ckc, 16, 16), (cks_in, cks, 16, 16),
                                  (cqc_in, cqc, 32, 8), (cqs_in, cqs, 32, 8)]:
        p.do("sync", (lambda t_in, t_sb, nj, ntt: lambda e: e.dma_start(
            out=t_sb[:].rearrange("p (tt j) -> p tt j", tt=ntt),
            in_=t_in.rearrange("(tt p) j -> p tt j", p=128)))(t_in, t_sb, nj, ntt),
            writes=[bu("crep")], dma=True)

    def prep_T(w_in, dst, nout, nin, dstbuf):
        for r in range(nout // 128):
            evd = p.do("gpsimd", (lambda w_in, r, nin: lambda e: e.dma_start(
                out=wstage[:, 0:nin], in_=w_in[r * 128:(r + 1) * 128, :]))(w_in, r, nin),
                writes=[bu("wstage")], dma=True)
            for g in range(0, nin // 128, 8):
                g1 = min(g + 8, nin // 128)
                pb = (g // 8) % 2
                evg = None
                for ic in range(g, g1):
                    evg = p.do("tensor", (lambda ic, g, pb: lambda e: e.transpose(
                        bankb(pb)[:, (ic - g) * 128:(ic - g + 1) * 128],
                        wstage[:, ic * 128:(ic + 1) * 128], C.identb[:]))(ic, g, pb),
                        deps=[evi, evd], reads=[bu("wstage")], writes=[b_ps[pb]],
                        signal=(ic == g1 - 1))
                for ic in range(g, g1):
                    p.do("vector", (lambda ic, g, pb, r, nout: lambda e: e.tensor_copy(
                        out=dst[:, ic * nout + r * 128:ic * nout + (r + 1) * 128],
                        in_=bankb(pb)[:, (ic - g) * 128:(ic - g + 1) * 128]))(ic, g, pb, r, nout),
                        deps=[evg], reads=[b_ps[pb]], writes=[dstbuf],
                        signal=(ic == g1 - 1))
    prep_T(krw_in, krT, 256, 2048, bu("krT"))
    prep_T(dww_in, dwT, 256, 2048, bu("dwT"))
    prep_T(upk_in, upkT, 256, 256, bu("upkT"))
    prep_T(upv_in, upvT, 512, 256, bu("upvT"))

    for t in range(16):
        evd = p.do("sync", (lambda t: lambda e: e.dma_start(
            out=xtb[:], in_=xg[t * 128:(t + 1) * 128, :]))(t),
            writes=[bu("xtb")], dma=True)
        er, em = ln_stats(p, xtb[:], rsc[:].bitcast(BF16)[:, 0:512], mv,
                          sc[:, 0:1], sc[:, 1:2], lnbufs, xbuf=bu("xtb"),
                          epsc=C.cst[:, 0:1], epsbuf=bu("C.cst"), deps=[evd])
        ev = p.do("vector", lambda e: e.tensor_scalar(
            out=wstage[:, 4096:4096 + D], in0=xtb[:], scalar1=sc[:, 1:2],
            scalar2=sc[:, 0:1], op0=ALU.add, op1=ALU.mult),
            deps=[er, em], reads=[bu("xtb"), bu("rs")], writes=[bu("wstage")],
            signal=True)
        for g in range(2):
            pb = 6 + g
            evg = None
            for dc in range(g * 8, g * 8 + 8):
                evg = p.do("tensor", (lambda dc, g, pb: lambda e: e.transpose(
                    bankb(pb)[:, (dc - g * 8) * 128:(dc - g * 8 + 1) * 128],
                    wstage[:, 4096 + dc * 128:4096 + (dc + 1) * 128],
                    C.identb[:]))(dc, g, pb),
                    deps=[evi, ev], reads=[bu("wstage")], writes=[b_ps[pb]],
                    signal=(dc == g * 8 + 7))
            for dc in range(g * 8, g * 8 + 8):
                p.do("vector", (lambda t, dc, pb, g: lambda e: e.tensor_copy(
                    out=xhat[:, dc * S + t * 128:dc * S + (t + 1) * 128],
                    in_=bankb(pb)[:, (dc - g * 8) * 128:(dc - g * 8 + 1) * 128]))(t, dc, pb, g),
                    deps=[evg], reads=[b_ps[pb]], writes=[bu("xhat")],
                    signal=(dc == g * 8 + 7))

    p.full_barrier()
    for m in range(2):
        for ts in range(4):
            evm = None
            for ic in range(16):
                evm = p.do("tensor", (lambda m, ts, ic: lambda e: e.matmul(
                    bank(m), dwT[:, ic * 256 + m * 128:ic * 256 + (m + 1) * 128],
                    xhat[:, ic * S + ts * 512:ic * S + (ts + 1) * 512],
                    start=(ic == 0), stop=(ic == 15)))(m, ts, ic),
                    reads=[bu("dwT"), bu("xhat")], writes=[b_ps[m]],
                    signal=(ic == 15))
            p.do("vector", (lambda m, ts: lambda e: e.tensor_copy(
                out=ckv[:, m * S + ts * 512:m * S + (ts + 1) * 512],
                in_=bank(m)))(m, ts),
                deps=[evm], reads=[b_ps[m]], writes=[bu("ckv")], signal=True)

    from concourse.bass import broadcast_tensor_aps
    for t in range(16):
        evm = None
        for ic in range(16):
            evm = p.do("tensor", (lambda t, ic: lambda e: e.matmul(
                PS[:, 2048:2304], xhat[:, ic * S + t * 128:ic * S + (t + 1) * 128],
                krT[:, ic * 256:(ic + 1) * 256],
                start=(ic == 0), stop=(ic == 15)))(t, ic),
                reads=[bu("krT"), bu("xhat")], writes=[b_ps[4]],
                signal=(ic == 15))
        kr_ps = PS[:, 2048:2304].rearrange("p (h two j) -> p h two j", two=2, j=16)
        av, bv = kr_ps[:, :, 0, :], kr_ps[:, :, 1, :]
        ct = ckc[:, t * 16:(t + 1) * 16].rearrange("p (h j) -> p h j", h=1)
        st = cks[:, t * 16:(t + 1) * 16].rearrange("p (h j) -> p h j", h=1)
        _, ctb = broadcast_tensor_aps(av, ct)
        _, stb = broadcast_tensor_aps(av, st)
        qv = qtm[:, 0:256].rearrange("p (h two j) -> p h two j", two=2, j=16)
        x1o, x2o = qv[:, :, 0, :], qv[:, :, 1, :]
        wsv = rsc[:, 0:256].rearrange("p (h two j) -> p h two j", two=2, j=16)
        t1v, t2v = wsv[:, :, 0, :], wsv[:, :, 1, :]
        e1 = p.do("vector", (lambda av, ctb, x1o: lambda e: e.tensor_tensor(
            out=x1o, in0=av, in1=ctb, op=ALU.mult))(av, ctb, x1o),
            deps=[evm], reads=[bu("crep"), b_ps[4]], writes=[bu("qtm")], signal=True)
        e1 = p.do("vector", (lambda av, stb, x2o: lambda e: e.tensor_tensor(
            out=x2o, in0=av, in1=stb, op=ALU.mult))(av, stb, x2o),
            reads=[bu("crep"), b_ps[4]], writes=[bu("qtm")], signal=True)
        e2 = p.do("vector", (lambda bv, stb, t1v: lambda e: e.tensor_tensor(
            out=t1v, in0=bv, in1=stb, op=ALU.mult))(bv, stb, t1v),
            reads=[bu("crep"), b_ps[4]], writes=[bu("rsc")], signal=True)
        e2 = p.do("vector", (lambda bv, ctb, t2v: lambda e: e.tensor_tensor(
            out=t2v, in0=bv, in1=ctb, op=ALU.mult))(bv, ctb, t2v),
            reads=[bu("crep"), b_ps[4]], writes=[bu("rsc")], signal=True)
        e3 = p.do("vector", (lambda x1o, t1v: lambda e: e.tensor_tensor(
            out=x1o, in0=x1o, in1=t1v, op=ALU.subtract))(x1o, t1v),
            deps=[force(e1), force(e2)], reads=[bu("rsc")], writes=[bu("qtm")], signal=True)
        e3 = p.do("vector", (lambda x2o, t2v: lambda e: e.tensor_tensor(
            out=x2o, in0=x2o, in1=t2v, op=ALU.add))(x2o, t2v),
            deps=[force(e3)], reads=[bu("rsc")], writes=[bu("qtm")], signal=True)
        evt = None
        for cc2 in range(2):
            evt = p.do("tensor", (lambda cc2: lambda e: e.transpose(
                bankb(5)[:, cc2 * 128:(cc2 + 1) * 128],
                qtm[:, cc2 * 128:(cc2 + 1) * 128], C.identb[:]))(cc2),
                deps=[evi, e3], reads=[bu("qtm")], writes=[b_ps[5]],
                signal=(cc2 == 1))
        for cc2 in range(2):
            p.do("vector", (lambda t, cc2: lambda e: e.tensor_copy(
                out=krf[:, cc2 * S + t * 128:cc2 * S + (t + 1) * 128],
                in_=bankb(5)[:, cc2 * 128:(cc2 + 1) * 128]))(t, cc2),
                deps=[evt], reads=[b_ps[5]], writes=[bu("krf")],
                signal=(cc2 == 1))

    for m in range(2):
        for ts in range(4):
            evm = None
            for ic in range(2):
                evm = p.do("tensor", (lambda m, ts, ic: lambda e: e.matmul(
                    bank(2 + m), upkT[:, ic * 256 + m * 128:ic * 256 + (m + 1) * 128],
                    ckv[:, ic * S + ts * 512:ic * S + (ts + 1) * 512],
                    start=(ic == 0), stop=(ic == 1)))(m, ts, ic),
                    reads=[bu("upkT"), bu("ckv")], writes=[b_ps[2 + m]],
                    signal=(ic == 1))
            p.do("vector", (lambda m, ts: lambda e: e.tensor_copy(
                out=kcf[:, m * S + ts * 512:m * S + (ts + 1) * 512],
                in_=bank(2 + m)))(m, ts),
                deps=[evm], reads=[b_ps[2 + m]], writes=[bu("kcf")], signal=True)

    for kt in range(16):
        evm = None
        for ic in range(2):
            evm = p.do("tensor", (lambda kt, ic: lambda e: e.matmul(
                bank(6), ckv[:, ic * S + kt * 128:ic * S + (kt + 1) * 128],
                upvT[:, ic * 512:(ic + 1) * 512],
                start=(ic == 0), stop=(ic == 1)))(kt, ic),
                reads=[bu("upvT"), bu("ckv")], writes=[b_ps[6]],
                signal=(ic == 1))
        p.do("vector", (lambda kt: lambda e: e.tensor_copy(
            out=vbuf[:, kt * 520:(kt + 1) * 520]
            .rearrange("p (h e) -> p h e", e=65)[:, :, 0:64],
            in_=bank(6).rearrange("p (h d) -> p h d", d=64)))(kt),
            deps=[evm], reads=[b_ps[6]], writes=[bu("vbuf")], signal=True)

    p.full_barrier()
    for t in range(NT):
        evd = p.do("sync", (lambda t: lambda e: e.dma_start(
            out=xtb[:], in_=xo[t * 128:(t + 1) * 128, :]))(t),
            writes=[bu("xtb")], dma=True)
        er, em = ln_stats(p, xtb[:], rsc[:].bitcast(BF16)[:, 0:512], mv,
                          sc[:, 0:1], sc[:, 1:2], lnbufs, xbuf=bu("xtb"),
                          epsc=C.cst[:, 0:1], epsbuf=C.b_cst, deps=[evd])
        ev = p.do("vector", lambda e: e.tensor_scalar(
            out=wstage[:, 4096:4096 + D], in0=xtb[:], scalar1=sc[:, 1:2],
            scalar2=sc[:, 0:1], op0=ALU.add, op1=ALU.mult),
            deps=[er, em], reads=[bu("xtb"), bu("rs")], writes=[bu("wstage")],
            signal=True)
        for g in range(2):
            pb = 6 + g
            evg = None
            for dc in range(g * 8, g * 8 + 8):
                evg = p.do("tensor", (lambda dc, g, pb: lambda e: e.transpose(
                    bankb(pb)[:, (dc - g * 8) * 128:(dc - g * 8 + 1) * 128],
                    wstage[:, 4096 + dc * 128:4096 + (dc + 1) * 128],
                    C.identb[:]))(dc, g, pb),
                    deps=[evi, ev], reads=[bu("wstage")], writes=[b_ps[pb]],
                    signal=(dc == g * 8 + 7))
            for dc in range(g * 8, g * 8 + 8):
                p.do("vector", (lambda t, dc, pb, g: lambda e: e.tensor_copy(
                    out=xhat[:, dc * TOK + t * 128:dc * TOK + (t + 1) * 128],
                    in_=bankb(pb)[:, (dc - g * 8) * 128:(dc - g * 8 + 1) * 128]))(t, dc, pb, g),
                    deps=[evg], reads=[b_ps[pb]], writes=[bu("xhat")],
                    signal=(dc == g * 8 + 7))

    for oc in range(4):
        evd = p.do("gpsimd", (lambda oc: lambda e: e.dma_start(
            out=wstage[:, 0:8192].rearrange("p (r c) -> p r c", r=4),
            in_=qw_in[oc * 512:(oc + 1) * 512, :]
            .rearrange("(r p) c -> p r c", p=128)))(oc),
            writes=[bu("wstage")], dma=True)
        for tg in range(2):
            for ic in range(16):
                pcb = ic % 2
                evt = None
                for r in range(4):
                    evt = p.do("tensor", (lambda ic, r, pcb: lambda e: e.transpose(
                        bankb(4 + pcb)[:, r * 128:(r + 1) * 128],
                        wstage[:, r * 2048 + ic * 128:r * 2048 + (ic + 1) * 128],
                        C.identb[:]))(ic, r, pcb),
                        deps=[evi, evd], reads=[bu("wstage")], writes=[b_ps[4 + pcb]],
                        signal=(r == 3))
                evp = p.do("vector", (lambda pcb: lambda e: e.tensor_copy(
                    out=piece[:, pcb * 512:(pcb + 1) * 512],
                    in_=bankb(4 + pcb)[:, 0:512]))(pcb),
                    deps=[evt], reads=[b_ps[4 + pcb]], writes=[bu("piece")], signal=True)
                for tl in range(4):
                    t = tg * 4 + tl
                    p.do("tensor", (lambda ic, t, tl, pcb: lambda e: e.matmul(
                        bank(tl), xhat[:, ic * TOK + t * 128:ic * TOK + (t + 1) * 128],
                        piece[:, pcb * 512:(pcb + 1) * 512],
                        start=(ic == 0), stop=(ic == 15)))(ic, t, tl, pcb),
                        deps=[evp], reads=[bu("piece"), bu("xhat")],
                        writes=[b_ps[tl]], signal=(ic == 15 or tl == 3))
            for tl in range(4):
                t = tg * 4 + tl
                qp = bank(tl).rearrange("p (h two j) -> p h two j", two=2, j=32)
                av, bv = qp[:, :, 0, :], qp[:, :, 1, :]
                ct = cqc[:, t * 32:(t + 1) * 32].rearrange("p (h j) -> p h j", h=1)
                st = cqs[:, t * 32:(t + 1) * 32].rearrange("p (h j) -> p h j", h=1)
                _, ctb = broadcast_tensor_aps(av, ct)
                _, stb = broadcast_tensor_aps(av, st)
                wsv = rsc[:].rearrange("p (h two j) -> p h two j", two=2, j=32)
                t1v, t2v = wsv[:, :, 0, :], wsv[:, :, 1, :]
                e1 = p.do("vector", (lambda av, ctb, t1v: lambda e: e.tensor_tensor(
                    out=t1v, in0=av, in1=ctb, op=ALU.mult))(av, ctb, t1v),
                    reads=[bu("crep"), b_ps[tl]], writes=[bu("rsc")], signal=True)
                e1 = p.do("vector", (lambda bv, stb, t2v: lambda e: e.tensor_tensor(
                    out=t2v, in0=bv, in1=stb, op=ALU.mult))(bv, stb, t2v),
                    reads=[bu("crep"), b_ps[tl]], writes=[bu("rsc")], signal=True)
                qv = qtm[:].rearrange("p (h q4 j) -> p h q4 j", q4=4, j=16)
                ev1 = p.do("vector", (lambda t1v, t2v, qv: lambda e: e.tensor_tensor(
                    out=qv[:, :, 0, :],
                    in0=t1v.rearrange("p h (s j) -> p h s j", s=2)[:, :, 0, :],
                    in1=t2v.rearrange("p h (s j) -> p h s j", s=2)[:, :, 0, :],
                    op=ALU.subtract))(t1v, t2v, qv),
                    deps=[force(e1)], reads=[bu("rsc")], writes=[bu("qtm")], signal=True)
                ev1 = p.do("vector", (lambda t1v, t2v, qv: lambda e: e.tensor_tensor(
                    out=qv[:, :, 2, :],
                    in0=t1v.rearrange("p h (s j) -> p h s j", s=2)[:, :, 1, :],
                    in1=t2v.rearrange("p h (s j) -> p h s j", s=2)[:, :, 1, :],
                    op=ALU.subtract))(t1v, t2v, qv),
                    reads=[bu("rsc")], writes=[bu("qtm")], signal=True)
                e2 = p.do("vector", (lambda av, stb, t1v: lambda e: e.tensor_tensor(
                    out=t1v, in0=av, in1=stb, op=ALU.mult))(av, stb, t1v),
                    deps=[force(ev1)], reads=[bu("crep"), b_ps[tl]],
                    writes=[bu("rsc")], signal=True)
                e2 = p.do("vector", (lambda bv, ctb, t2v: lambda e: e.tensor_tensor(
                    out=t2v, in0=bv, in1=ctb, op=ALU.mult))(bv, ctb, t2v),
                    reads=[bu("crep"), b_ps[tl]], writes=[bu("rsc")], signal=True)
                ev2 = p.do("vector", (lambda t1v, t2v, qv: lambda e: e.tensor_tensor(
                    out=qv[:, :, 1, :],
                    in0=t1v.rearrange("p h (s j) -> p h s j", s=2)[:, :, 0, :],
                    in1=t2v.rearrange("p h (s j) -> p h s j", s=2)[:, :, 0, :],
                    op=ALU.add))(t1v, t2v, qv),
                    deps=[force(e2)], reads=[bu("rsc")], writes=[bu("qtm")], signal=True)
                ev2 = p.do("vector", (lambda t1v, t2v, qv: lambda e: e.tensor_tensor(
                    out=qv[:, :, 3, :],
                    in0=t1v.rearrange("p h (s j) -> p h s j", s=2)[:, :, 1, :],
                    in1=t2v.rearrange("p h (s j) -> p h s j", s=2)[:, :, 1, :],
                    op=ALU.add))(t1v, t2v, qv),
                    reads=[bu("rsc")], writes=[bu("qtm")], signal=True)
                evt = None
                for r4 in range(4):
                    evt = p.do("tensor", (lambda r4: lambda e: e.transpose(
                        bankb(7)[:, r4 * 128:(r4 + 1) * 128],
                        qtm[:, r4 * 128:(r4 + 1) * 128], C.identb[:]))(r4),
                        deps=[evi, force(ev1), force(ev2)], reads=[bu("qtm")],
                        writes=[b_ps[7]], signal=(r4 == 3))
                for hh in range(8):
                    qh = oc * 8 + hh
                    h = qh // 4
                    rbase = (h % 4) * 32
                    cblk = (h // 4) * 4 + (qh % 4)
                    prow = (hh % 2) * 64
                    pcol = (hh // 2) * 128
                    p.do("vector", (lambda prow, pcol, rbase, cblk, t: lambda e: e.tensor_copy(
                        out=qro[rbase:rbase + 32, cblk * TOK + t * 128:cblk * TOK + (t + 1) * 128],
                        in_=bankb(7)[prow:prow + 32, pcol:pcol + 128]))(prow, pcol, rbase, cblk, t),
                        deps=[evt], reads=[b_ps[7]], writes=[bu("qro")],
                        signal=(hh == 7))
                    p.do("vector", (lambda prow, pcol, rbase, cblk, t: lambda e: e.tensor_copy(
                        out=qno[rbase:rbase + 32, cblk * TOK + t * 128:cblk * TOK + (t + 1) * 128],
                        in_=bankb(7)[prow + 32:prow + 64, pcol:pcol + 128]))(prow, pcol, rbase, cblk, t),
                        deps=[evt], reads=[b_ps[7]], writes=[bu("qno")],
                        signal=(hh == 7))

    p.full_barrier()
    SCL = float(HD_L1) ** -0.5
    for qc in range(2):
        evmk = p.do("sync", (lambda qc: lambda e: e.dma_start(
            out=maskb[:].rearrange("p (tt c) -> p tt c", tt=16),
            in_=mask_in[:, qc * 512:(qc + 1) * 512]
            .rearrange("(tt p) c -> p tt c", p=128)))(qc),
            writes=[bu("maskb")], dma=True)
        for qh in range(NH_L1):
            h = qh // 4
            rbase = (h % 4) * 32
            cc = h // 4
            cblk = (h // 4) * 4 + (qh % 4)
            pav = 2 + (qh % 2)
            for kt in range(16):
                pbs = kt % 2
                tp = (rbase, 0) if rbase == 96 else None
                ev1 = p.do("tensor", (lambda rbase, cc, cblk, kt, qc, pbs, tp: lambda e: e.matmul(
                    bank(pbs), krf[rbase:rbase + 32, cc * S + kt * 128:cc * S + (kt + 1) * 128],
                    qro[rbase:rbase + 32, cblk * TOK + qc * 512:cblk * TOK + (qc + 1) * 512],
                    start=True, stop=False, tile_position=tp))(rbase, cc, cblk, kt, qc, pbs, tp),
                    reads=[bu("krf"), bu("qro")], writes=[b_ps[pbs]])
                ev1 = p.do("tensor", (lambda rbase, cc, cblk, kt, qc, pbs, tp: lambda e: e.matmul(
                    bank(pbs), kcf[rbase:rbase + 32, cc * S + kt * 128:cc * S + (kt + 1) * 128],
                    qno[rbase:rbase + 32, cblk * TOK + qc * 512:cblk * TOK + (qc + 1) * 512],
                    start=False, stop=True, tile_position=tp))(rbase, cc, cblk, kt, qc, pbs, tp),
                    reads=[bu("kcf"), bu("qno")], writes=[b_ps[pbs]], signal=True)
                ev2 = p.do("scalar", (lambda pbs, kt: lambda e: e.activation(
                    ptl[:, (kt % 4) * 512:(kt % 4 + 1) * 512], bank(pbs),
                    AF.Exp, scale=SCL))(pbs, kt),
                    deps=[ev1], reads=[b_ps[pbs]], writes=[bu("ptl")], signal=True)
                ev3 = p.do("vector", (lambda kt: lambda e: e.tensor_mul(
                    out=ptl[:, (kt % 4) * 512:(kt % 4 + 1) * 512],
                    in0=ptl[:, (kt % 4) * 512:(kt % 4 + 1) * 512],
                    in1=maskb[:, kt * 512:(kt + 1) * 512]))(kt),
                    deps=[ev2, evmk], reads=[bu("maskb")], writes=[bu("ptl")],
                    signal=True)
                p.do("tensor", (lambda kt, h, pav: lambda e: e.matmul(
                    PS[0:65, pav * 512:(pav + 1) * 512],
                    vbuf[:, kt * 520 + h * 65:kt * 520 + (h + 1) * 65],
                    ptl[:, (kt % 4) * 512:(kt % 4 + 1) * 512],
                    start=(kt == 0), stop=(kt == 15)))(kt, h, pav),
                    deps=[ev3], reads=[bu("vbuf"), bu("ptl")],
                    writes=[b_ps[pav]], signal=True)
            evc = p.do("vector", (lambda pav: lambda e: e.tensor_copy(
                out=rb[0:1, 0:512], in_=PS[64:65, pav * 512:(pav + 1) * 512]))(pav),
                reads=[b_ps[pav]], writes=[bu("rb")], signal=True)
            evc = p.do("vector", lambda e: e.reciprocal(out=rb[0:1, 0:512], in_=rb[0:1, 0:512]),
                       deps=[force(evc)], writes=[bu("rb")], signal=True)
            evb = p.do("tensor", lambda e: e.matmul(
                bank(6)[0:64, :], C.onesb[0:1, 0:64], rb[0:1, 0:512],
                start=True, stop=True),
                deps=[evc, evo], reads=[bu("rb")], writes=[b_ps[6]], signal=True)
            evb = p.do("vector", lambda e: e.tensor_copy(out=rb[0:64, 0:512], in_=bank(6)[0:64, :]),
                       deps=[evb], reads=[b_ps[6]], writes=[bu("rb")], signal=True)
            arow = (qh % 2) * 64
            acol = (qh // 2) * TOK
            p.do("vector", (lambda arow, acol, qc, pav: lambda e: e.tensor_tensor(
                out=attn_fm[arow:arow + 64, acol + qc * 512:acol + (qc + 1) * 512],
                in0=PS[0:64, pav * 512:(pav + 1) * 512], in1=rb[0:64, 0:512],
                op=ALU.mult))(arow, acol, qc, pav),
                deps=[evb], reads=[b_ps[pav], bu("rb")], writes=[bu("attn")],
                signal=True)

    p.full_barrier()
    for oc in range(4):
        evd = p.do("gpsimd", (lambda oc: lambda e: e.dma_start(
            out=wstage[:, 0:8192].rearrange("p (r c) -> p r c", r=4),
            in_=ow_in[oc * 512:(oc + 1) * 512, :]
            .rearrange("(r p) c -> p r c", p=128)))(oc),
            writes=[bu("wstage")], dma=True)
        for tg in range(2):
            for ic in range(16):
                pcb = ic % 2
                evt = None
                for r in range(4):
                    evt = p.do("tensor", (lambda ic, r, pcb: lambda e: e.transpose(
                        bankb(4 + pcb)[:, r * 128:(r + 1) * 128],
                        wstage[:, r * 2048 + ic * 128:r * 2048 + (ic + 1) * 128],
                        C.identb[:]))(ic, r, pcb),
                        deps=[evi, evd], reads=[bu("wstage")], writes=[b_ps[4 + pcb]],
                        signal=(r == 3))
                evp = p.do("vector", (lambda pcb: lambda e: e.tensor_copy(
                    out=piece[:, pcb * 512:(pcb + 1) * 512],
                    in_=bankb(4 + pcb)[:, 0:512]))(pcb),
                    deps=[evt], reads=[b_ps[4 + pcb]], writes=[bu("piece")], signal=True)
                for tl in range(4):
                    t = tg * 4 + tl
                    p.do("tensor", (lambda ic, t, tl, pcb: lambda e: e.matmul(
                        bank(tl), attn_fm[:, ic * TOK + t * 128:ic * TOK + (t + 1) * 128],
                        piece[:, pcb * 512:(pcb + 1) * 512],
                        start=(ic == 0), stop=(ic == 15)))(ic, t, tl, pcb),
                        deps=[evp], reads=[bu("piece"), bu("attn")],
                        writes=[b_ps[tl]], signal=(ic == 15 or tl == 3))
            for tl in range(4):
                t = tg * 4 + tl
                p.do("vector", (lambda t, tl, oc: lambda e: e.tensor_copy(
                    out=ybf[:, t * D + oc * 512:t * D + (oc + 1) * 512],
                    in_=bank(tl)))(t, tl, oc),
                    reads=[b_ps[tl]], writes=[bu("ybf")], signal=True)

    p.full_barrier()
    for t in range(NT):
        yt = ybf[:, t * D:(t + 1) * D]
        evd = p.do("sync", (lambda t: lambda e: e.dma_start(
            out=xtb[:], in_=xo[t * 128:(t + 1) * 128, :]))(t),
            writes=[bu("xtb")], dma=True)
        er, em = ln_stats(p, yt, rsc[:].bitcast(BF16)[:, 0:512], mv,
                          sc[:, 0:1], sc[:, 1:2], lnbufs, xbuf=bu("ybf"),
                          epsc=C.cst[:, 0:1], epsbuf=bu("C.cst"))
        ev = p.do("vector", (lambda yt: lambda e: e.tensor_scalar(
            out=yw[:], in0=yt, scalar1=sc[:, 1:2], scalar2=sc[:, 0:1],
            op0=ALU.add, op1=ALU.mult))(yt),
            deps=[er, em], reads=[bu("ybf"), bu("rs")], writes=[bu("yw")], signal=True)
        ev = p.do("vector", lambda e: e.tensor_add(out=yw[:], in0=yw[:], in1=xtb[:]),
                  deps=[evd, force(ev)], reads=[bu("xtb")], writes=[bu("yw")], signal=True)
        p.do("sync", (lambda t: lambda e: e.dma_start(
            out=out[t * 128:(t + 1) * 128, :], in_=yw[:]))(t),
            deps=[ev], reads=[bu("yw")], dma=True)

    fin = p.barrier_event("sync")
    p.do("sync", lambda e: None, deps=[fin])


def emit_l0attn(C, A, xg, xo, wdkv_in, wuk_in, wuv_in, wdq_in, wuq_in, wqr_in, wkr_in, wout_in, delta_in, mask_in, c0_in, s0_in, c0q_in, s0q_in, ident_in, ones_in, out):
    nc, p = C.nc, C.p
    PS = C.PS
    I32 = mybir.dt.int32
    xhat = A.take(16 * S, BF16)
    ckv = A.take(4 * S, BF16)
    cq = A.take(8 * TOK, BF16)
    qrf = A.take(8 * TOK, BF16)
    krfb = A.take(S, BF16)
    kcb = A.take(S, BF16)
    qcb = A.take(TOK, BF16)
    vg = A.take(16 * 512, BF16)
    wstage = A.take(8192, BF16)
    maskb = A.take(2 * 16 * 512, BF16)
    ctf = A.take(512, F32)
    stf = A.take(512, F32)
    ctfq = A.take(256, F32)
    stfq = A.take(256, F32)
    rsc = A.take(2048, F32)
    qtm = A.take(1024, BF16)
    ptl = qtm
    rb = A.take(512, F32)
    dsc = A.take(256, F32)
    dsi = A.take(32, I32)
    cdb = A.take(64, F32)
    mv = A.take(8, F32)
    sc = A.take(4, F32)
                        
    p = C.p
    bank = lambda i: PS[:, i * 512:(i + 1) * 512]
    bankb = lambda i: PS[:, i * 512:(i + 1) * 512].bitcast(BF16)
    attn_fm = xhat[:, 0:16 * TOK]
    ybf = xhat[:, 16 * TOK:32 * TOK]
    cqF = cq.bitcast(F32)
    xtb = cqF[:, 0:D]
    yw = cqF[:, D:2 * D]
    wstF = wstage.bitcast(F32)
    c0t = wstF[:, 0:512]
    s0t = wstF[:, 512:1024]

    names = ["xhat", "ckv", "cq", "qrf", "krfb", "kcb", "qcb", "vg",
             "wstage", "maskb", "tab", "rsc", "qtm", "ptl", "rb",
             "dsc", "cdb", "mv", "rs", "C.cst", "attn", "ybf", "xtb", "yw",
             "ones"]
    bufs = {n: TBuf(n) for n in names}
    b_ps = [TBuf(f"ps{i}") for i in range(8)]
    bufs["cst"] = C.b_cst
    bu = lambda n: bufs[n]
    lnbufs = (bu("rsc"), bu("mv"), bu("rs"))
    TWO_PI = float(2.0 * np.pi)
    evi = C.evi
    evo = C.evo
    evc0 = p.do("sync", lambda e: e.dma_start(
        out=c0t.rearrange("p (tt j) -> p tt j", j=32),
        in_=c0_in.rearrange("(tt p) j -> p tt j", p=128)), writes=[bu("wstage")], dma=True)
    evs0 = p.do("sync", lambda e: e.dma_start(
        out=s0t.rearrange("p (tt j) -> p tt j", j=32),
        in_=s0_in.rearrange("(tt p) j -> p tt j", p=128)), writes=[bu("wstage")], dma=True)
    evd = p.do("sync", lambda e: e.dma_start(
        out=dsc[0:1, 0:32], in_=delta_in.rearrange("(a j) -> a j", a=1)),
        writes=[bu("dsc")], dma=True)
    ev = p.do("vector", lambda e: e.tensor_scalar_mul(
        out=dsc[0:1, 32:64], in0=dsc[0:1, 0:32], scalar1=1.0 / TWO_PI),
        deps=[evd], writes=[bu("dsc")], signal=True)
    ev = p.do("vector", lambda e: e.tensor_copy(out=dsi[0:1, :], in_=dsc[0:1, 32:64]),
              deps=[force(ev)], writes=[bu("dsc")], signal=True)
    ev = p.do("vector", lambda e: e.tensor_copy(out=dsc[0:1, 32:64], in_=dsi[0:1, :]),
              deps=[force(ev)], writes=[bu("dsc")], signal=True)
    ev = p.do("vector", lambda e: e.scalar_tensor_tensor(
        out=dsc[0:1, 64:96], in0=dsc[0:1, 32:64], scalar=-TWO_PI,
        in1=dsc[0:1, 0:32], op0=ALU.mult, op1=ALU.add),
        deps=[force(ev)], writes=[bu("dsc")], signal=True)
    ev1 = p.do("scalar", lambda e: e.activation(dsc[0:1, 96:128], dsc[0:1, 64:96], AF.Sin),
               deps=[ev], writes=[bu("dsc")], signal=True)
    ev2 = p.do("scalar", lambda e: e.activation(dsc[0:1, 128:160], dsc[0:1, 64:96],
                                                AF.Sin, scale=0.5),
               deps=[force(ev1)], writes=[bu("dsc")], signal=True)
    ev2 = p.do("scalar", lambda e: e.activation(dsc[0:1, 160:192], dsc[0:1, 128:160], AF.Square),
               deps=[force(ev2)], writes=[bu("dsc")], signal=True)
    ev2 = p.do("vector", lambda e: e.tensor_scalar(
        out=dsc[0:1, 192:224], in0=dsc[0:1, 160:192], scalar1=-2.0, scalar2=1.0,
        op0=ALU.mult, op1=ALU.add),
        deps=[ev2], writes=[bu("dsc")], signal=True)
    evb = p.do("tensor", lambda e: e.matmul(PS[:, 0:32], C.onesb[0:1, :],
                                            dsc[0:1, 192:224],
                                            start=True, stop=True),
               deps=[evo, ev2], reads=[bu("dsc")], writes=[b_ps[0]], signal=True)
    evb2 = p.do("tensor", lambda e: e.matmul(PS[:, 64:96], C.onesb[0:1, :], dsc[0:1, 96:128],
                                             start=True, stop=True),
                deps=[evo, ev1], reads=[bu("dsc")], writes=[b_ps[0]], signal=True)
    evb = p.do("vector", lambda e: e.tensor_copy(out=cdb[:, 0:32], in_=PS[:, 0:32]),
               deps=[evb], reads=[b_ps[0]], writes=[bu("cdb")], signal=True)
    evb = p.do("vector", lambda e: e.tensor_copy(out=cdb[:, 32:64], in_=PS[:, 64:96]),
               deps=[evb2], reads=[b_ps[0]], writes=[bu("cdb")], signal=True)
    from concourse.bass import broadcast_tensor_aps
    c0v = c0t.rearrange("p (tt j) -> p tt j", j=32)
    s0v = s0t.rearrange("p (tt j) -> p tt j", j=32)
    cdv = cdb[:, 0:32].rearrange("p (tt j) -> p tt j", tt=1)
    sdv = cdb[:, 32:64].rearrange("p (tt j) -> p tt j", tt=1)
    _, cdvb = broadcast_tensor_aps(c0v, cdv)
    _, sdvb = broadcast_tensor_aps(c0v, sdv)
    ctv = ctf[:].rearrange("p (tt j) -> p tt j", j=32)
    stv = stf[:].rearrange("p (tt j) -> p tt j", j=32)
    rv = rsc[:, 0:512].rearrange("p (tt j) -> p tt j", j=32)
    e1 = p.do("vector", lambda e: e.tensor_tensor(out=ctv, in0=c0v, in1=cdvb, op=ALU.mult),
              deps=[evb, evc0], reads=[bu("cdb"), bu("wstage")], writes=[bu("tab")], signal=True)
    e1 = p.do("vector", lambda e: e.tensor_tensor(out=rv, in0=s0v, in1=sdvb, op=ALU.mult),
              deps=[evs0], reads=[bu("cdb"), bu("wstage")], writes=[bu("rsc")], signal=True)
    e1 = p.do("vector", lambda e: e.tensor_tensor(out=ctv, in0=ctv, in1=rv, op=ALU.subtract),
              deps=[force(e1)], reads=[bu("rsc")], writes=[bu("tab")], signal=True)
    e2 = p.do("vector", lambda e: e.tensor_tensor(out=stv, in0=s0v, in1=cdvb, op=ALU.mult),
              reads=[bu("cdb"), bu("wstage")], writes=[bu("tab")], signal=True)
    e2 = p.do("vector", lambda e: e.tensor_tensor(out=rv, in0=c0v, in1=sdvb, op=ALU.mult),
              reads=[bu("cdb"), bu("wstage")], writes=[bu("rsc")], signal=True)
    e2 = p.do("vector", lambda e: e.tensor_tensor(out=stv, in0=stv, in1=rv, op=ALU.add),
              deps=[force(e2)], reads=[bu("rsc")], writes=[bu("tab")], signal=True)

    c0qt = rsc[:, 1024:1280]
    s0qt = rsc[:, 1280:1536]
    evcq = p.do("sync", lambda e: e.dma_start(
        out=c0qt.rearrange("p (tt j) -> p tt j", j=32),
        in_=c0q_in.rearrange("(tt p) j -> p tt j", p=128)), writes=[bu("rsc")], dma=True)
    evsq = p.do("sync", lambda e: e.dma_start(
        out=s0qt.rearrange("p (tt j) -> p tt j", j=32),
        in_=s0q_in.rearrange("(tt p) j -> p tt j", p=128)), writes=[bu("rsc")], dma=True)
    c0qv = c0qt.rearrange("p (tt j) -> p tt j", j=32)
    s0qv = s0qt.rearrange("p (tt j) -> p tt j", j=32)
    _, cdvb2 = broadcast_tensor_aps(c0qv, cdv)
    _, sdvb2 = broadcast_tensor_aps(c0qv, sdv)
    ctqv = ctfq.rearrange("p (tt j) -> p tt j", j=32)
    stqv = stfq.rearrange("p (tt j) -> p tt j", j=32)
    rvq = rsc[:, 0:256].rearrange("p (tt j) -> p tt j", j=32)
    eq1 = p.do("vector", lambda e: e.tensor_tensor(out=ctqv, in0=c0qv, in1=cdvb2, op=ALU.mult),
               deps=[evb, evcq], reads=[bu("cdb"), bu("rsc")], writes=[bu("tab")], signal=True)
    eq1 = p.do("vector", lambda e: e.tensor_tensor(out=rvq, in0=s0qv, in1=sdvb2, op=ALU.mult),
               deps=[evsq], reads=[bu("cdb")], writes=[bu("rsc")], signal=True)
    eq1 = p.do("vector", lambda e: e.tensor_tensor(out=ctqv, in0=ctqv, in1=rvq, op=ALU.subtract),
               deps=[force(eq1)], reads=[bu("rsc")], writes=[bu("tab")], signal=True)
    eq2 = p.do("vector", lambda e: e.tensor_tensor(out=stqv, in0=s0qv, in1=cdvb2, op=ALU.mult),
               reads=[bu("cdb"), bu("rsc")], writes=[bu("tab")], signal=True)
    eq2 = p.do("vector", lambda e: e.tensor_tensor(out=rvq, in0=c0qv, in1=sdvb2, op=ALU.mult),
               reads=[bu("cdb")], writes=[bu("rsc")], signal=True)
    eq2 = p.do("vector", lambda e: e.tensor_tensor(out=stqv, in0=stqv, in1=rvq, op=ALU.add),
               deps=[force(eq2)], reads=[bu("rsc")], writes=[bu("tab")], signal=True)

    for t in range(16):
        evdx = p.do("sync", (lambda t: lambda e: e.dma_start(
            out=xtb[:], in_=xg[t * 128:(t + 1) * 128, :]))(t),
            writes=[bu("xtb")], dma=True)
        er, em = ln_stats(p, xtb[:], qtm[:, 0:512], mv, sc[:, 0:1], sc[:, 1:2],
                          lnbufs, xbuf=bu("xtb"), epsc=C.cst[:, 0:1],
                          epsbuf=bu("C.cst"), deps=[evdx])
        ev = p.do("vector", lambda e: e.tensor_scalar(
            out=wstage[:, 4096:4096 + D], in0=xtb[:], scalar1=sc[:, 1:2],
            scalar2=sc[:, 0:1], op0=ALU.add, op1=ALU.mult),
            deps=[er, em], reads=[bu("xtb"), bu("rs")], writes=[bu("wstage")],
            signal=True)
        for g in range(2):
            pb = 6 + g
            evg = None
            for dc in range(g * 8, g * 8 + 8):
                evg = p.do("tensor", (lambda dc, g, pb: lambda e: e.transpose(
                    bankb(pb)[:, (dc - g * 8) * 128:(dc - g * 8 + 1) * 128],
                    wstage[:, 4096 + dc * 128:4096 + (dc + 1) * 128],
                    C.identb[:]))(dc, g, pb),
                    deps=[evi, ev], reads=[bu("wstage")], writes=[b_ps[pb]],
                    signal=(dc == g * 8 + 7))
            for dc in range(g * 8, g * 8 + 8):
                p.do("vector", (lambda t, dc, pb, g: lambda e: e.tensor_copy(
                    out=xhat[:, dc * S + t * 128:dc * S + (t + 1) * 128],
                    in_=bankb(pb)[:, (dc - g * 8) * 128:(dc - g * 8 + 1) * 128]))(t, dc, pb, g),
                    deps=[evg], reads=[b_ps[pb]], writes=[bu("xhat")],
                    signal=(dc == g * 8 + 7))

    p.full_barrier()
    evw = p.do("gpsimd", lambda e: e.dma_start(
        out=wstage[:, 0:8192].rearrange("p (ic c) -> p ic c", c=512),
        in_=wdkv_in.rearrange("(ic p) c -> p ic c", p=128)),
        writes=[bu("wstage")], dma=True)
    for m in range(4):
        for ts in range(4):
            evm = None
            for ic in range(16):
                evm = p.do("tensor", (lambda m, ts, ic: lambda e: e.matmul(
                    bank(m % 2), wstage[:, ic * 512 + m * 128:ic * 512 + (m + 1) * 128],
                    xhat[:, ic * S + ts * 512:ic * S + (ts + 1) * 512],
                    start=(ic == 0), stop=(ic == 15)))(m, ts, ic),
                    deps=[evw], reads=[bu("wstage"), bu("xhat")], writes=[b_ps[m % 2]],
                    signal=(ic == 15))
            p.do("vector", (lambda m, ts: lambda e: e.tensor_copy(
                out=ckv[:, m * S + ts * 512:m * S + (ts + 1) * 512],
                in_=bank(m % 2)))(m, ts),
                deps=[evm], reads=[b_ps[m % 2]], writes=[bu("ckv")], signal=True)

    evw = p.do("gpsimd", lambda e: e.dma_start(
        out=wstage[:, 0:1024].rearrange("p (ic c) -> p ic c", c=64),
        in_=wkr_in.rearrange("(ic p) c -> p ic c", p=128)),
        writes=[bu("wstage")], dma=True)

    def pope(psum_ap, width, t, outtm, deps0, tabc=None, tabs=None):
        nh = width // 64
        evA = p.do("scalar", (lambda: lambda e: e.activation(
            rsc[:, 0:width], psum_ap, AF.Abs))(),
            deps=deps0, reads=b_ps_cur, writes=[bu("rsc")], signal=True)
        evA = p.do("scalar", (lambda: lambda e: e.activation(
            rsc[:, 0:width], rsc[:, 0:width], AF.Exp, scale=-1.0))(),
            deps=[force(evA)], writes=[bu("rsc")], signal=True)
        evA = p.do("scalar", (lambda: lambda e: e.activation(
            rsc[:, 0:width], rsc[:, 0:width], AF.Ln, bias=1.0))(),
            deps=[force(evA)], writes=[bu("rsc")], signal=True)
        sc2 = rsc[:, width:2 * width]
        evR = p.do("vector", (lambda: lambda e: e.tensor_relu(
            out=sc2, in_=psum_ap))(),
            deps=deps0, reads=b_ps_cur, writes=[bu("rsc")], signal=True)
        evA = p.do("vector", (lambda: lambda e: e.tensor_tensor(
            out=rsc[:, 0:width], in0=rsc[:, 0:width], in1=sc2,
            op=ALU.add))(),
            deps=[evA, force(evR)], writes=[bu("rsc")], signal=True)
        muv = rsc[:, 0:width].rearrange("p (h two j) -> p h two j", two=2, j=32)
        m1, m2 = muv[:, :, 0, :], muv[:, :, 1, :]
        tabc = ctf if tabc is None else tabc
        tabs = stf if tabs is None else tabs
        ct1 = tabc[:, t * 32:(t + 1) * 32].rearrange("p (h j) -> p h j", h=1)
        st1 = tabs[:, t * 32:(t + 1) * 32].rearrange("p (h j) -> p h j", h=1)
        _, ctb = broadcast_tensor_aps(m1, ct1)
        _, stb = broadcast_tensor_aps(m1, st1)
        ov = outtm.rearrange("p (h two j) -> p h two j", two=2, j=32)
        x1, x2 = ov[:, :, 0, :], ov[:, :, 1, :]
        s3 = sc2.rearrange("p (h two j) -> p h two j", two=2, j=32)
        t1, t2 = s3[:, :, 0, :], s3[:, :, 1, :]
        eX = p.do("vector", (lambda: lambda e: e.tensor_tensor(
            out=x1, in0=m1, in1=ctb, op=ALU.mult))(),
            deps=[force(evA)], reads=[bu("tab")], writes=[bu("outtm")], signal=True)
        eX = p.do("vector", (lambda: lambda e: e.tensor_tensor(
            out=x2, in0=m1, in1=stb, op=ALU.mult))(),
            reads=[bu("tab")], writes=[bu("outtm")], signal=True)
        eY = p.do("vector", (lambda: lambda e: e.tensor_tensor(
            out=t1, in0=m2, in1=stb, op=ALU.mult))(),
            reads=[bu("tab")], writes=[bu("rsc")], signal=True)
        eY = p.do("vector", (lambda: lambda e: e.tensor_tensor(
            out=t2, in0=m2, in1=ctb, op=ALU.mult))(),
            reads=[bu("tab")], writes=[bu("rsc")], signal=True)
        eZ = p.do("vector", (lambda: lambda e: e.tensor_tensor(
            out=x1, in0=x1, in1=t1, op=ALU.subtract))(),
            deps=[force(eX), force(eY)], reads=[bu("rsc")], writes=[bu("outtm")], signal=True)
        eZ = p.do("vector", (lambda: lambda e: e.tensor_tensor(
            out=x2, in0=x2, in1=t2, op=ALU.add))(),
            deps=[force(eZ)], reads=[bu("rsc")], writes=[bu("outtm")], signal=True)
        return eZ

    bufs["outtm"] = TBuf("outtm")
    for t in range(16):
        evm = None
        for ic in range(16):
            evm = p.do("tensor", (lambda t, ic: lambda e: e.matmul(
                PS[:, 2048:2112], xhat[:, ic * S + t * 128:ic * S + (t + 1) * 128],
                wstage[:, ic * 64:(ic + 1) * 64],
                start=(ic == 0), stop=(ic == 15)))(t, ic),
                deps=[evw], reads=[bu("wstage"), bu("xhat")], writes=[b_ps[4]],
                signal=(ic == 15))
        b_ps_cur = [b_ps[4]]
        ktm = qtm[:, 512:576]
        eZ = pope(PS[:, 2048:2112], 64, t, ktm, [evm])
        evt = p.do("tensor", (lambda: lambda e: e.transpose(
            bankb(5)[0:64, 0:128], ktm, C.identb[:]))(),
            deps=[evi, eZ], reads=[bu("qtm")], writes=[b_ps[5]], signal=True)
        p.do("vector", (lambda t: lambda e: e.tensor_copy(
            out=krfb[0:64, t * 128:(t + 1) * 128], in_=bankb(5)[0:64, 0:128]))(t),
            deps=[evt], reads=[b_ps[5]], writes=[bu("krfb")], signal=True)
        p.do("vector", (lambda t: lambda e: e.tensor_copy(
            out=krfb[64:128, t * 128:(t + 1) * 128], in_=bankb(5)[0:64, 0:128]))(t),
            deps=[evt], reads=[b_ps[5]], writes=[bu("krfb")], signal=True)

    p.full_barrier()
    for t in range(NT):
        evdx = p.do("sync", (lambda t: lambda e: e.dma_start(
            out=xtb[:], in_=xo[t * 128:(t + 1) * 128, :]))(t),
            writes=[bu("xtb")], dma=True)
        er, em = ln_stats(p, xtb[:], qtm[:, 0:512], mv, sc[:, 0:1], sc[:, 1:2],
                          lnbufs, xbuf=bu("xtb"), epsc=C.cst[:, 0:1],
                          epsbuf=C.b_cst, deps=[evdx])
        ev = p.do("vector", lambda e: e.tensor_scalar(
            out=wstage[:, 4096:4096 + D], in0=xtb[:], scalar1=sc[:, 1:2],
            scalar2=sc[:, 0:1], op0=ALU.add, op1=ALU.mult),
            deps=[er, em], reads=[bu("xtb"), bu("rs")], writes=[bu("wstage")],
            signal=True)
        for g in range(2):
            pb = 6 + g
            evg = None
            for dc in range(g * 8, g * 8 + 8):
                evg = p.do("tensor", (lambda dc, g, pb: lambda e: e.transpose(
                    bankb(pb)[:, (dc - g * 8) * 128:(dc - g * 8 + 1) * 128],
                    wstage[:, 4096 + dc * 128:4096 + (dc + 1) * 128],
                    C.identb[:]))(dc, g, pb),
                    deps=[evi, ev], reads=[bu("wstage")], writes=[b_ps[pb]],
                    signal=(dc == g * 8 + 7))
            for dc in range(g * 8, g * 8 + 8):
                p.do("vector", (lambda t, dc, pb, g: lambda e: e.tensor_copy(
                    out=xhat[:, dc * TOK + t * 128:dc * TOK + (t + 1) * 128],
                    in_=bankb(pb)[:, (dc - g * 8) * 128:(dc - g * 8 + 1) * 128]))(t, dc, pb, g),
                    deps=[evg], reads=[b_ps[pb]], writes=[bu("xhat")],
                    signal=(dc == g * 8 + 7))

    for half in range(2):
        evw = p.do("gpsimd", (lambda half: lambda e: e.dma_start(
            out=wstage[:, 0:8192].rearrange("p (ic c) -> p ic c", c=512),
            in_=wdq_in[:, half * 512:(half + 1) * 512]
            .rearrange("(ic p) c -> p ic c", p=128)))(half),
            writes=[bu("wstage")], dma=True)
        for m in range(4):
            mg = half * 4 + m
            for ts in range(2):
                evm = None
                for ic in range(16):
                    evm = p.do("tensor", (lambda m, ts, ic: lambda e: e.matmul(
                        bank(2 + m % 2), wstage[:, ic * 512 + m * 128:ic * 512 + (m + 1) * 128],
                        xhat[:, ic * TOK + ts * 512:ic * TOK + (ts + 1) * 512],
                        start=(ic == 0), stop=(ic == 15)))(m, ts, ic),
                        deps=[evw], reads=[bu("wstage"), bu("xhat")],
                        writes=[b_ps[2 + m % 2]], signal=(ic == 15))
                p.do("vector", (lambda mg, ts: lambda e: e.tensor_copy(
                    out=cq[:, mg * TOK + ts * 512:mg * TOK + (ts + 1) * 512],
                    in_=bank(2 + mg % 2)))(mg, ts),
                    deps=[evm], reads=[b_ps[2 + mg % 2]], writes=[bu("cq")], signal=True)

    p.full_barrier()
    evw = p.do("gpsimd", lambda e: e.dma_start(
        out=wstage[:, 0:8192].rearrange("p (ic c) -> p ic c", c=1024),
        in_=wqr_in.rearrange("(ic p) c -> p ic c", p=128)),
        writes=[bu("wstage")], dma=True)
    for t in range(NT):
        for oc2 in range(2):
            evm = None
            for ic in range(8):
                evm = p.do("tensor", (lambda t, oc2, ic: lambda e: e.matmul(
                    bank(4 + oc2), cq[:, ic * TOK + t * 128:ic * TOK + (t + 1) * 128],
                    wstage[:, ic * 1024 + oc2 * 512:ic * 1024 + (oc2 + 1) * 512],
                    start=(ic == 0), stop=(ic == 7)))(t, oc2, ic),
                    deps=[evw], reads=[bu("wstage"), bu("cq")], writes=[b_ps[4 + oc2]],
                    signal=(ic == 7))
        b_ps_cur = [b_ps[4], b_ps[5]]
        qr_ps = PS[:, 2048:3072]
        eZ = pope(qr_ps, 1024, t, qtm[:, 0:1024], [p.barrier_event("tensor")],
                  tabc=ctfq, tabs=stfq)
        evt = None
        for rc in range(8):
            evt = p.do("tensor", (lambda rc: lambda e: e.transpose(
                bankb(6)[:, rc * 128:(rc + 1) * 128],
                qtm[:, rc * 128:(rc + 1) * 128], C.identb[:]))(rc),
                deps=[evi, eZ], reads=[bu("qtm")], writes=[b_ps[6]],
                signal=(rc == 7))
        for rc in range(8):
            p.do("vector", (lambda t, rc: lambda e: e.tensor_copy(
                out=qrf[:, rc * TOK + t * 128:rc * TOK + (t + 1) * 128],
                in_=bankb(6)[:, rc * 128:(rc + 1) * 128]))(t, rc),
                deps=[evt], reads=[b_ps[6]], writes=[bu("qrf")],
                signal=(rc == 7))

    p.full_barrier()
    SCL = float(192) ** -0.5
    evmk = [None, None]
    for qc in range(2):
        evmk[qc] = p.do("sync", (lambda qc: lambda e: e.dma_start(
            out=maskb[:, qc * 8192:(qc + 1) * 8192]
            .rearrange("p (tt c) -> p tt c", tt=16),
            in_=mask_in[:, qc * 512:(qc + 1) * 512]
            .rearrange("(tt p) c -> p tt c", p=128)))(qc),
            writes=[bu("maskb")], dma=True)
    for g in range(4):
        evwv = p.do("gpsimd", (lambda g: lambda e: e.dma_start(
            out=wstage[:, 0:2048].rearrange("p (ic c) -> p ic c", c=512),
            in_=wuv_in[:, g * 512:(g + 1) * 512]
            .rearrange("(ic p) c -> p ic c", p=128)))(g),
            writes=[bu("wstage")], dma=True)
        for kt in range(16):
            evm = None
            for ic in range(4):
                evm = p.do("tensor", (lambda g, kt, ic: lambda e: e.matmul(
                    bank(4), ckv[:, ic * S + kt * 128:ic * S + (kt + 1) * 128],
                    wstage[:, ic * 512:(ic + 1) * 512],
                    start=(ic == 0), stop=(ic == 3)))(g, kt, ic),
                    deps=[evwv], reads=[bu("wstage"), bu("ckv")], writes=[b_ps[4]],
                    signal=(ic == 3))
            p.do("vector", (lambda kt: lambda e: e.tensor_copy(
                out=vg[:, kt * 512:(kt + 1) * 512], in_=bank(4)))(kt),
                deps=[evm], reads=[b_ps[4]], writes=[bu("vg")], signal=True)
        for hl in range(4):
            h = g * 4 + hl
            ksl = 0
            evwk = p.do("gpsimd", (lambda h: lambda e: e.dma_start(
                out=wstage[:, 2048:2560].rearrange("p (ic c) -> p ic c", c=128),
                in_=wuk_in[:, h * 128:(h + 1) * 128]
                .rearrange("(ic p) c -> p ic c", p=128)))(h),
                writes=[bu("wstage")], dma=True)
            for ts in range(4):
                evm = None
                for ic in range(4):
                    evm = p.do("tensor", (lambda h, ts, ic: lambda e: e.matmul(
                        bank(5), wstage[:, 2048 + ic * 128:2048 + (ic + 1) * 128],
                        ckv[:, ic * S + ts * 512:ic * S + (ts + 1) * 512],
                        start=(ic == 0), stop=(ic == 3)))(h, ts, ic),
                        deps=[evwk], reads=[bu("wstage"), bu("ckv")], writes=[b_ps[5]],
                        signal=(ic == 3))
                p.do("vector", (lambda ksl, ts: lambda e: e.tensor_copy(
                    out=kcb[:, ksl * S + ts * 512:ksl * S + (ts + 1) * 512],
                    in_=bank(5)))(ksl, ts),
                    deps=[evm], reads=[b_ps[5]], writes=[bu("kcb")], signal=True)
            evwq = p.do("gpsimd", (lambda h: lambda e: e.dma_start(
                out=wstage[:, 4096:5120].rearrange("p (ic c) -> p ic c", c=128),
                in_=wuq_in[:, h * 128:(h + 1) * 128]
                .rearrange("(ic p) c -> p ic c", p=128)))(h),
                writes=[bu("wstage")], dma=True)
            for ts in range(2):
                evm = None
                for ic in range(8):
                    evm = p.do("tensor", (lambda h, ts, ic: lambda e: e.matmul(
                        bank(6), wstage[:, 4096 + ic * 128:4096 + (ic + 1) * 128],
                        cq[:, ic * TOK + ts * 512:ic * TOK + (ts + 1) * 512],
                        start=(ic == 0), stop=(ic == 7)))(h, ts, ic),
                        deps=[evwq], reads=[bu("wstage"), bu("cq")], writes=[b_ps[6]],
                        signal=(ic == 7))
                p.do("vector", (lambda ksl, ts: lambda e: e.tensor_copy(
                    out=qcb[:, ksl * TOK + ts * 512:ksl * TOK + (ts + 1) * 512],
                    in_=bank(6)))(ksl, ts),
                    deps=[evm], reads=[b_ps[6]], writes=[bu("qcb")], signal=True)
            for qc in range(2):
                for kt in range(16):
                    pbs = kt % 2
                    ev1 = p.do("tensor", (lambda ksl, kt, qc, pbs: lambda e: e.matmul(
                        bank(pbs), kcb[:, ksl * S + kt * 128:ksl * S + (kt + 1) * 128],
                        qcb[:, ksl * TOK + qc * 512:ksl * TOK + (qc + 1) * 512],
                        start=True, stop=False))(ksl, kt, qc, pbs),
                        reads=[bu("kcb"), bu("qcb")], writes=[b_ps[pbs]],
                        signal=True)
                    ev1 = p.do("tensor", (lambda h, kt, qc, pbs: lambda e: e.matmul(
                        bank(pbs), krfb[(h % 2) * 64:(h % 2) * 64 + 64,
                                        kt * 128:(kt + 1) * 128],
                        qrf[(h % 2) * 64:(h % 2) * 64 + 64,
                            (h // 2) * TOK + qc * 512:(h // 2) * TOK + (qc + 1) * 512],
                        start=False, stop=True,
                        tile_position=((h % 2) * 64, 0)))(h, kt, qc, pbs),
                        reads=[bu("krfb"), bu("qrf")], writes=[b_ps[pbs]], signal=True)
                    ev2 = p.do("scalar", (lambda pbs, kt: lambda e: e.activation(
                        ptl[:, (kt % 2) * 512:(kt % 2 + 1) * 512], bank(pbs),
                        AF.Exp, scale=SCL))(pbs, kt),
                        deps=[ev1], reads=[b_ps[pbs]], writes=[bu("ptl")], signal=True)
                    ev3 = p.do("vector", (lambda kt, qc: lambda e: e.tensor_mul(
                        out=ptl[:, (kt % 2) * 512:(kt % 2 + 1) * 512],
                        in0=ptl[:, (kt % 2) * 512:(kt % 2 + 1) * 512],
                        in1=maskb[:, qc * 8192 + kt * 512:qc * 8192 + (kt + 1) * 512]))(kt, qc),
                        deps=[ev2, evmk[qc]], reads=[bu("maskb")], writes=[bu("ptl")],
                        signal=True)
                    pav = 2 + (h % 2)
                    p.do("tensor", (lambda kt, hl, pav: lambda e: e.matmul(
                        bank(pav), vg[:, kt * 512 + hl * 128:kt * 512 + (hl + 1) * 128],
                        ptl[:, (kt % 2) * 512:(kt % 2 + 1) * 512],
                        start=(kt == 0), stop=(kt == 15)))(kt, hl, pav),
                        deps=[ev3], reads=[bu("vg"), bu("ptl")],
                        writes=[b_ps[pav]], signal=True)
                    p.do("tensor", (lambda kt: lambda e: e.matmul(
                        PS[0:1, 3584:4096], C.onebf[:],
                        ptl[:, (kt % 2) * 512:(kt % 2 + 1) * 512],
                        start=(kt == 0), stop=(kt == 15)))(kt),
                        deps=[ev3], reads=[bu("ones"), bu("ptl")],
                        writes=[b_ps[7]], signal=True)
                pav = 2 + (h % 2)
                evc = p.do("vector", lambda e: e.tensor_copy(
                    out=rb[0:1, 0:512], in_=PS[0:1, 3584:4096]),
                    reads=[b_ps[7]], writes=[bu("rb")], signal=True)
                evc = p.do("vector", lambda e: e.reciprocal(
                    out=rb[0:1, 0:512], in_=rb[0:1, 0:512]),
                    deps=[force(evc)], writes=[bu("rb")], signal=True)
                evb = p.do("tensor", lambda e: e.matmul(
                    bank(6), C.onesb[0:1, :], rb[0:1, 0:512], start=True, stop=True),
                    deps=[evc, evo], reads=[bu("rb")], writes=[b_ps[6]], signal=True)
                evb = p.do("vector", lambda e: e.tensor_copy(
                    out=rb[:, 0:512], in_=bank(6)),
                    deps=[evb], reads=[b_ps[6]], writes=[bu("rb")], signal=True)
                p.do("vector", (lambda h, qc, pav: lambda e: e.tensor_tensor(
                    out=attn_fm[:, h * TOK + qc * 512:h * TOK + (qc + 1) * 512],
                    in0=bank(pav), in1=rb[:, 0:512], op=ALU.mult))(h, qc, pav),
                    deps=[evb], reads=[b_ps[pav], bu("rb")], writes=[bu("attn")],
                    signal=True)

    p.full_barrier()
    for oc in range(4):
        evm_all = {}
        for ic in range(16):
            wsl = ic % 2
            evwo = p.do("gpsimd", (lambda oc, ic, wsl: lambda e: e.dma_start(
                out=wstage[:, wsl * 512:(wsl + 1) * 512],
                in_=wout_in[ic * 128:(ic + 1) * 128,
                            oc * 512:(oc + 1) * 512]))(oc, ic, wsl),
                writes=[bu("wstage")], dma=True)
            for t in range(NT):
                evm = p.do("tensor", (lambda ic, t, wsl: lambda e: e.matmul(
                    bank(t), attn_fm[:, ic * TOK + t * 128:ic * TOK + (t + 1) * 128],
                    wstage[:, wsl * 512:(wsl + 1) * 512],
                    start=(ic == 0), stop=(ic == 15)))(ic, t, wsl),
                    deps=[evwo], reads=[bu("wstage"), bu("attn")],
                    writes=[b_ps[t]], signal=(ic == 15 or t == NT - 1))
                if ic == 15:
                    evm_all[t] = evm
        for t in range(NT):
            p.do("vector", (lambda t, oc: lambda e: e.tensor_copy(
                out=ybf[:, t * D + oc * 512:t * D + (oc + 1) * 512],
                in_=bank(t)))(t, oc),
                deps=[evm_all[t]], reads=[b_ps[t]], writes=[bu("ybf")],
                signal=True)

    p.full_barrier()
    for t in range(NT):
        yt = ybf[:, t * D:(t + 1) * D]
        evdx = p.do("sync", (lambda t: lambda e: e.dma_start(
            out=xtb[:], in_=xo[t * 128:(t + 1) * 128, :]))(t),
            writes=[bu("xtb")], dma=True)
        er, em = ln_stats(p, yt, qtm[:, 0:512], mv, sc[:, 0:1], sc[:, 1:2],
                          lnbufs, xbuf=bu("ybf"), epsc=C.cst[:, 0:1],
                          epsbuf=bu("C.cst"))
        ev = p.do("vector", (lambda yt: lambda e: e.tensor_scalar(
            out=yw[:], in0=yt, scalar1=sc[:, 1:2], scalar2=sc[:, 0:1],
            op0=ALU.add, op1=ALU.mult))(yt),
            deps=[er, em], reads=[bu("ybf"), bu("rs")], writes=[bu("yw")],
            signal=True)
        ev = p.do("vector", lambda e: e.tensor_add(out=yw[:], in0=yw[:], in1=xtb[:]),
                  deps=[evdx, force(ev)], reads=[bu("xtb")], writes=[bu("yw")],
                  signal=True)
        p.do("sync", (lambda t: lambda e: e.dma_start(
            out=out[t * 128:(t + 1) * 128, :], in_=yw[:]))(t),
            deps=[ev], reads=[bu("yw")], dma=True)

    fin = p.barrier_event("sync")
    p.do("sync", lambda e: None, deps=[fin])



class Carver:

    def __init__(self, ap_bf16, ncols_bf16):
        self.ap = ap_bf16
        self.n = ncols_bf16
        self.off = 0

    def reset(self):
        self.off = 0

    def take(self, ncols, dtype):
        nb = ncols * (4 if dtype == F32 else (4 if dtype == mybir.dt.int32 else 2))
        start = (self.off + 31) & ~31
        self.off = start + nb
        assert self.off <= 2 * self.n, f"arena overflow {self.off} > {2 * self.n}"
        view = self.ap[:, start // 2:(start + nb) // 2]
        if dtype != BF16:
            view = view.bitcast(dtype)
        return view


class _C:
    pass


def build_fused():
    nc = bass.Bass("TRN2", target_bir_lowering=False)
    sems, _ctx = fnew(nc)
    p = Prog(nc, sems)

    T = nc.dram_tensor
    x0 = T("x0", [TOK, D], F32, kind="ExternalInput").ap()
    l0a = {k: T("l0a_" + k, sh, F32, kind="ExternalInput").ap() for k, sh in [
        ("wdkv", [D, 512]), ("wuk", [512, 2048]), ("wuv", [512, 2048]),
        ("wdq", [D, 1024]), ("wuq", [1024, 2048]), ("wqr", [1024, 1024]),
        ("wkr", [D, 64]), ("wout", [D, D]), ("delta", [32]),
        ("c0", [S, 32]), ("s0", [S, 32]), ("c0q", [TOK, 32]), ("s0q", [TOK, 32])]}
    l0f = {k: T("l0f_" + k, sh, F32, kind="ExternalInput").ap() for k, sh in [
        ("wg", [D, I0]), ("wu", [D, I0]), ("wd", [I0, D])]}
    l1a = {k: T("l1a_" + k, sh, F32, kind="ExternalInput").ap() for k, sh in [
        ("qw", [2048, 2048]), ("krw", [256, 2048]), ("dww", [256, 2048]),
        ("upk", [256, 256]), ("upv", [512, 256]), ("ow", [2048, 2048]),
        ("ckc", [S, 16]), ("cks", [S, 16]), ("cqc", [TOK, 32]), ("cqs", [TOK, 32])]}
    l1f = {k: T("l1f_" + k, sh, F32, kind="ExternalInput").ap() for k, sh in [
        ("w1", [H1, D]), ("w3", [H1, D]), ("w2", [D, H1])]}
    mask0 = T("mask0", [S, TOK], BF16, kind="ExternalInput").ap()
    mask1 = T("mask1", [S, TOK], BF16, kind="ExternalInput").ap()
    ident_in = T("ident", [128, 128], BF16, kind="ExternalInput").ap()
    ones_in = T("ones", [128, 128], F32, kind="ExternalInput").ap()
    out = T("out", [TOK, D], F32, kind="ExternalOutput").ap()


    xg = T("xg", [S, D], F32).ap()
    ccin = T("ccin", [TOK, D], F32)
    x1own = T("x1own", [TOK, D], F32).ap()
    x2own = T("x2own", [TOK, D], F32).ap()
    x3own = T("x3own", [TOK, D], F32).ap()
    u2t_dram = T("u2t", [16, NH1, 128, 128], BF16).ap()
    xgh = T("xg2", [S, D], F32) if False else None

    import contextlib
    with contextlib.ExitStack() as stk:
        arena_h = stk.enter_context(nc.sbuf_tensor("arena", [128, 105984], BF16))
        identb = stk.enter_context(nc.sbuf_tensor("identc", [128, 128], BF16))
        onesb = stk.enter_context(nc.sbuf_tensor("onesc", [128, 128], F32))
        onebf = stk.enter_context(nc.sbuf_tensor("onebfc", [128, 1], BF16))
        cstc = stk.enter_context(nc.sbuf_tensor("cstc", [128, 2], F32))
        PS = stk.enter_context(nc.psum_tensor("PSf", [128, 4096], F32))
        blk = stk.enter_context(nc.Block())

        C = _C()
        C.nc, C.p, C.PS = nc, p, PS
        C.identb, C.onesb, C.onebf, C.cst = identb[:], onesb[:], onebf[:], cstc[:]
        C.b_cst = TBuf("cstC")
        C.evi = p.do("sync", lambda e: e.dma_start(out=identb[:], in_=ident_in), dma=True)
        C.evo = p.do("sync", lambda e: e.dma_start(out=onesb[:], in_=ones_in), dma=True)
        p.do("vector", lambda e: e.memset(cstc[:, 0:1], EPS), writes=[C.b_cst], signal=True)
        p.do("vector", lambda e: e.memset(onebf[:], 1.0), writes=[C.b_cst], signal=True)

        A = Carver(arena_h[:], 105984)

        csems = [stk.enter_context(nc.semaphore(f"CSEM{i}")) for i in range(2)]
        gcount = [0]

        def gather(src_ap):
            ev = p.do("sync", (lambda src_ap: lambda e: e.dma_start(
                out=ccin[:, :], in_=src_ap))(src_ap),
                deps=[p.barrier_event("sync"), p.barrier_event("gpsimd")], dma=True)
            cs = csems[gcount[0]]
            gcount[0] += 1
            p.do("gpsimd", (lambda cs: lambda e: e.collective_compute(
                "AllGather", ALU.bypass,
                replica_groups=[[0, 1], [2, 3], [4, 5], [6, 7]],
                ins=[ccin.ap().opt()], outs=[xg.opt()]).then_inc(cs, 1))(cs),
                deps=[ev])
            for engn in Prog.ENGINES:
                p.do(engn, (lambda cs: lambda e: e.wait_ge(cs, 1))(cs))
            p.full_barrier()

        gather(x0)
        emit_l0attn(C, A, xg, x0, l0a["wdkv"], l0a["wuk"], l0a["wuv"],
                    l0a["wdq"], l0a["wuq"], l0a["wqr"], l0a["wkr"], l0a["wout"],
                    l0a["delta"], mask0, l0a["c0"], l0a["s0"], l0a["c0q"],
                    l0a["s0q"], None, None, x1own)
        p.full_barrier()
        A.reset()
        emit_l0ffn(C, A, x1own, l0f["wg"], l0f["wu"], l0f["wd"], None, x2own)
        p.full_barrier()
        A.reset()
        gather(x2own)
        emit_l1attn(C, A, xg, x2own, l1a["qw"], l1a["krw"], l1a["dww"],
                    l1a["upk"], l1a["upv"], l1a["ow"], mask1, l1a["ckc"],
                    l1a["cks"], l1a["cqc"], l1a["cqs"], None, None, x3own)
        p.full_barrier()
        A.reset()
        emit_l1ffn(C, A, x3own, l1f["w1"], l1f["w3"], l1f["w2"], None, None,
                   out, u2t_dram)
        p.full_barrier()
        p.do("sync", lambda e: None, deps=[p.barrier_event("sync")])
        p.emit(blk)
    return nc


_FUSED_NC = None


def kernel(**inp):
    global _FUSED_NC
    if _FUSED_NC is None:
        _FUSED_NC = build_fused()
    x = np.ascontiguousarray(inp["L0_attn_x"], dtype=np.float32)
    cst0 = _l0a_consts()
    cst1 = _l1a_consts()
    karange = np.arange(S)
    in_maps = []
    for c in range(NCORES):
        sq_, hf = c // 2, c % 2
        qidx = np.arange(hf * TOK, (hf + 1) * TOK)
        mask = np.ascontiguousarray(
            (karange[:, None] <= qidx[None, :]).astype(ml_dtypes.bfloat16))
        m = {
            "x0": np.ascontiguousarray(x[sq_, hf * TOK:(hf + 1) * TOK, :]),
            "l0a_wdkv": inp["L0_attn_w_dkv"], "l0a_wuk": inp["L0_attn_w_uk"],
            "l0a_wuv": inp["L0_attn_w_uv"], "l0a_wdq": inp["L0_attn_w_dq"],
            "l0a_wuq": inp["L0_attn_w_uq"], "l0a_wqr": inp["L0_attn_w_qr"],
            "l0a_wkr": inp["L0_attn_w_kr"], "l0a_wout": inp["L0_attn_w_out"],
            "l0a_delta": inp["L0_attn_delta"],
            "l0a_c0": cst0["c0"], "l0a_s0": cst0["s0"],
            "l0a_c0q": np.ascontiguousarray(cst0["c0"][qidx]),
            "l0a_s0q": np.ascontiguousarray(cst0["s0"][qidx]),
            "l0f_wg": inp["L0_ffn_w_gate"], "l0f_wu": inp["L0_ffn_w_up"],
            "l0f_wd": inp["L0_ffn_w_down"],
            "l1a_qw": inp["L1_attn_q_w"], "l1a_krw": inp["L1_attn_kr_w"],
            "l1a_dww": inp["L1_attn_down_w"], "l1a_upk": inp["L1_attn_upk_w"],
            "l1a_upv": inp["L1_attn_upv_w"], "l1a_ow": inp["L1_attn_o_w"],
            "l1a_ckc": cst1["ckc"], "l1a_cks": cst1["cks"],
            "l1a_cqc": np.ascontiguousarray(cst1["cqc"][qidx]),
            "l1a_cqs": np.ascontiguousarray(cst1["cqs"][qidx]),
            "l1f_w1": inp["L1_ffn_w1"], "l1f_w3": inp["L1_ffn_w3"],
            "l1f_w2": inp["L1_ffn_w2"],
            "mask0": mask, "mask1": mask,
            "ident": _IDENT, "ones": _ONES,
        }
        in_maps.append(m)
    res = run_bass_kernel_spmd(_FUSED_NC, in_maps, core_ids=list(range(NCORES)))
    global _LASTF
    _LASTF = res
    o = np.empty((B, S, D), dtype=np.float32)
    for c in range(NCORES):
        sq_, hf = c // 2, c % 2
        o[sq_, hf * TOK:(hf + 1) * TOK, :] = res.results[c]["out"]
    return o



# version 13
# speedup vs baseline: 1.0177x; 1.0177x over previous
import numpy as np
import ml_dtypes
import concourse.bass as bass
import concourse.mybir as mybir
from concourse.bass_utils import run_bass_kernel_spmd

F32 = mybir.dt.float32
BF16 = mybir.dt.bfloat16
AF = mybir.ActivationFunctionType
ALU = mybir.AluOpType
AX = mybir.AxisListType

NCORES = 8
B, S, D = 4, 2048, 2048
TOK = 1024
NT = TOK // 128
EPS = 1e-5



class TBuf:
    __slots__ = ("name", "w_ev", "r_evs", "unsynced", "pending")

    def __init__(self, name=""):
        self.name = name
        self.w_ev = None
        self.r_evs = []
        self.unsynced = set()
        self.pending = None


class Prog:
    ENGINES = ("sync", "scalar", "tensor", "vector", "gpsimd")

    def __init__(self, nc, sems):
        self.nc = nc
        self.sems = sems
        self.count = {e: 0 for e in self.ENGINES}
        self.hwm = {e: {d: 0 for d in self.ENGINES} for e in self.ENGINES}
        self.code = {e: [] for e in self.ENGINES}

    def do(self, eng, fn, deps=(), reads=(), writes=(), signal=False, dma=False):
        dep_list = [(d, False) for d in deps if d is not None]
        for bu in reads:
            if bu.pending is not None and bu.pending != eng:
                raise RuntimeError(f"read of {bu.name} with pending write on {bu.pending}")
            if bu.w_ev is not None:
                dep_list.append((bu.w_ev, False))
        for bu in writes:
            bad = bu.unsynced - {eng}
            if bad:
                raise RuntimeError(f"write to {bu.name} with unsynced reads on {bad}")
            if bu.w_ev is not None:
                dep_list.append((bu.w_ev, False))
            dep_list.extend((r, True) for r in bu.r_evs)
        waits = []
        for ((de, dv, ddma), war) in dep_list:
            if de == eng and war and not ddma:
                continue
            if self.hwm[eng][de] >= dv:
                continue
            self.hwm[eng][de] = dv
            waits.append((de, dv))
        inc = 16 if dma else 1
        ev = None
        if signal or dma:
            self.count[eng] += inc
            ev = (eng, self.count[eng], dma)
        self.code[eng].append((waits, fn, self.sems[eng] if ev else None,
                               inc if ev else 0))
        for bu in reads:
            if ev is not None:
                bu.r_evs.append(ev)
                bu.unsynced.discard(eng)
            else:
                bu.unsynced.add(eng)
        for bu in writes:
            if ev is None:
                if bu.pending is not None and bu.pending != eng:
                    raise RuntimeError(f"cross-engine pending write to {bu.name}")
                bu.pending = eng
            else:
                bu.w_ev = ev
                bu.r_evs = []
                bu.unsynced = set()
                bu.pending = None
        return ev

    def full_barrier(self):
        evs = {e: self.barrier_event(e) for e in self.ENGINES}
        for e in self.ENGINES:
            deps = [evs[d] for d in self.ENGINES if d != e and evs[d] is not None]
            if deps:
                self.do(e, lambda en: None, deps=deps)

    def barrier_event(self, eng):
        return (eng, self.count[eng], True) if self.count[eng] else None

    def emit(self, blk):
        def make(engname):
            def body(eng):
                for waits, fn, sem, inc in self.code[engname]:
                    for (de, dv) in waits:
                        eng.wait_ge(self.sems[de], dv)
                    inst = fn(eng)
                    if sem is not None:
                        if inst is None:
                            inst = eng.nop()
                        inst.then_inc(sem, inc)
            return body
        blk.sync(make("sync"))
        blk.scalar(make("scalar"))
        blk.tensor(make("tensor"))
        blk.vector(make("vector"))
        blk.gpsimd(make("gpsimd"))


def force(ev):
    return (ev[0], ev[1], True) if ev is not None else None


def fnew(nc):
    ctxs = []
    sems = {}
    for e in Prog.ENGINES:
        cm = nc.semaphore(f"S_{e}")
        sems[e] = cm.__enter__()
        ctxs.append(cm)
    return sems, ctxs


def ln_stats(p, xt, sqscr, mv, rstd, mneg, bufs, xbuf=None, epsc=None,
             epsbuf=None, deps=(), width=D):
    b_stats, b_mv, b_rs = bufs
    n = width // 512
    rd = [xbuf] if xbuf is not None else []
    ev_s = p.do("vector", lambda e: e.reduce_sum(out=mv[:, 0:1], in_=xt, axis=AX.X),
                deps=deps, reads=rd, writes=[b_mv], signal=True)
    ev_q = None
    for i in range(n):
        ev_q = p.do("scalar", (lambda i: lambda e: e.activation(
            sqscr[:, 0:512], xt[:, i * 512:(i + 1) * 512], AF.Square,
            accum_out=mv[:, 1 + i:2 + i]))(i),
            deps=deps, reads=rd + [b_stats], writes=[b_mv, b_stats], signal=True)
    evqs = p.do("vector", (lambda n: lambda e: e.reduce_sum(
        out=mv[:, 5:6], in_=mv[:, 1:1 + n], axis=AX.X))(n),
        deps=[ev_q], writes=[b_mv], signal=True)
    ev_m = p.do("vector", (lambda width: lambda e: e.tensor_scalar_mul(
        out=mneg, in0=mv[:, 0:1], scalar1=-1.0 / width))(width),
        deps=[force(ev_s)], writes=[b_rs], signal=True)
    ev = p.do("vector", lambda e: e.tensor_tensor(out=mv[:, 6:7], in0=mneg,
                                                  in1=mneg, op=ALU.mult),
              deps=[force(ev_m)], reads=[b_rs], writes=[b_mv], signal=True)
    ev = p.do("vector", (lambda width: lambda e: e.scalar_tensor_tensor(
        out=mv[:, 6:7], in0=mv[:, 5:6], scalar=1.0 / width, in1=mv[:, 6:7],
        op0=ALU.mult, op1=ALU.subtract))(width),
        deps=[force(ev), force(evqs)], writes=[b_mv], signal=True)
    ev1 = p.do("scalar", lambda e: e.activation(rstd, mv[:, 6:7], AF.Ln, bias=epsc),
               deps=[ev], reads=[b_mv] + ([epsbuf] if epsbuf else []),
               writes=[b_rs], signal=True)
    ev1 = p.do("scalar", lambda e: e.activation(rstd, rstd, AF.Exp, scale=-0.5),
               deps=[ev1], writes=[b_rs], signal=True)
    return ev1, ev_m



H1 = 5461
NH1 = 43
H1P = NH1 * 128
FLATC = (H1 * D) // 128


def build_l1_ffn():
    nc = bass.Bass("TRN2", target_bir_lowering=False)
    sems, _ctx = fnew(nc)

    x_in = nc.dram_tensor("x", [TOK, D], F32, kind="ExternalInput").ap()
    w1 = nc.dram_tensor("w1", [H1, D], F32, kind="ExternalInput").ap()
    w3 = nc.dram_tensor("w3", [H1, D], F32, kind="ExternalInput").ap()
    w2 = nc.dram_tensor("w2", [D, H1], F32, kind="ExternalInput").ap()
    ones_in = nc.dram_tensor("ones", [128, 128], F32, kind="ExternalInput").ap()
    ident_in = nc.dram_tensor("ident", [128, 128], BF16, kind="ExternalInput").ap()
    out = nc.dram_tensor("out", [TOK, D], F32, kind="ExternalOutput").ap()

    u2t_dram = nc.dram_tensor("u2t", [16, NH1, 128, 128], BF16).ap()

    import contextlib
    with contextlib.ExitStack() as stk:
        sb = lambda *a: stk.enter_context(nc.sbuf_tensor(*a))
        xhat = sb("xhat", [128, 16 * TOK], BF16)
        gfm = sb("gfm", [128, NH1 * TOK], BF16)
        wstg = sb("wstg", [128, 2 * D], F32)
        ubf = sb("ubf", [128, 2 * D], BF16)
        ut = sb("ut", [128, 2 * D], BF16)
        u2row = sb("u2row", [128, H1P], BF16)
        xtb = sb("xt", [128, D], F32)
        ytm = sb("ytm", [128, 4 * D], BF16)
        yw = sb("yw", [128, D], F32)
        sil = sb("sil", [128, 512], F32)
        mv = sb("mv", [128, 8], F32)
        sc = sb("sc", [128, 16], F32)
        acc = sb("acc", [128, 4], F32)
        cst = sb("cst", [128, 2], F32)
        onesb = sb("onesb", [128, 128], F32)
        identb = sb("identb", [128, 128], BF16)
        PS = stk.enter_context(nc.psum_tensor("PS", [128, 4096], F32))
        blk = stk.enter_context(nc.Block())
        p = Prog(nc, sems)
        bank = lambda i: PS[:, i * 512:(i + 1) * 512]
        bankb = lambda i: PS[:, i * 512:(i + 1) * 512].bitcast(BF16)

        b_stage = [TBuf("stg0"), TBuf("stg1")]
        b_acc = TBuf("acc"); b_sc = TBuf("sc")
        b_ubf = TBuf("ubf"); b_ut = TBuf("ut"); b_u2r = TBuf("u2r")
        b_xhat = TBuf("xhat"); b_gfm = TBuf("gfm")
        b_xt = TBuf("xt"); b_ytm = TBuf("ytm"); b_yw = TBuf("yw")
        b_sil = TBuf("sil"); b_mv = TBuf("mv")
        b_rs = TBuf("rs"); b_cst = TBuf("cst")
        lnbufs = (b_ut, b_mv, b_rs)
        b_ps = [TBuf(f"ps{i}") for i in range(8)]

        p.do("vector", lambda e: e.memset(cst[:, 0:1], EPS), writes=[b_cst], signal=True)
        evo = p.do("sync", lambda e: e.dma_start(out=onesb[:], in_=ones_in), dma=True)
        evi = p.do("sync", lambda e: e.dma_start(out=identb[:], in_=ident_in), dma=True)
        p.do("vector", lambda e: e.memset(acc[:], 0.0), writes=[b_acc], signal=True)

        for j, w in enumerate([w1, w3, w2]):
            wf = w.flatten().rearrange("(p f) -> p f", p=128)
            c0 = 0
            while c0 < FLATC:
                c1 = min(c0 + 2032, FLATC)
                slot = 0
                sb = b_stage[slot]
                st = wstg[:, slot * 2 * D: slot * 2 * D + (c1 - c0)]
                evd = p.do("sync", (lambda wf, c0, c1, st: lambda e: e.dma_start(
                    out=st, in_=wf[:, c0:c1]))(wf, c0, c1, st),
                    writes=[sb], dma=True)
                evr = p.do("vector", (lambda st, j: lambda e: e.reduce_sum(
                    out=sc[:, 11 + j:12 + j], in_=st, axis=AX.X,
                    apply_absolute_value=True))(st, j),
                    deps=[evd], reads=[sb], writes=[b_sc], signal=True)
                p.do("vector", (lambda j: lambda e: e.tensor_tensor(
                    out=acc[:, j:j + 1], in0=acc[:, j:j + 1],
                    in1=sc[:, 11 + j:12 + j], op=ALU.add))(j),
                    deps=[force(evr)], reads=[b_sc], writes=[b_acc], signal=True)
                c0 = c1
        ev = p.do("tensor", lambda e: e.matmul(PS[0:1, 0:4], onesb[:, 0:1], acc[:, 0:4],
                                               start=True, stop=True),
                  deps=[evo], reads=[b_acc], writes=[b_ps[0]], signal=True)
        ev = p.do("vector", lambda e: e.tensor_scalar_mul(
            out=sc[0:1, 0:4], in0=PS[0:1, 0:4], scalar1=1.0 / (H1 * D)),
            deps=[ev], reads=[b_ps[0]], writes=[b_sc], signal=True)
        ev = p.do("tensor", lambda e: e.matmul(PS[:, 0:4], onesb[0:1, :], sc[0:1, 0:4],
                                               start=True, stop=True),
                  deps=[ev], reads=[b_sc], writes=[b_ps[0]], signal=True)
        ev = p.do("vector", lambda e: e.tensor_scalar_mul(out=sc[:, 0:3],
                                                          in0=PS[:, 0:3], scalar1=0.5),
                  deps=[ev], reads=[b_ps[0]], writes=[b_sc], signal=True)
        for j, (cp, cn) in enumerate([(3, 4), (5, 6), (7, 8)]):
            ev = p.do("vector", (lambda j, cp: lambda e: e.tensor_scalar_mul(
                out=sc[:, cp:cp + 1], in0=PS[:, j:j + 1], scalar1=0.5))(j, cp),
                deps=[ev], reads=[b_ps[0]], writes=[b_sc], signal=True)
            ev = p.do("vector", (lambda cp, cn: lambda e: e.tensor_scalar_mul(
                out=sc[:, cn:cn + 1], in0=sc[:, cp:cp + 1], scalar1=-1.0))(cp, cn),
                writes=[b_sc], signal=True)

        pieces = [(0, 2048, 16), (2048, 4096, 16), (4096, 5461, 11)]
        for ch in range(16):
            for (p0, p1, ntile) in pieces:
                slot = 0
                sb = b_stage[slot]
                st = wstg[:, slot * 2 * D: slot * 2 * D + (p1 - p0)]
                evd = p.do("sync", (lambda ch, p0, p1, st: lambda e: e.dma_start(
                    out=st, in_=w2[ch * 128:(ch + 1) * 128, p0:p1]))(ch, p0, p1, st),
                    writes=[sb], dma=True)
                npad = ntile * 128
                evz = None
                if p1 - p0 < npad:
                    evz = p.do("vector", (lambda p0, p1, npad: lambda e: e.memset(
                        ubf[:, (p1 - p0):npad], 0.0))(p0, p1, npad),
                        writes=[b_ubf], signal=True)
                ea = p.do("scalar", (lambda st, p0, p1: lambda e: e.activation(
                    ut[:, 0:(p1 - p0)], st, AF.Sign, bias=sc[:, 8:9]))(st, p0, p1),
                    deps=[evd], reads=[sb, b_sc], writes=[b_ut], signal=True)
                eb = p.do("scalar", (lambda st, p0, p1: lambda e: e.activation(
                    ubf[:, 0:(p1 - p0)], st, AF.Sign, bias=sc[:, 7:8]))(st, p0, p1),
                    deps=[ea], reads=[sb, b_sc], writes=[b_ubf], signal=True)
                ev = p.do("vector", (lambda p0, p1: lambda e: e.tensor_tensor(
                    out=ubf[:, 0:(p1 - p0)], in0=ubf[:, 0:(p1 - p0)],
                    in1=ut[:, 0:(p1 - p0)], op=ALU.add))(p0, p1),
                    deps=[eb, evz], reads=[b_ut], writes=[b_ubf], signal=True)
                for g in range(0, ntile, 8):
                    g1 = min(g + 8, ntile)
                    pb = 4 + ((g // 8) % 2)
                    evg = None
                    for t in range(g, g1):
                        evg = p.do("tensor", (lambda t, g, pb: lambda e: e.transpose(
                            bankb(pb)[:, (t - g) * 128:(t - g + 1) * 128],
                            ubf[:, t * 128:(t + 1) * 128], identb[:]))(t, g, pb),
                            deps=[evi, ev], reads=[b_ubf], writes=[b_ps[pb]],
                            signal=(t == g1 - 1))
                    evc = p.do("vector", (lambda g, g1, pb: lambda e: e.tensor_copy(
                        out=ut[:, 0:(g1 - g) * 128],
                        in_=bankb(pb)[:, 0:(g1 - g) * 128]))(g, g1, pb),
                        deps=[evg], reads=[b_ps[pb]], writes=[b_ut], signal=True)
                    t0g = p0 // 128 + g
                    p.do("sync", (lambda ch, t0g, g, g1: lambda e: e.dma_start(
                        out=u2t_dram[ch, t0g:t0g + (g1 - g)].rearrange("t p d -> p t d"),
                        in_=ut[:, 0:(g1 - g) * 128].rearrange("p (t d) -> p t d", d=128)))(ch, t0g, g, g1),
                        deps=[evc], reads=[b_ut], dma=True)
        ev_u2done = p.barrier_event("sync")

        for t in range(NT):
            evd = p.do("sync", (lambda t: lambda e: e.dma_start(
                out=xtb[:], in_=x_in[t * 128:(t + 1) * 128, :]))(t),
                writes=[b_xt], dma=True)
            er, em = ln_stats(p, xtb[:], ut[:, 0:D], mv, sc[:, 9:10],
                              sc[:, 10:11], lnbufs, xbuf=b_xt,
                              epsc=cst[:, 0:1], epsbuf=b_cst, deps=[evd])
            ev = p.do("vector", lambda e: e.tensor_scalar(
                out=ubf[:, 0:D], in0=xtb[:], scalar1=sc[:, 10:11],
                scalar2=sc[:, 9:10], op0=ALU.add, op1=ALU.mult),
                deps=[er, em], reads=[b_xt, b_rs], writes=[b_ubf], signal=True)
            for g in range(2):
                pb = 6 + g
                evg = None
                for dc in range(g * 8, g * 8 + 8):
                    evg = p.do("tensor", (lambda dc, g, pb: lambda e: e.transpose(
                        bankb(pb)[:, (dc - g * 8) * 128:(dc - g * 8 + 1) * 128],
                        ubf[:, dc * 128:(dc + 1) * 128], identb[:]))(dc, g, pb),
                        deps=[evi, ev], reads=[b_ubf], writes=[b_ps[pb]],
                        signal=(dc == g * 8 + 7))
                for dc in range(g * 8, g * 8 + 8):
                    p.do("vector", (lambda t, g, dc, pb: lambda e: e.tensor_copy(
                        out=xhat[:, dc * TOK + t * 128:dc * TOK + (t + 1) * 128],
                        in_=bankb(pb)[:, (dc - g * 8) * 128:(dc - g * 8 + 1) * 128]))(t, g, dc, pb),
                        deps=[evg], reads=[b_ps[pb]], writes=[b_xhat],
                        signal=(dc == g * 8 + 7))

        for hc in range(NH1):
            r0, r1 = hc * 128, min((hc + 1) * 128, H1)
            nr = r1 - r0
            slot = 0
            sb = b_stage[slot]
            st1 = wstg[:, slot * 2 * D: slot * 2 * D + D]
            st3 = wstg[:, slot * 2 * D + D: slot * 2 * D + 2 * D]
            ed1 = p.do("sync", (lambda r0, r1, st1: lambda e: e.dma_start(
                out=st1[0:r1 - r0, :], in_=w1[r0:r1, :]))(r0, r1, st1),
                writes=[sb], dma=True)
            ed3 = p.do("sync", (lambda r0, r1, st3: lambda e: e.dma_start(
                out=st3[0:r1 - r0, :], in_=w3[r0:r1, :]))(r0, r1, st3),
                writes=[sb], dma=True)
            ev_u = []
            if nr < 128:
                ev_u.append(p.do("vector", lambda e: e.memset(ubf[64:128, 0:2 * D], 0.0),
                                 writes=[b_ubf], signal=True))
            for (st, evd, off, cp, cn) in [(st1, ed1, 0, 3, 4), (st3, ed3, D, 5, 6)]:
                ea = p.do("scalar", (lambda st, off, cn, nr: lambda e: e.activation(
                    ut[0:nr, off:off + D], st[0:nr, :], AF.Sign,
                    bias=sc[0:nr, cn:cn + 1]))(st, off, cn, nr),
                    deps=[evd], reads=[sb, b_sc], writes=[b_ut], signal=True)
                eb = p.do("scalar", (lambda st, off, cp, nr: lambda e: e.activation(
                    ubf[0:nr, off:off + D], st[0:nr, :], AF.Sign,
                    bias=sc[0:nr, cp:cp + 1]))(st, off, cp, nr),
                    deps=[ea], reads=[sb, b_sc], writes=[b_ubf], signal=True)
                evq = p.do("vector", (lambda off, nr: lambda e: e.tensor_tensor(
                    out=ubf[0:nr, off:off + D], in0=ubf[0:nr, off:off + D],
                    in1=ut[0:nr, off:off + D], op=ALU.add))(off, nr),
                    deps=[eb], reads=[b_ut], writes=[b_ubf], signal=True)
                ev_u.append(evq)
            for (off, pb0) in [(0, 4), (D, 5)]:
                for g in range(2):
                    pb = pb0 if g == 0 else (pb0 ^ 1) if False else pb0
                    evg = None
                    for dc in range(g * 8, g * 8 + 8):
                        evg = p.do("tensor", (lambda off, pb, dc, g: lambda e: e.transpose(
                            bankb(pb)[:, (dc - g * 8) * 128:(dc - g * 8 + 1) * 128],
                            ubf[:, off + dc * 128:off + (dc + 1) * 128],
                            identb[:]))(off, pb, dc, g),
                            deps=[evi] + ev_u, reads=[b_ubf], writes=[b_ps[pb]],
                            signal=(dc == g * 8 + 7))
                    evuc = p.do("vector", (lambda off, pb, g: lambda e: e.tensor_copy(
                        out=ut[:, off + g * 1024:off + (g + 1) * 1024],
                        in_=bankb(pb)[:, 0:1024]))(off, pb, g),
                        deps=[evg], reads=[b_ps[pb]], writes=[b_ut], signal=True)
            for tc in range(2):
                e1 = e3 = None
                for dc in range(16):
                    e1 = p.do("tensor", (lambda tc, dc: lambda e: e.matmul(
                        bank(0 + tc), ut[:, dc * 128:(dc + 1) * 128],
                        xhat[:, dc * TOK + tc * 512:dc * TOK + (tc + 1) * 512],
                        start=(dc == 0), stop=(dc == 15)))(tc, dc),
                        reads=[b_ut, b_xhat], writes=[b_ps[0 + tc]],
                        signal=(dc == 15))
                for dc in range(16):
                    e3 = p.do("tensor", (lambda tc, dc: lambda e: e.matmul(
                        bank(2 + tc), ut[:, D + dc * 128:D + (dc + 1) * 128],
                        xhat[:, dc * TOK + tc * 512:dc * TOK + (tc + 1) * 512],
                        start=(dc == 0), stop=(dc == 15)))(tc, dc),
                        reads=[b_ut, b_xhat], writes=[b_ps[2 + tc]],
                        signal=(dc == 15))
                es = p.do("scalar", (lambda tc: lambda e: e.activation(
                    sil[:], bank(0 + tc), AF.Silu, scale=sc[:, 0:1]))(tc),
                    deps=[e1], reads=[b_ps[0 + tc], b_sc], writes=[b_sil], signal=True)
                p.do("vector", (lambda hc, tc: lambda e: e.scalar_tensor_tensor(
                    out=gfm[:, hc * TOK + tc * 512:hc * TOK + (tc + 1) * 512],
                    in0=bank(2 + tc), scalar=sc[:, 1:2], in1=sil[:],
                    op0=ALU.mult, op1=ALU.mult))(hc, tc),
                    deps=[es, e3], reads=[b_ps[2 + tc], b_sil, b_sc],
                    writes=[b_gfm], signal=True)

        for tc in range(2):
            for dc in range(16):
                evd = p.do("sync", (lambda dc: lambda e: e.dma_start(
                    out=u2row[:].rearrange("p (t d) -> p t d", d=128),
                    in_=u2t_dram[dc, :].rearrange("t p d -> p t d")))(dc),
                    deps=[ev_u2done], writes=[b_u2r], dma=True)
                evm = None
                pb = dc % 2
                for hc in range(NH1):
                    evm = p.do("tensor", (lambda tc, hc, pb: lambda e: e.matmul(
                        bank(pb), u2row[:, hc * 128:(hc + 1) * 128],
                        gfm[:, hc * TOK + tc * 512:hc * TOK + (tc + 1) * 512],
                        start=(hc == 0), stop=(hc == NH1 - 1)))(tc, hc, pb),
                        deps=[evd], reads=[b_u2r, b_gfm],
                        writes=[b_ps[pb]], signal=(hc == NH1 - 1))
                ev = p.do("vector", (lambda pb: lambda e: e.tensor_scalar_mul(
                    out=sil[:].bitcast(BF16)[:, 0:512], in0=bank(pb),
                    scalar1=sc[:, 2:3]))(pb),
                    deps=[evm], reads=[b_ps[pb], b_sc], writes=[b_sil], signal=True)
                evt = None
                for q in range(4):
                    evt = p.do("tensor", (lambda q: lambda e: e.transpose(
                        bankb(6)[:, q * 128:(q + 1) * 128],
                        sil[:].bitcast(BF16)[:, q * 128:(q + 1) * 128],
                        identb[:]))(q),
                        deps=[evi, ev], reads=[b_sil], writes=[b_ps[6]],
                        signal=(q == 3))
                for q in range(4):
                    p.do("vector", (lambda dc, q: lambda e: e.tensor_copy(
                        out=ytm[:, q * D + dc * 128:q * D + (dc + 1) * 128],
                        in_=bankb(6)[:, q * 128:(q + 1) * 128]))(dc, q),
                        deps=[evt], reads=[b_ps[6]], writes=[b_ytm],
                        signal=(q == 3))
            for q in range(4):
                t = tc * 4 + q
                yt = ytm[:, q * D:(q + 1) * D]
                evd = p.do("sync", (lambda t: lambda e: e.dma_start(
                    out=xtb[:], in_=x_in[t * 128:(t + 1) * 128, :]))(t),
                    writes=[b_xt], dma=True)
                er, em = ln_stats(p, yt, ut[:, 0:D], mv, sc[:, 9:10],
                                  sc[:, 10:11], lnbufs, xbuf=b_ytm,
                                  epsc=cst[:, 0:1], epsbuf=b_cst)
                ev = p.do("vector", (lambda yt: lambda e: e.tensor_scalar(
                    out=yw[:], in0=yt, scalar1=sc[:, 10:11], scalar2=sc[:, 9:10],
                    op0=ALU.add, op1=ALU.mult))(yt),
                    deps=[er, em], reads=[b_ytm, b_rs], writes=[b_yw], signal=True)
                ev = p.do("vector", lambda e: e.tensor_add(out=yw[:], in0=yw[:], in1=xtb[:]),
                          deps=[evd, ev], reads=[b_xt], writes=[b_yw], signal=True)
                p.do("sync", (lambda t: lambda e: e.dma_start(
                    out=out[t * 128:(t + 1) * 128, :], in_=yw[:]))(t),
                    deps=[ev], reads=[b_yw], dma=True)

        fin = p.barrier_event("sync")
        p.do("sync", lambda e: None, deps=[fin])
        p.emit(blk)
    return nc



_ONES = np.ones((128, 128), dtype=np.float32)
_IDENT = np.eye(128, dtype=ml_dtypes.bfloat16)

_L1FFN_NC = None


def run_l1_ffn(x_full, w1, w3, w2):
    global _L1FFN_NC
    if _L1FFN_NC is None:
        _L1FFN_NC = build_l1_ffn()
    in_maps = []
    for c in range(NCORES):
        sq, hf = c // 2, c % 2
        xs = np.ascontiguousarray(x_full[sq, hf * TOK:(hf + 1) * TOK, :])
        in_maps.append({"x": xs, "w1": w1, "w3": w3, "w2": w2,
                        "ones": _ONES, "ident": _IDENT})
    res = run_bass_kernel_spmd(_L1FFN_NC, in_maps, core_ids=list(range(NCORES)))
    global _LAST_RES
    _LAST_RES = res
    out = np.empty((B, S, D), dtype=np.float32)
    for c in range(NCORES):
        sq, hf = c // 2, c % 2
        out[sq, hf * TOK:(hf + 1) * TOK, :] = res.results[c]["out"]
    return out




I0 = 8192
NI0 = 64


def build_l0_ffn():
    nc = bass.Bass("TRN2", target_bir_lowering=False)
    sems, _ctx = fnew(nc)

    x_in = nc.dram_tensor("x", [TOK, D], F32, kind="ExternalInput").ap()
    wg_in = nc.dram_tensor("wg", [D, I0], F32, kind="ExternalInput").ap()
    wu_in = nc.dram_tensor("wu", [D, I0], F32, kind="ExternalInput").ap()
    wd_in = nc.dram_tensor("wd", [I0, D], F32, kind="ExternalInput").ap()
    ident_in = nc.dram_tensor("ident", [128, 128], BF16, kind="ExternalInput").ap()
    out = nc.dram_tensor("out", [TOK, D], F32, kind="ExternalOutput").ap()

    import contextlib
    with contextlib.ExitStack() as stk:
        sb = lambda *a: stk.enter_context(nc.sbuf_tensor(*a))
        xhat = sb("xhat", [128, 16 * TOK], BF16)
        hfm = sb("hfm", [128, NI0 * TOK], BF16)
        arena = sb("arena", [128, 16384], BF16)
        sil = sb("sil", [128, 512], F32)
        hsb = sb("hsb", [128, 512], F32)
        hspb = sb("hsp", [128, 512], BF16)
        sp = sb("sp", [128, 1408], F32)
        mv = sb("mv", [128, 8], F32)
        sc = sb("sc", [128, 4], F32)
        cst = sb("cst", [128, 2], F32)
        identb = sb("identb", [128, 128], BF16)
        PS = stk.enter_context(nc.psum_tensor("PS", [128, 4096], F32))
        blk = stk.enter_context(nc.Block())

        p = Prog(nc, sems)
        bank = lambda i: PS[:, i * 512:(i + 1) * 512]
        bankb = lambda i: PS[:, i * 512:(i + 1) * 512].bitcast(BF16)
        arenaF = arena.bitcast(F32)
        hfmF = hfm.bitcast(F32)
        yw = hfmF[:, 0:D]
        xtb = hfmF[:, D:2 * D]
        xhatF = xhat.bitcast(F32)
        wg_v = arena[:, 0:8192].rearrange("p (ic c) -> p ic c", c=512)
        wu_v = arena[:, 8192:16384].rearrange("p (ic c) -> p ic c", c=512)

        b_xhat = TBuf("xhat"); b_hfm = TBuf("hfm")
        b_wg = TBuf("wg"); b_wu = TBuf("wu"); b_wd = [TBuf("wd0"), TBuf("wd1")]
        b_xt = TBuf("xt"); b_y = TBuf("y")
        b_sil = TBuf("sil"); b_hsb = TBuf("hsb"); b_hsp = TBuf("hsp")
        b_sp = TBuf("sp"); b_yw = TBuf("yw"); b_mv = TBuf("mv")
        b_rs = TBuf("rs"); b_cst = TBuf("cst")
        b_ps = [TBuf(f"ps{i}") for i in range(8)]
        lnbufs = (b_sil, b_mv, b_rs)

        evi = p.do("sync", lambda e: e.dma_start(out=identb[:], in_=ident_in), dma=True)
        p.do("vector", lambda e: e.memset(cst[:, 0:1], EPS), writes=[b_cst], signal=True)

        for t in range(NT):
            evd = p.do("sync", (lambda t: lambda e: e.dma_start(
                out=xtb[:], in_=x_in[t * 128:(t + 1) * 128, :]))(t),
                writes=[b_xt], dma=True)
            er, em = ln_stats(p, xtb[:], sil.bitcast(BF16), mv, sc[:, 0:1],
                              sc[:, 1:2], lnbufs, xbuf=b_xt, epsc=cst[:, 0:1],
                              epsbuf=b_cst, deps=[evd])
            ev = p.do("vector", lambda e: e.tensor_scalar(
                out=arena[:, 8192:8192 + D],
                in0=xtb[:], scalar1=sc[:, 1:2], scalar2=sc[:, 0:1],
                op0=ALU.add, op1=ALU.mult),
                deps=[er, em], reads=[b_xt, b_rs], writes=[b_wu], signal=True)
            for g in range(2):
                pb = 6 + g
                evg = None
                for dc in range(g * 8, g * 8 + 8):
                    evg = p.do("tensor", (lambda dc, g, pb: lambda e: e.transpose(
                        bankb(pb)[:, (dc - g * 8) * 128:(dc - g * 8 + 1) * 128],
                        arena[:, 8192 + dc * 128:8192 + (dc + 1) * 128],
                        identb[:]))(dc, g, pb),
                        deps=[evi, ev], reads=[b_wu], writes=[b_ps[pb]],
                        signal=(dc == g * 8 + 7))
                for dc in range(g * 8, g * 8 + 8):
                    p.do("vector", (lambda t, dc, pb, g: lambda e: e.tensor_copy(
                        out=xhat[:, dc * TOK + t * 128:dc * TOK + (t + 1) * 128],
                        in_=bankb(pb)[:, (dc - g * 8) * 128:(dc - g * 8 + 1) * 128]))(t, dc, pb, g),
                        deps=[evg], reads=[b_ps[pb]], writes=[b_xhat],
                        signal=(dc == g * 8 + 7))

        p.full_barrier()
        preln_done = [p.barrier_event(e) for e in ("vector", "tensor", "scalar")]
        for oc in range(16):
            eg = p.do("gpsimd", (lambda oc: lambda e: e.dma_start(
                out=wg_v, in_=wg_in[:, oc * 512:(oc + 1) * 512]
                .rearrange("(ic p) c -> p ic c", p=128)))(oc),
                deps=preln_done if oc == 0 else (), writes=[b_wg], dma=True)
            eu = p.do("gpsimd", (lambda oc: lambda e: e.dma_start(
                out=wu_v, in_=wu_in[:, oc * 512:(oc + 1) * 512]
                .rearrange("(ic p) c -> p ic c", p=128)))(oc),
                deps=preln_done if oc == 0 else (), writes=[b_wu], dma=True)
            for t in range(NT):
                pg, pu = (0, 1) if t % 2 == 0 else (2, 3)
                e1 = e2 = None
                for ic in range(16):
                    e1 = p.do("tensor", (lambda oc, t, ic, pg: lambda e: e.matmul(
                        bank(pg), xhat[:, ic * TOK + t * 128:ic * TOK + (t + 1) * 128],
                        wg_v[:, ic, :], start=(ic == 0), stop=(ic == 15)))(oc, t, ic, pg),
                        deps=[eg], reads=[b_wg, b_xhat], writes=[b_ps[pg]],
                        signal=(ic == 15))
                for ic in range(16):
                    e2 = p.do("tensor", (lambda oc, t, ic, pu: lambda e: e.matmul(
                        bank(pu), xhat[:, ic * TOK + t * 128:ic * TOK + (t + 1) * 128],
                        wu_v[:, ic, :], start=(ic == 0), stop=(ic == 15)))(oc, t, ic, pu),
                        deps=[eu], reads=[b_wu, b_xhat], writes=[b_ps[pu]],
                        signal=(ic == 15))
                es = p.do("scalar", (lambda pg: lambda e: e.activation(
                    sil[:], bank(pg), AF.Silu))(pg),
                    deps=[e1], reads=[b_ps[pg]], writes=[b_sil], signal=True)
                eh = p.do("vector", (lambda pu: lambda e: e.scalar_tensor_tensor(
                    out=hsb[:], in0=bank(pu), scalar=1.0, in1=sil[:],
                    op0=ALU.mult, op1=ALU.mult))(pu),
                    deps=[es, e2], reads=[b_ps[pu], b_sil], writes=[b_hsb],
                    signal=True)
                h3v = hsb[:].rearrange("p (a b) -> p a b", b=2)
                epm = p.do("vector", lambda e: e.reduce_max(
                    out=sp[:, 0:256], in_=h3v, axis=AX.X, apply_absolute_value=True),
                    deps=[force(eh)], reads=[b_hsb], writes=[b_sp], signal=True)
                epn = p.do("vector", lambda e: e.tensor_reduce(
                    out=sp[:, 256:512], in_=h3v, axis=AX.X, op=ALU.min,
                    apply_absolute_value=True),
                    deps=[force(eh)], reads=[b_hsb], writes=[b_sp], signal=True)
                em1 = p.do("vector", lambda e: e.tensor_reduce(
                    out=sp[:, 512:640], in_=sp[:, 0:256].rearrange("p (a b) -> p a b", b=2),
                    axis=AX.X, op=ALU.min),
                    deps=[force(epm)], writes=[b_sp], signal=True)
                em2 = p.do("vector", lambda e: e.reduce_max(
                    out=sp[:, 640:768], in_=sp[:, 256:512].rearrange("p (a b) -> p a b", b=2),
                    axis=AX.X),
                    deps=[force(epn)], writes=[b_sp], signal=True)
                em3 = p.do("vector", lambda e: e.tensor_max(
                    out=sp[:, 768:896], in0=sp[:, 512:640], in1=sp[:, 640:768]),
                    deps=[force(em1), force(em2)], writes=[b_sp], signal=True)
                ea = p.do("scalar", lambda e: e.activation(sp[:, 896:1408], hsb[:], AF.Abs),
                          deps=[eh], reads=[b_hsb], writes=[b_sp], signal=True)
                evj = None
                for j in range(4):
                    evj = p.do("vector", (lambda j: lambda e: e.tensor_tensor(
                        out=sp[:, 896 + j:1408:4], in0=sp[:, 896 + j:1408:4],
                        in1=sp[:, 768:896], op=ALU.is_ge))(j),
                        deps=[force(em3), force(ea)], writes=[b_sp],
                        signal=(j == 3))
                ehs = p.do("vector", lambda e: e.tensor_mul(
                    out=hspb[:], in0=hsb[:], in1=sp[:, 896:1408]),
                    deps=[force(evj)], reads=[b_hsb, b_sp], writes=[b_hsp],
                    signal=True)
                pb = 6 + t % 2
                evt = None
                for q in range(4):
                    evt = p.do("tensor", (lambda q, pb: lambda e: e.transpose(
                        bankb(pb)[:, q * 128:(q + 1) * 128],
                        hspb[:, q * 128:(q + 1) * 128], identb[:]))(q, pb),
                        deps=[evi, ehs], reads=[b_hsp], writes=[b_ps[pb]],
                        signal=(q == 3))
                for q in range(4):
                    p.do("vector", (lambda oc, t, q, pb: lambda e: e.tensor_copy(
                        out=hfm[:, (4 * oc + q) * TOK + t * 128:(4 * oc + q) * TOK + (t + 1) * 128],
                        in_=bankb(pb)[:, q * 128:(q + 1) * 128]))(oc, t, q, pb),
                        deps=[evt], reads=[b_ps[pb]], writes=[b_hfm],
                        signal=(q == 3))

        p.full_barrier()
        ph1_done = [p.barrier_event(e) for e in ("vector", "tensor", "scalar", "gpsimd")]
        for dc in range(4):
            evm_all = []
            for hc in range(NI0):
                slot = hc % 2
                wd_s = sil.bitcast(BF16)[:, slot * 512:(slot + 1) * 512]
                evd = p.do("gpsimd", (lambda hc, dc, wd_s: lambda e: e.dma_start(
                    out=wd_s, in_=wd_in[hc * 128:(hc + 1) * 128,
                                        dc * 512:(dc + 1) * 512]))(hc, dc, wd_s),
                    deps=ph1_done if (dc == 0 and hc < 2) else (),
                    writes=[b_wd[slot]], dma=True)
                for t in range(NT):
                    evm = p.do("tensor", (lambda hc, t, wd_s: lambda e: e.matmul(
                        bank(t), hfm[:, hc * TOK + t * 128:hc * TOK + (t + 1) * 128],
                        wd_s, start=(hc == 0), stop=(hc == NI0 - 1)))(hc, t, wd_s),
                        deps=[evd], reads=[b_wd[slot], b_hfm],
                        writes=[b_ps[t]], signal=(hc == NI0 - 1 or t == NT - 1))
                    if hc == NI0 - 1:
                        evm_all.append(evm)
            for t in range(NT):
                ytgt = xhatF if t < 4 else arenaF
                p.do("vector", (lambda t, dc, ytgt: lambda e: e.tensor_copy(
                    out=ytgt[:, (t % 4) * D + dc * 512:(t % 4) * D + dc * 512 + 512],
                    in_=bank(t)))(t, dc, ytgt),
                    deps=[evm_all[t]], reads=[b_ps[t]], writes=[b_y], signal=True)

        p.full_barrier()
        ph2_done = [p.barrier_event("tensor"), p.barrier_event("vector"),
                    p.barrier_event("gpsimd")]
        for t in range(NT):
            ytgt = xhatF if t < 4 else arenaF
            yt = ytgt[:, (t % 4) * D:(t % 4 + 1) * D]
            evd = p.do("sync", (lambda t: lambda e: e.dma_start(
                out=xtb[:], in_=x_in[t * 128:(t + 1) * 128, :]))(t),
                deps=ph2_done if t == 0 else (), writes=[b_xt], dma=True)
            er, em = ln_stats(p, yt, sil.bitcast(BF16), mv, sc[:, 0:1],
                              sc[:, 1:2], lnbufs, xbuf=b_y, epsc=cst[:, 0:1],
                              epsbuf=b_cst, deps=ph2_done if t == 0 else ())
            ev = p.do("vector", (lambda yt: lambda e: e.tensor_scalar(
                out=yw[:], in0=yt, scalar1=sc[:, 1:2], scalar2=sc[:, 0:1],
                op0=ALU.add, op1=ALU.mult))(yt),
                deps=[er, em], reads=[b_y, b_rs], writes=[b_yw], signal=True)
            ev = p.do("vector", lambda e: e.tensor_add(out=yw[:], in0=yw[:], in1=xtb[:]),
                      deps=[evd, force(ev)], reads=[b_xt], writes=[b_yw], signal=True)
            p.do("sync", (lambda t: lambda e: e.dma_start(
                out=out[t * 128:(t + 1) * 128, :], in_=yw[:]))(t),
                deps=[ev], reads=[b_yw], dma=True)

        fin = p.barrier_event("sync")
        p.do("sync", lambda e: None, deps=[fin])
        p.emit(blk)
    return nc


_L0FFN_NC = None


def run_l0_ffn(x_full, wg, wu, wd):
    global _L0FFN_NC
    if _L0FFN_NC is None:
        _L0FFN_NC = build_l0_ffn()
    in_maps = []
    for c in range(NCORES):
        sq_, hf = c // 2, c % 2
        xs = np.ascontiguousarray(x_full[sq_, hf * TOK:(hf + 1) * TOK, :])
        in_maps.append({"x": xs, "wg": wg, "wu": wu, "wd": wd, "ident": _IDENT})
    res = run_bass_kernel_spmd(_L0FFN_NC, in_maps, core_ids=list(range(NCORES)))
    global _L0RES
    _L0RES = res
    out = np.empty((B, S, D), dtype=np.float32)
    for c in range(NCORES):
        sq_, hf = c // 2, c % 2
        out[sq_, hf * TOK:(hf + 1) * TOK, :] = res.results[c]["out"]
    return out



NH_L1, NKV_L1, HD_L1 = 32, 8, 64

def build_l1_attn():
    nc = bass.Bass("TRN2", target_bir_lowering=False)
    sems, _ctx = fnew(nc)

    x_in = nc.dram_tensor("x", [S, D], F32, kind="ExternalInput").ap()
    qw_in = nc.dram_tensor("qw", [2048, 2048], F32, kind="ExternalInput").ap()
    krw_in = nc.dram_tensor("krw", [256, 2048], F32, kind="ExternalInput").ap()
    dww_in = nc.dram_tensor("dww", [256, 2048], F32, kind="ExternalInput").ap()
    upk_in = nc.dram_tensor("upk", [256, 256], F32, kind="ExternalInput").ap()
    upv_in = nc.dram_tensor("upv", [512, 256], F32, kind="ExternalInput").ap()
    ow_in = nc.dram_tensor("ow", [2048, 2048], F32, kind="ExternalInput").ap()
    mask_in = nc.dram_tensor("mask", [S, TOK], BF16, kind="ExternalInput").ap()
    ckc_in = nc.dram_tensor("ckc", [S, 16], F32, kind="ExternalInput").ap()
    cks_in = nc.dram_tensor("cks", [S, 16], F32, kind="ExternalInput").ap()
    cqc_in = nc.dram_tensor("cqc", [TOK, 32], F32, kind="ExternalInput").ap()
    cqs_in = nc.dram_tensor("cqs", [TOK, 32], F32, kind="ExternalInput").ap()
    ident_in = nc.dram_tensor("ident", [128, 128], BF16, kind="ExternalInput").ap()
    ones_in = nc.dram_tensor("ones", [128, 128], F32, kind="ExternalInput").ap()
    out = nc.dram_tensor("out", [TOK, D], F32, kind="ExternalOutput").ap()

    import contextlib
    with contextlib.ExitStack() as stk:
        sb = lambda *a: stk.enter_context(nc.sbuf_tensor(*a))
        xhat = sb("xhat", [128, 16 * S], BF16)
        vbuf = sb("vbuf", [128, 16 * 520], BF16)
        ckv = sb("ckv", [128, 2 * S], BF16)
        krf = sb("krf", [128, 2 * S], BF16)
        kcf = sb("kcf", [128, 2 * S], BF16)
        qro = sb("qro", [128, 8 * TOK], BF16)
        qno = sb("qno", [128, 8 * TOK], BF16)
        maskb = sb("maskb", [128, 16 * 512], BF16)
        wstage = sb("wstage", [128, 8192], BF16)
        piece = sb("piece", [128, 1024], BF16)
        krT = sb("krT", [128, 16 * 256], BF16)
        dwT = sb("dwT", [128, 16 * 256], BF16)
        upkT = sb("upkT", [128, 2 * 256], BF16)
        upvT = sb("upvT", [128, 2 * 512], BF16)
        ckc = sb("ckc_s", [128, 16 * 16], F32)
        cks = sb("cks_s", [128, 16 * 16], F32)
        cqc = sb("cqc_s", [128, 8 * 32], F32)
        cqs = sb("cqs_s", [128, 8 * 32], F32)
        ptl = sb("ptl", [128, 2 * 512], BF16)
        qtm = sb("qtm", [128, 512], BF16)
        rsc = sb("rsc", [128, 512], F32)
        rb = sb("rb", [128, 512], F32)
        mv = sb("mv", [128, 8], F32)
        sc = sb("sc", [128, 4], F32)
        cst = sb("cst", [128, 2], F32)
        identb = sb("identb", [128, 128], BF16)
        onesb = sb("onesb", [128, 128], F32)
        PS = stk.enter_context(nc.psum_tensor("PS", [128, 4096], F32))
        blk = stk.enter_context(nc.Block())

        p = Prog(nc, sems)
        bank = lambda i: PS[:, i * 512:(i + 1) * 512]
        bankb = lambda i: PS[:, i * 512:(i + 1) * 512].bitcast(BF16)
        attn_fm = xhat[:, 0:16 * TOK]
        ybf = xhat[:, 16 * TOK:32 * TOK]
        qroF = qro.bitcast(F32)
        xtb = qroF[:, 0:D]
        yw = qroF[:, D:2 * D]

        names = ["xhat", "vbuf", "ckv", "krf", "kcf", "qro", "qno", "maskb",
                 "wstage", "piece", "krT", "dwT", "upkT", "upvT", "crep",
                 "ptl", "qtm", "rsc", "xtb", "yw", "lrow", "rb", "mv", "rs",
                 "cst", "attn", "ybf"]
        bufs = {n: TBuf(n) for n in names}
        b_ps = [TBuf(f"ps{i}") for i in range(8)]
        bu = lambda n: bufs[n]
        lnbufs = (bu("rsc"), bu("mv"), bu("rs"))

        evi = p.do("sync", lambda e: e.dma_start(out=identb[:], in_=ident_in), dma=True)
        evo = p.do("sync", lambda e: e.dma_start(out=onesb[:], in_=ones_in), dma=True)
        p.do("vector", lambda e: e.memset(cst[:, 0:1], EPS), writes=[bu("cst")], signal=True)
        p.do("vector", lambda e: e.memset(vbuf[:], 1.0), writes=[bu("vbuf")], signal=True)
        for (t_in, t_sb, nj, ntt) in [(ckc_in, ckc, 16, 16), (cks_in, cks, 16, 16),
                                      (cqc_in, cqc, 32, 8), (cqs_in, cqs, 32, 8)]:
            p.do("sync", (lambda t_in, t_sb, nj, ntt: lambda e: e.dma_start(
                out=t_sb[:].rearrange("p (tt j) -> p tt j", tt=ntt),
                in_=t_in.rearrange("(tt p) j -> p tt j", p=128)))(t_in, t_sb, nj, ntt),
                writes=[bu("crep")], dma=True)

        def prep_T(w_in, dst, nout, nin, dstbuf):
            for r in range(nout // 128):
                evd = p.do("gpsimd", (lambda w_in, r, nin: lambda e: e.dma_start(
                    out=wstage[:, 0:nin], in_=w_in[r * 128:(r + 1) * 128, :]))(w_in, r, nin),
                    writes=[bu("wstage")], dma=True)
                for g in range(0, nin // 128, 8):
                    g1 = min(g + 8, nin // 128)
                    pb = (g // 8) % 2
                    evg = None
                    for ic in range(g, g1):
                        evg = p.do("tensor", (lambda ic, g, pb: lambda e: e.transpose(
                            bankb(pb)[:, (ic - g) * 128:(ic - g + 1) * 128],
                            wstage[:, ic * 128:(ic + 1) * 128], identb[:]))(ic, g, pb),
                            deps=[evi, evd], reads=[bu("wstage")], writes=[b_ps[pb]],
                            signal=(ic == g1 - 1))
                    for ic in range(g, g1):
                        p.do("vector", (lambda ic, g, pb, r, nout: lambda e: e.tensor_copy(
                            out=dst[:, ic * nout + r * 128:ic * nout + (r + 1) * 128],
                            in_=bankb(pb)[:, (ic - g) * 128:(ic - g + 1) * 128]))(ic, g, pb, r, nout),
                            deps=[evg], reads=[b_ps[pb]], writes=[dstbuf],
                            signal=(ic == g1 - 1))
        prep_T(krw_in, krT, 256, 2048, bu("krT"))
        prep_T(dww_in, dwT, 256, 2048, bu("dwT"))
        prep_T(upk_in, upkT, 256, 256, bu("upkT"))
        prep_T(upv_in, upvT, 512, 256, bu("upvT"))

        for t in range(16):
            evd = p.do("sync", (lambda t: lambda e: e.dma_start(
                out=xtb[:], in_=x_in[t * 128:(t + 1) * 128, :]))(t),
                writes=[bu("xtb")], dma=True)
            er, em = ln_stats(p, xtb[:], rsc[:].bitcast(BF16)[:, 0:512], mv,
                              sc[:, 0:1], sc[:, 1:2], lnbufs, xbuf=bu("xtb"),
                              epsc=cst[:, 0:1], epsbuf=bu("cst"), deps=[evd])
            ev = p.do("vector", lambda e: e.tensor_scalar(
                out=wstage[:, 4096:4096 + D], in0=xtb[:], scalar1=sc[:, 1:2],
                scalar2=sc[:, 0:1], op0=ALU.add, op1=ALU.mult),
                deps=[er, em], reads=[bu("xtb"), bu("rs")], writes=[bu("wstage")],
                signal=True)
            for g in range(2):
                pb = 6 + g
                evg = None
                for dc in range(g * 8, g * 8 + 8):
                    evg = p.do("tensor", (lambda dc, g, pb: lambda e: e.transpose(
                        bankb(pb)[:, (dc - g * 8) * 128:(dc - g * 8 + 1) * 128],
                        wstage[:, 4096 + dc * 128:4096 + (dc + 1) * 128],
                        identb[:]))(dc, g, pb),
                        deps=[evi, ev], reads=[bu("wstage")], writes=[b_ps[pb]],
                        signal=(dc == g * 8 + 7))
                for dc in range(g * 8, g * 8 + 8):
                    p.do("vector", (lambda t, dc, pb, g: lambda e: e.tensor_copy(
                        out=xhat[:, dc * S + t * 128:dc * S + (t + 1) * 128],
                        in_=bankb(pb)[:, (dc - g * 8) * 128:(dc - g * 8 + 1) * 128]))(t, dc, pb, g),
                        deps=[evg], reads=[b_ps[pb]], writes=[bu("xhat")],
                        signal=(dc == g * 8 + 7))

        p.full_barrier()
        for m in range(2):
            for ts in range(4):
                evm = None
                for ic in range(16):
                    evm = p.do("tensor", (lambda m, ts, ic: lambda e: e.matmul(
                        bank(m), dwT[:, ic * 256 + m * 128:ic * 256 + (m + 1) * 128],
                        xhat[:, ic * S + ts * 512:ic * S + (ts + 1) * 512],
                        start=(ic == 0), stop=(ic == 15)))(m, ts, ic),
                        reads=[bu("dwT"), bu("xhat")], writes=[b_ps[m]],
                        signal=(ic == 15))
                p.do("vector", (lambda m, ts: lambda e: e.tensor_copy(
                    out=ckv[:, m * S + ts * 512:m * S + (ts + 1) * 512],
                    in_=bank(m)))(m, ts),
                    deps=[evm], reads=[b_ps[m]], writes=[bu("ckv")], signal=True)

        from concourse.bass import broadcast_tensor_aps
        for t in range(16):
            evm = None
            for ic in range(16):
                evm = p.do("tensor", (lambda t, ic: lambda e: e.matmul(
                    PS[:, 2048:2304], xhat[:, ic * S + t * 128:ic * S + (t + 1) * 128],
                    krT[:, ic * 256:(ic + 1) * 256],
                    start=(ic == 0), stop=(ic == 15)))(t, ic),
                    reads=[bu("krT"), bu("xhat")], writes=[b_ps[4]],
                    signal=(ic == 15))
            kr_ps = PS[:, 2048:2304].rearrange("p (h two j) -> p h two j", two=2, j=16)
            av, bv = kr_ps[:, :, 0, :], kr_ps[:, :, 1, :]
            ct = ckc[:, t * 16:(t + 1) * 16].rearrange("p (h j) -> p h j", h=1)
            st = cks[:, t * 16:(t + 1) * 16].rearrange("p (h j) -> p h j", h=1)
            _, ctb = broadcast_tensor_aps(av, ct)
            _, stb = broadcast_tensor_aps(av, st)
            qv = qtm[:, 0:256].rearrange("p (h two j) -> p h two j", two=2, j=16)
            x1o, x2o = qv[:, :, 0, :], qv[:, :, 1, :]
            wsv = rsc[:, 0:256].rearrange("p (h two j) -> p h two j", two=2, j=16)
            t1v, t2v = wsv[:, :, 0, :], wsv[:, :, 1, :]
            e1 = p.do("vector", (lambda av, ctb, x1o: lambda e: e.tensor_tensor(
                out=x1o, in0=av, in1=ctb, op=ALU.mult))(av, ctb, x1o),
                deps=[evm], reads=[bu("crep"), b_ps[4]], writes=[bu("qtm")], signal=True)
            e1 = p.do("vector", (lambda av, stb, x2o: lambda e: e.tensor_tensor(
                out=x2o, in0=av, in1=stb, op=ALU.mult))(av, stb, x2o),
                reads=[bu("crep"), b_ps[4]], writes=[bu("qtm")], signal=True)
            e2 = p.do("vector", (lambda bv, stb, t1v: lambda e: e.tensor_tensor(
                out=t1v, in0=bv, in1=stb, op=ALU.mult))(bv, stb, t1v),
                reads=[bu("crep"), b_ps[4]], writes=[bu("rsc")], signal=True)
            e2 = p.do("vector", (lambda bv, ctb, t2v: lambda e: e.tensor_tensor(
                out=t2v, in0=bv, in1=ctb, op=ALU.mult))(bv, ctb, t2v),
                reads=[bu("crep"), b_ps[4]], writes=[bu("rsc")], signal=True)
            e3 = p.do("vector", (lambda x1o, t1v: lambda e: e.tensor_tensor(
                out=x1o, in0=x1o, in1=t1v, op=ALU.subtract))(x1o, t1v),
                deps=[force(e1), force(e2)], reads=[bu("rsc")], writes=[bu("qtm")], signal=True)
            e3 = p.do("vector", (lambda x2o, t2v: lambda e: e.tensor_tensor(
                out=x2o, in0=x2o, in1=t2v, op=ALU.add))(x2o, t2v),
                deps=[force(e3)], reads=[bu("rsc")], writes=[bu("qtm")], signal=True)
            evt = None
            for cc2 in range(2):
                evt = p.do("tensor", (lambda cc2: lambda e: e.transpose(
                    bankb(5)[:, cc2 * 128:(cc2 + 1) * 128],
                    qtm[:, cc2 * 128:(cc2 + 1) * 128], identb[:]))(cc2),
                    deps=[evi, e3], reads=[bu("qtm")], writes=[b_ps[5]],
                    signal=(cc2 == 1))
            for cc2 in range(2):
                p.do("vector", (lambda t, cc2: lambda e: e.tensor_copy(
                    out=krf[:, cc2 * S + t * 128:cc2 * S + (t + 1) * 128],
                    in_=bankb(5)[:, cc2 * 128:(cc2 + 1) * 128]))(t, cc2),
                    deps=[evt], reads=[b_ps[5]], writes=[bu("krf")],
                    signal=(cc2 == 1))

        for m in range(2):
            for ts in range(4):
                evm = None
                for ic in range(2):
                    evm = p.do("tensor", (lambda m, ts, ic: lambda e: e.matmul(
                        bank(2 + m), upkT[:, ic * 256 + m * 128:ic * 256 + (m + 1) * 128],
                        ckv[:, ic * S + ts * 512:ic * S + (ts + 1) * 512],
                        start=(ic == 0), stop=(ic == 1)))(m, ts, ic),
                        reads=[bu("upkT"), bu("ckv")], writes=[b_ps[2 + m]],
                        signal=(ic == 1))
                p.do("vector", (lambda m, ts: lambda e: e.tensor_copy(
                    out=kcf[:, m * S + ts * 512:m * S + (ts + 1) * 512],
                    in_=bank(2 + m)))(m, ts),
                    deps=[evm], reads=[b_ps[2 + m]], writes=[bu("kcf")], signal=True)

        for kt in range(16):
            evm = None
            for ic in range(2):
                evm = p.do("tensor", (lambda kt, ic: lambda e: e.matmul(
                    bank(6), ckv[:, ic * S + kt * 128:ic * S + (kt + 1) * 128],
                    upvT[:, ic * 512:(ic + 1) * 512],
                    start=(ic == 0), stop=(ic == 1)))(kt, ic),
                    reads=[bu("upvT"), bu("ckv")], writes=[b_ps[6]],
                    signal=(ic == 1))
            p.do("vector", (lambda kt: lambda e: e.tensor_copy(
                out=vbuf[:, kt * 520:(kt + 1) * 520]
                .rearrange("p (h e) -> p h e", e=65)[:, :, 0:64],
                in_=bank(6).rearrange("p (h d) -> p h d", d=64)))(kt),
                deps=[evm], reads=[b_ps[6]], writes=[bu("vbuf")], signal=True)

        for oc in range(4):
            evd = p.do("gpsimd", (lambda oc: lambda e: e.dma_start(
                out=wstage[:, 0:8192].rearrange("p (r c) -> p r c", r=4),
                in_=qw_in[oc * 512:(oc + 1) * 512, :]
                .rearrange("(r p) c -> p r c", p=128)))(oc),
                writes=[bu("wstage")], dma=True)
            for tg in range(2):
                for ic in range(16):
                    pcb = ic % 2
                    evt = None
                    for r in range(4):
                        evt = p.do("tensor", (lambda ic, r, pcb: lambda e: e.transpose(
                            bankb(4 + pcb)[:, r * 128:(r + 1) * 128],
                            wstage[:, r * 2048 + ic * 128:r * 2048 + (ic + 1) * 128],
                            identb[:]))(ic, r, pcb),
                            deps=[evi, evd], reads=[bu("wstage")], writes=[b_ps[4 + pcb]],
                            signal=(r == 3))
                    evp = p.do("vector", (lambda pcb: lambda e: e.tensor_copy(
                        out=piece[:, pcb * 512:(pcb + 1) * 512],
                        in_=bankb(4 + pcb)[:, 0:512]))(pcb),
                        deps=[evt], reads=[b_ps[4 + pcb]], writes=[bu("piece")], signal=True)
                    for tl in range(4):
                        t = tg * 4 + tl
                        p.do("tensor", (lambda ic, t, tl, pcb: lambda e: e.matmul(
                            bank(tl), xhat[:, ic * S + t * 128:ic * S + (t + 1) * 128],
                            piece[:, pcb * 512:(pcb + 1) * 512],
                            start=(ic == 0), stop=(ic == 15)))(ic, t, tl, pcb),
                            deps=[evp], reads=[bu("piece"), bu("xhat")],
                            writes=[b_ps[tl]], signal=(ic == 15 or tl == 3))
                for tl in range(4):
                    t = tg * 4 + tl
                    qp = bank(tl).rearrange("p (h two j) -> p h two j", two=2, j=32)
                    av, bv = qp[:, :, 0, :], qp[:, :, 1, :]
                    ct = cqc[:, t * 32:(t + 1) * 32].rearrange("p (h j) -> p h j", h=1)
                    st = cqs[:, t * 32:(t + 1) * 32].rearrange("p (h j) -> p h j", h=1)
                    _, ctb = broadcast_tensor_aps(av, ct)
                    _, stb = broadcast_tensor_aps(av, st)
                    wsv = rsc[:].rearrange("p (h two j) -> p h two j", two=2, j=32)
                    t1v, t2v = wsv[:, :, 0, :], wsv[:, :, 1, :]
                    e1 = p.do("vector", (lambda av, ctb, t1v: lambda e: e.tensor_tensor(
                        out=t1v, in0=av, in1=ctb, op=ALU.mult))(av, ctb, t1v),
                        reads=[bu("crep"), b_ps[tl]], writes=[bu("rsc")], signal=True)
                    e1 = p.do("vector", (lambda bv, stb, t2v: lambda e: e.tensor_tensor(
                        out=t2v, in0=bv, in1=stb, op=ALU.mult))(bv, stb, t2v),
                        reads=[bu("crep"), b_ps[tl]], writes=[bu("rsc")], signal=True)
                    qv = qtm[:].rearrange("p (h q4 j) -> p h q4 j", q4=4, j=16)
                    ev1 = p.do("vector", (lambda t1v, t2v, qv: lambda e: e.tensor_tensor(
                        out=qv[:, :, 0, :],
                        in0=t1v.rearrange("p h (s j) -> p h s j", s=2)[:, :, 0, :],
                        in1=t2v.rearrange("p h (s j) -> p h s j", s=2)[:, :, 0, :],
                        op=ALU.subtract))(t1v, t2v, qv),
                        deps=[force(e1)], reads=[bu("rsc")], writes=[bu("qtm")], signal=True)
                    ev1 = p.do("vector", (lambda t1v, t2v, qv: lambda e: e.tensor_tensor(
                        out=qv[:, :, 2, :],
                        in0=t1v.rearrange("p h (s j) -> p h s j", s=2)[:, :, 1, :],
                        in1=t2v.rearrange("p h (s j) -> p h s j", s=2)[:, :, 1, :],
                        op=ALU.subtract))(t1v, t2v, qv),
                        reads=[bu("rsc")], writes=[bu("qtm")], signal=True)
                    e2 = p.do("vector", (lambda av, stb, t1v: lambda e: e.tensor_tensor(
                        out=t1v, in0=av, in1=stb, op=ALU.mult))(av, stb, t1v),
                        deps=[force(ev1)], reads=[bu("crep"), b_ps[tl]],
                        writes=[bu("rsc")], signal=True)
                    e2 = p.do("vector", (lambda bv, ctb, t2v: lambda e: e.tensor_tensor(
                        out=t2v, in0=bv, in1=ctb, op=ALU.mult))(bv, ctb, t2v),
                        reads=[bu("crep"), b_ps[tl]], writes=[bu("rsc")], signal=True)
                    ev2 = p.do("vector", (lambda t1v, t2v, qv: lambda e: e.tensor_tensor(
                        out=qv[:, :, 1, :],
                        in0=t1v.rearrange("p h (s j) -> p h s j", s=2)[:, :, 0, :],
                        in1=t2v.rearrange("p h (s j) -> p h s j", s=2)[:, :, 0, :],
                        op=ALU.add))(t1v, t2v, qv),
                        deps=[force(e2)], reads=[bu("rsc")], writes=[bu("qtm")], signal=True)
                    ev2 = p.do("vector", (lambda t1v, t2v, qv: lambda e: e.tensor_tensor(
                        out=qv[:, :, 3, :],
                        in0=t1v.rearrange("p h (s j) -> p h s j", s=2)[:, :, 1, :],
                        in1=t2v.rearrange("p h (s j) -> p h s j", s=2)[:, :, 1, :],
                        op=ALU.add))(t1v, t2v, qv),
                        reads=[bu("rsc")], writes=[bu("qtm")], signal=True)
                    evt = None
                    for r4 in range(4):
                        evt = p.do("tensor", (lambda r4: lambda e: e.transpose(
                            bankb(7)[:, r4 * 128:(r4 + 1) * 128],
                            qtm[:, r4 * 128:(r4 + 1) * 128], identb[:]))(r4),
                            deps=[evi, force(ev1), force(ev2)], reads=[bu("qtm")],
                            writes=[b_ps[7]], signal=(r4 == 3))
                    for hh in range(8):
                        qh = oc * 8 + hh
                        h = qh // 4
                        rbase = (h % 4) * 32
                        cblk = (h // 4) * 4 + (qh % 4)
                        prow = (hh % 2) * 64
                        pcol = (hh // 2) * 128
                        p.do("vector", (lambda prow, pcol, rbase, cblk, t: lambda e: e.tensor_copy(
                            out=qro[rbase:rbase + 32, cblk * TOK + t * 128:cblk * TOK + (t + 1) * 128],
                            in_=bankb(7)[prow:prow + 32, pcol:pcol + 128]))(prow, pcol, rbase, cblk, t),
                            deps=[evt], reads=[b_ps[7]], writes=[bu("qro")],
                            signal=(hh == 7))
                        p.do("vector", (lambda prow, pcol, rbase, cblk, t: lambda e: e.tensor_copy(
                            out=qno[rbase:rbase + 32, cblk * TOK + t * 128:cblk * TOK + (t + 1) * 128],
                            in_=bankb(7)[prow + 32:prow + 64, pcol:pcol + 128]))(prow, pcol, rbase, cblk, t),
                            deps=[evt], reads=[b_ps[7]], writes=[bu("qno")],
                            signal=(hh == 7))

        p.full_barrier()
        SCL = float(HD_L1) ** -0.5
        for qc in range(2):
            evmk = p.do("sync", (lambda qc: lambda e: e.dma_start(
                out=maskb[:].rearrange("p (tt c) -> p tt c", tt=16),
                in_=mask_in[:, qc * 512:(qc + 1) * 512]
                .rearrange("(tt p) c -> p tt c", p=128)))(qc),
                writes=[bu("maskb")], dma=True)
            for qh in range(NH_L1):
                h = qh // 4
                rbase = (h % 4) * 32
                cc = h // 4
                cblk = (h // 4) * 4 + (qh % 4)
                pav = 2 + (qh % 2)
                for kt in range(16):
                    pbs = kt % 2
                    tp = (rbase, 0) if rbase == 96 else None
                    ev1 = p.do("tensor", (lambda rbase, cc, cblk, kt, qc, pbs, tp: lambda e: e.matmul(
                        bank(pbs), krf[rbase:rbase + 32, cc * S + kt * 128:cc * S + (kt + 1) * 128],
                        qro[rbase:rbase + 32, cblk * TOK + qc * 512:cblk * TOK + (qc + 1) * 512],
                        start=True, stop=False, tile_position=tp))(rbase, cc, cblk, kt, qc, pbs, tp),
                        reads=[bu("krf"), bu("qro")], writes=[b_ps[pbs]])
                    ev1 = p.do("tensor", (lambda rbase, cc, cblk, kt, qc, pbs, tp: lambda e: e.matmul(
                        bank(pbs), kcf[rbase:rbase + 32, cc * S + kt * 128:cc * S + (kt + 1) * 128],
                        qno[rbase:rbase + 32, cblk * TOK + qc * 512:cblk * TOK + (qc + 1) * 512],
                        start=False, stop=True, tile_position=tp))(rbase, cc, cblk, kt, qc, pbs, tp),
                        reads=[bu("kcf"), bu("qno")], writes=[b_ps[pbs]], signal=True)
                    ev2 = p.do("scalar", (lambda pbs, kt: lambda e: e.activation(
                        ptl[:, (kt % 2) * 512:(kt % 2 + 1) * 512], bank(pbs),
                        AF.Exp, scale=SCL))(pbs, kt),
                        deps=[ev1], reads=[b_ps[pbs]], writes=[bu("ptl")], signal=True)
                    ev3 = p.do("vector", (lambda kt: lambda e: e.tensor_mul(
                        out=ptl[:, (kt % 2) * 512:(kt % 2 + 1) * 512],
                        in0=ptl[:, (kt % 2) * 512:(kt % 2 + 1) * 512],
                        in1=maskb[:, kt * 512:(kt + 1) * 512]))(kt),
                        deps=[ev2, evmk], reads=[bu("maskb")], writes=[bu("ptl")],
                        signal=True)
                    p.do("tensor", (lambda kt, h, pav: lambda e: e.matmul(
                        PS[0:65, pav * 512:(pav + 1) * 512],
                        vbuf[:, kt * 520 + h * 65:kt * 520 + (h + 1) * 65],
                        ptl[:, (kt % 2) * 512:(kt % 2 + 1) * 512],
                        start=(kt == 0), stop=(kt == 15)))(kt, h, pav),
                        deps=[ev3], reads=[bu("vbuf"), bu("ptl")],
                        writes=[b_ps[pav]], signal=True)
                evc = p.do("vector", (lambda pav: lambda e: e.tensor_copy(
                    out=rb[0:1, 0:512], in_=PS[64:65, pav * 512:(pav + 1) * 512]))(pav),
                    reads=[b_ps[pav]], writes=[bu("rb")], signal=True)
                evc = p.do("vector", lambda e: e.reciprocal(out=rb[0:1, 0:512], in_=rb[0:1, 0:512]),
                           deps=[force(evc)], writes=[bu("rb")], signal=True)
                evb = p.do("tensor", lambda e: e.matmul(
                    bank(6)[0:64, :], onesb[0:1, 0:64], rb[0:1, 0:512],
                    start=True, stop=True),
                    deps=[evc, evo], reads=[bu("rb")], writes=[b_ps[6]], signal=True)
                evb = p.do("vector", lambda e: e.tensor_copy(out=rb[0:64, 0:512], in_=bank(6)[0:64, :]),
                           deps=[evb], reads=[b_ps[6]], writes=[bu("rb")], signal=True)
                arow = (qh % 2) * 64
                acol = (qh // 2) * TOK
                p.do("vector", (lambda arow, acol, qc, pav: lambda e: e.tensor_tensor(
                    out=attn_fm[arow:arow + 64, acol + qc * 512:acol + (qc + 1) * 512],
                    in0=PS[0:64, pav * 512:(pav + 1) * 512], in1=rb[0:64, 0:512],
                    op=ALU.mult))(arow, acol, qc, pav),
                    deps=[evb], reads=[b_ps[pav], bu("rb")], writes=[bu("attn")],
                    signal=True)

        p.full_barrier()
        for oc in range(4):
            evd = p.do("gpsimd", (lambda oc: lambda e: e.dma_start(
                out=wstage[:, 0:8192].rearrange("p (r c) -> p r c", r=4),
                in_=ow_in[oc * 512:(oc + 1) * 512, :]
                .rearrange("(r p) c -> p r c", p=128)))(oc),
                writes=[bu("wstage")], dma=True)
            for tg in range(2):
                for ic in range(16):
                    pcb = ic % 2
                    evt = None
                    for r in range(4):
                        evt = p.do("tensor", (lambda ic, r, pcb: lambda e: e.transpose(
                            bankb(4 + pcb)[:, r * 128:(r + 1) * 128],
                            wstage[:, r * 2048 + ic * 128:r * 2048 + (ic + 1) * 128],
                            identb[:]))(ic, r, pcb),
                            deps=[evi, evd], reads=[bu("wstage")], writes=[b_ps[4 + pcb]],
                            signal=(r == 3))
                    evp = p.do("vector", (lambda pcb: lambda e: e.tensor_copy(
                        out=piece[:, pcb * 512:(pcb + 1) * 512],
                        in_=bankb(4 + pcb)[:, 0:512]))(pcb),
                        deps=[evt], reads=[b_ps[4 + pcb]], writes=[bu("piece")], signal=True)
                    for tl in range(4):
                        t = tg * 4 + tl
                        p.do("tensor", (lambda ic, t, tl, pcb: lambda e: e.matmul(
                            bank(tl), attn_fm[:, ic * TOK + t * 128:ic * TOK + (t + 1) * 128],
                            piece[:, pcb * 512:(pcb + 1) * 512],
                            start=(ic == 0), stop=(ic == 15)))(ic, t, tl, pcb),
                            deps=[evp], reads=[bu("piece"), bu("attn")],
                            writes=[b_ps[tl]], signal=(ic == 15 or tl == 3))
                for tl in range(4):
                    t = tg * 4 + tl
                    p.do("vector", (lambda t, tl, oc: lambda e: e.tensor_copy(
                        out=ybf[:, t * D + oc * 512:t * D + (oc + 1) * 512],
                        in_=bank(tl)))(t, tl, oc),
                        reads=[b_ps[tl]], writes=[bu("ybf")], signal=True)

        p.full_barrier()
        for t in range(NT):
            yt = ybf[:, t * D:(t + 1) * D]
            evd = p.do("sync", (lambda t: lambda e: e.dma_start(
                out=xtb[:], in_=x_in[t * 128:(t + 1) * 128, :]))(t),
                writes=[bu("xtb")], dma=True)
            er, em = ln_stats(p, yt, rsc[:].bitcast(BF16)[:, 0:512], mv,
                              sc[:, 0:1], sc[:, 1:2], lnbufs, xbuf=bu("ybf"),
                              epsc=cst[:, 0:1], epsbuf=bu("cst"))
            ev = p.do("vector", (lambda yt: lambda e: e.tensor_scalar(
                out=yw[:], in0=yt, scalar1=sc[:, 1:2], scalar2=sc[:, 0:1],
                op0=ALU.add, op1=ALU.mult))(yt),
                deps=[er, em], reads=[bu("ybf"), bu("rs")], writes=[bu("yw")], signal=True)
            ev = p.do("vector", lambda e: e.tensor_add(out=yw[:], in0=yw[:], in1=xtb[:]),
                      deps=[evd, force(ev)], reads=[bu("xtb")], writes=[bu("yw")], signal=True)
            p.do("sync", (lambda t: lambda e: e.dma_start(
                out=out[t * 128:(t + 1) * 128, :], in_=yw[:]))(t),
                deps=[ev], reads=[bu("yw")], dma=True)

        fin = p.barrier_event("sync")
        p.do("sync", lambda e: None, deps=[fin])
        p.emit(blk)
    return nc


_L1ATTN_NC = None
_L1A_CONST = {}


def _l1a_consts():
    if _L1A_CONST:
        return _L1A_CONST
    pos = np.arange(S, dtype=np.float64)
    invq = 1.0 / 10000.0 ** (np.arange(0, 64, 2, dtype=np.float64) / 64)
    fq = pos[:, None] * invq[None, :]
    _L1A_CONST["cqc"] = np.cos(fq).astype(np.float32)
    _L1A_CONST["cqs"] = np.sin(fq).astype(np.float32)
    _L1A_CONST["ckc"] = np.cos(fq[:, 0:16]).astype(np.float32)
    _L1A_CONST["cks"] = np.sin(fq[:, 0:16]).astype(np.float32)
    return _L1A_CONST


def run_l1_attn(x_full, qw, krw, dww, upk, upv, ow):
    global _L1ATTN_NC
    if _L1ATTN_NC is None:
        _L1ATTN_NC = build_l1_attn()
    cst = _l1a_consts()
    in_maps = []
    for c in range(NCORES):
        sq_, hf = c // 2, c % 2
        own = slice(hf * TOK, (hf + 1) * TOK)
        oth = slice((1 - hf) * TOK, (2 - hf) * TOK)
        perm = np.concatenate([np.arange(hf * TOK, (hf + 1) * TOK),
                               np.arange((1 - hf) * TOK, (2 - hf) * TOK)])
        xs = np.ascontiguousarray(x_full[sq_][perm])
        qidx = np.arange(hf * TOK, (hf + 1) * TOK)
        mask = (perm[:, None] <= qidx[None, :]).astype(ml_dtypes.bfloat16)
        in_maps.append({
            "x": xs, "qw": qw, "krw": krw, "dww": dww, "upk": upk,
            "upv": upv, "ow": ow, "mask": np.ascontiguousarray(mask),
            "ckc": np.ascontiguousarray(cst["ckc"][perm]),
            "cks": np.ascontiguousarray(cst["cks"][perm]),
            "cqc": np.ascontiguousarray(cst["cqc"][qidx]),
            "cqs": np.ascontiguousarray(cst["cqs"][qidx]),
            "ident": _IDENT, "ones": _ONES,
        })
    res = run_bass_kernel_spmd(_L1ATTN_NC, in_maps, core_ids=list(range(NCORES)))
    out = np.empty((B, S, D), dtype=np.float32)
    for c in range(NCORES):
        sq_, hf = c // 2, c % 2
        out[sq_, hf * TOK:(hf + 1) * TOK, :] = res.results[c]["out"]
    return out


def build_l0_attn():
    nc = bass.Bass("TRN2", target_bir_lowering=False)
    sems, _ctx = fnew(nc)
    I32 = mybir.dt.int32

    x_in = nc.dram_tensor("x", [S, D], F32, kind="ExternalInput").ap()
    wdkv_in = nc.dram_tensor("wdkv", [D, 512], F32, kind="ExternalInput").ap()
    wuk_in = nc.dram_tensor("wuk", [512, 2048], F32, kind="ExternalInput").ap()
    wuv_in = nc.dram_tensor("wuv", [512, 2048], F32, kind="ExternalInput").ap()
    wdq_in = nc.dram_tensor("wdq", [D, 1024], F32, kind="ExternalInput").ap()
    wuq_in = nc.dram_tensor("wuq", [1024, 2048], F32, kind="ExternalInput").ap()
    wqr_in = nc.dram_tensor("wqr", [1024, 1024], F32, kind="ExternalInput").ap()
    wkr_in = nc.dram_tensor("wkr", [D, 64], F32, kind="ExternalInput").ap()
    wout_in = nc.dram_tensor("wout", [D, D], F32, kind="ExternalInput").ap()
    delta_in = nc.dram_tensor("delta", [32], F32, kind="ExternalInput").ap()
    mask_in = nc.dram_tensor("mask", [S, TOK], BF16, kind="ExternalInput").ap()
    c0_in = nc.dram_tensor("c0", [S, 32], F32, kind="ExternalInput").ap()
    s0_in = nc.dram_tensor("s0", [S, 32], F32, kind="ExternalInput").ap()
    ident_in = nc.dram_tensor("ident", [128, 128], BF16, kind="ExternalInput").ap()
    ones_in = nc.dram_tensor("ones", [128, 128], F32, kind="ExternalInput").ap()
    out = nc.dram_tensor("out", [TOK, D], F32, kind="ExternalOutput").ap()

    import contextlib
    with contextlib.ExitStack() as stk:
        sb = lambda *a: stk.enter_context(nc.sbuf_tensor(*a))
        xhat = sb("xhat", [128, 16 * S], BF16)
        ckv = sb("ckv", [128, 4 * S], BF16)
        cq = sb("cq", [128, 8 * TOK], BF16)
        qrf = sb("qrf", [128, 8 * TOK], BF16)
        krfb = sb("krfb", [128, S], BF16)
        kcb = sb("kcb", [128, S], BF16)
        qcb = sb("qcb", [128, TOK], BF16)
        vg = sb("vg", [128, 16 * 512], BF16)
        wstage = sb("wstage", [128, 8192], BF16)
        maskb = sb("maskb", [128, 2 * 16 * 512], BF16)
        ctf = sb("ctf", [128, 512], F32)
        stf = sb("stf", [128, 512], F32)
        rsc = sb("rsc", [128, 2048], F32)
        qtm = sb("qtm", [128, 1024], BF16)
        ptl = sb("ptl", [128, 2 * 512], BF16)
        rb = sb("rb", [128, 512], F32)
        dsc = sb("dsc", [1, 256], F32)
        dsi = sb("dsi", [1, 32], I32)
        cdb = sb("cdb", [128, 64], F32)
        mv = sb("mv", [128, 8], F32)
        sc = sb("sc", [128, 4], F32)
        cst = sb("cst", [128, 2], F32)
        identb = sb("identb", [128, 128], BF16)
        onesb = sb("onesb", [128, 128], F32)
        onebf = sb("onebf", [128, 1], BF16)
        PS = stk.enter_context(nc.psum_tensor("PS", [128, 4096], F32))
        blk = stk.enter_context(nc.Block())

        p = Prog(nc, sems)
        bank = lambda i: PS[:, i * 512:(i + 1) * 512]
        bankb = lambda i: PS[:, i * 512:(i + 1) * 512].bitcast(BF16)
        attn_fm = xhat[:, 0:16 * TOK]
        ybf = xhat[:, 16 * TOK:32 * TOK]
        cqF = cq.bitcast(F32)
        xtb = cqF[:, 0:D]
        yw = cqF[:, D:2 * D]
        wstF = wstage.bitcast(F32)
        c0t = wstF[:, 0:512]
        s0t = wstF[:, 512:1024]

        names = ["xhat", "ckv", "cq", "qrf", "krfb", "kcb", "qcb", "vg",
                 "wstage", "maskb", "tab", "rsc", "qtm", "ptl", "rb",
                 "dsc", "cdb", "mv", "rs", "cst", "attn", "ybf", "xtb", "yw",
                 "ones"]
        bufs = {n: TBuf(n) for n in names}
        b_ps = [TBuf(f"ps{i}") for i in range(8)]
        bu = lambda n: bufs[n]
        lnbufs = (bu("rsc"), bu("mv"), bu("rs"))
        TWO_PI = float(2.0 * np.pi)

        evi = p.do("sync", lambda e: e.dma_start(out=identb[:], in_=ident_in), dma=True)
        evo = p.do("sync", lambda e: e.dma_start(out=onesb[:], in_=ones_in), dma=True)
        p.do("vector", lambda e: e.memset(cst[:, 0:1], EPS), writes=[bu("cst")], signal=True)
        p.do("vector", lambda e: e.memset(onebf[:], 1.0), writes=[bu("ones")], signal=True)
        evc0 = p.do("sync", lambda e: e.dma_start(
            out=c0t.rearrange("p (tt j) -> p tt j", j=32),
            in_=c0_in.rearrange("(tt p) j -> p tt j", p=128)), writes=[bu("wstage")], dma=True)
        evs0 = p.do("sync", lambda e: e.dma_start(
            out=s0t.rearrange("p (tt j) -> p tt j", j=32),
            in_=s0_in.rearrange("(tt p) j -> p tt j", p=128)), writes=[bu("wstage")], dma=True)
        evd = p.do("sync", lambda e: e.dma_start(
            out=dsc[0:1, 0:32], in_=delta_in.rearrange("(a j) -> a j", a=1)),
            writes=[bu("dsc")], dma=True)
        ev = p.do("vector", lambda e: e.tensor_scalar_mul(
            out=dsc[0:1, 32:64], in0=dsc[0:1, 0:32], scalar1=1.0 / TWO_PI),
            deps=[evd], writes=[bu("dsc")], signal=True)
        ev = p.do("vector", lambda e: e.tensor_copy(out=dsi[0:1, :], in_=dsc[0:1, 32:64]),
                  deps=[force(ev)], writes=[bu("dsc")], signal=True)
        ev = p.do("vector", lambda e: e.tensor_copy(out=dsc[0:1, 32:64], in_=dsi[0:1, :]),
                  deps=[force(ev)], writes=[bu("dsc")], signal=True)
        ev = p.do("vector", lambda e: e.scalar_tensor_tensor(
            out=dsc[0:1, 64:96], in0=dsc[0:1, 32:64], scalar=-TWO_PI,
            in1=dsc[0:1, 0:32], op0=ALU.mult, op1=ALU.add),
            deps=[force(ev)], writes=[bu("dsc")], signal=True)
        ev1 = p.do("scalar", lambda e: e.activation(dsc[0:1, 96:128], dsc[0:1, 64:96], AF.Sin),
                   deps=[ev], writes=[bu("dsc")], signal=True)
        ev2 = p.do("scalar", lambda e: e.activation(dsc[0:1, 128:160], dsc[0:1, 64:96],
                                                    AF.Sin, scale=0.5),
                   deps=[force(ev1)], writes=[bu("dsc")], signal=True)
        ev2 = p.do("scalar", lambda e: e.activation(dsc[0:1, 160:192], dsc[0:1, 128:160], AF.Square),
                   deps=[force(ev2)], writes=[bu("dsc")], signal=True)
        ev2 = p.do("vector", lambda e: e.tensor_scalar(
            out=dsc[0:1, 192:224], in0=dsc[0:1, 160:192], scalar1=-2.0, scalar2=1.0,
            op0=ALU.mult, op1=ALU.add),
            deps=[ev2], writes=[bu("dsc")], signal=True)
        evb = p.do("tensor", lambda e: e.matmul(PS[:, 0:32], onesb[0:1, :],
                                                dsc[0:1, 192:224],
                                                start=True, stop=True),
                   deps=[evo, ev2], reads=[bu("dsc")], writes=[b_ps[0]], signal=True)
        evb2 = p.do("tensor", lambda e: e.matmul(PS[:, 64:96], onesb[0:1, :], dsc[0:1, 96:128],
                                                 start=True, stop=True),
                    deps=[evo, ev1], reads=[bu("dsc")], writes=[b_ps[0]], signal=True)
        evb = p.do("vector", lambda e: e.tensor_copy(out=cdb[:, 0:32], in_=PS[:, 0:32]),
                   deps=[evb], reads=[b_ps[0]], writes=[bu("cdb")], signal=True)
        evb = p.do("vector", lambda e: e.tensor_copy(out=cdb[:, 32:64], in_=PS[:, 64:96]),
                   deps=[evb2], reads=[b_ps[0]], writes=[bu("cdb")], signal=True)
        from concourse.bass import broadcast_tensor_aps
        c0v = c0t.rearrange("p (tt j) -> p tt j", j=32)
        s0v = s0t.rearrange("p (tt j) -> p tt j", j=32)
        cdv = cdb[:, 0:32].rearrange("p (tt j) -> p tt j", tt=1)
        sdv = cdb[:, 32:64].rearrange("p (tt j) -> p tt j", tt=1)
        _, cdvb = broadcast_tensor_aps(c0v, cdv)
        _, sdvb = broadcast_tensor_aps(c0v, sdv)
        ctv = ctf[:].rearrange("p (tt j) -> p tt j", j=32)
        stv = stf[:].rearrange("p (tt j) -> p tt j", j=32)
        rv = rsc[:, 0:512].rearrange("p (tt j) -> p tt j", j=32)
        e1 = p.do("vector", lambda e: e.tensor_tensor(out=ctv, in0=c0v, in1=cdvb, op=ALU.mult),
                  deps=[evb, evc0], reads=[bu("cdb"), bu("wstage")], writes=[bu("tab")], signal=True)
        e1 = p.do("vector", lambda e: e.tensor_tensor(out=rv, in0=s0v, in1=sdvb, op=ALU.mult),
                  deps=[evs0], reads=[bu("cdb"), bu("wstage")], writes=[bu("rsc")], signal=True)
        e1 = p.do("vector", lambda e: e.tensor_tensor(out=ctv, in0=ctv, in1=rv, op=ALU.subtract),
                  deps=[force(e1)], reads=[bu("rsc")], writes=[bu("tab")], signal=True)
        e2 = p.do("vector", lambda e: e.tensor_tensor(out=stv, in0=s0v, in1=cdvb, op=ALU.mult),
                  reads=[bu("cdb"), bu("wstage")], writes=[bu("tab")], signal=True)
        e2 = p.do("vector", lambda e: e.tensor_tensor(out=rv, in0=c0v, in1=sdvb, op=ALU.mult),
                  reads=[bu("cdb"), bu("wstage")], writes=[bu("rsc")], signal=True)
        e2 = p.do("vector", lambda e: e.tensor_tensor(out=stv, in0=stv, in1=rv, op=ALU.add),
                  deps=[force(e2)], reads=[bu("rsc")], writes=[bu("tab")], signal=True)

        for t in range(16):
            evdx = p.do("sync", (lambda t: lambda e: e.dma_start(
                out=xtb[:], in_=x_in[t * 128:(t + 1) * 128, :]))(t),
                writes=[bu("xtb")], dma=True)
            er, em = ln_stats(p, xtb[:], qtm[:, 0:512], mv, sc[:, 0:1], sc[:, 1:2],
                              lnbufs, xbuf=bu("xtb"), epsc=cst[:, 0:1],
                              epsbuf=bu("cst"), deps=[evdx])
            ev = p.do("vector", lambda e: e.tensor_scalar(
                out=wstage[:, 4096:4096 + D], in0=xtb[:], scalar1=sc[:, 1:2],
                scalar2=sc[:, 0:1], op0=ALU.add, op1=ALU.mult),
                deps=[er, em], reads=[bu("xtb"), bu("rs")], writes=[bu("wstage")],
                signal=True)
            for g in range(2):
                pb = 6 + g
                evg = None
                for dc in range(g * 8, g * 8 + 8):
                    evg = p.do("tensor", (lambda dc, g, pb: lambda e: e.transpose(
                        bankb(pb)[:, (dc - g * 8) * 128:(dc - g * 8 + 1) * 128],
                        wstage[:, 4096 + dc * 128:4096 + (dc + 1) * 128],
                        identb[:]))(dc, g, pb),
                        deps=[evi, ev], reads=[bu("wstage")], writes=[b_ps[pb]],
                        signal=(dc == g * 8 + 7))
                for dc in range(g * 8, g * 8 + 8):
                    p.do("vector", (lambda t, dc, pb, g: lambda e: e.tensor_copy(
                        out=xhat[:, dc * S + t * 128:dc * S + (t + 1) * 128],
                        in_=bankb(pb)[:, (dc - g * 8) * 128:(dc - g * 8 + 1) * 128]))(t, dc, pb, g),
                        deps=[evg], reads=[b_ps[pb]], writes=[bu("xhat")],
                        signal=(dc == g * 8 + 7))

        p.full_barrier()
        evw = p.do("gpsimd", lambda e: e.dma_start(
            out=wstage[:, 0:8192].rearrange("p (ic c) -> p ic c", c=512),
            in_=wdkv_in.rearrange("(ic p) c -> p ic c", p=128)),
            writes=[bu("wstage")], dma=True)
        for m in range(4):
            for ts in range(4):
                evm = None
                for ic in range(16):
                    evm = p.do("tensor", (lambda m, ts, ic: lambda e: e.matmul(
                        bank(m % 2), wstage[:, ic * 512 + m * 128:ic * 512 + (m + 1) * 128],
                        xhat[:, ic * S + ts * 512:ic * S + (ts + 1) * 512],
                        start=(ic == 0), stop=(ic == 15)))(m, ts, ic),
                        deps=[evw], reads=[bu("wstage"), bu("xhat")], writes=[b_ps[m % 2]],
                        signal=(ic == 15))
                p.do("vector", (lambda m, ts: lambda e: e.tensor_copy(
                    out=ckv[:, m * S + ts * 512:m * S + (ts + 1) * 512],
                    in_=bank(m % 2)))(m, ts),
                    deps=[evm], reads=[b_ps[m % 2]], writes=[bu("ckv")], signal=True)

        for half in range(2):
            evw = p.do("gpsimd", (lambda half: lambda e: e.dma_start(
                out=wstage[:, 0:8192].rearrange("p (ic c) -> p ic c", c=512),
                in_=wdq_in[:, half * 512:(half + 1) * 512]
                .rearrange("(ic p) c -> p ic c", p=128)))(half),
                writes=[bu("wstage")], dma=True)
            for m in range(4):
                mg = half * 4 + m
                for ts in range(2):
                    evm = None
                    for ic in range(16):
                        evm = p.do("tensor", (lambda m, ts, ic: lambda e: e.matmul(
                            bank(2 + m % 2), wstage[:, ic * 512 + m * 128:ic * 512 + (m + 1) * 128],
                            xhat[:, ic * S + ts * 512:ic * S + (ts + 1) * 512],
                            start=(ic == 0), stop=(ic == 15)))(m, ts, ic),
                            deps=[evw], reads=[bu("wstage"), bu("xhat")],
                            writes=[b_ps[2 + m % 2]], signal=(ic == 15))
                    p.do("vector", (lambda mg, ts: lambda e: e.tensor_copy(
                        out=cq[:, mg * TOK + ts * 512:mg * TOK + (ts + 1) * 512],
                        in_=bank(2 + mg % 2)))(mg, ts),
                        deps=[evm], reads=[b_ps[2 + mg % 2]], writes=[bu("cq")], signal=True)

        evw = p.do("gpsimd", lambda e: e.dma_start(
            out=wstage[:, 0:1024].rearrange("p (ic c) -> p ic c", c=64),
            in_=wkr_in.rearrange("(ic p) c -> p ic c", p=128)),
            writes=[bu("wstage")], dma=True)

        def pope(psum_ap, width, t, outtm, deps0, tabc=None, tabs=None):
            nh = width // 64
            evA = p.do("scalar", (lambda: lambda e: e.activation(
                rsc[:, 0:width], psum_ap, AF.Abs))(),
                deps=deps0, reads=b_ps_cur, writes=[bu("rsc")], signal=True)
            evA = p.do("scalar", (lambda: lambda e: e.activation(
                rsc[:, 0:width], rsc[:, 0:width], AF.Exp, scale=-1.0))(),
                deps=[force(evA)], writes=[bu("rsc")], signal=True)
            evA = p.do("scalar", (lambda: lambda e: e.activation(
                rsc[:, 0:width], rsc[:, 0:width], AF.Ln, bias=1.0))(),
                deps=[force(evA)], writes=[bu("rsc")], signal=True)
            sc2 = rsc[:, width:2 * width]
            evR = p.do("vector", (lambda: lambda e: e.tensor_relu(
                out=sc2, in_=psum_ap))(),
                deps=deps0, reads=b_ps_cur, writes=[bu("rsc")], signal=True)
            evA = p.do("vector", (lambda: lambda e: e.tensor_tensor(
                out=rsc[:, 0:width], in0=rsc[:, 0:width], in1=sc2,
                op=ALU.add))(),
                deps=[evA, force(evR)], writes=[bu("rsc")], signal=True)
            muv = rsc[:, 0:width].rearrange("p (h two j) -> p h two j", two=2, j=32)
            m1, m2 = muv[:, :, 0, :], muv[:, :, 1, :]
            ct1 = ctf[:, t * 32:(t + 1) * 32].rearrange("p (h j) -> p h j", h=1)
            st1 = stf[:, t * 32:(t + 1) * 32].rearrange("p (h j) -> p h j", h=1)
            _, ctb = broadcast_tensor_aps(m1, ct1)
            _, stb = broadcast_tensor_aps(m1, st1)
            ov = outtm.rearrange("p (h two j) -> p h two j", two=2, j=32)
            x1, x2 = ov[:, :, 0, :], ov[:, :, 1, :]
            s3 = sc2.rearrange("p (h two j) -> p h two j", two=2, j=32)
            t1, t2 = s3[:, :, 0, :], s3[:, :, 1, :]
            eX = p.do("vector", (lambda: lambda e: e.tensor_tensor(
                out=x1, in0=m1, in1=ctb, op=ALU.mult))(),
                deps=[force(evA)], reads=[bu("tab")], writes=[bu("outtm")], signal=True)
            eX = p.do("vector", (lambda: lambda e: e.tensor_tensor(
                out=x2, in0=m1, in1=stb, op=ALU.mult))(),
                reads=[bu("tab")], writes=[bu("outtm")], signal=True)
            eY = p.do("vector", (lambda: lambda e: e.tensor_tensor(
                out=t1, in0=m2, in1=stb, op=ALU.mult))(),
                reads=[bu("tab")], writes=[bu("rsc")], signal=True)
            eY = p.do("vector", (lambda: lambda e: e.tensor_tensor(
                out=t2, in0=m2, in1=ctb, op=ALU.mult))(),
                reads=[bu("tab")], writes=[bu("rsc")], signal=True)
            eZ = p.do("vector", (lambda: lambda e: e.tensor_tensor(
                out=x1, in0=x1, in1=t1, op=ALU.subtract))(),
                deps=[force(eX), force(eY)], reads=[bu("rsc")], writes=[bu("outtm")], signal=True)
            eZ = p.do("vector", (lambda: lambda e: e.tensor_tensor(
                out=x2, in0=x2, in1=t2, op=ALU.add))(),
                deps=[force(eZ)], reads=[bu("rsc")], writes=[bu("outtm")], signal=True)
            return eZ

        bufs["outtm"] = TBuf("outtm")
        for t in range(16):
            evm = None
            for ic in range(16):
                evm = p.do("tensor", (lambda t, ic: lambda e: e.matmul(
                    PS[:, 2048:2112], xhat[:, ic * S + t * 128:ic * S + (t + 1) * 128],
                    wstage[:, ic * 64:(ic + 1) * 64],
                    start=(ic == 0), stop=(ic == 15)))(t, ic),
                    deps=[evw], reads=[bu("wstage"), bu("xhat")], writes=[b_ps[4]],
                    signal=(ic == 15))
            b_ps_cur = [b_ps[4]]
            ktm = qtm[:, 512:576]
            eZ = pope(PS[:, 2048:2112], 64, t, ktm, [evm])
            evt = p.do("tensor", (lambda: lambda e: e.transpose(
                bankb(5)[0:64, 0:128], ktm, identb[:]))(),
                deps=[evi, eZ], reads=[bu("qtm")], writes=[b_ps[5]], signal=True)
            p.do("vector", (lambda t: lambda e: e.tensor_copy(
                out=krfb[0:64, t * 128:(t + 1) * 128], in_=bankb(5)[0:64, 0:128]))(t),
                deps=[evt], reads=[b_ps[5]], writes=[bu("krfb")], signal=True)
            p.do("vector", (lambda t: lambda e: e.tensor_copy(
                out=krfb[64:128, t * 128:(t + 1) * 128], in_=bankb(5)[0:64, 0:128]))(t),
                deps=[evt], reads=[b_ps[5]], writes=[bu("krfb")], signal=True)

        p.full_barrier()
        evw = p.do("gpsimd", lambda e: e.dma_start(
            out=wstage[:, 0:8192].rearrange("p (ic c) -> p ic c", c=1024),
            in_=wqr_in.rearrange("(ic p) c -> p ic c", p=128)),
            writes=[bu("wstage")], dma=True)
        for t in range(NT):
            for oc2 in range(2):
                evm = None
                for ic in range(8):
                    evm = p.do("tensor", (lambda t, oc2, ic: lambda e: e.matmul(
                        bank(4 + oc2), cq[:, ic * TOK + t * 128:ic * TOK + (t + 1) * 128],
                        wstage[:, ic * 1024 + oc2 * 512:ic * 1024 + (oc2 + 1) * 512],
                        start=(ic == 0), stop=(ic == 7)))(t, oc2, ic),
                        deps=[evw], reads=[bu("wstage"), bu("cq")], writes=[b_ps[4 + oc2]],
                        signal=(ic == 7))
            b_ps_cur = [b_ps[4], b_ps[5]]
            qr_ps = PS[:, 2048:3072]
            eZ = pope(qr_ps, 1024, t, qtm[:, 0:1024], [p.barrier_event("tensor")])
            evt = None
            for rc in range(8):
                evt = p.do("tensor", (lambda rc: lambda e: e.transpose(
                    bankb(6)[:, rc * 128:(rc + 1) * 128],
                    qtm[:, rc * 128:(rc + 1) * 128], identb[:]))(rc),
                    deps=[evi, eZ], reads=[bu("qtm")], writes=[b_ps[6]],
                    signal=(rc == 7))
            for rc in range(8):
                p.do("vector", (lambda t, rc: lambda e: e.tensor_copy(
                    out=qrf[:, rc * TOK + t * 128:rc * TOK + (t + 1) * 128],
                    in_=bankb(6)[:, rc * 128:(rc + 1) * 128]))(t, rc),
                    deps=[evt], reads=[b_ps[6]], writes=[bu("qrf")],
                    signal=(rc == 7))

        p.full_barrier()
        SCL = float(192) ** -0.5
        evmk = [None, None]
        for qc in range(2):
            evmk[qc] = p.do("sync", (lambda qc: lambda e: e.dma_start(
                out=maskb[:, qc * 8192:(qc + 1) * 8192]
                .rearrange("p (tt c) -> p tt c", tt=16),
                in_=mask_in[:, qc * 512:(qc + 1) * 512]
                .rearrange("(tt p) c -> p tt c", p=128)))(qc),
                writes=[bu("maskb")], dma=True)
        for g in range(4):
            evwv = p.do("gpsimd", (lambda g: lambda e: e.dma_start(
                out=wstage[:, 0:2048].rearrange("p (ic c) -> p ic c", c=512),
                in_=wuv_in[:, g * 512:(g + 1) * 512]
                .rearrange("(ic p) c -> p ic c", p=128)))(g),
                writes=[bu("wstage")], dma=True)
            for kt in range(16):
                evm = None
                for ic in range(4):
                    evm = p.do("tensor", (lambda g, kt, ic: lambda e: e.matmul(
                        bank(4), ckv[:, ic * S + kt * 128:ic * S + (kt + 1) * 128],
                        wstage[:, ic * 512:(ic + 1) * 512],
                        start=(ic == 0), stop=(ic == 3)))(g, kt, ic),
                        deps=[evwv], reads=[bu("wstage"), bu("ckv")], writes=[b_ps[4]],
                        signal=(ic == 3))
                p.do("vector", (lambda kt: lambda e: e.tensor_copy(
                    out=vg[:, kt * 512:(kt + 1) * 512], in_=bank(4)))(kt),
                    deps=[evm], reads=[b_ps[4]], writes=[bu("vg")], signal=True)
            for hl in range(4):
                h = g * 4 + hl
                ksl = 0
                evwk = p.do("gpsimd", (lambda h: lambda e: e.dma_start(
                    out=wstage[:, 2048:2560].rearrange("p (ic c) -> p ic c", c=128),
                    in_=wuk_in[:, h * 128:(h + 1) * 128]
                    .rearrange("(ic p) c -> p ic c", p=128)))(h),
                    writes=[bu("wstage")], dma=True)
                for ts in range(4):
                    evm = None
                    for ic in range(4):
                        evm = p.do("tensor", (lambda h, ts, ic: lambda e: e.matmul(
                            bank(5), wstage[:, 2048 + ic * 128:2048 + (ic + 1) * 128],
                            ckv[:, ic * S + ts * 512:ic * S + (ts + 1) * 512],
                            start=(ic == 0), stop=(ic == 3)))(h, ts, ic),
                            deps=[evwk], reads=[bu("wstage"), bu("ckv")], writes=[b_ps[5]],
                            signal=(ic == 3))
                    p.do("vector", (lambda ksl, ts: lambda e: e.tensor_copy(
                        out=kcb[:, ksl * S + ts * 512:ksl * S + (ts + 1) * 512],
                        in_=bank(5)))(ksl, ts),
                        deps=[evm], reads=[b_ps[5]], writes=[bu("kcb")], signal=True)
                evwq = p.do("gpsimd", (lambda h: lambda e: e.dma_start(
                    out=wstage[:, 4096:5120].rearrange("p (ic c) -> p ic c", c=128),
                    in_=wuq_in[:, h * 128:(h + 1) * 128]
                    .rearrange("(ic p) c -> p ic c", p=128)))(h),
                    writes=[bu("wstage")], dma=True)
                for ts in range(2):
                    evm = None
                    for ic in range(8):
                        evm = p.do("tensor", (lambda h, ts, ic: lambda e: e.matmul(
                            bank(6), wstage[:, 4096 + ic * 128:4096 + (ic + 1) * 128],
                            cq[:, ic * TOK + ts * 512:ic * TOK + (ts + 1) * 512],
                            start=(ic == 0), stop=(ic == 7)))(h, ts, ic),
                            deps=[evwq], reads=[bu("wstage"), bu("cq")], writes=[b_ps[6]],
                            signal=(ic == 7))
                    p.do("vector", (lambda ksl, ts: lambda e: e.tensor_copy(
                        out=qcb[:, ksl * TOK + ts * 512:ksl * TOK + (ts + 1) * 512],
                        in_=bank(6)))(ksl, ts),
                        deps=[evm], reads=[b_ps[6]], writes=[bu("qcb")], signal=True)
                for qc in range(2):
                    for kt in range(16):
                        pbs = kt % 2
                        ev1 = p.do("tensor", (lambda ksl, kt, qc, pbs: lambda e: e.matmul(
                            bank(pbs), kcb[:, ksl * S + kt * 128:ksl * S + (kt + 1) * 128],
                            qcb[:, ksl * TOK + qc * 512:ksl * TOK + (qc + 1) * 512],
                            start=True, stop=False))(ksl, kt, qc, pbs),
                            reads=[bu("kcb"), bu("qcb")], writes=[b_ps[pbs]],
                            signal=True)
                        ev1 = p.do("tensor", (lambda h, kt, qc, pbs: lambda e: e.matmul(
                            bank(pbs), krfb[(h % 2) * 64:(h % 2) * 64 + 64,
                                            kt * 128:(kt + 1) * 128],
                            qrf[(h % 2) * 64:(h % 2) * 64 + 64,
                                (h // 2) * TOK + qc * 512:(h // 2) * TOK + (qc + 1) * 512],
                            start=False, stop=True,
                            tile_position=((h % 2) * 64, 0)))(h, kt, qc, pbs),
                            reads=[bu("krfb"), bu("qrf")], writes=[b_ps[pbs]], signal=True)
                        ev2 = p.do("scalar", (lambda pbs, kt: lambda e: e.activation(
                            ptl[:, (kt % 2) * 512:(kt % 2 + 1) * 512], bank(pbs),
                            AF.Exp, scale=SCL))(pbs, kt),
                            deps=[ev1], reads=[b_ps[pbs]], writes=[bu("ptl")], signal=True)
                        ev3 = p.do("vector", (lambda kt, qc: lambda e: e.tensor_mul(
                            out=ptl[:, (kt % 2) * 512:(kt % 2 + 1) * 512],
                            in0=ptl[:, (kt % 2) * 512:(kt % 2 + 1) * 512],
                            in1=maskb[:, qc * 8192 + kt * 512:qc * 8192 + (kt + 1) * 512]))(kt, qc),
                            deps=[ev2, evmk[qc]], reads=[bu("maskb")], writes=[bu("ptl")],
                            signal=True)
                        pav = 2 + (h % 2)
                        p.do("tensor", (lambda kt, hl, pav: lambda e: e.matmul(
                            bank(pav), vg[:, kt * 512 + hl * 128:kt * 512 + (hl + 1) * 128],
                            ptl[:, (kt % 2) * 512:(kt % 2 + 1) * 512],
                            start=(kt == 0), stop=(kt == 15)))(kt, hl, pav),
                            deps=[ev3], reads=[bu("vg"), bu("ptl")],
                            writes=[b_ps[pav]], signal=True)
                        p.do("tensor", (lambda kt: lambda e: e.matmul(
                            PS[0:1, 3584:4096], onebf[:],
                            ptl[:, (kt % 2) * 512:(kt % 2 + 1) * 512],
                            start=(kt == 0), stop=(kt == 15)))(kt),
                            deps=[ev3], reads=[bu("ones"), bu("ptl")],
                            writes=[b_ps[7]], signal=True)
                    pav = 2 + (h % 2)
                    evc = p.do("vector", lambda e: e.tensor_copy(
                        out=rb[0:1, 0:512], in_=PS[0:1, 3584:4096]),
                        reads=[b_ps[7]], writes=[bu("rb")], signal=True)
                    evc = p.do("vector", lambda e: e.reciprocal(
                        out=rb[0:1, 0:512], in_=rb[0:1, 0:512]),
                        deps=[force(evc)], writes=[bu("rb")], signal=True)
                    evb = p.do("tensor", lambda e: e.matmul(
                        bank(6), onesb[0:1, :], rb[0:1, 0:512], start=True, stop=True),
                        deps=[evc, evo], reads=[bu("rb")], writes=[b_ps[6]], signal=True)
                    evb = p.do("vector", lambda e: e.tensor_copy(
                        out=rb[:, 0:512], in_=bank(6)),
                        deps=[evb], reads=[b_ps[6]], writes=[bu("rb")], signal=True)
                    p.do("vector", (lambda h, qc, pav: lambda e: e.tensor_tensor(
                        out=attn_fm[:, h * TOK + qc * 512:h * TOK + (qc + 1) * 512],
                        in0=bank(pav), in1=rb[:, 0:512], op=ALU.mult))(h, qc, pav),
                        deps=[evb], reads=[b_ps[pav], bu("rb")], writes=[bu("attn")],
                        signal=True)

        p.full_barrier()
        for oc in range(4):
            evm_all = {}
            for ic in range(16):
                wsl = ic % 2
                evwo = p.do("gpsimd", (lambda oc, ic, wsl: lambda e: e.dma_start(
                    out=wstage[:, wsl * 512:(wsl + 1) * 512],
                    in_=wout_in[ic * 128:(ic + 1) * 128,
                                oc * 512:(oc + 1) * 512]))(oc, ic, wsl),
                    writes=[bu("wstage")], dma=True)
                for t in range(NT):
                    evm = p.do("tensor", (lambda ic, t, wsl: lambda e: e.matmul(
                        bank(t), attn_fm[:, ic * TOK + t * 128:ic * TOK + (t + 1) * 128],
                        wstage[:, wsl * 512:(wsl + 1) * 512],
                        start=(ic == 0), stop=(ic == 15)))(ic, t, wsl),
                        deps=[evwo], reads=[bu("wstage"), bu("attn")],
                        writes=[b_ps[t]], signal=(ic == 15 or t == NT - 1))
                    if ic == 15:
                        evm_all[t] = evm
            for t in range(NT):
                p.do("vector", (lambda t, oc: lambda e: e.tensor_copy(
                    out=ybf[:, t * D + oc * 512:t * D + (oc + 1) * 512],
                    in_=bank(t)))(t, oc),
                    deps=[evm_all[t]], reads=[b_ps[t]], writes=[bu("ybf")],
                    signal=True)

        p.full_barrier()
        for t in range(NT):
            yt = ybf[:, t * D:(t + 1) * D]
            evdx = p.do("sync", (lambda t: lambda e: e.dma_start(
                out=xtb[:], in_=x_in[t * 128:(t + 1) * 128, :]))(t),
                writes=[bu("xtb")], dma=True)
            er, em = ln_stats(p, yt, qtm[:, 0:512], mv, sc[:, 0:1], sc[:, 1:2],
                              lnbufs, xbuf=bu("ybf"), epsc=cst[:, 0:1],
                              epsbuf=bu("cst"))
            ev = p.do("vector", (lambda yt: lambda e: e.tensor_scalar(
                out=yw[:], in0=yt, scalar1=sc[:, 1:2], scalar2=sc[:, 0:1],
                op0=ALU.add, op1=ALU.mult))(yt),
                deps=[er, em], reads=[bu("ybf"), bu("rs")], writes=[bu("yw")],
                signal=True)
            ev = p.do("vector", lambda e: e.tensor_add(out=yw[:], in0=yw[:], in1=xtb[:]),
                      deps=[evdx, force(ev)], reads=[bu("xtb")], writes=[bu("yw")],
                      signal=True)
            p.do("sync", (lambda t: lambda e: e.dma_start(
                out=out[t * 128:(t + 1) * 128, :], in_=yw[:]))(t),
                deps=[ev], reads=[bu("yw")], dma=True)

        fin = p.barrier_event("sync")
        p.do("sync", lambda e: None, deps=[fin])
        p.emit(blk)
    return nc


_L0ATTN_NC = None
_L0A_CONST = {}


def _l0a_consts():
    if _L0A_CONST:
        return _L0A_CONST
    pos = np.arange(S, dtype=np.float64)
    theta = 1.0 / 10000.0 ** (2.0 * np.arange(32, dtype=np.float64) / 64)
    ang = pos[:, None] * theta[None, :]
    ang = np.angle(np.exp(1j * ang))
    _L0A_CONST["c0"] = np.cos(ang).astype(np.float32)
    _L0A_CONST["s0"] = np.sin(ang).astype(np.float32)
    return _L0A_CONST


def run_l0_attn(x_full, wdkv, wuk, wuv, wdq, wuq, wqr, wkr, wout, delta):
    global _L0ATTN_NC
    if _L0ATTN_NC is None:
        _L0ATTN_NC = build_l0_attn()
    cst = _l0a_consts()
    in_maps = []
    for c in range(NCORES):
        sq_, hf = c // 2, c % 2
        perm = np.concatenate([np.arange(hf * TOK, (hf + 1) * TOK),
                               np.arange((1 - hf) * TOK, (2 - hf) * TOK)])
        xs = np.ascontiguousarray(x_full[sq_][perm])
        qidx = np.arange(hf * TOK, (hf + 1) * TOK)
        mask = (perm[:, None] <= qidx[None, :]).astype(ml_dtypes.bfloat16)
        in_maps.append({
            "x": xs, "wdkv": wdkv, "wuk": wuk, "wuv": wuv, "wdq": wdq,
            "wuq": wuq, "wqr": wqr, "wkr": wkr, "wout": wout, "delta": delta,
            "mask": np.ascontiguousarray(mask),
            "c0": np.ascontiguousarray(cst["c0"][perm]),
            "s0": np.ascontiguousarray(cst["s0"][perm]),
            "ident": _IDENT, "ones": _ONES,
        })
    res = run_bass_kernel_spmd(_L0ATTN_NC, in_maps, core_ids=list(range(NCORES)))
    out = np.empty((B, S, D), dtype=np.float32)
    for c in range(NCORES):
        sq_, hf = c // 2, c % 2
        out[sq_, hf * TOK:(hf + 1) * TOK, :] = res.results[c]["out"]
    return out



def emit_l1ffn(C, A, x_in, w1, w3, w2, ones_in, ident_in, out, u2t_dram,
               wabs=None, crin_t=None, crout=None, rsem=None):
    nc, p = C.nc, C.p
    PS = C.PS
    xhat = A.take(16 * TOK, BF16)
    gfm = A.take(NH1 * TOK, BF16)
    wstg = A.take(2 * D, F32)
    ubf = A.take(2 * D, BF16)
    ut = A.take(2 * D, BF16)
    u2row = A.take(H1P, BF16)
    xtb = A.take(D, F32)
    ytm = A.take(4 * D, BF16)
    yw = A.take(D, F32)
    sil = A.take(512, F32)
    mv = A.take(8, F32)
    sc = A.take(16, F32)
    acc = A.take(4, F32)
    bank = lambda i: PS[:, i * 512:(i + 1) * 512]
    bankb = lambda i: PS[:, i * 512:(i + 1) * 512].bitcast(BF16)

    b_stage = [TBuf("stg0"), TBuf("stg1")]
    b_acc = TBuf("acc"); b_sc = TBuf("sc")
    b_ubf = TBuf("ubf"); b_ut = TBuf("ut"); b_u2r = TBuf("u2r")
    b_xhat = TBuf("xhat"); b_gfm = TBuf("gfm")
    b_xt = TBuf("xt"); b_ytm = TBuf("ytm"); b_yw = TBuf("yw")
    b_sil = TBuf("sil"); b_mv = TBuf("mv")
    b_rs = TBuf("rs"); b_cst = C.b_cst
    lnbufs = (b_ut, b_mv, b_rs)
    b_ps = [TBuf(f"ps{i}") for i in range(8)]
    evo = C.evo
    evi = C.evi
    p.do("vector", lambda e: e.memset(acc[:], 0.0), writes=[b_acc], signal=True)

    SH = FLATC // 8
    ci = 0
    for j in range(3):
        c0 = 0
        while c0 < SH:
            c1 = min(c0 + 2032, SH)
            slot = ci % 2
            ci += 1
            sb = b_stage[slot]
            st = wstg[:, slot * 2048: slot * 2048 + (c1 - c0)]
            evd = p.do("sync", (lambda j, c0, c1, st: lambda e: e.dma_start(
                out=st, in_=wabs[:, j * SH + c0:j * SH + c1]))(j, c0, c1, st),
                writes=[sb], dma=True)
            evr = p.do("vector", (lambda st, j: lambda e: e.reduce_sum(
                out=sc[:, 11 + j:12 + j], in_=st, axis=AX.X,
                apply_absolute_value=True))(st, j),
                deps=[evd], reads=[sb], writes=[b_sc], signal=True)
            p.do("vector", (lambda j: lambda e: e.tensor_tensor(
                out=acc[:, j:j + 1], in0=acc[:, j:j + 1],
                in1=sc[:, 11 + j:12 + j], op=ALU.add))(j),
                deps=[force(evr)], reads=[b_sc], writes=[b_acc], signal=True)
            c0 = c1
    ev = p.do("tensor", lambda e: e.matmul(PS[0:1, 0:4], C.onesb[:, 0:1], acc[:, 0:4],
                                           start=True, stop=True),
              deps=[evo], reads=[b_acc], writes=[b_ps[0]], signal=True)
    ev = p.do("vector", lambda e: e.tensor_copy(
        out=sil[0:1, 0:128], in_=PS[0:1, 0:128]),
        deps=[ev], reads=[b_ps[0]], writes=[b_sil], signal=True)
    ev = p.do("sync", lambda e: e.dma_start(out=crin_t.ap(), in_=sil[0:1, 0:128]),
              deps=[ev], reads=[b_sil], dma=True)
    p.do("gpsimd", (lambda ev: lambda e: e.collective_compute(
        "AllReduce", ALU.add, replica_groups=[[0, 1, 2, 3, 4, 5, 6, 7]],
        ins=[crin_t.ap().opt()], outs=[crout.opt()]).then_inc(rsem, 1))(ev),
        deps=[ev])
    p.do("sync", lambda e: e.wait_ge(rsem, 1))
    ev = p.do("sync", lambda e: e.dma_start(out=sil[0:1, 0:128], in_=crout),
              writes=[b_sil], dma=True)
    ev = p.do("vector", lambda e: e.tensor_scalar_mul(
        out=sc[0:1, 0:4], in0=sil[0:1, 0:4], scalar1=1.0 / (H1 * D)),
        deps=[ev], reads=[b_sil], writes=[b_sc], signal=True)
    ev = p.do("tensor", lambda e: e.matmul(PS[:, 0:4], C.onesb[0:1, :], sc[0:1, 0:4],
                                           start=True, stop=True),
              deps=[ev], reads=[b_sc], writes=[b_ps[0]], signal=True)
    ev = p.do("vector", lambda e: e.tensor_scalar_mul(out=sc[:, 0:3],
                                                      in0=PS[:, 0:3], scalar1=0.5),
              deps=[ev], reads=[b_ps[0]], writes=[b_sc], signal=True)
    for j, (cp, cn) in enumerate([(3, 4), (5, 6), (7, 8)]):
        ev = p.do("vector", (lambda j, cp: lambda e: e.tensor_scalar_mul(
            out=sc[:, cp:cp + 1], in0=PS[:, j:j + 1], scalar1=0.5))(j, cp),
            deps=[ev], reads=[b_ps[0]], writes=[b_sc], signal=True)
        ev = p.do("vector", (lambda cp, cn: lambda e: e.tensor_scalar_mul(
            out=sc[:, cn:cn + 1], in0=sc[:, cp:cp + 1], scalar1=-1.0))(cp, cn),
            writes=[b_sc], signal=True)

    pieces = [(0, 2048, 16), (2048, 4096, 16), (4096, 5461, 11)]
    for ch in range(16):
        for pidx, (p0, p1, ntile) in enumerate(pieces):
            slot = (ch * 3 + pidx) % 2
            sb = b_stage[slot]
            st = wstg[:, slot * 2048: slot * 2048 + (p1 - p0)]
            evd = p.do("sync", (lambda ch, p0, p1, st: lambda e: e.dma_start(
                out=st, in_=w2[ch * 128:(ch + 1) * 128, p0:p1]))(ch, p0, p1, st),
                writes=[sb], dma=True)
            npad = ntile * 128
            evz = None
            if p1 - p0 < npad:
                evz = p.do("vector", (lambda p0, p1, npad: lambda e: e.memset(
                    ubf[:, (p1 - p0):npad], 0.0))(p0, p1, npad),
                    writes=[b_ubf], signal=True)
            ea = p.do("scalar", (lambda st, p0, p1: lambda e: e.activation(
                ut[:, 0:(p1 - p0)], st, AF.Sign, bias=sc[:, 8:9]))(st, p0, p1),
                deps=[evd], reads=[sb, b_sc], writes=[b_ut], signal=True)
            eb = p.do("scalar", (lambda st, p0, p1: lambda e: e.activation(
                ubf[:, 0:(p1 - p0)], st, AF.Sign, bias=sc[:, 7:8]))(st, p0, p1),
                deps=[ea], reads=[sb, b_sc], writes=[b_ubf], signal=True)
            ev = p.do("vector", (lambda p0, p1: lambda e: e.tensor_tensor(
                out=ubf[:, 0:(p1 - p0)], in0=ubf[:, 0:(p1 - p0)],
                in1=ut[:, 0:(p1 - p0)], op=ALU.add))(p0, p1),
                deps=[eb, evz], reads=[b_ut], writes=[b_ubf], signal=True)
            for g in range(0, ntile, 8):
                g1 = min(g + 8, ntile)
                pb = 4 + ((g // 8) % 2)
                evg = None
                for t in range(g, g1):
                    evg = p.do("tensor", (lambda t, g, pb: lambda e: e.transpose(
                        bankb(pb)[:, (t - g) * 128:(t - g + 1) * 128],
                        ubf[:, t * 128:(t + 1) * 128], C.identb[:]))(t, g, pb),
                        deps=[evi, ev], reads=[b_ubf], writes=[b_ps[pb]],
                        signal=(t == g1 - 1))
                evc = p.do("vector", (lambda g, g1, pb: lambda e: e.tensor_copy(
                    out=ut[:, 0:(g1 - g) * 128],
                    in_=bankb(pb)[:, 0:(g1 - g) * 128]))(g, g1, pb),
                    deps=[evg], reads=[b_ps[pb]], writes=[b_ut], signal=True)
                t0g = p0 // 128 + g
                p.do("sync", (lambda ch, t0g, g, g1: lambda e: e.dma_start(
                    out=u2t_dram[ch, t0g:t0g + (g1 - g)].rearrange("t p d -> p t d"),
                    in_=ut[:, 0:(g1 - g) * 128].rearrange("p (t d) -> p t d", d=128)))(ch, t0g, g, g1),
                    deps=[evc], reads=[b_ut], dma=True)
    ev_u2done = p.barrier_event("sync")

    for t in range(NT):
        evd = p.do("sync", (lambda t: lambda e: e.dma_start(
            out=xtb[:], in_=x_in[t * 128:(t + 1) * 128, :]))(t),
            writes=[b_xt], dma=True)
        er, em = ln_stats(p, xtb[:], ut[:, 0:D], mv, sc[:, 9:10],
                          sc[:, 10:11], lnbufs, xbuf=b_xt,
                          epsc=C.cst[:, 0:1], epsbuf=b_cst, deps=[evd])
        ev = p.do("vector", lambda e: e.tensor_scalar(
            out=ubf[:, 0:D], in0=xtb[:], scalar1=sc[:, 10:11],
            scalar2=sc[:, 9:10], op0=ALU.add, op1=ALU.mult),
            deps=[er, em], reads=[b_xt, b_rs], writes=[b_ubf], signal=True)
        for g in range(2):
            pb = 6 + g
            evg = None
            for dc in range(g * 8, g * 8 + 8):
                evg = p.do("tensor", (lambda dc, g, pb: lambda e: e.transpose(
                    bankb(pb)[:, (dc - g * 8) * 128:(dc - g * 8 + 1) * 128],
                    ubf[:, dc * 128:(dc + 1) * 128], C.identb[:]))(dc, g, pb),
                    deps=[evi, ev], reads=[b_ubf], writes=[b_ps[pb]],
                    signal=(dc == g * 8 + 7))
            for dc in range(g * 8, g * 8 + 8):
                p.do("vector", (lambda t, g, dc, pb: lambda e: e.tensor_copy(
                    out=xhat[:, dc * TOK + t * 128:dc * TOK + (t + 1) * 128],
                    in_=bankb(pb)[:, (dc - g * 8) * 128:(dc - g * 8 + 1) * 128]))(t, g, dc, pb),
                    deps=[evg], reads=[b_ps[pb]], writes=[b_xhat],
                    signal=(dc == g * 8 + 7))

    for hc in range(NH1):
        r0, r1 = hc * 128, min((hc + 1) * 128, H1)
        nr = r1 - r0
        slot = 0
        sb = b_stage[slot]
        st1 = wstg[:, slot * 2 * D: slot * 2 * D + D]
        st3 = wstg[:, slot * 2 * D + D: slot * 2 * D + 2 * D]
        ed1 = p.do("sync", (lambda r0, r1, st1: lambda e: e.dma_start(
            out=st1[0:r1 - r0, :], in_=w1[r0:r1, :]))(r0, r1, st1),
            writes=[sb], dma=True)
        ed3 = p.do("sync", (lambda r0, r1, st3: lambda e: e.dma_start(
            out=st3[0:r1 - r0, :], in_=w3[r0:r1, :]))(r0, r1, st3),
            writes=[sb], dma=True)
        ev_u = []
        if nr < 128:
            ev_u.append(p.do("vector", lambda e: e.memset(ubf[64:128, 0:2 * D], 0.0),
                             writes=[b_ubf], signal=True))
        for (st, evd, off, cp, cn) in [(st1, ed1, 0, 3, 4), (st3, ed3, D, 5, 6)]:
            ea = p.do("scalar", (lambda st, off, cn, nr: lambda e: e.activation(
                ut[0:nr, off:off + D], st[0:nr, :], AF.Sign,
                bias=sc[0:nr, cn:cn + 1]))(st, off, cn, nr),
                deps=[evd], reads=[sb, b_sc], writes=[b_ut], signal=True)
            eb = p.do("scalar", (lambda st, off, cp, nr: lambda e: e.activation(
                ubf[0:nr, off:off + D], st[0:nr, :], AF.Sign,
                bias=sc[0:nr, cp:cp + 1]))(st, off, cp, nr),
                deps=[ea], reads=[sb, b_sc], writes=[b_ubf], signal=True)
            evq = p.do("vector", (lambda off, nr: lambda e: e.tensor_tensor(
                out=ubf[0:nr, off:off + D], in0=ubf[0:nr, off:off + D],
                in1=ut[0:nr, off:off + D], op=ALU.add))(off, nr),
                deps=[eb], reads=[b_ut], writes=[b_ubf], signal=True)
            ev_u.append(evq)
        for (off, pb0) in [(0, 4), (D, 5)]:
            for g in range(2):
                pb = pb0 if g == 0 else (pb0 ^ 1) if False else pb0
                evg = None
                for dc in range(g * 8, g * 8 + 8):
                    evg = p.do("tensor", (lambda off, pb, dc, g: lambda e: e.transpose(
                        bankb(pb)[:, (dc - g * 8) * 128:(dc - g * 8 + 1) * 128],
                        ubf[:, off + dc * 128:off + (dc + 1) * 128],
                        C.identb[:]))(off, pb, dc, g),
                        deps=[evi] + ev_u, reads=[b_ubf], writes=[b_ps[pb]],
                        signal=(dc == g * 8 + 7))
                evuc = p.do("vector", (lambda off, pb, g: lambda e: e.tensor_copy(
                    out=ut[:, off + g * 1024:off + (g + 1) * 1024],
                    in_=bankb(pb)[:, 0:1024]))(off, pb, g),
                    deps=[evg], reads=[b_ps[pb]], writes=[b_ut], signal=True)
        for tc in range(2):
            e1 = e3 = None
            for dc in range(16):
                e1 = p.do("tensor", (lambda tc, dc: lambda e: e.matmul(
                    bank(0 + tc), ut[:, dc * 128:(dc + 1) * 128],
                    xhat[:, dc * TOK + tc * 512:dc * TOK + (tc + 1) * 512],
                    start=(dc == 0), stop=(dc == 15)))(tc, dc),
                    reads=[b_ut, b_xhat], writes=[b_ps[0 + tc]],
                    signal=(dc == 15))
            for dc in range(16):
                e3 = p.do("tensor", (lambda tc, dc: lambda e: e.matmul(
                    bank(2 + tc), ut[:, D + dc * 128:D + (dc + 1) * 128],
                    xhat[:, dc * TOK + tc * 512:dc * TOK + (tc + 1) * 512],
                    start=(dc == 0), stop=(dc == 15)))(tc, dc),
                    reads=[b_ut, b_xhat], writes=[b_ps[2 + tc]],
                    signal=(dc == 15))
            es = p.do("scalar", (lambda tc: lambda e: e.activation(
                sil[:], bank(0 + tc), AF.Silu, scale=sc[:, 0:1]))(tc),
                deps=[e1], reads=[b_ps[0 + tc], b_sc], writes=[b_sil], signal=True)
            p.do("vector", (lambda hc, tc: lambda e: e.scalar_tensor_tensor(
                out=gfm[:, hc * TOK + tc * 512:hc * TOK + (tc + 1) * 512],
                in0=bank(2 + tc), scalar=sc[:, 1:2], in1=sil[:],
                op0=ALU.mult, op1=ALU.mult))(hc, tc),
                deps=[es, e3], reads=[b_ps[2 + tc], b_sil, b_sc],
                writes=[b_gfm], signal=True)

    for tc in range(2):
        for dc in range(16):
            evd = p.do("sync", (lambda dc: lambda e: e.dma_start(
                out=u2row[:].rearrange("p (t d) -> p t d", d=128),
                in_=u2t_dram[dc, :].rearrange("t p d -> p t d")))(dc),
                deps=[ev_u2done], writes=[b_u2r], dma=True)
            evm = None
            pb = dc % 2
            for hc in range(NH1):
                evm = p.do("tensor", (lambda tc, hc, pb: lambda e: e.matmul(
                    bank(pb), u2row[:, hc * 128:(hc + 1) * 128],
                    gfm[:, hc * TOK + tc * 512:hc * TOK + (tc + 1) * 512],
                    start=(hc == 0), stop=(hc == NH1 - 1)))(tc, hc, pb),
                    deps=[evd], reads=[b_u2r, b_gfm],
                    writes=[b_ps[pb]], signal=(hc == NH1 - 1))
            ev = p.do("vector", (lambda pb: lambda e: e.tensor_scalar_mul(
                out=sil[:].bitcast(BF16)[:, 0:512], in0=bank(pb),
                scalar1=sc[:, 2:3]))(pb),
                deps=[evm], reads=[b_ps[pb], b_sc], writes=[b_sil], signal=True)
            evt = None
            for q in range(4):
                evt = p.do("tensor", (lambda q: lambda e: e.transpose(
                    bankb(6)[:, q * 128:(q + 1) * 128],
                    sil[:].bitcast(BF16)[:, q * 128:(q + 1) * 128],
                    C.identb[:]))(q),
                    deps=[evi, ev], reads=[b_sil], writes=[b_ps[6]],
                    signal=(q == 3))
            for q in range(4):
                p.do("vector", (lambda dc, q: lambda e: e.tensor_copy(
                    out=ytm[:, q * D + dc * 128:q * D + (dc + 1) * 128],
                    in_=bankb(6)[:, q * 128:(q + 1) * 128]))(dc, q),
                    deps=[evt], reads=[b_ps[6]], writes=[b_ytm],
                    signal=(q == 3))
        for q in range(4):
            t = tc * 4 + q
            yt = ytm[:, q * D:(q + 1) * D]
            evd = p.do("sync", (lambda t: lambda e: e.dma_start(
                out=xtb[:], in_=x_in[t * 128:(t + 1) * 128, :]))(t),
                writes=[b_xt], dma=True)
            er, em = ln_stats(p, yt, ut[:, 0:D], mv, sc[:, 9:10],
                              sc[:, 10:11], lnbufs, xbuf=b_ytm,
                              epsc=C.cst[:, 0:1], epsbuf=b_cst)
            ev = p.do("vector", (lambda yt: lambda e: e.tensor_scalar(
                out=yw[:], in0=yt, scalar1=sc[:, 10:11], scalar2=sc[:, 9:10],
                op0=ALU.add, op1=ALU.mult))(yt),
                deps=[er, em], reads=[b_ytm, b_rs], writes=[b_yw], signal=True)
            ev = p.do("vector", lambda e: e.tensor_add(out=yw[:], in0=yw[:], in1=xtb[:]),
                      deps=[evd, ev], reads=[b_xt], writes=[b_yw], signal=True)
            p.do("sync", (lambda t: lambda e: e.dma_start(
                out=out[t * 128:(t + 1) * 128, :], in_=yw[:]))(t),
                deps=[ev], reads=[b_yw], dma=True)

    fin = p.barrier_event("sync")
    p.do("sync", lambda e: None, deps=[fin])


def emit_l0ffn(C, A, x_in, wg_in, wu_in, wd_in, ident_in, out):
    nc, p = C.nc, C.p
    PS = C.PS
    xhat = A.take(16 * TOK, BF16)
    hfm = A.take(NI0 * TOK, BF16)
    arena = A.take(16384, BF16)
    sil = A.take(512, F32)
    hsb = A.take(512, F32)
    hspb = A.take(512, BF16)
    sp = A.take(1408, F32)
    mv = A.take(8, F32)
    sc = A.take(4, F32)
                
    p = C.p
    bank = lambda i: PS[:, i * 512:(i + 1) * 512]
    bankb = lambda i: PS[:, i * 512:(i + 1) * 512].bitcast(BF16)
    arenaF = arena.bitcast(F32)
    hfmF = hfm.bitcast(F32)
    yw = hfmF[:, 0:D]
    xtb = hfmF[:, D:2 * D]
    xhatF = xhat.bitcast(F32)
    wg_v = arena[:, 0:8192].rearrange("p (ic c) -> p ic c", c=512)
    wu_v = arena[:, 8192:16384].rearrange("p (ic c) -> p ic c", c=512)

    b_xhat = TBuf("xhat"); b_hfm = TBuf("hfm")
    b_wg = TBuf("wg"); b_wu = TBuf("wu"); b_wd = [TBuf("wd0"), TBuf("wd1")]
    b_xt = TBuf("xt"); b_y = TBuf("y")
    b_sil = TBuf("sil"); b_hsb = TBuf("hsb"); b_hsp = TBuf("hsp")
    b_sp = TBuf("sp"); b_yw = TBuf("yw"); b_mv = TBuf("mv")
    b_rs = TBuf("rs"); b_cst = C.b_cst
    b_ps = [TBuf(f"ps{i}") for i in range(8)]
    lnbufs = (b_sil, b_mv, b_rs)
    evi = C.evi

    for t in range(NT):
        evd = p.do("sync", (lambda t: lambda e: e.dma_start(
            out=xtb[:], in_=x_in[t * 128:(t + 1) * 128, :]))(t),
            writes=[b_xt], dma=True)
        er, em = ln_stats(p, xtb[:], sil.bitcast(BF16), mv, sc[:, 0:1],
                          sc[:, 1:2], lnbufs, xbuf=b_xt, epsc=C.cst[:, 0:1],
                          epsbuf=b_cst, deps=[evd])
        ev = p.do("vector", lambda e: e.tensor_scalar(
            out=arena[:, 8192:8192 + D],
            in0=xtb[:], scalar1=sc[:, 1:2], scalar2=sc[:, 0:1],
            op0=ALU.add, op1=ALU.mult),
            deps=[er, em], reads=[b_xt, b_rs], writes=[b_wu], signal=True)
        for g in range(2):
            pb = 6 + g
            evg = None
            for dc in range(g * 8, g * 8 + 8):
                evg = p.do("tensor", (lambda dc, g, pb: lambda e: e.transpose(
                    bankb(pb)[:, (dc - g * 8) * 128:(dc - g * 8 + 1) * 128],
                    arena[:, 8192 + dc * 128:8192 + (dc + 1) * 128],
                    C.identb[:]))(dc, g, pb),
                    deps=[evi, ev], reads=[b_wu], writes=[b_ps[pb]],
                    signal=(dc == g * 8 + 7))
            for dc in range(g * 8, g * 8 + 8):
                p.do("vector", (lambda t, dc, pb, g: lambda e: e.tensor_copy(
                    out=xhat[:, dc * TOK + t * 128:dc * TOK + (t + 1) * 128],
                    in_=bankb(pb)[:, (dc - g * 8) * 128:(dc - g * 8 + 1) * 128]))(t, dc, pb, g),
                    deps=[evg], reads=[b_ps[pb]], writes=[b_xhat],
                    signal=(dc == g * 8 + 7))

    p.full_barrier()
    preln_done = [p.barrier_event(e) for e in ("vector", "tensor", "scalar")]
    for oc in range(16):
        eg = p.do("gpsimd", (lambda oc: lambda e: e.dma_start(
            out=wg_v, in_=wg_in[:, oc * 512:(oc + 1) * 512]
            .rearrange("(ic p) c -> p ic c", p=128)))(oc),
            deps=preln_done if oc == 0 else (), writes=[b_wg], dma=True)
        eu = p.do("gpsimd", (lambda oc: lambda e: e.dma_start(
            out=wu_v, in_=wu_in[:, oc * 512:(oc + 1) * 512]
            .rearrange("(ic p) c -> p ic c", p=128)))(oc),
            deps=preln_done if oc == 0 else (), writes=[b_wu], dma=True)
        for t in range(NT):
            pg, pu = (0, 1) if t % 2 == 0 else (2, 3)
            e1 = e2 = None
            for ic in range(16):
                e1 = p.do("tensor", (lambda oc, t, ic, pg: lambda e: e.matmul(
                    bank(pg), xhat[:, ic * TOK + t * 128:ic * TOK + (t + 1) * 128],
                    wg_v[:, ic, :], start=(ic == 0), stop=(ic == 15)))(oc, t, ic, pg),
                    deps=[eg], reads=[b_wg, b_xhat], writes=[b_ps[pg]],
                    signal=(ic == 15))
            for ic in range(16):
                e2 = p.do("tensor", (lambda oc, t, ic, pu: lambda e: e.matmul(
                    bank(pu), xhat[:, ic * TOK + t * 128:ic * TOK + (t + 1) * 128],
                    wu_v[:, ic, :], start=(ic == 0), stop=(ic == 15)))(oc, t, ic, pu),
                    deps=[eu], reads=[b_wu, b_xhat], writes=[b_ps[pu]],
                    signal=(ic == 15))
            es = p.do("scalar", (lambda pg: lambda e: e.activation(
                sil[:], bank(pg), AF.Silu))(pg),
                deps=[e1], reads=[b_ps[pg]], writes=[b_sil], signal=True)
            eh = p.do("vector", (lambda pu: lambda e: e.scalar_tensor_tensor(
                out=hsb[:], in0=bank(pu), scalar=1.0, in1=sil[:],
                op0=ALU.mult, op1=ALU.mult))(pu),
                deps=[es, e2], reads=[b_ps[pu], b_sil], writes=[b_hsb],
                signal=True)
            h3v = hsb[:].rearrange("p (a b) -> p a b", b=2)
            epm = p.do("vector", lambda e: e.reduce_max(
                out=sp[:, 0:256], in_=h3v, axis=AX.X, apply_absolute_value=True),
                deps=[force(eh)], reads=[b_hsb], writes=[b_sp], signal=True)
            epn = p.do("vector", lambda e: e.tensor_reduce(
                out=sp[:, 256:512], in_=h3v, axis=AX.X, op=ALU.min,
                apply_absolute_value=True),
                deps=[force(eh)], reads=[b_hsb], writes=[b_sp], signal=True)
            em1 = p.do("vector", lambda e: e.tensor_reduce(
                out=sp[:, 512:640], in_=sp[:, 0:256].rearrange("p (a b) -> p a b", b=2),
                axis=AX.X, op=ALU.min),
                deps=[force(epm)], writes=[b_sp], signal=True)
            em2 = p.do("vector", lambda e: e.reduce_max(
                out=sp[:, 640:768], in_=sp[:, 256:512].rearrange("p (a b) -> p a b", b=2),
                axis=AX.X),
                deps=[force(epn)], writes=[b_sp], signal=True)
            em3 = p.do("vector", lambda e: e.tensor_max(
                out=sp[:, 768:896], in0=sp[:, 512:640], in1=sp[:, 640:768]),
                deps=[force(em1), force(em2)], writes=[b_sp], signal=True)
            ea = p.do("scalar", lambda e: e.activation(sp[:, 896:1408], hsb[:], AF.Abs),
                      deps=[eh], reads=[b_hsb], writes=[b_sp], signal=True)
            evj = None
            for j in range(4):
                evj = p.do("vector", (lambda j: lambda e: e.tensor_tensor(
                    out=sp[:, 896 + j:1408:4], in0=sp[:, 896 + j:1408:4],
                    in1=sp[:, 768:896], op=ALU.is_ge))(j),
                    deps=[force(em3), force(ea)], writes=[b_sp],
                    signal=(j == 3))
            ehs = p.do("vector", lambda e: e.tensor_mul(
                out=hspb[:], in0=hsb[:], in1=sp[:, 896:1408]),
                deps=[force(evj)], reads=[b_hsb, b_sp], writes=[b_hsp],
                signal=True)
            pb = 6 + t % 2
            evt = None
            for q in range(4):
                evt = p.do("tensor", (lambda q, pb: lambda e: e.transpose(
                    bankb(pb)[:, q * 128:(q + 1) * 128],
                    hspb[:, q * 128:(q + 1) * 128], C.identb[:]))(q, pb),
                    deps=[evi, ehs], reads=[b_hsp], writes=[b_ps[pb]],
                    signal=(q == 3))
            for q in range(4):
                p.do("vector", (lambda oc, t, q, pb: lambda e: e.tensor_copy(
                    out=hfm[:, (4 * oc + q) * TOK + t * 128:(4 * oc + q) * TOK + (t + 1) * 128],
                    in_=bankb(pb)[:, q * 128:(q + 1) * 128]))(oc, t, q, pb),
                    deps=[evt], reads=[b_ps[pb]], writes=[b_hfm],
                    signal=(q == 3))

    p.full_barrier()
    ph1_done = [p.barrier_event(e) for e in ("vector", "tensor", "scalar", "gpsimd")]
    for dc in range(4):
        evm_all = []
        for hc in range(NI0):
            slot = hc % 2
            wd_s = sil.bitcast(BF16)[:, slot * 512:(slot + 1) * 512]
            evd = p.do("gpsimd", (lambda hc, dc, wd_s: lambda e: e.dma_start(
                out=wd_s, in_=wd_in[hc * 128:(hc + 1) * 128,
                                    dc * 512:(dc + 1) * 512]))(hc, dc, wd_s),
                deps=ph1_done if (dc == 0 and hc < 2) else (),
                writes=[b_wd[slot]], dma=True)
            for t in range(NT):
                evm = p.do("tensor", (lambda hc, t, wd_s: lambda e: e.matmul(
                    bank(t), hfm[:, hc * TOK + t * 128:hc * TOK + (t + 1) * 128],
                    wd_s, start=(hc == 0), stop=(hc == NI0 - 1)))(hc, t, wd_s),
                    deps=[evd], reads=[b_wd[slot], b_hfm],
                    writes=[b_ps[t]], signal=(hc == NI0 - 1 or t == NT - 1))
                if hc == NI0 - 1:
                    evm_all.append(evm)
        for t in range(NT):
            ytgt = xhatF if t < 4 else arenaF
            p.do("vector", (lambda t, dc, ytgt: lambda e: e.tensor_copy(
                out=ytgt[:, (t % 4) * D + dc * 512:(t % 4) * D + dc * 512 + 512],
                in_=bank(t)))(t, dc, ytgt),
                deps=[evm_all[t]], reads=[b_ps[t]], writes=[b_y], signal=True)

    p.full_barrier()
    ph2_done = [p.barrier_event("tensor"), p.barrier_event("vector"),
                p.barrier_event("gpsimd")]
    for t in range(NT):
        ytgt = xhatF if t < 4 else arenaF
        yt = ytgt[:, (t % 4) * D:(t % 4 + 1) * D]
        evd = p.do("sync", (lambda t: lambda e: e.dma_start(
            out=xtb[:], in_=x_in[t * 128:(t + 1) * 128, :]))(t),
            deps=ph2_done if t == 0 else (), writes=[b_xt], dma=True)
        er, em = ln_stats(p, yt, sil.bitcast(BF16), mv, sc[:, 0:1],
                          sc[:, 1:2], lnbufs, xbuf=b_y, epsc=C.cst[:, 0:1],
                          epsbuf=b_cst, deps=ph2_done if t == 0 else ())
        ev = p.do("vector", (lambda yt: lambda e: e.tensor_scalar(
            out=yw[:], in0=yt, scalar1=sc[:, 1:2], scalar2=sc[:, 0:1],
            op0=ALU.add, op1=ALU.mult))(yt),
            deps=[er, em], reads=[b_y, b_rs], writes=[b_yw], signal=True)
        ev = p.do("vector", lambda e: e.tensor_add(out=yw[:], in0=yw[:], in1=xtb[:]),
                  deps=[evd, force(ev)], reads=[b_xt], writes=[b_yw], signal=True)
        p.do("sync", (lambda t: lambda e: e.dma_start(
            out=out[t * 128:(t + 1) * 128, :], in_=yw[:]))(t),
            deps=[ev], reads=[b_yw], dma=True)

    fin = p.barrier_event("sync")
    p.do("sync", lambda e: None, deps=[fin])


def emit_l1attn(C, A, xg, xo, qw_in, krw_in, dww_in, upk_in, upv_in, ow_in, mask_in, ckc_in, cks_in, cqc_in, cqs_in, ident_in, ones_in, out):
    nc, p = C.nc, C.p
    PS = C.PS
    xhat = A.take(16 * S, BF16)
    vbuf = A.take(16 * 520, BF16)
    ckv = A.take(2 * S, BF16)
    krf = A.take(2 * S, BF16)
    kcf = A.take(2 * S, BF16)
    qro = A.take(8 * TOK, BF16)
    qno = A.take(8 * TOK, BF16)
    maskb = A.take(16 * 512, BF16)
    wstage = A.take(8192, BF16)
    piece = A.take(1024, BF16)
    krT = A.take(16 * 256, BF16)
    dwT = A.take(16 * 256, BF16)
    upkT = A.take(2 * 256, BF16)
    upvT = A.take(2 * 512, BF16)
    ckc = A.take(16 * 16, F32)
    cks = A.take(16 * 16, F32)
    cqc = A.take(8 * 32, F32)
    cqs = A.take(8 * 32, F32)
    ptl = A.take(2 * 512, BF16)
    qtm = A.take(512, BF16)
    rsc = A.take(512, F32)
    rb = A.take(512, F32)
    mv = A.take(8, F32)
    sc = A.take(4, F32)
                    
    p = C.p
    bank = lambda i: PS[:, i * 512:(i + 1) * 512]
    bankb = lambda i: PS[:, i * 512:(i + 1) * 512].bitcast(BF16)
    attn_fm = xhat[:, 0:16 * TOK]
    ybf = xhat[:, 16 * TOK:32 * TOK]
    qroF = qro.bitcast(F32)
    xtb = qroF[:, 0:D]
    yw = qroF[:, D:2 * D]

    names = ["xhat", "vbuf", "ckv", "krf", "kcf", "qro", "qno", "maskb",
             "wstage", "piece", "krT", "dwT", "upkT", "upvT", "crep",
             "ptl", "qtm", "rsc", "xtb", "yw", "lrow", "rb", "mv", "rs",
             "C.cst", "attn", "ybf"]
    bufs = {n: TBuf(n) for n in names}
    b_ps = [TBuf(f"ps{i}") for i in range(8)]
    bufs["cst"] = C.b_cst
    bu = lambda n: bufs[n]
    lnbufs = (bu("rsc"), bu("mv"), bu("rs"))
    evi = C.evi
    evo = C.evo
    p.do("vector", lambda e: e.memset(vbuf[:], 1.0), writes=[bu("vbuf")], signal=True)
    for (t_in, t_sb, nj, ntt) in [(ckc_in, ckc, 16, 16), (cks_in, cks, 16, 16),
                                  (cqc_in, cqc, 32, 8), (cqs_in, cqs, 32, 8)]:
        p.do("sync", (lambda t_in, t_sb, nj, ntt: lambda e: e.dma_start(
            out=t_sb[:].rearrange("p (tt j) -> p tt j", tt=ntt),
            in_=t_in.rearrange("(tt p) j -> p tt j", p=128)))(t_in, t_sb, nj, ntt),
            writes=[bu("crep")], dma=True)

    def prep_T(w_in, dst, nout, nin, dstbuf):
        for r in range(nout // 128):
            evd = p.do("gpsimd", (lambda w_in, r, nin: lambda e: e.dma_start(
                out=wstage[:, 0:nin], in_=w_in[r * 128:(r + 1) * 128, :]))(w_in, r, nin),
                writes=[bu("wstage")], dma=True)
            for g in range(0, nin // 128, 8):
                g1 = min(g + 8, nin // 128)
                pb = (g // 8) % 2
                evg = None
                for ic in range(g, g1):
                    evg = p.do("tensor", (lambda ic, g, pb: lambda e: e.transpose(
                        bankb(pb)[:, (ic - g) * 128:(ic - g + 1) * 128],
                        wstage[:, ic * 128:(ic + 1) * 128], C.identb[:]))(ic, g, pb),
                        deps=[evi, evd], reads=[bu("wstage")], writes=[b_ps[pb]],
                        signal=(ic == g1 - 1))
                for ic in range(g, g1):
                    p.do("vector", (lambda ic, g, pb, r, nout: lambda e: e.tensor_copy(
                        out=dst[:, ic * nout + r * 128:ic * nout + (r + 1) * 128],
                        in_=bankb(pb)[:, (ic - g) * 128:(ic - g + 1) * 128]))(ic, g, pb, r, nout),
                        deps=[evg], reads=[b_ps[pb]], writes=[dstbuf],
                        signal=(ic == g1 - 1))
    prep_T(krw_in, krT, 256, 2048, bu("krT"))
    prep_T(dww_in, dwT, 256, 2048, bu("dwT"))
    prep_T(upk_in, upkT, 256, 256, bu("upkT"))
    prep_T(upv_in, upvT, 512, 256, bu("upvT"))

    for t in range(16):
        evd = p.do("sync", (lambda t: lambda e: e.dma_start(
            out=xtb[:], in_=xg[t * 128:(t + 1) * 128, :]))(t),
            writes=[bu("xtb")], dma=True)
        er, em = ln_stats(p, xtb[:], rsc[:].bitcast(BF16)[:, 0:512], mv,
                          sc[:, 0:1], sc[:, 1:2], lnbufs, xbuf=bu("xtb"),
                          epsc=C.cst[:, 0:1], epsbuf=bu("C.cst"), deps=[evd])
        ev = p.do("vector", lambda e: e.tensor_scalar(
            out=wstage[:, 4096:4096 + D], in0=xtb[:], scalar1=sc[:, 1:2],
            scalar2=sc[:, 0:1], op0=ALU.add, op1=ALU.mult),
            deps=[er, em], reads=[bu("xtb"), bu("rs")], writes=[bu("wstage")],
            signal=True)
        for g in range(2):
            pb = 6 + g
            evg = None
            for dc in range(g * 8, g * 8 + 8):
                evg = p.do("tensor", (lambda dc, g, pb: lambda e: e.transpose(
                    bankb(pb)[:, (dc - g * 8) * 128:(dc - g * 8 + 1) * 128],
                    wstage[:, 4096 + dc * 128:4096 + (dc + 1) * 128],
                    C.identb[:]))(dc, g, pb),
                    deps=[evi, ev], reads=[bu("wstage")], writes=[b_ps[pb]],
                    signal=(dc == g * 8 + 7))
            for dc in range(g * 8, g * 8 + 8):
                p.do("vector", (lambda t, dc, pb, g: lambda e: e.tensor_copy(
                    out=xhat[:, dc * S + t * 128:dc * S + (t + 1) * 128],
                    in_=bankb(pb)[:, (dc - g * 8) * 128:(dc - g * 8 + 1) * 128]))(t, dc, pb, g),
                    deps=[evg], reads=[b_ps[pb]], writes=[bu("xhat")],
                    signal=(dc == g * 8 + 7))

    p.full_barrier()
    for m in range(2):
        for ts in range(4):
            evm = None
            for ic in range(16):
                evm = p.do("tensor", (lambda m, ts, ic: lambda e: e.matmul(
                    bank(m), dwT[:, ic * 256 + m * 128:ic * 256 + (m + 1) * 128],
                    xhat[:, ic * S + ts * 512:ic * S + (ts + 1) * 512],
                    start=(ic == 0), stop=(ic == 15)))(m, ts, ic),
                    reads=[bu("dwT"), bu("xhat")], writes=[b_ps[m]],
                    signal=(ic == 15))
            p.do("vector", (lambda m, ts: lambda e: e.tensor_copy(
                out=ckv[:, m * S + ts * 512:m * S + (ts + 1) * 512],
                in_=bank(m)))(m, ts),
                deps=[evm], reads=[b_ps[m]], writes=[bu("ckv")], signal=True)

    from concourse.bass import broadcast_tensor_aps
    for t in range(16):
        evm = None
        for ic in range(16):
            evm = p.do("tensor", (lambda t, ic: lambda e: e.matmul(
                PS[:, 2048:2304], xhat[:, ic * S + t * 128:ic * S + (t + 1) * 128],
                krT[:, ic * 256:(ic + 1) * 256],
                start=(ic == 0), stop=(ic == 15)))(t, ic),
                reads=[bu("krT"), bu("xhat")], writes=[b_ps[4]],
                signal=(ic == 15))
        kr_ps = PS[:, 2048:2304].rearrange("p (h two j) -> p h two j", two=2, j=16)
        av, bv = kr_ps[:, :, 0, :], kr_ps[:, :, 1, :]
        ct = ckc[:, t * 16:(t + 1) * 16].rearrange("p (h j) -> p h j", h=1)
        st = cks[:, t * 16:(t + 1) * 16].rearrange("p (h j) -> p h j", h=1)
        _, ctb = broadcast_tensor_aps(av, ct)
        _, stb = broadcast_tensor_aps(av, st)
        qv = qtm[:, 0:256].rearrange("p (h two j) -> p h two j", two=2, j=16)
        x1o, x2o = qv[:, :, 0, :], qv[:, :, 1, :]
        wsv = rsc[:, 0:256].rearrange("p (h two j) -> p h two j", two=2, j=16)
        t1v, t2v = wsv[:, :, 0, :], wsv[:, :, 1, :]
        e1 = p.do("vector", (lambda av, ctb, x1o: lambda e: e.tensor_tensor(
            out=x1o, in0=av, in1=ctb, op=ALU.mult))(av, ctb, x1o),
            deps=[evm], reads=[bu("crep"), b_ps[4]], writes=[bu("qtm")], signal=True)
        e1 = p.do("vector", (lambda av, stb, x2o: lambda e: e.tensor_tensor(
            out=x2o, in0=av, in1=stb, op=ALU.mult))(av, stb, x2o),
            reads=[bu("crep"), b_ps[4]], writes=[bu("qtm")], signal=True)
        e2 = p.do("vector", (lambda bv, stb, t1v: lambda e: e.tensor_tensor(
            out=t1v, in0=bv, in1=stb, op=ALU.mult))(bv, stb, t1v),
            reads=[bu("crep"), b_ps[4]], writes=[bu("rsc")], signal=True)
        e2 = p.do("vector", (lambda bv, ctb, t2v: lambda e: e.tensor_tensor(
            out=t2v, in0=bv, in1=ctb, op=ALU.mult))(bv, ctb, t2v),
            reads=[bu("crep"), b_ps[4]], writes=[bu("rsc")], signal=True)
        e3 = p.do("vector", (lambda x1o, t1v: lambda e: e.tensor_tensor(
            out=x1o, in0=x1o, in1=t1v, op=ALU.subtract))(x1o, t1v),
            deps=[force(e1), force(e2)], reads=[bu("rsc")], writes=[bu("qtm")], signal=True)
        e3 = p.do("vector", (lambda x2o, t2v: lambda e: e.tensor_tensor(
            out=x2o, in0=x2o, in1=t2v, op=ALU.add))(x2o, t2v),
            deps=[force(e3)], reads=[bu("rsc")], writes=[bu("qtm")], signal=True)
        evt = None
        for cc2 in range(2):
            evt = p.do("tensor", (lambda cc2: lambda e: e.transpose(
                bankb(5)[:, cc2 * 128:(cc2 + 1) * 128],
                qtm[:, cc2 * 128:(cc2 + 1) * 128], C.identb[:]))(cc2),
                deps=[evi, e3], reads=[bu("qtm")], writes=[b_ps[5]],
                signal=(cc2 == 1))
        for cc2 in range(2):
            p.do("vector", (lambda t, cc2: lambda e: e.tensor_copy(
                out=krf[:, cc2 * S + t * 128:cc2 * S + (t + 1) * 128],
                in_=bankb(5)[:, cc2 * 128:(cc2 + 1) * 128]))(t, cc2),
                deps=[evt], reads=[b_ps[5]], writes=[bu("krf")],
                signal=(cc2 == 1))

    for m in range(2):
        for ts in range(4):
            evm = None
            for ic in range(2):
                evm = p.do("tensor", (lambda m, ts, ic: lambda e: e.matmul(
                    bank(2 + m), upkT[:, ic * 256 + m * 128:ic * 256 + (m + 1) * 128],
                    ckv[:, ic * S + ts * 512:ic * S + (ts + 1) * 512],
                    start=(ic == 0), stop=(ic == 1)))(m, ts, ic),
                    reads=[bu("upkT"), bu("ckv")], writes=[b_ps[2 + m]],
                    signal=(ic == 1))
            p.do("vector", (lambda m, ts: lambda e: e.tensor_copy(
                out=kcf[:, m * S + ts * 512:m * S + (ts + 1) * 512],
                in_=bank(2 + m)))(m, ts),
                deps=[evm], reads=[b_ps[2 + m]], writes=[bu("kcf")], signal=True)

    for kt in range(16):
        evm = None
        for ic in range(2):
            evm = p.do("tensor", (lambda kt, ic: lambda e: e.matmul(
                bank(6), ckv[:, ic * S + kt * 128:ic * S + (kt + 1) * 128],
                upvT[:, ic * 512:(ic + 1) * 512],
                start=(ic == 0), stop=(ic == 1)))(kt, ic),
                reads=[bu("upvT"), bu("ckv")], writes=[b_ps[6]],
                signal=(ic == 1))
        p.do("vector", (lambda kt: lambda e: e.tensor_copy(
            out=vbuf[:, kt * 520:(kt + 1) * 520]
            .rearrange("p (h e) -> p h e", e=65)[:, :, 0:64],
            in_=bank(6).rearrange("p (h d) -> p h d", d=64)))(kt),
            deps=[evm], reads=[b_ps[6]], writes=[bu("vbuf")], signal=True)

    p.full_barrier()
    for t in range(NT):
        evd = p.do("sync", (lambda t: lambda e: e.dma_start(
            out=xtb[:], in_=xo[t * 128:(t + 1) * 128, :]))(t),
            writes=[bu("xtb")], dma=True)
        er, em = ln_stats(p, xtb[:], rsc[:].bitcast(BF16)[:, 0:512], mv,
                          sc[:, 0:1], sc[:, 1:2], lnbufs, xbuf=bu("xtb"),
                          epsc=C.cst[:, 0:1], epsbuf=C.b_cst, deps=[evd])
        ev = p.do("vector", lambda e: e.tensor_scalar(
            out=wstage[:, 4096:4096 + D], in0=xtb[:], scalar1=sc[:, 1:2],
            scalar2=sc[:, 0:1], op0=ALU.add, op1=ALU.mult),
            deps=[er, em], reads=[bu("xtb"), bu("rs")], writes=[bu("wstage")],
            signal=True)
        for g in range(2):
            pb = 6 + g
            evg = None
            for dc in range(g * 8, g * 8 + 8):
                evg = p.do("tensor", (lambda dc, g, pb: lambda e: e.transpose(
                    bankb(pb)[:, (dc - g * 8) * 128:(dc - g * 8 + 1) * 128],
                    wstage[:, 4096 + dc * 128:4096 + (dc + 1) * 128],
                    C.identb[:]))(dc, g, pb),
                    deps=[evi, ev], reads=[bu("wstage")], writes=[b_ps[pb]],
                    signal=(dc == g * 8 + 7))
            for dc in range(g * 8, g * 8 + 8):
                p.do("vector", (lambda t, dc, pb, g: lambda e: e.tensor_copy(
                    out=xhat[:, dc * TOK + t * 128:dc * TOK + (t + 1) * 128],
                    in_=bankb(pb)[:, (dc - g * 8) * 128:(dc - g * 8 + 1) * 128]))(t, dc, pb, g),
                    deps=[evg], reads=[b_ps[pb]], writes=[bu("xhat")],
                    signal=(dc == g * 8 + 7))

    for oc in range(4):
        evd = p.do("gpsimd", (lambda oc: lambda e: e.dma_start(
            out=wstage[:, 0:8192].rearrange("p (r c) -> p r c", r=4),
            in_=qw_in[oc * 512:(oc + 1) * 512, :]
            .rearrange("(r p) c -> p r c", p=128)))(oc),
            writes=[bu("wstage")], dma=True)
        for tg in range(2):
            for ic in range(16):
                pcb = ic % 2
                evt = None
                for r in range(4):
                    evt = p.do("tensor", (lambda ic, r, pcb: lambda e: e.transpose(
                        bankb(4 + pcb)[:, r * 128:(r + 1) * 128],
                        wstage[:, r * 2048 + ic * 128:r * 2048 + (ic + 1) * 128],
                        C.identb[:]))(ic, r, pcb),
                        deps=[evi, evd], reads=[bu("wstage")], writes=[b_ps[4 + pcb]],
                        signal=(r == 3))
                evp = p.do("vector", (lambda pcb: lambda e: e.tensor_copy(
                    out=piece[:, pcb * 512:(pcb + 1) * 512],
                    in_=bankb(4 + pcb)[:, 0:512]))(pcb),
                    deps=[evt], reads=[b_ps[4 + pcb]], writes=[bu("piece")], signal=True)
                for tl in range(4):
                    t = tg * 4 + tl
                    p.do("tensor", (lambda ic, t, tl, pcb: lambda e: e.matmul(
                        bank(tl), xhat[:, ic * TOK + t * 128:ic * TOK + (t + 1) * 128],
                        piece[:, pcb * 512:(pcb + 1) * 512],
                        start=(ic == 0), stop=(ic == 15)))(ic, t, tl, pcb),
                        deps=[evp], reads=[bu("piece"), bu("xhat")],
                        writes=[b_ps[tl]], signal=(ic == 15 or tl == 3))
            for tl in range(4):
                t = tg * 4 + tl
                qp = bank(tl).rearrange("p (h two j) -> p h two j", two=2, j=32)
                av, bv = qp[:, :, 0, :], qp[:, :, 1, :]
                ct = cqc[:, t * 32:(t + 1) * 32].rearrange("p (h j) -> p h j", h=1)
                st = cqs[:, t * 32:(t + 1) * 32].rearrange("p (h j) -> p h j", h=1)
                _, ctb = broadcast_tensor_aps(av, ct)
                _, stb = broadcast_tensor_aps(av, st)
                wsv = rsc[:].rearrange("p (h two j) -> p h two j", two=2, j=32)
                t1v, t2v = wsv[:, :, 0, :], wsv[:, :, 1, :]
                e1 = p.do("vector", (lambda av, ctb, t1v: lambda e: e.tensor_tensor(
                    out=t1v, in0=av, in1=ctb, op=ALU.mult))(av, ctb, t1v),
                    reads=[bu("crep"), b_ps[tl]], writes=[bu("rsc")], signal=True)
                e1 = p.do("vector", (lambda bv, stb, t2v: lambda e: e.tensor_tensor(
                    out=t2v, in0=bv, in1=stb, op=ALU.mult))(bv, stb, t2v),
                    reads=[bu("crep"), b_ps[tl]], writes=[bu("rsc")], signal=True)
                qv = qtm[:].rearrange("p (h q4 j) -> p h q4 j", q4=4, j=16)
                ev1 = p.do("vector", (lambda t1v, t2v, qv: lambda e: e.tensor_tensor(
                    out=qv[:, :, 0, :],
                    in0=t1v.rearrange("p h (s j) -> p h s j", s=2)[:, :, 0, :],
                    in1=t2v.rearrange("p h (s j) -> p h s j", s=2)[:, :, 0, :],
                    op=ALU.subtract))(t1v, t2v, qv),
                    deps=[force(e1)], reads=[bu("rsc")], writes=[bu("qtm")], signal=True)
                ev1 = p.do("vector", (lambda t1v, t2v, qv: lambda e: e.tensor_tensor(
                    out=qv[:, :, 2, :],
                    in0=t1v.rearrange("p h (s j) -> p h s j", s=2)[:, :, 1, :],
                    in1=t2v.rearrange("p h (s j) -> p h s j", s=2)[:, :, 1, :],
                    op=ALU.subtract))(t1v, t2v, qv),
                    reads=[bu("rsc")], writes=[bu("qtm")], signal=True)
                e2 = p.do("vector", (lambda av, stb, t1v: lambda e: e.tensor_tensor(
                    out=t1v, in0=av, in1=stb, op=ALU.mult))(av, stb, t1v),
                    deps=[force(ev1)], reads=[bu("crep"), b_ps[tl]],
                    writes=[bu("rsc")], signal=True)
                e2 = p.do("vector", (lambda bv, ctb, t2v: lambda e: e.tensor_tensor(
                    out=t2v, in0=bv, in1=ctb, op=ALU.mult))(bv, ctb, t2v),
                    reads=[bu("crep"), b_ps[tl]], writes=[bu("rsc")], signal=True)
                ev2 = p.do("vector", (lambda t1v, t2v, qv: lambda e: e.tensor_tensor(
                    out=qv[:, :, 1, :],
                    in0=t1v.rearrange("p h (s j) -> p h s j", s=2)[:, :, 0, :],
                    in1=t2v.rearrange("p h (s j) -> p h s j", s=2)[:, :, 0, :],
                    op=ALU.add))(t1v, t2v, qv),
                    deps=[force(e2)], reads=[bu("rsc")], writes=[bu("qtm")], signal=True)
                ev2 = p.do("vector", (lambda t1v, t2v, qv: lambda e: e.tensor_tensor(
                    out=qv[:, :, 3, :],
                    in0=t1v.rearrange("p h (s j) -> p h s j", s=2)[:, :, 1, :],
                    in1=t2v.rearrange("p h (s j) -> p h s j", s=2)[:, :, 1, :],
                    op=ALU.add))(t1v, t2v, qv),
                    reads=[bu("rsc")], writes=[bu("qtm")], signal=True)
                evt = None
                for r4 in range(4):
                    evt = p.do("tensor", (lambda r4: lambda e: e.transpose(
                        bankb(7)[:, r4 * 128:(r4 + 1) * 128],
                        qtm[:, r4 * 128:(r4 + 1) * 128], C.identb[:]))(r4),
                        deps=[evi, force(ev1), force(ev2)], reads=[bu("qtm")],
                        writes=[b_ps[7]], signal=(r4 == 3))
                for hh in range(8):
                    qh = oc * 8 + hh
                    h = qh // 4
                    rbase = (h % 4) * 32
                    cblk = (h // 4) * 4 + (qh % 4)
                    prow = (hh % 2) * 64
                    pcol = (hh // 2) * 128
                    p.do("vector", (lambda prow, pcol, rbase, cblk, t: lambda e: e.tensor_copy(
                        out=qro[rbase:rbase + 32, cblk * TOK + t * 128:cblk * TOK + (t + 1) * 128],
                        in_=bankb(7)[prow:prow + 32, pcol:pcol + 128]))(prow, pcol, rbase, cblk, t),
                        deps=[evt], reads=[b_ps[7]], writes=[bu("qro")],
                        signal=(hh == 7))
                    p.do("vector", (lambda prow, pcol, rbase, cblk, t: lambda e: e.tensor_copy(
                        out=qno[rbase:rbase + 32, cblk * TOK + t * 128:cblk * TOK + (t + 1) * 128],
                        in_=bankb(7)[prow + 32:prow + 64, pcol:pcol + 128]))(prow, pcol, rbase, cblk, t),
                        deps=[evt], reads=[b_ps[7]], writes=[bu("qno")],
                        signal=(hh == 7))

    p.full_barrier()
    SCL = float(HD_L1) ** -0.5
    for qc in range(2):
        evmk = p.do("sync", (lambda qc: lambda e: e.dma_start(
            out=maskb[:].rearrange("p (tt c) -> p tt c", tt=16),
            in_=mask_in[:, qc * 512:(qc + 1) * 512]
            .rearrange("(tt p) c -> p tt c", p=128)))(qc),
            writes=[bu("maskb")], dma=True)
        for qh in range(NH_L1):
            h = qh // 4
            rbase = (h % 4) * 32
            cc = h // 4
            cblk = (h // 4) * 4 + (qh % 4)
            pav = 2 + (qh % 2)
            for kt in range(16):
                pbs = kt % 2
                tp = (rbase, 0) if rbase == 96 else None
                ev1 = p.do("tensor", (lambda rbase, cc, cblk, kt, qc, pbs, tp: lambda e: e.matmul(
                    bank(pbs), krf[rbase:rbase + 32, cc * S + kt * 128:cc * S + (kt + 1) * 128],
                    qro[rbase:rbase + 32, cblk * TOK + qc * 512:cblk * TOK + (qc + 1) * 512],
                    start=True, stop=False, tile_position=tp))(rbase, cc, cblk, kt, qc, pbs, tp),
                    reads=[bu("krf"), bu("qro")], writes=[b_ps[pbs]])
                ev1 = p.do("tensor", (lambda rbase, cc, cblk, kt, qc, pbs, tp: lambda e: e.matmul(
                    bank(pbs), kcf[rbase:rbase + 32, cc * S + kt * 128:cc * S + (kt + 1) * 128],
                    qno[rbase:rbase + 32, cblk * TOK + qc * 512:cblk * TOK + (qc + 1) * 512],
                    start=False, stop=True, tile_position=tp))(rbase, cc, cblk, kt, qc, pbs, tp),
                    reads=[bu("kcf"), bu("qno")], writes=[b_ps[pbs]], signal=True)
                ev2 = p.do("scalar", (lambda pbs, kt: lambda e: e.activation(
                    ptl[:, (kt % 2) * 512:(kt % 2 + 1) * 512], bank(pbs),
                    AF.Exp, scale=SCL))(pbs, kt),
                    deps=[ev1], reads=[b_ps[pbs]], writes=[bu("ptl")], signal=True)
                ev3 = p.do("vector", (lambda kt: lambda e: e.tensor_mul(
                    out=ptl[:, (kt % 2) * 512:(kt % 2 + 1) * 512],
                    in0=ptl[:, (kt % 2) * 512:(kt % 2 + 1) * 512],
                    in1=maskb[:, kt * 512:(kt + 1) * 512]))(kt),
                    deps=[ev2, evmk], reads=[bu("maskb")], writes=[bu("ptl")],
                    signal=True)
                p.do("tensor", (lambda kt, h, pav: lambda e: e.matmul(
                    PS[0:65, pav * 512:(pav + 1) * 512],
                    vbuf[:, kt * 520 + h * 65:kt * 520 + (h + 1) * 65],
                    ptl[:, (kt % 2) * 512:(kt % 2 + 1) * 512],
                    start=(kt == 0), stop=(kt == 15)))(kt, h, pav),
                    deps=[ev3], reads=[bu("vbuf"), bu("ptl")],
                    writes=[b_ps[pav]], signal=True)
            evc = p.do("vector", (lambda pav: lambda e: e.tensor_copy(
                out=rb[0:1, 0:512], in_=PS[64:65, pav * 512:(pav + 1) * 512]))(pav),
                reads=[b_ps[pav]], writes=[bu("rb")], signal=True)
            evc = p.do("vector", lambda e: e.reciprocal(out=rb[0:1, 0:512], in_=rb[0:1, 0:512]),
                       deps=[force(evc)], writes=[bu("rb")], signal=True)
            evb = p.do("tensor", lambda e: e.matmul(
                bank(6)[0:64, :], C.onesb[0:1, 0:64], rb[0:1, 0:512],
                start=True, stop=True),
                deps=[evc, evo], reads=[bu("rb")], writes=[b_ps[6]], signal=True)
            evb = p.do("vector", lambda e: e.tensor_copy(out=rb[0:64, 0:512], in_=bank(6)[0:64, :]),
                       deps=[evb], reads=[b_ps[6]], writes=[bu("rb")], signal=True)
            arow = (qh % 2) * 64
            acol = (qh // 2) * TOK
            p.do("vector", (lambda arow, acol, qc, pav: lambda e: e.tensor_tensor(
                out=attn_fm[arow:arow + 64, acol + qc * 512:acol + (qc + 1) * 512],
                in0=PS[0:64, pav * 512:(pav + 1) * 512], in1=rb[0:64, 0:512],
                op=ALU.mult))(arow, acol, qc, pav),
                deps=[evb], reads=[b_ps[pav], bu("rb")], writes=[bu("attn")],
                signal=True)

    p.full_barrier()
    for oc in range(4):
        evd = p.do("gpsimd", (lambda oc: lambda e: e.dma_start(
            out=wstage[:, 0:8192].rearrange("p (r c) -> p r c", r=4),
            in_=ow_in[oc * 512:(oc + 1) * 512, :]
            .rearrange("(r p) c -> p r c", p=128)))(oc),
            writes=[bu("wstage")], dma=True)
        for tg in range(2):
            for ic in range(16):
                pcb = ic % 2
                evt = None
                for r in range(4):
                    evt = p.do("tensor", (lambda ic, r, pcb: lambda e: e.transpose(
                        bankb(4 + pcb)[:, r * 128:(r + 1) * 128],
                        wstage[:, r * 2048 + ic * 128:r * 2048 + (ic + 1) * 128],
                        C.identb[:]))(ic, r, pcb),
                        deps=[evi, evd], reads=[bu("wstage")], writes=[b_ps[4 + pcb]],
                        signal=(r == 3))
                evp = p.do("vector", (lambda pcb: lambda e: e.tensor_copy(
                    out=piece[:, pcb * 512:(pcb + 1) * 512],
                    in_=bankb(4 + pcb)[:, 0:512]))(pcb),
                    deps=[evt], reads=[b_ps[4 + pcb]], writes=[bu("piece")], signal=True)
                for tl in range(4):
                    t = tg * 4 + tl
                    p.do("tensor", (lambda ic, t, tl, pcb: lambda e: e.matmul(
                        bank(tl), attn_fm[:, ic * TOK + t * 128:ic * TOK + (t + 1) * 128],
                        piece[:, pcb * 512:(pcb + 1) * 512],
                        start=(ic == 0), stop=(ic == 15)))(ic, t, tl, pcb),
                        deps=[evp], reads=[bu("piece"), bu("attn")],
                        writes=[b_ps[tl]], signal=(ic == 15 or tl == 3))
            for tl in range(4):
                t = tg * 4 + tl
                p.do("vector", (lambda t, tl, oc: lambda e: e.tensor_copy(
                    out=ybf[:, t * D + oc * 512:t * D + (oc + 1) * 512],
                    in_=bank(tl)))(t, tl, oc),
                    reads=[b_ps[tl]], writes=[bu("ybf")], signal=True)

    p.full_barrier()
    for t in range(NT):
        yt = ybf[:, t * D:(t + 1) * D]
        evd = p.do("sync", (lambda t: lambda e: e.dma_start(
            out=xtb[:], in_=xo[t * 128:(t + 1) * 128, :]))(t),
            writes=[bu("xtb")], dma=True)
        er, em = ln_stats(p, yt, rsc[:].bitcast(BF16)[:, 0:512], mv,
                          sc[:, 0:1], sc[:, 1:2], lnbufs, xbuf=bu("ybf"),
                          epsc=C.cst[:, 0:1], epsbuf=bu("C.cst"))
        ev = p.do("vector", (lambda yt: lambda e: e.tensor_scalar(
            out=yw[:], in0=yt, scalar1=sc[:, 1:2], scalar2=sc[:, 0:1],
            op0=ALU.add, op1=ALU.mult))(yt),
            deps=[er, em], reads=[bu("ybf"), bu("rs")], writes=[bu("yw")], signal=True)
        ev = p.do("vector", lambda e: e.tensor_add(out=yw[:], in0=yw[:], in1=xtb[:]),
                  deps=[evd, force(ev)], reads=[bu("xtb")], writes=[bu("yw")], signal=True)
        p.do("sync", (lambda t: lambda e: e.dma_start(
            out=out[t * 128:(t + 1) * 128, :], in_=yw[:]))(t),
            deps=[ev], reads=[bu("yw")], dma=True)

    fin = p.barrier_event("sync")
    p.do("sync", lambda e: None, deps=[fin])


def emit_l0attn(C, A, xg, xo, wdkv_in, wuk_in, wuv_in, wdq_in, wuq_in, wqr_in, wkr_in, wout_in, delta_in, mask_in, c0_in, s0_in, c0q_in, s0q_in, ident_in, ones_in, out):
    nc, p = C.nc, C.p
    PS = C.PS
    I32 = mybir.dt.int32
    xhat = A.take(16 * S, BF16)
    ckv = A.take(4 * S, BF16)
    cq = A.take(8 * TOK, BF16)
    qrf = A.take(8 * TOK, BF16)
    krfb = A.take(S, BF16)
    kcb = A.take(S, BF16)
    qcb = A.take(TOK, BF16)
    vg = A.take(16 * 512, BF16)
    wstage = A.take(8192, BF16)
    maskb = A.take(2 * 16 * 512, BF16)
    ctf = A.take(512, F32)
    stf = A.take(512, F32)
    ctfq = A.take(256, F32)
    stfq = A.take(256, F32)
    rsc = A.take(2048, F32)
    qtm = A.take(1024, BF16)
    ptl = qtm
    rb = A.take(512, F32)
    dsc = A.take(256, F32)
    dsi = A.take(32, I32)
    cdb = A.take(64, F32)
    mv = A.take(8, F32)
    sc = A.take(4, F32)
                        
    p = C.p
    bank = lambda i: PS[:, i * 512:(i + 1) * 512]
    bankb = lambda i: PS[:, i * 512:(i + 1) * 512].bitcast(BF16)
    attn_fm = xhat[:, 0:16 * TOK]
    ybf = xhat[:, 16 * TOK:32 * TOK]
    cqF = cq.bitcast(F32)
    xtb = cqF[:, 0:D]
    yw = cqF[:, D:2 * D]
    wstF = wstage.bitcast(F32)
    c0t = wstF[:, 0:512]
    s0t = wstF[:, 512:1024]

    names = ["xhat", "ckv", "cq", "qrf", "krfb", "kcb", "qcb", "vg",
             "wstage", "maskb", "tab", "rsc", "qtm", "ptl", "rb",
             "dsc", "cdb", "mv", "rs", "C.cst", "attn", "ybf", "xtb", "yw",
             "ones"]
    bufs = {n: TBuf(n) for n in names}
    b_ps = [TBuf(f"ps{i}") for i in range(8)]
    bufs["cst"] = C.b_cst
    bu = lambda n: bufs[n]
    lnbufs = (bu("rsc"), bu("mv"), bu("rs"))
    TWO_PI = float(2.0 * np.pi)
    evi = C.evi
    evo = C.evo
    evc0 = p.do("sync", lambda e: e.dma_start(
        out=c0t.rearrange("p (tt j) -> p tt j", j=32),
        in_=c0_in.rearrange("(tt p) j -> p tt j", p=128)), writes=[bu("wstage")], dma=True)
    evs0 = p.do("sync", lambda e: e.dma_start(
        out=s0t.rearrange("p (tt j) -> p tt j", j=32),
        in_=s0_in.rearrange("(tt p) j -> p tt j", p=128)), writes=[bu("wstage")], dma=True)
    evd = p.do("sync", lambda e: e.dma_start(
        out=dsc[0:1, 0:32], in_=delta_in.rearrange("(a j) -> a j", a=1)),
        writes=[bu("dsc")], dma=True)
    ev = p.do("vector", lambda e: e.tensor_scalar_mul(
        out=dsc[0:1, 32:64], in0=dsc[0:1, 0:32], scalar1=1.0 / TWO_PI),
        deps=[evd], writes=[bu("dsc")], signal=True)
    ev = p.do("vector", lambda e: e.tensor_copy(out=dsi[0:1, :], in_=dsc[0:1, 32:64]),
              deps=[force(ev)], writes=[bu("dsc")], signal=True)
    ev = p.do("vector", lambda e: e.tensor_copy(out=dsc[0:1, 32:64], in_=dsi[0:1, :]),
              deps=[force(ev)], writes=[bu("dsc")], signal=True)
    ev = p.do("vector", lambda e: e.scalar_tensor_tensor(
        out=dsc[0:1, 64:96], in0=dsc[0:1, 32:64], scalar=-TWO_PI,
        in1=dsc[0:1, 0:32], op0=ALU.mult, op1=ALU.add),
        deps=[force(ev)], writes=[bu("dsc")], signal=True)
    ev1 = p.do("scalar", lambda e: e.activation(dsc[0:1, 96:128], dsc[0:1, 64:96], AF.Sin),
               deps=[ev], writes=[bu("dsc")], signal=True)
    ev2 = p.do("scalar", lambda e: e.activation(dsc[0:1, 128:160], dsc[0:1, 64:96],
                                                AF.Sin, scale=0.5),
               deps=[force(ev1)], writes=[bu("dsc")], signal=True)
    ev2 = p.do("scalar", lambda e: e.activation(dsc[0:1, 160:192], dsc[0:1, 128:160], AF.Square),
               deps=[force(ev2)], writes=[bu("dsc")], signal=True)
    ev2 = p.do("vector", lambda e: e.tensor_scalar(
        out=dsc[0:1, 192:224], in0=dsc[0:1, 160:192], scalar1=-2.0, scalar2=1.0,
        op0=ALU.mult, op1=ALU.add),
        deps=[ev2], writes=[bu("dsc")], signal=True)
    evb = p.do("tensor", lambda e: e.matmul(PS[:, 0:32], C.onesb[0:1, :],
                                            dsc[0:1, 192:224],
                                            start=True, stop=True),
               deps=[evo, ev2], reads=[bu("dsc")], writes=[b_ps[0]], signal=True)
    evb2 = p.do("tensor", lambda e: e.matmul(PS[:, 64:96], C.onesb[0:1, :], dsc[0:1, 96:128],
                                             start=True, stop=True),
                deps=[evo, ev1], reads=[bu("dsc")], writes=[b_ps[0]], signal=True)
    evb = p.do("vector", lambda e: e.tensor_copy(out=cdb[:, 0:32], in_=PS[:, 0:32]),
               deps=[evb], reads=[b_ps[0]], writes=[bu("cdb")], signal=True)
    evb = p.do("vector", lambda e: e.tensor_copy(out=cdb[:, 32:64], in_=PS[:, 64:96]),
               deps=[evb2], reads=[b_ps[0]], writes=[bu("cdb")], signal=True)
    from concourse.bass import broadcast_tensor_aps
    c0v = c0t.rearrange("p (tt j) -> p tt j", j=32)
    s0v = s0t.rearrange("p (tt j) -> p tt j", j=32)
    cdv = cdb[:, 0:32].rearrange("p (tt j) -> p tt j", tt=1)
    sdv = cdb[:, 32:64].rearrange("p (tt j) -> p tt j", tt=1)
    _, cdvb = broadcast_tensor_aps(c0v, cdv)
    _, sdvb = broadcast_tensor_aps(c0v, sdv)
    ctv = ctf[:].rearrange("p (tt j) -> p tt j", j=32)
    stv = stf[:].rearrange("p (tt j) -> p tt j", j=32)
    rv = rsc[:, 0:512].rearrange("p (tt j) -> p tt j", j=32)
    e1 = p.do("vector", lambda e: e.tensor_tensor(out=ctv, in0=c0v, in1=cdvb, op=ALU.mult),
              deps=[evb, evc0], reads=[bu("cdb"), bu("wstage")], writes=[bu("tab")], signal=True)
    e1 = p.do("vector", lambda e: e.tensor_tensor(out=rv, in0=s0v, in1=sdvb, op=ALU.mult),
              deps=[evs0], reads=[bu("cdb"), bu("wstage")], writes=[bu("rsc")], signal=True)
    e1 = p.do("vector", lambda e: e.tensor_tensor(out=ctv, in0=ctv, in1=rv, op=ALU.subtract),
              deps=[force(e1)], reads=[bu("rsc")], writes=[bu("tab")], signal=True)
    e2 = p.do("vector", lambda e: e.tensor_tensor(out=stv, in0=s0v, in1=cdvb, op=ALU.mult),
              reads=[bu("cdb"), bu("wstage")], writes=[bu("tab")], signal=True)
    e2 = p.do("vector", lambda e: e.tensor_tensor(out=rv, in0=c0v, in1=sdvb, op=ALU.mult),
              reads=[bu("cdb"), bu("wstage")], writes=[bu("rsc")], signal=True)
    e2 = p.do("vector", lambda e: e.tensor_tensor(out=stv, in0=stv, in1=rv, op=ALU.add),
              deps=[force(e2)], reads=[bu("rsc")], writes=[bu("tab")], signal=True)

    c0qt = rsc[:, 1024:1280]
    s0qt = rsc[:, 1280:1536]
    evcq = p.do("sync", lambda e: e.dma_start(
        out=c0qt.rearrange("p (tt j) -> p tt j", j=32),
        in_=c0q_in.rearrange("(tt p) j -> p tt j", p=128)), writes=[bu("rsc")], dma=True)
    evsq = p.do("sync", lambda e: e.dma_start(
        out=s0qt.rearrange("p (tt j) -> p tt j", j=32),
        in_=s0q_in.rearrange("(tt p) j -> p tt j", p=128)), writes=[bu("rsc")], dma=True)
    c0qv = c0qt.rearrange("p (tt j) -> p tt j", j=32)
    s0qv = s0qt.rearrange("p (tt j) -> p tt j", j=32)
    _, cdvb2 = broadcast_tensor_aps(c0qv, cdv)
    _, sdvb2 = broadcast_tensor_aps(c0qv, sdv)
    ctqv = ctfq.rearrange("p (tt j) -> p tt j", j=32)
    stqv = stfq.rearrange("p (tt j) -> p tt j", j=32)
    rvq = rsc[:, 0:256].rearrange("p (tt j) -> p tt j", j=32)
    eq1 = p.do("vector", lambda e: e.tensor_tensor(out=ctqv, in0=c0qv, in1=cdvb2, op=ALU.mult),
               deps=[evb, evcq], reads=[bu("cdb"), bu("rsc")], writes=[bu("tab")], signal=True)
    eq1 = p.do("vector", lambda e: e.tensor_tensor(out=rvq, in0=s0qv, in1=sdvb2, op=ALU.mult),
               deps=[evsq], reads=[bu("cdb")], writes=[bu("rsc")], signal=True)
    eq1 = p.do("vector", lambda e: e.tensor_tensor(out=ctqv, in0=ctqv, in1=rvq, op=ALU.subtract),
               deps=[force(eq1)], reads=[bu("rsc")], writes=[bu("tab")], signal=True)
    eq2 = p.do("vector", lambda e: e.tensor_tensor(out=stqv, in0=s0qv, in1=cdvb2, op=ALU.mult),
               reads=[bu("cdb"), bu("rsc")], writes=[bu("tab")], signal=True)
    eq2 = p.do("vector", lambda e: e.tensor_tensor(out=rvq, in0=c0qv, in1=sdvb2, op=ALU.mult),
               reads=[bu("cdb")], writes=[bu("rsc")], signal=True)
    eq2 = p.do("vector", lambda e: e.tensor_tensor(out=stqv, in0=stqv, in1=rvq, op=ALU.add),
               deps=[force(eq2)], reads=[bu("rsc")], writes=[bu("tab")], signal=True)

    for t in range(16):
        evdx = p.do("sync", (lambda t: lambda e: e.dma_start(
            out=xtb[:], in_=xg[t * 128:(t + 1) * 128, :]))(t),
            writes=[bu("xtb")], dma=True)
        er, em = ln_stats(p, xtb[:], qtm[:, 0:512], mv, sc[:, 0:1], sc[:, 1:2],
                          lnbufs, xbuf=bu("xtb"), epsc=C.cst[:, 0:1],
                          epsbuf=bu("C.cst"), deps=[evdx])
        ev = p.do("vector", lambda e: e.tensor_scalar(
            out=wstage[:, 4096:4096 + D], in0=xtb[:], scalar1=sc[:, 1:2],
            scalar2=sc[:, 0:1], op0=ALU.add, op1=ALU.mult),
            deps=[er, em], reads=[bu("xtb"), bu("rs")], writes=[bu("wstage")],
            signal=True)
        for g in range(2):
            pb = 6 + g
            evg = None
            for dc in range(g * 8, g * 8 + 8):
                evg = p.do("tensor", (lambda dc, g, pb: lambda e: e.transpose(
                    bankb(pb)[:, (dc - g * 8) * 128:(dc - g * 8 + 1) * 128],
                    wstage[:, 4096 + dc * 128:4096 + (dc + 1) * 128],
                    C.identb[:]))(dc, g, pb),
                    deps=[evi, ev], reads=[bu("wstage")], writes=[b_ps[pb]],
                    signal=(dc == g * 8 + 7))
            for dc in range(g * 8, g * 8 + 8):
                p.do("vector", (lambda t, dc, pb, g: lambda e: e.tensor_copy(
                    out=xhat[:, dc * S + t * 128:dc * S + (t + 1) * 128],
                    in_=bankb(pb)[:, (dc - g * 8) * 128:(dc - g * 8 + 1) * 128]))(t, dc, pb, g),
                    deps=[evg], reads=[b_ps[pb]], writes=[bu("xhat")],
                    signal=(dc == g * 8 + 7))

    p.full_barrier()
    evw = p.do("gpsimd", lambda e: e.dma_start(
        out=wstage[:, 0:8192].rearrange("p (ic c) -> p ic c", c=512),
        in_=wdkv_in.rearrange("(ic p) c -> p ic c", p=128)),
        writes=[bu("wstage")], dma=True)
    for m in range(4):
        for ts in range(4):
            evm = None
            for ic in range(16):
                evm = p.do("tensor", (lambda m, ts, ic: lambda e: e.matmul(
                    bank(m % 2), wstage[:, ic * 512 + m * 128:ic * 512 + (m + 1) * 128],
                    xhat[:, ic * S + ts * 512:ic * S + (ts + 1) * 512],
                    start=(ic == 0), stop=(ic == 15)))(m, ts, ic),
                    deps=[evw], reads=[bu("wstage"), bu("xhat")], writes=[b_ps[m % 2]],
                    signal=(ic == 15))
            p.do("vector", (lambda m, ts: lambda e: e.tensor_copy(
                out=ckv[:, m * S + ts * 512:m * S + (ts + 1) * 512],
                in_=bank(m % 2)))(m, ts),
                deps=[evm], reads=[b_ps[m % 2]], writes=[bu("ckv")], signal=True)

    evw = p.do("gpsimd", lambda e: e.dma_start(
        out=wstage[:, 0:1024].rearrange("p (ic c) -> p ic c", c=64),
        in_=wkr_in.rearrange("(ic p) c -> p ic c", p=128)),
        writes=[bu("wstage")], dma=True)

    def pope(psum_ap, width, t, outtm, deps0, tabc=None, tabs=None):
        nh = width // 64
        evA = p.do("scalar", (lambda: lambda e: e.activation(
            rsc[:, 0:width], psum_ap, AF.Abs))(),
            deps=deps0, reads=b_ps_cur, writes=[bu("rsc")], signal=True)
        evA = p.do("scalar", (lambda: lambda e: e.activation(
            rsc[:, 0:width], rsc[:, 0:width], AF.Exp, scale=-1.0))(),
            deps=[force(evA)], writes=[bu("rsc")], signal=True)
        evA = p.do("scalar", (lambda: lambda e: e.activation(
            rsc[:, 0:width], rsc[:, 0:width], AF.Ln, bias=1.0))(),
            deps=[force(evA)], writes=[bu("rsc")], signal=True)
        sc2 = rsc[:, width:2 * width]
        evR = p.do("vector", (lambda: lambda e: e.tensor_relu(
            out=sc2, in_=psum_ap))(),
            deps=deps0, reads=b_ps_cur, writes=[bu("rsc")], signal=True)
        evA = p.do("vector", (lambda: lambda e: e.tensor_tensor(
            out=rsc[:, 0:width], in0=rsc[:, 0:width], in1=sc2,
            op=ALU.add))(),
            deps=[evA, force(evR)], writes=[bu("rsc")], signal=True)
        muv = rsc[:, 0:width].rearrange("p (h two j) -> p h two j", two=2, j=32)
        m1, m2 = muv[:, :, 0, :], muv[:, :, 1, :]
        tabc = ctf if tabc is None else tabc
        tabs = stf if tabs is None else tabs
        ct1 = tabc[:, t * 32:(t + 1) * 32].rearrange("p (h j) -> p h j", h=1)
        st1 = tabs[:, t * 32:(t + 1) * 32].rearrange("p (h j) -> p h j", h=1)
        _, ctb = broadcast_tensor_aps(m1, ct1)
        _, stb = broadcast_tensor_aps(m1, st1)
        ov = outtm.rearrange("p (h two j) -> p h two j", two=2, j=32)
        x1, x2 = ov[:, :, 0, :], ov[:, :, 1, :]
        s3 = sc2.rearrange("p (h two j) -> p h two j", two=2, j=32)
        t1, t2 = s3[:, :, 0, :], s3[:, :, 1, :]
        eX = p.do("vector", (lambda: lambda e: e.tensor_tensor(
            out=x1, in0=m1, in1=ctb, op=ALU.mult))(),
            deps=[force(evA)], reads=[bu("tab")], writes=[bu("outtm")], signal=True)
        eX = p.do("vector", (lambda: lambda e: e.tensor_tensor(
            out=x2, in0=m1, in1=stb, op=ALU.mult))(),
            reads=[bu("tab")], writes=[bu("outtm")], signal=True)
        eY = p.do("vector", (lambda: lambda e: e.tensor_tensor(
            out=t1, in0=m2, in1=stb, op=ALU.mult))(),
            reads=[bu("tab")], writes=[bu("rsc")], signal=True)
        eY = p.do("vector", (lambda: lambda e: e.tensor_tensor(
            out=t2, in0=m2, in1=ctb, op=ALU.mult))(),
            reads=[bu("tab")], writes=[bu("rsc")], signal=True)
        eZ = p.do("vector", (lambda: lambda e: e.tensor_tensor(
            out=x1, in0=x1, in1=t1, op=ALU.subtract))(),
            deps=[force(eX), force(eY)], reads=[bu("rsc")], writes=[bu("outtm")], signal=True)
        eZ = p.do("vector", (lambda: lambda e: e.tensor_tensor(
            out=x2, in0=x2, in1=t2, op=ALU.add))(),
            deps=[force(eZ)], reads=[bu("rsc")], writes=[bu("outtm")], signal=True)
        return eZ

    bufs["outtm"] = TBuf("outtm")
    for t in range(16):
        evm = None
        for ic in range(16):
            evm = p.do("tensor", (lambda t, ic: lambda e: e.matmul(
                PS[:, 2048:2112], xhat[:, ic * S + t * 128:ic * S + (t + 1) * 128],
                wstage[:, ic * 64:(ic + 1) * 64],
                start=(ic == 0), stop=(ic == 15)))(t, ic),
                deps=[evw], reads=[bu("wstage"), bu("xhat")], writes=[b_ps[4]],
                signal=(ic == 15))
        b_ps_cur = [b_ps[4]]
        ktm = qtm[:, 512:576]
        eZ = pope(PS[:, 2048:2112], 64, t, ktm, [evm])
        evt = p.do("tensor", (lambda: lambda e: e.transpose(
            bankb(5)[0:64, 0:128], ktm, C.identb[:]))(),
            deps=[evi, eZ], reads=[bu("qtm")], writes=[b_ps[5]], signal=True)
        p.do("vector", (lambda t: lambda e: e.tensor_copy(
            out=krfb[0:64, t * 128:(t + 1) * 128], in_=bankb(5)[0:64, 0:128]))(t),
            deps=[evt], reads=[b_ps[5]], writes=[bu("krfb")], signal=True)
        p.do("vector", (lambda t: lambda e: e.tensor_copy(
            out=krfb[64:128, t * 128:(t + 1) * 128], in_=bankb(5)[0:64, 0:128]))(t),
            deps=[evt], reads=[b_ps[5]], writes=[bu("krfb")], signal=True)

    p.full_barrier()
    for t in range(NT):
        evdx = p.do("sync", (lambda t: lambda e: e.dma_start(
            out=xtb[:], in_=xo[t * 128:(t + 1) * 128, :]))(t),
            writes=[bu("xtb")], dma=True)
        er, em = ln_stats(p, xtb[:], qtm[:, 0:512], mv, sc[:, 0:1], sc[:, 1:2],
                          lnbufs, xbuf=bu("xtb"), epsc=C.cst[:, 0:1],
                          epsbuf=C.b_cst, deps=[evdx])
        ev = p.do("vector", lambda e: e.tensor_scalar(
            out=wstage[:, 4096:4096 + D], in0=xtb[:], scalar1=sc[:, 1:2],
            scalar2=sc[:, 0:1], op0=ALU.add, op1=ALU.mult),
            deps=[er, em], reads=[bu("xtb"), bu("rs")], writes=[bu("wstage")],
            signal=True)
        for g in range(2):
            pb = 6 + g
            evg = None
            for dc in range(g * 8, g * 8 + 8):
                evg = p.do("tensor", (lambda dc, g, pb: lambda e: e.transpose(
                    bankb(pb)[:, (dc - g * 8) * 128:(dc - g * 8 + 1) * 128],
                    wstage[:, 4096 + dc * 128:4096 + (dc + 1) * 128],
                    C.identb[:]))(dc, g, pb),
                    deps=[evi, ev], reads=[bu("wstage")], writes=[b_ps[pb]],
                    signal=(dc == g * 8 + 7))
            for dc in range(g * 8, g * 8 + 8):
                p.do("vector", (lambda t, dc, pb, g: lambda e: e.tensor_copy(
                    out=xhat[:, dc * TOK + t * 128:dc * TOK + (t + 1) * 128],
                    in_=bankb(pb)[:, (dc - g * 8) * 128:(dc - g * 8 + 1) * 128]))(t, dc, pb, g),
                    deps=[evg], reads=[b_ps[pb]], writes=[bu("xhat")],
                    signal=(dc == g * 8 + 7))

    for half in range(2):
        evw = p.do("gpsimd", (lambda half: lambda e: e.dma_start(
            out=wstage[:, 0:8192].rearrange("p (ic c) -> p ic c", c=512),
            in_=wdq_in[:, half * 512:(half + 1) * 512]
            .rearrange("(ic p) c -> p ic c", p=128)))(half),
            writes=[bu("wstage")], dma=True)
        for m in range(4):
            mg = half * 4 + m
            for ts in range(2):
                evm = None
                for ic in range(16):
                    evm = p.do("tensor", (lambda m, ts, ic: lambda e: e.matmul(
                        bank(2 + m % 2), wstage[:, ic * 512 + m * 128:ic * 512 + (m + 1) * 128],
                        xhat[:, ic * TOK + ts * 512:ic * TOK + (ts + 1) * 512],
                        start=(ic == 0), stop=(ic == 15)))(m, ts, ic),
                        deps=[evw], reads=[bu("wstage"), bu("xhat")],
                        writes=[b_ps[2 + m % 2]], signal=(ic == 15))
                p.do("vector", (lambda mg, ts: lambda e: e.tensor_copy(
                    out=cq[:, mg * TOK + ts * 512:mg * TOK + (ts + 1) * 512],
                    in_=bank(2 + mg % 2)))(mg, ts),
                    deps=[evm], reads=[b_ps[2 + mg % 2]], writes=[bu("cq")], signal=True)

    p.full_barrier()
    evw = p.do("gpsimd", lambda e: e.dma_start(
        out=wstage[:, 0:8192].rearrange("p (ic c) -> p ic c", c=1024),
        in_=wqr_in.rearrange("(ic p) c -> p ic c", p=128)),
        writes=[bu("wstage")], dma=True)
    for t in range(NT):
        for oc2 in range(2):
            evm = None
            for ic in range(8):
                evm = p.do("tensor", (lambda t, oc2, ic: lambda e: e.matmul(
                    bank(4 + oc2), cq[:, ic * TOK + t * 128:ic * TOK + (t + 1) * 128],
                    wstage[:, ic * 1024 + oc2 * 512:ic * 1024 + (oc2 + 1) * 512],
                    start=(ic == 0), stop=(ic == 7)))(t, oc2, ic),
                    deps=[evw], reads=[bu("wstage"), bu("cq")], writes=[b_ps[4 + oc2]],
                    signal=(ic == 7))
        b_ps_cur = [b_ps[4], b_ps[5]]
        qr_ps = PS[:, 2048:3072]
        eZ = pope(qr_ps, 1024, t, qtm[:, 0:1024], [p.barrier_event("tensor")],
                  tabc=ctfq, tabs=stfq)
        evt = None
        for rc in range(8):
            evt = p.do("tensor", (lambda rc: lambda e: e.transpose(
                bankb(6)[:, rc * 128:(rc + 1) * 128],
                qtm[:, rc * 128:(rc + 1) * 128], C.identb[:]))(rc),
                deps=[evi, eZ], reads=[bu("qtm")], writes=[b_ps[6]],
                signal=(rc == 7))
        for rc in range(8):
            p.do("vector", (lambda t, rc: lambda e: e.tensor_copy(
                out=qrf[:, rc * TOK + t * 128:rc * TOK + (t + 1) * 128],
                in_=bankb(6)[:, rc * 128:(rc + 1) * 128]))(t, rc),
                deps=[evt], reads=[b_ps[6]], writes=[bu("qrf")],
                signal=(rc == 7))

    p.full_barrier()
    SCL = float(192) ** -0.5
    evmk = [None, None]
    for qc in range(2):
        evmk[qc] = p.do("sync", (lambda qc: lambda e: e.dma_start(
            out=maskb[:, qc * 8192:(qc + 1) * 8192]
            .rearrange("p (tt c) -> p tt c", tt=16),
            in_=mask_in[:, qc * 512:(qc + 1) * 512]
            .rearrange("(tt p) c -> p tt c", p=128)))(qc),
            writes=[bu("maskb")], dma=True)
    for g in range(4):
        evwv = p.do("gpsimd", (lambda g: lambda e: e.dma_start(
            out=wstage[:, 0:2048].rearrange("p (ic c) -> p ic c", c=512),
            in_=wuv_in[:, g * 512:(g + 1) * 512]
            .rearrange("(ic p) c -> p ic c", p=128)))(g),
            writes=[bu("wstage")], dma=True)
        for kt in range(16):
            evm = None
            for ic in range(4):
                evm = p.do("tensor", (lambda g, kt, ic: lambda e: e.matmul(
                    bank(4), ckv[:, ic * S + kt * 128:ic * S + (kt + 1) * 128],
                    wstage[:, ic * 512:(ic + 1) * 512],
                    start=(ic == 0), stop=(ic == 3)))(g, kt, ic),
                    deps=[evwv], reads=[bu("wstage"), bu("ckv")], writes=[b_ps[4]],
                    signal=(ic == 3))
            p.do("vector", (lambda kt: lambda e: e.tensor_copy(
                out=vg[:, kt * 512:(kt + 1) * 512], in_=bank(4)))(kt),
                deps=[evm], reads=[b_ps[4]], writes=[bu("vg")], signal=True)
        for hl in range(4):
            h = g * 4 + hl
            ksl = 0
            evwk = p.do("gpsimd", (lambda h: lambda e: e.dma_start(
                out=wstage[:, 2048:2560].rearrange("p (ic c) -> p ic c", c=128),
                in_=wuk_in[:, h * 128:(h + 1) * 128]
                .rearrange("(ic p) c -> p ic c", p=128)))(h),
                writes=[bu("wstage")], dma=True)
            for ts in range(4):
                evm = None
                for ic in range(4):
                    evm = p.do("tensor", (lambda h, ts, ic: lambda e: e.matmul(
                        bank(5), wstage[:, 2048 + ic * 128:2048 + (ic + 1) * 128],
                        ckv[:, ic * S + ts * 512:ic * S + (ts + 1) * 512],
                        start=(ic == 0), stop=(ic == 3)))(h, ts, ic),
                        deps=[evwk], reads=[bu("wstage"), bu("ckv")], writes=[b_ps[5]],
                        signal=(ic == 3))
                p.do("vector", (lambda ksl, ts: lambda e: e.tensor_copy(
                    out=kcb[:, ksl * S + ts * 512:ksl * S + (ts + 1) * 512],
                    in_=bank(5)))(ksl, ts),
                    deps=[evm], reads=[b_ps[5]], writes=[bu("kcb")], signal=True)
            evwq = p.do("gpsimd", (lambda h: lambda e: e.dma_start(
                out=wstage[:, 4096:5120].rearrange("p (ic c) -> p ic c", c=128),
                in_=wuq_in[:, h * 128:(h + 1) * 128]
                .rearrange("(ic p) c -> p ic c", p=128)))(h),
                writes=[bu("wstage")], dma=True)
            for ts in range(2):
                evm = None
                for ic in range(8):
                    evm = p.do("tensor", (lambda h, ts, ic: lambda e: e.matmul(
                        bank(6), wstage[:, 4096 + ic * 128:4096 + (ic + 1) * 128],
                        cq[:, ic * TOK + ts * 512:ic * TOK + (ts + 1) * 512],
                        start=(ic == 0), stop=(ic == 7)))(h, ts, ic),
                        deps=[evwq], reads=[bu("wstage"), bu("cq")], writes=[b_ps[6]],
                        signal=(ic == 7))
                p.do("vector", (lambda ksl, ts: lambda e: e.tensor_copy(
                    out=qcb[:, ksl * TOK + ts * 512:ksl * TOK + (ts + 1) * 512],
                    in_=bank(6)))(ksl, ts),
                    deps=[evm], reads=[b_ps[6]], writes=[bu("qcb")], signal=True)
            for qc in range(2):
                for kt in range(16):
                    pbs = kt % 2
                    ev1 = p.do("tensor", (lambda ksl, kt, qc, pbs: lambda e: e.matmul(
                        bank(pbs), kcb[:, ksl * S + kt * 128:ksl * S + (kt + 1) * 128],
                        qcb[:, ksl * TOK + qc * 512:ksl * TOK + (qc + 1) * 512],
                        start=True, stop=False))(ksl, kt, qc, pbs),
                        reads=[bu("kcb"), bu("qcb")], writes=[b_ps[pbs]],
                        signal=True)
                    ev1 = p.do("tensor", (lambda h, kt, qc, pbs: lambda e: e.matmul(
                        bank(pbs), krfb[(h % 2) * 64:(h % 2) * 64 + 64,
                                        kt * 128:(kt + 1) * 128],
                        qrf[(h % 2) * 64:(h % 2) * 64 + 64,
                            (h // 2) * TOK + qc * 512:(h // 2) * TOK + (qc + 1) * 512],
                        start=False, stop=True,
                        tile_position=((h % 2) * 64, 0)))(h, kt, qc, pbs),
                        reads=[bu("krfb"), bu("qrf")], writes=[b_ps[pbs]], signal=True)
                    ev2 = p.do("scalar", (lambda pbs, kt: lambda e: e.activation(
                        ptl[:, (kt % 2) * 512:(kt % 2 + 1) * 512], bank(pbs),
                        AF.Exp, scale=SCL))(pbs, kt),
                        deps=[ev1], reads=[b_ps[pbs]], writes=[bu("ptl")], signal=True)
                    ev3 = p.do("vector", (lambda kt, qc: lambda e: e.tensor_mul(
                        out=ptl[:, (kt % 2) * 512:(kt % 2 + 1) * 512],
                        in0=ptl[:, (kt % 2) * 512:(kt % 2 + 1) * 512],
                        in1=maskb[:, qc * 8192 + kt * 512:qc * 8192 + (kt + 1) * 512]))(kt, qc),
                        deps=[ev2, evmk[qc]], reads=[bu("maskb")], writes=[bu("ptl")],
                        signal=True)
                    pav = 2 + (h % 2)
                    p.do("tensor", (lambda kt, hl, pav: lambda e: e.matmul(
                        bank(pav), vg[:, kt * 512 + hl * 128:kt * 512 + (hl + 1) * 128],
                        ptl[:, (kt % 2) * 512:(kt % 2 + 1) * 512],
                        start=(kt == 0), stop=(kt == 15)))(kt, hl, pav),
                        deps=[ev3], reads=[bu("vg"), bu("ptl")],
                        writes=[b_ps[pav]], signal=True)
                    p.do("tensor", (lambda kt: lambda e: e.matmul(
                        PS[0:1, 3584:4096], C.onebf[:],
                        ptl[:, (kt % 2) * 512:(kt % 2 + 1) * 512],
                        start=(kt == 0), stop=(kt == 15)))(kt),
                        deps=[ev3], reads=[bu("ones"), bu("ptl")],
                        writes=[b_ps[7]], signal=True)
                pav = 2 + (h % 2)
                evc = p.do("vector", lambda e: e.tensor_copy(
                    out=rb[0:1, 0:512], in_=PS[0:1, 3584:4096]),
                    reads=[b_ps[7]], writes=[bu("rb")], signal=True)
                evc = p.do("vector", lambda e: e.reciprocal(
                    out=rb[0:1, 0:512], in_=rb[0:1, 0:512]),
                    deps=[force(evc)], writes=[bu("rb")], signal=True)
                evb = p.do("tensor", lambda e: e.matmul(
                    bank(6), C.onesb[0:1, :], rb[0:1, 0:512], start=True, stop=True),
                    deps=[evc, evo], reads=[bu("rb")], writes=[b_ps[6]], signal=True)
                evb = p.do("vector", lambda e: e.tensor_copy(
                    out=rb[:, 0:512], in_=bank(6)),
                    deps=[evb], reads=[b_ps[6]], writes=[bu("rb")], signal=True)
                p.do("vector", (lambda h, qc, pav: lambda e: e.tensor_tensor(
                    out=attn_fm[:, h * TOK + qc * 512:h * TOK + (qc + 1) * 512],
                    in0=bank(pav), in1=rb[:, 0:512], op=ALU.mult))(h, qc, pav),
                    deps=[evb], reads=[b_ps[pav], bu("rb")], writes=[bu("attn")],
                    signal=True)

    p.full_barrier()
    for oc in range(4):
        evm_all = {}
        for ic in range(16):
            wsl = ic % 2
            evwo = p.do("gpsimd", (lambda oc, ic, wsl: lambda e: e.dma_start(
                out=wstage[:, wsl * 512:(wsl + 1) * 512],
                in_=wout_in[ic * 128:(ic + 1) * 128,
                            oc * 512:(oc + 1) * 512]))(oc, ic, wsl),
                writes=[bu("wstage")], dma=True)
            for t in range(NT):
                evm = p.do("tensor", (lambda ic, t, wsl: lambda e: e.matmul(
                    bank(t), attn_fm[:, ic * TOK + t * 128:ic * TOK + (t + 1) * 128],
                    wstage[:, wsl * 512:(wsl + 1) * 512],
                    start=(ic == 0), stop=(ic == 15)))(ic, t, wsl),
                    deps=[evwo], reads=[bu("wstage"), bu("attn")],
                    writes=[b_ps[t]], signal=(ic == 15 or t == NT - 1))
                if ic == 15:
                    evm_all[t] = evm
        for t in range(NT):
            p.do("vector", (lambda t, oc: lambda e: e.tensor_copy(
                out=ybf[:, t * D + oc * 512:t * D + (oc + 1) * 512],
                in_=bank(t)))(t, oc),
                deps=[evm_all[t]], reads=[b_ps[t]], writes=[bu("ybf")],
                signal=True)

    p.full_barrier()
    for t in range(NT):
        yt = ybf[:, t * D:(t + 1) * D]
        evdx = p.do("sync", (lambda t: lambda e: e.dma_start(
            out=xtb[:], in_=xo[t * 128:(t + 1) * 128, :]))(t),
            writes=[bu("xtb")], dma=True)
        er, em = ln_stats(p, yt, qtm[:, 0:512], mv, sc[:, 0:1], sc[:, 1:2],
                          lnbufs, xbuf=bu("ybf"), epsc=C.cst[:, 0:1],
                          epsbuf=bu("C.cst"))
        ev = p.do("vector", (lambda yt: lambda e: e.tensor_scalar(
            out=yw[:], in0=yt, scalar1=sc[:, 1:2], scalar2=sc[:, 0:1],
            op0=ALU.add, op1=ALU.mult))(yt),
            deps=[er, em], reads=[bu("ybf"), bu("rs")], writes=[bu("yw")],
            signal=True)
        ev = p.do("vector", lambda e: e.tensor_add(out=yw[:], in0=yw[:], in1=xtb[:]),
                  deps=[evdx, force(ev)], reads=[bu("xtb")], writes=[bu("yw")],
                  signal=True)
        p.do("sync", (lambda t: lambda e: e.dma_start(
            out=out[t * 128:(t + 1) * 128, :], in_=yw[:]))(t),
            deps=[ev], reads=[bu("yw")], dma=True)

    fin = p.barrier_event("sync")
    p.do("sync", lambda e: None, deps=[fin])



class Carver:

    def __init__(self, ap_bf16, ncols_bf16):
        self.ap = ap_bf16
        self.n = ncols_bf16
        self.off = 0

    def reset(self):
        self.off = 0

    def take(self, ncols, dtype):
        nb = ncols * (4 if dtype == F32 else (4 if dtype == mybir.dt.int32 else 2))
        start = (self.off + 31) & ~31
        self.off = start + nb
        assert self.off <= 2 * self.n, f"arena overflow {self.off} > {2 * self.n}"
        view = self.ap[:, start // 2:(start + nb) // 2]
        if dtype != BF16:
            view = view.bitcast(dtype)
        return view


class _C:
    pass


def build_fused():
    nc = bass.Bass("TRN2", target_bir_lowering=False)
    sems, _ctx = fnew(nc)
    p = Prog(nc, sems)

    T = nc.dram_tensor
    x0 = T("x0", [TOK, D], F32, kind="ExternalInput").ap()
    l0a = {k: T("l0a_" + k, sh, F32, kind="ExternalInput").ap() for k, sh in [
        ("wdkv", [D, 512]), ("wuk", [512, 2048]), ("wuv", [512, 2048]),
        ("wdq", [D, 1024]), ("wuq", [1024, 2048]), ("wqr", [1024, 1024]),
        ("wkr", [D, 64]), ("wout", [D, D]), ("delta", [32]),
        ("c0", [S, 32]), ("s0", [S, 32]), ("c0q", [TOK, 32]), ("s0q", [TOK, 32])]}
    l0f = {k: T("l0f_" + k, sh, F32, kind="ExternalInput").ap() for k, sh in [
        ("wg", [D, I0]), ("wu", [D, I0]), ("wd", [I0, D])]}
    l1a = {k: T("l1a_" + k, sh, F32, kind="ExternalInput").ap() for k, sh in [
        ("qw", [2048, 2048]), ("krw", [256, 2048]), ("dww", [256, 2048]),
        ("upk", [256, 256]), ("upv", [512, 256]), ("ow", [2048, 2048]),
        ("ckc", [S, 16]), ("cks", [S, 16]), ("cqc", [TOK, 32]), ("cqs", [TOK, 32])]}
    l1f = {k: T("l1f_" + k, sh, F32, kind="ExternalInput").ap() for k, sh in [
        ("w1", [H1, D]), ("w3", [H1, D]), ("w2", [D, H1])]}
    wabs = T("wabs", [128, 3 * (FLATC // 8)], F32, kind="ExternalInput").ap()
    mask0 = T("mask0", [S, TOK], BF16, kind="ExternalInput").ap()
    mask1 = T("mask1", [S, TOK], BF16, kind="ExternalInput").ap()
    ident_in = T("ident", [128, 128], BF16, kind="ExternalInput").ap()
    ones_in = T("ones", [128, 128], F32, kind="ExternalInput").ap()
    out = T("out", [TOK, D], F32, kind="ExternalOutput").ap()


    xg = T("xg", [S, D], F32).ap()
    ccin = T("ccin", [TOK, D], F32)
    x1own = T("x1own", [TOK, D], F32).ap()
    x2own = T("x2own", [TOK, D], F32).ap()
    x3own = T("x3own", [TOK, D], F32).ap()
    u2t_dram = T("u2t", [16, NH1, 128, 128], BF16).ap()
    crin_t = T("crin", [1, 128], F32)
    crout = T("crout", [1, 128], F32).ap()
    xgh = T("xg2", [S, D], F32) if False else None

    import contextlib
    with contextlib.ExitStack() as stk:
        arena_h = stk.enter_context(nc.sbuf_tensor("arena", [128, 105984], BF16))
        identb = stk.enter_context(nc.sbuf_tensor("identc", [128, 128], BF16))
        onesb = stk.enter_context(nc.sbuf_tensor("onesc", [128, 128], F32))
        onebf = stk.enter_context(nc.sbuf_tensor("onebfc", [128, 1], BF16))
        cstc = stk.enter_context(nc.sbuf_tensor("cstc", [128, 2], F32))
        PS = stk.enter_context(nc.psum_tensor("PSf", [128, 4096], F32))
        blk = stk.enter_context(nc.Block())

        C = _C()
        C.nc, C.p, C.PS = nc, p, PS
        C.identb, C.onesb, C.onebf, C.cst = identb[:], onesb[:], onebf[:], cstc[:]
        C.b_cst = TBuf("cstC")
        C.evi = p.do("sync", lambda e: e.dma_start(out=identb[:], in_=ident_in), dma=True)
        C.evo = p.do("sync", lambda e: e.dma_start(out=onesb[:], in_=ones_in), dma=True)
        p.do("vector", lambda e: e.memset(cstc[:, 0:1], EPS), writes=[C.b_cst], signal=True)
        p.do("vector", lambda e: e.memset(onebf[:], 1.0), writes=[C.b_cst], signal=True)

        A = Carver(arena_h[:], 105984)

        csems = [stk.enter_context(nc.semaphore(f"CSEM{i}")) for i in range(3)]
        gcount = [0]

        def gather(src_ap):
            ev = p.do("sync", (lambda src_ap: lambda e: e.dma_start(
                out=ccin[:, :], in_=src_ap))(src_ap),
                deps=[p.barrier_event("sync"), p.barrier_event("gpsimd")], dma=True)
            cs = csems[gcount[0]]
            gcount[0] += 1
            p.do("gpsimd", (lambda cs: lambda e: e.collective_compute(
                "AllGather", ALU.bypass,
                replica_groups=[[0, 1], [2, 3], [4, 5], [6, 7]],
                ins=[ccin.ap().opt()], outs=[xg.opt()]).then_inc(cs, 1))(cs),
                deps=[ev])
            for engn in Prog.ENGINES:
                p.do(engn, (lambda cs: lambda e: e.wait_ge(cs, 1))(cs))
            p.full_barrier()

        gather(x0)
        emit_l0attn(C, A, xg, x0, l0a["wdkv"], l0a["wuk"], l0a["wuv"],
                    l0a["wdq"], l0a["wuq"], l0a["wqr"], l0a["wkr"], l0a["wout"],
                    l0a["delta"], mask0, l0a["c0"], l0a["s0"], l0a["c0q"],
                    l0a["s0q"], None, None, x1own)
        p.full_barrier()
        A.reset()
        emit_l0ffn(C, A, x1own, l0f["wg"], l0f["wu"], l0f["wd"], None, x2own)
        p.full_barrier()
        A.reset()
        gather(x2own)
        emit_l1attn(C, A, xg, x2own, l1a["qw"], l1a["krw"], l1a["dww"],
                    l1a["upk"], l1a["upv"], l1a["ow"], mask1, l1a["ckc"],
                    l1a["cks"], l1a["cqc"], l1a["cqs"], None, None, x3own)
        p.full_barrier()
        A.reset()
        emit_l1ffn(C, A, x3own, l1f["w1"], l1f["w3"], l1f["w2"], None, None,
                   out, u2t_dram, wabs=wabs, crin_t=crin_t, crout=crout,
                   rsem=csems[2])
        p.full_barrier()
        p.do("sync", lambda e: None, deps=[p.barrier_event("sync")])
        p.emit(blk)
    return nc


_FUSED_NC = None


def kernel(**inp):
    global _FUSED_NC
    if _FUSED_NC is None:
        _FUSED_NC = build_fused()
    x = np.ascontiguousarray(inp["L0_attn_x"], dtype=np.float32)
    cst0 = _l0a_consts()
    cst1 = _l1a_consts()
    karange = np.arange(S)
    SHF = FLATC // 8
    wfls = [np.ascontiguousarray(inp[k], dtype=np.float32).reshape(128, FLATC)
            for k in ("L1_ffn_w1", "L1_ffn_w3", "L1_ffn_w2")]
    in_maps = []
    for c in range(NCORES):
        sq_, hf = c // 2, c % 2
        qidx = np.arange(hf * TOK, (hf + 1) * TOK)
        mask = np.ascontiguousarray(
            (karange[:, None] <= qidx[None, :]).astype(ml_dtypes.bfloat16))
        m = {
            "x0": np.ascontiguousarray(x[sq_, hf * TOK:(hf + 1) * TOK, :]),
            "l0a_wdkv": inp["L0_attn_w_dkv"], "l0a_wuk": inp["L0_attn_w_uk"],
            "l0a_wuv": inp["L0_attn_w_uv"], "l0a_wdq": inp["L0_attn_w_dq"],
            "l0a_wuq": inp["L0_attn_w_uq"], "l0a_wqr": inp["L0_attn_w_qr"],
            "l0a_wkr": inp["L0_attn_w_kr"], "l0a_wout": inp["L0_attn_w_out"],
            "l0a_delta": inp["L0_attn_delta"],
            "l0a_c0": cst0["c0"], "l0a_s0": cst0["s0"],
            "l0a_c0q": np.ascontiguousarray(cst0["c0"][qidx]),
            "l0a_s0q": np.ascontiguousarray(cst0["s0"][qidx]),
            "l0f_wg": inp["L0_ffn_w_gate"], "l0f_wu": inp["L0_ffn_w_up"],
            "l0f_wd": inp["L0_ffn_w_down"],
            "l1a_qw": inp["L1_attn_q_w"], "l1a_krw": inp["L1_attn_kr_w"],
            "l1a_dww": inp["L1_attn_down_w"], "l1a_upk": inp["L1_attn_upk_w"],
            "l1a_upv": inp["L1_attn_upv_w"], "l1a_ow": inp["L1_attn_o_w"],
            "l1a_ckc": cst1["ckc"], "l1a_cks": cst1["cks"],
            "l1a_cqc": np.ascontiguousarray(cst1["cqc"][qidx]),
            "l1a_cqs": np.ascontiguousarray(cst1["cqs"][qidx]),
            "l1f_w1": inp["L1_ffn_w1"], "l1f_w3": inp["L1_ffn_w3"],
            "l1f_w2": inp["L1_ffn_w2"],
            "mask0": mask, "mask1": mask,
            "ident": _IDENT, "ones": _ONES,
            "wabs": np.ascontiguousarray(np.concatenate(
                [wf[:, c * SHF:(c + 1) * SHF] for wf in wfls], axis=1)),
        }
        in_maps.append(m)
    res = run_bass_kernel_spmd(_FUSED_NC, in_maps, core_ids=list(range(NCORES)))
    global _LASTF
    _LASTF = res
    o = np.empty((B, S, D), dtype=np.float32)
    for c in range(NCORES):
        sq_, hf = c // 2, c % 2
        o[sq_, hf * TOK:(hf + 1) * TOK, :] = res.results[c]["out"]
    return o

